# version 02
# baseline (speedup 1.0000x reference)
import numpy as np
import ml_dtypes
from contextlib import ExitStack

import concourse.bass as bass
import concourse.mybir as mybir
from concourse.ap import AP
from concourse.bass_utils import run_bass_kernel_spmd

F32 = mybir.dt.float32
BF16 = mybir.dt.bfloat16
AF = mybir.ActivationFunctionType
ALU = mybir.AluOpType
AX = mybir.AxisListType

NCORE = 8
B, S, D = 4, 4096, 2048
SOWN = S // NCORE
TOWN = B * SOWN
KT = D // 128
EPS = 1e-5

BFD = ml_dtypes.bfloat16


class Sem:

    def __init__(self, handle):
        self.h = handle
        self.n = 0

    def inc(self, ins, k=1):
        ins.then_inc(self.h, k)
        self.n += k
        return self.n


class Env:

    def __init__(self, name):
        self.nc = bass.Bass("TRN2", target_bir_lowering=False, detect_race_conditions=False)
        self.ctx = ExitStack()
        self.block = None
        self.sems = {}

    def sem(self, name):
        if name not in self.sems:
            self.sems[name] = Sem(self.ctx.enter_context(self.nc.semaphore(name)))
        return self.sems[name]

    def sbuf(self, name, shape, dtype):
        return self.ctx.enter_context(self.nc.sbuf_tensor(name, list(shape), dtype))

    def psum(self, name, shape, dtype=F32):
        return self.ctx.enter_context(self.nc.psum_tensor(name, list(shape), dtype))

    def open_block(self):
        self.block = self.ctx.enter_context(self.nc.Block())

    def set_fence(self, out_ap, lhs_ap, rhs_ap):
        self._f_out, self._f_lhs, self._f_rhs = out_ap, lhs_ap, rhs_ap

    def set_junk(self, junk):
        self._junk = junk

    FENCE_N = 4

    def vfence(self, v):
        v.memset(self._junk[0:1, 0:512], 0.0)
        for _ in range(self.FENCE_N):
            ins = v.tensor_copy(self._junk[0:1, 512:1024], self._junk[0:1, 0:512])
        return ins

    def sfence(self, sc):
        sc.memzero(self._junk[0:1, 0:512])
        for _ in range(self.FENCE_N):
            ins = sc.copy(self._junk[0:1, 512:1024], self._junk[0:1, 0:512])
        return ins

    def vspace(self, v):
        v.memset(self._junk[0:1, 0:64], 0.0)
        return v.tensor_copy(self._junk[0:1, 64:128], self._junk[0:1, 0:64])

    def gfence(self, g):
        g.memset(self._junk[0:1, 0:512], 0.0)
        for _ in range(self.FENCE_N):
            ins = g.tensor_copy(self._junk[0:1, 512:1024], self._junk[0:1, 0:512])
        return ins

    def close(self):
        self.ctx.close()
        return self.nc



def emit_consts(env, ones_b, ones_f):
    ini = env.sem("ini")

    @env.block.gpsimd
    def _(g):
        g.memset(ones_b[:, :], 1.0)
        g.memset(ones_f[:, :], 1.0)
        ini.inc(env.gfence(g))
        ini.inc(g.engine_nop())


def emit_ln(env, src_view, dst_view, bufs, T, CH=512, tag=""):
    nchunk = T // CH
    xb, sq = bufs["xb"], bufs["sq"]
    row, rB, mB, tmp = bufs["row"], bufs["rB"], bufs["mB"], bufs["tmp"]
    ones_b, ones_f = bufs["ones_b"], bufs["ones_f"]
    psA, psB, psR, psM = bufs["psA"], bufs["psB"], bufs["psR"], bufs["psM"]

    ldp = [env.sem("ln_ld0" + tag), env.sem("ln_ld1" + tag)]
    cv = env.sem("ln_cv" + tag)
    st1 = env.sem("ln_st1" + tag)
    rw = env.sem("ln_rw" + tag)
    ev = env.sem("ln_ev" + tag)
    nm = env.sem("ln_nm" + tag)
    sop = [env.sem("ln_so0" + tag), env.sem("ln_so1" + tag)]
    ini = env.sem("ini")

    ld0 = [ldp[0].n, ldp[1].n]
    so0 = [sop[0].n, sop[1].n]
    cv0, st10, rw0, ev0, nm0 = cv.n, st1.n, rw.n, ev.n, nm.n

    @env.block.gpsimd
    def _(g):
        for ci in range(nchunk):
            bsl = (ci % 2) * KT * CH
            if ci >= 2:
                g.wait_ge(nm.h, nm0 + KT * (ci - 1))
            for k in range(KT):
                ldp[ci % 2].inc(g.dma_start(out=xb[:, bsl + k * CH:bsl + (k + 1) * CH],
                                            in_=src_view[k][:, ci * CH:(ci + 1) * CH]), 16)

    @env.block.scalar
    def _(sc):
        for ci in range(nchunk):
            bsl = (ci % 2) * KT * CH
            sc.wait_ge(ldp[ci % 2].h, ld0[ci % 2] + 16 * KT * (ci // 2 + 1))
            if ci >= 1:
                sc.wait_ge(st1.h, st10 + 2 * (ci - 1) + 1)
            for k in range(KT):
                sc.activation(sq[:, k * CH:(k + 1) * CH],
                              xb[:, bsl + k * CH:bsl + (k + 1) * CH], AF.Square)
            cv.inc(env.sfence(sc))
            sc.wait_ge(rw.h, rw0 + 3 * ci + 1)
            env.sfence(sc)
            sc.activation(row[0:1, 3 * CH:4 * CH], row[0:1, 2 * CH:3 * CH], AF.Sqrt)
            rw.inc(env.sfence(sc))

    @env.block.vector
    def _(v):
        for ci in range(nchunk):
            bsl = (ci % 2) * KT * CH
            v.wait_ge(st1.h, st10 + 2 * ci + 1)
            env.vfence(v)
            v.tensor_scalar(row[0:1, 0:CH], psA[0:1, :], 1.0 / D, None, ALU.mult)
            v.tensor_scalar(row[0:1, CH:2 * CH], psB[0:1, :], 1.0 / D, None, ALU.mult)
            v.tensor_mul(row[0:1, 4 * CH:5 * CH], row[0:1, 0:CH], row[0:1, 0:CH])
            v.tensor_sub(row[0:1, 2 * CH:3 * CH], row[0:1, CH:2 * CH], row[0:1, 4 * CH:5 * CH])
            v.tensor_scalar(row[0:1, 2 * CH:3 * CH], row[0:1, 2 * CH:3 * CH], EPS, None, ALU.add)
            ins = v.reciprocal(row[0:1, 2 * CH:3 * CH], row[0:1, 2 * CH:3 * CH])
            rw.inc(ins)
            v.wait_ge(rw.h, rw0 + 3 * ci + 2)
            v.tensor_mul(row[0:1, 4 * CH:5 * CH], row[0:1, 0:CH], row[0:1, 3 * CH:4 * CH])
            rw.inc(env.vfence(v))
            v.wait_ge(st1.h, st10 + 2 * ci + 2)
            env.vfence(v)
            ins = v.tensor_copy(rB[:, :], psR[:, :])
            ev.inc(ins)
            ins = v.tensor_copy(mB[:, :], psM[:, :])
            ev.inc(ins)
            if ci >= 2:
                v.wait_ge(sop[ci % 2].h, so0[ci % 2] + 16 * KT * (ci // 2))
            for k in range(KT):
                v.tensor_mul(tmp[:, 0:CH], xb[:, bsl + k * CH:bsl + (k + 1) * CH], rB[:, :])
                ins = v.tensor_sub(bufs["xn"][:, bsl + k * CH:bsl + (k + 1) * CH],
                                   tmp[:, 0:CH], mB[:, :])
                nm.inc(ins)

    @env.block.tensor
    def _(t):
        t.wait_ge(ini.h, 2)
        for ci in range(nchunk):
            bsl = (ci % 2) * KT * CH
            t.wait_ge(ldp[ci % 2].h, ld0[ci % 2] + 16 * KT * (ci // 2 + 1))
            t.wait_ge(cv.h, cv0 + ci + 1)
            if ci >= 1:
                t.wait_ge(rw.h, rw0 + 3 * (ci - 1) + 1)
            for k in range(KT):
                t.matmul(psA[0:1, 0:CH], ones_b[:, :], xb[:, bsl + k * CH:bsl + (k + 1) * CH],
                         start=(k == 0), stop=(k == KT - 1))
            for k in range(KT):
                ins = t.matmul(psB[0:1, 0:CH], ones_b[:, :], sq[:, k * CH:(k + 1) * CH],
                               start=(k == 0), stop=(k == KT - 1))
            st1.inc(ins)
            t.wait_ge(rw.h, rw0 + 3 * ci + 3)
            if ci >= 1:
                t.wait_ge(ev.h, ev0 + 2 * ci)
            t.matmul(psR[:, 0:CH], ones_f[:, :], row[0:1, 3 * CH:4 * CH], start=True, stop=True)
            ins = t.matmul(psM[:, 0:CH], ones_f[:, :], row[0:1, 4 * CH:5 * CH], start=True, stop=True)
            st1.inc(ins)

    @env.block.sync
    def _(s):
        for ci in range(nchunk):
            bsl = (ci % 2) * KT * CH
            s.wait_ge(nm.h, nm0 + KT * (ci + 1))
            for k in range(KT):
                sop[ci % 2].inc(s.dma_start(out=dst_view[k][:, ci * CH:(ci + 1) * CH],
                                            in_=bufs["xn"][:, bsl + k * CH:bsl + (k + 1) * CH]), 16)


def ln_bufs(env, CH=512, tag=""):
    return dict(
        xb=env.sbuf("ln_xb" + tag, [128, 2 * KT * CH], BF16),
        sq=env.sbuf("ln_sq" + tag, [128, KT * CH], BF16),
        row=env.sbuf("ln_row" + tag, [1, 6 * CH], F32),
        rB=env.sbuf("ln_rB" + tag, [128, CH], BF16),
        mB=env.sbuf("ln_mB" + tag, [128, CH], BF16),
        tmp=env.sbuf("ln_tmp" + tag, [128, CH], BF16),
        xn=env.sbuf("ln_xn" + tag, [128, 2 * KT * CH], BF16),
        psA=env.psum("ln_psA" + tag, [1, CH]),
        psB=env.psum("ln_psB" + tag, [1, CH]),
        psR=env.psum("ln_psR" + tag, [128, CH]),
        psM=env.psum("ln_psM" + tag, [128, CH]),
    )


def dram_tiles(ap_2d):
    v = ap_2d.rearrange("(k p) t -> k p t", p=128)
    return [v[k] for k in range(KT)]



def build_launch_A():
    env = Env("A")
    nc = env.nc
    x = nc.dram_tensor("x", [D, TOWN], F32, kind="ExternalInput").ap()
    xn = nc.dram_tensor("xn", [D, TOWN], BF16, kind="ExternalOutput").ap()
    bufs = ln_bufs(env)
    bufs["ones_b"] = env.sbuf("ones_b", [128, 1], BF16)
    bufs["ones_f"] = env.sbuf("ones_f", [1, 128], F32)
    env.set_junk(env.sbuf("junk", [1, 1024], F32))
    env.open_block()
    emit_consts(env, bufs["ones_b"], bufs["ones_f"])
    emit_ln(env, dram_tiles(x), dram_tiles(xn), bufs, TOWN)
    so0, so1 = env.sem("ln_so0"), env.sem("ln_so1")

    @env.block.sync
    def _(s):
        s.wait_ge(so0.h, so0.n)
        s.wait_ge(so1.h, so1.n)

    return env.close()




def build_launch_B(S_=S, dbg=None, no_inject=False):
    env = Env("B")
    nc = env.nc
    NQT = S_ // 128
    NKC = S_ // 512
    NNQ = S_ // 512
    BT = B * S_
    NDSL = NKC + 4

    xn0 = nc.dram_tensor("xn0", [D, BT], BF16, kind="ExternalInput").ap()
    wq = nc.dram_tensor("wq", [D, 256], F32, kind="ExternalInput").ap()
    wk = nc.dram_tensor("wk", [D, 256], F32, kind="ExternalInput").ap()
    wv = nc.dram_tensor("wv", [D, 256], F32, kind="ExternalInput").ap()
    cst = nc.dram_tensor("cst", [128, S_], F32, kind="ExternalInput").ap()
    snt = nc.dram_tensor("snt", [128, S_], F32, kind="ExternalInput").ap()
    brev = nc.dram_tensor("brev", [4, 2432], F32, kind="ExternalInput").ap()
    bankt = nc.dram_tensor("bankt", [48, 128, 512], F32, kind="ExternalInput").ap()
    banktv = [bankt[i] for i in range(48)]
    offs = nc.dram_tensor("offs", [128, 4 * (S_ // 128)], F32, kind="ExternalInput").ap()
    hsel = nc.dram_tensor("hsel", [128, 4], F32, kind="ExternalInput").ap()
    jmat = nc.dram_tensor("jmat", [128, 128], F32, kind="ExternalInput").ap()
    idm = nc.dram_tensor("idm", [128, 128], F32, kind="ExternalInput").ap()
    oslc = nc.dram_tensor("oslc", [2, B, 128, (S_ // 128) * 128], BF16, kind="ExternalOutput").ap()
    oslc_v = oslc
    if dbg is not None:
        dbg_nm = nc.dram_tensor("dbg_nm", [128, 3 * 4 * (S_ // 128)], F32, kind="ExternalOutput").ap()
        dbg_dc = nc.dram_tensor("dbg_dc", [128, 8 * (S_ // 128)], F32, kind="ExternalOutput").ap()
        dbg_ds = nc.dram_tensor("dbg_ds", [(S_ // 128), 128, 2 * (S_ // 512 + 4)], F32, kind="ExternalOutput").ap()
        dbg_pq = nc.dram_tensor("dbg_pq", [(S_ // 128), 128, 512], BF16, kind="ExternalOutput").ap()
        dbg_pt = nc.dram_tensor("dbg_pt", [2, 128, 512], BF16, kind="ExternalOutput").ap()
        dbg_av = nc.dram_tensor("dbg_av", [128, 4], F32, kind="ExternalOutput").ap()
        dbg_p0 = nc.dram_tensor("dbg_p0", [128, 512], BF16, kind="ExternalOutput").ap()
        dbg_qb = nc.dram_tensor("dbg_qb", [128, 768], BF16, kind="ExternalOutput").ap()
        dbg_qb2 = nc.dram_tensor("dbg_qb2", [128, 256], BF16, kind="ExternalOutput").ap()

    CQ = 512
    xn_st = env.sbuf("xn_st", [128, 2 * KT * CQ], BF16)
    cs_st = env.sbuf("cs_st", [128, 2 * 2 * CQ], F32)
    rp_scr = env.sbuf("rp_scr", [128, 2 * CQ], F32)
    q_b = env.sbuf("q_b", [128, 2 * S_], BF16)
    k_b = env.sbuf("k_b", [128, 2 * S_], BF16)
    v_b = env.sbuf("v_b", [128, (S_ // 128) * 256], BF16)
    wq_sb = env.sbuf("wq_sb", [128, KT * 256], BF16)
    wk_sb = env.sbuf("wk_sb", [128, KT * 256], BF16)
    wv_sb = env.sbuf("wv_sb", [128, KT * 256], BF16)
    bank = env.sbuf("bank", [128, 24 * 512], BF16)
    jm_sb = env.sbuf("jm_sb", [128, 128], BF16)
    id4 = env.sbuf("id4", [4, 4], BF16)
    hsel_sb = env.sbuf("hsel_sb", [128, 4], BF16)
    p_sb = env.sbuf("p_sb", [128, S_], BF16)
    pt_sb = env.sbuf("pt_sb", [128, 2 * S_], BF16)
    osl_sb = env.sbuf("osl_sb", [128, 2 * 128], BF16)
    sq_scr = env.sbuf("sq_scr", [128, 2 * 512], BF16)
    brev_f = sq_scr.ap().bitcast(F32)[0:4, 0:512]
    qn_sb = env.sbuf("qn_sb", [4, S_], BF16)
    negm16 = env.sbuf("negm16", [16, S_], BF16)
    nmt16t = env.sbuf("nmt16t", [128, (S_ // 128) * 16], BF16)
    nmt_all = env.sbuf("nmt_all", [128, 16 * (S_ // 128)], F32)
    eoff_all = env.sbuf("eoff_all", [128, 16 * (S_ // 128)], F32)
    offb_sb = env.sbuf("offb_sb", [128, 4 * (S_ // 128)], F32)
    den_sb = env.sbuf("den_sb", [128, 2 * NDSL], F32)
    dcol_sb = env.sbuf("dcol_sb", [128, 2], F32)
    if dbg is not None:
        dcarch = env.sbuf("dcarch", [128, 8 * (S_ // 128) + 8], F32)
    row4 = env.sbuf("row4", [4, 8], F32)
    ones1 = env.sbuf("ones1", [1, 128], F32)
    ones_b1 = env.sbuf("ones_b1", [128, 1], BF16)

    psq = [env.psum(f"psq{i}", [128, 512]) for i in range(2)]
    psk = [env.psum(f"psk{i}", [128, 512]) for i in range(2)]
    psv = [env.psum(f"psv{i}", [128, 512]) for i in range(2)]
    psx = env.psum("psx", [128, 512])
    psy = env.psum("psy", [128, 512])
    psx_bf = psx.ap().bitcast(BF16)

    s_w = env.sem("b_wg")
    s_ws = env.sem("b_ws")
    s_wk = env.sem("b_wk")
    s_xn = [env.sem("b_xn0"), env.sem("b_xn1")]
    s_cs = [env.sem("b_cs0"), env.sem("b_cs1")]
    s_pj = env.sem("b_pj")
    s_pe = env.sem("b_pe")
    s_nr = env.sem("b_nr")
    s_bk = env.sem("b_bk")
    s_sc = env.sem("b_sc")
    s_px = env.sem("b_px")
    s_pt = env.sem("b_pt")
    s_av = env.sem("b_av")
    s_dn = env.sem("b_dn")
    s_os = env.sem("b_os")
    s_so = env.sem("b_so")
    s_jk = env.sem("b_jk")
    ini = env.sem("ini")
    jnk_dram = nc.dram_tensor("b_jnkd", [1, 512], F32, kind="Internal")

    env.set_junk(env.sbuf("junk", [1, 1024], F32))
    env.open_block()
    emit_consts(env, ones_b1, ones1)
    if dbg is not None:
        @env.block.gpsimd
        def _(g):
            g.memset(den_sb[:, :], 0.0)
            g.memset(dcol_sb[:, :], 0.0)
            g.memset(dcarch[:, :], 0.0)
            env.sem("ini").inc(g.engine_nop())

    xnv = xn0.rearrange("(k p) t -> k p t", p=128)
    wqv = wq.rearrange("(k p) m -> k p m", p=128)
    wkv = wk.rearrange("(k p) m -> k p m", p=128)
    wvv = wv.rearrange("(k p) m -> k p m", p=128)

    @env.block.gpsimd
    def _(g):
        for k in range(KT):
            s_w.inc(g.dma_start(out=wq_sb[:, k * 256:(k + 1) * 256], in_=wqv[k]), 16)
            s_w.inc(g.dma_start(out=wk_sb[:, k * 256:(k + 1) * 256], in_=wkv[k]), 16)
            s_w.inc(g.dma_start(out=wv_sb[:, k * 256:(k + 1) * 256], in_=wvv[k]), 16)
        s_w.inc(g.dma_start(out=jm_sb[:, :], in_=jmat[:, :]), 16)
        s_w.inc(g.dma_start(out=hsel_sb[:, :], in_=hsel[:, :]), 16)
        s_w.inc(g.dma_start(out=id4[:, :], in_=idm[0:4, 0:4]), 16)

    @env.block.sync
    def _(s):
        s_ws.inc(s.dma_start(out=offb_sb[:, :], in_=offs[:, :]), 16)

    NW1 = s_w.n
    NWS = s_ws.n

    @env.block.scalar
    def _(sc):
        sc.wait_ge(s_w.h, NW1)
        s_wk.inc(sc.mul(wq_sb[:, :], wq_sb[:, :], 0.125))

    NW = s_w.n

    for mchunk in range(5):
        cw = min(512, 2432 - mchunk * 512)

        @env.block.sync
        def _(s, mchunk=mchunk, cw=cw):
            if mchunk > 0:
                s.wait_ge(s_nr.h, s_nr.n)
            s_ws.inc(s.dma_start(out=brev_f[:, 0:cw], in_=brev[:, mchunk * 512:mchunk * 512 + cw]), 16)

        @env.block.vector
        def _(v, mchunk=mchunk, cw=cw):
            v.wait_ge(s_ws.h, s_ws.n)
            v.reduce_max(row4[0:4, 4:5], brev_f[:, 0:cw], axis=AX.X)
            env.vspace(v)
            if mchunk == 0:
                ins = v.tensor_copy(row4[0:4, 2:3], row4[0:4, 4:5])
            else:
                ins = v.tensor_max(row4[0:4, 2:3], row4[0:4, 2:3], row4[0:4, 4:5])
            s_nr.inc(ins)

    @env.block.vector
    def _(v):
        v.wait_ge(s_nr.h, s_nr.n)
        env.vspace(v)
        s_nr.inc(v.tensor_scalar(row4[0:4, 3:4], row4[0:4, 2:3], -1.0, None, ALU.mult))

    pt_nm_guard = [0]
    osl_use = [0]
    so_hist = []
    for b in range(B):
        for nq in range(NNQ):
            i = b * NNQ + nq
            buf = i % 2
            pjn0 = s_pj.n

            @env.block.scalar
            def _(sc, b=b, nq=nq, buf=buf, i=i):
                if i >= 2:
                    sc.wait_ge(s_pe.h, (i - 1) * 3)
                for k in range(KT):
                    s_xn[buf].inc(sc.dma_start(
                        out=xn_st[:, buf * KT * CQ + k * CQ:buf * KT * CQ + (k + 1) * CQ],
                        in_=xnv[k][:, b * S_ + nq * CQ:b * S_ + (nq + 1) * CQ]), 16)

            @env.block.sync
            def _(s, b=b, nq=nq, buf=buf, i=i):
                if i >= 2:
                    s.wait_ge(s_pe.h, (i - 1) * 3)
                s_cs[buf].inc(s.dma_start(out=cs_st[:, buf * 2 * CQ:buf * 2 * CQ + CQ],
                                          in_=cst[:, nq * CQ:(nq + 1) * CQ]), 16)
                s_cs[buf].inc(s.dma_start(out=cs_st[:, buf * 2 * CQ + CQ:(buf + 1) * 2 * CQ],
                                          in_=snt[:, nq * CQ:(nq + 1) * CQ]), 16)

            @env.block.tensor
            def _(t, b=b, nq=nq, buf=buf, i=i):
                t.wait_ge(s_w.h, NW)
                t.wait_ge(s_ws.h, NWS)
                t.wait_ge(s_wk.h, 1)
                t.wait_ge(s_xn[buf].h, 16 * KT * (i // 2 + 1))
                if nq == 0:
                    t.wait_ge(s_os.h, s_os.n)
                    t.wait_ge(s_dn.h, s_dn.n)
                if i >= 1:
                    t.wait_ge(s_pe.h, i * 3)
                xo = buf * KT * CQ
                for mt in range(2):
                    for c in range(KT):
                        ins = t.matmul(psq[mt][:, :],
                                       wq_sb[:, c * 256 + mt * 128:c * 256 + (mt + 1) * 128],
                                       xn_st[:, xo + c * CQ:xo + (c + 1) * CQ],
                                       start=(c == 0), stop=(c == KT - 1))
                s_pj.inc(ins)
                for mt in range(2):
                    for c in range(KT):
                        ins = t.matmul(psk[mt][:, :],
                                       wk_sb[:, c * 256 + mt * 128:c * 256 + (mt + 1) * 128],
                                       xn_st[:, xo + c * CQ:xo + (c + 1) * CQ],
                                       start=(c == 0), stop=(c == KT - 1))
                s_pj.inc(ins)
                vbanks = [psv[0], psv[1], psx, psy]
                for tt in range(4):
                    for c in range(KT):
                        ins = t.matmul(vbanks[tt][:, 0:256],
                                       xn_st[:, xo + c * CQ + tt * 128:xo + c * CQ + tt * 128 + 128],
                                       wv_sb[:, c * 256:(c + 1) * 256],
                                       start=(c == 0), stop=(c == KT - 1))
                s_pj.inc(ins)

            @env.block.vector
            def _(v, b=b, nq=nq, buf=buf, i=i, pjn0=pjn0):
                co = buf * 2 * CQ
                cc, ss = cs_st[:, co:co + CQ], cs_st[:, co + CQ:co + 2 * CQ]
                v.wait_ge(s_cs[buf].h, 32 * (i // 2 + 1))
                for which, ps2, dstb in ((0, psq, q_b), (1, psk, k_b)):
                    v.wait_ge(s_pj.h, pjn0 + (3 if nq == 0 else which + 1))
                    env.vfence(v)
                    t1, t2 = rp_scr[:, 0:CQ], rp_scr[:, CQ:2 * CQ]
                    if nq == 0:
                        v.tensor_copy(t1, ps2[0][:, :])
                        v.tensor_copy(t2, ps2[1][:, :])
                    v.tensor_mul(t1, ps2[0][:, :], cc)
                    v.tensor_mul(t2, ps2[1][:, :], ss)
                    v.tensor_sub(dstb[:, 0 * S_ + nq * CQ:0 * S_ + (nq + 1) * CQ], t1, t2)
                    v.tensor_mul(t1, ps2[0][:, :], ss)
                    v.tensor_mul(t2, ps2[1][:, :], cc)
                    ins = v.tensor_add(dstb[:, 1 * S_ + nq * CQ:1 * S_ + (nq + 1) * CQ], t1, t2)
                    s_pe.inc(ins)

            @env.block.scalar
            def _(sc, b=b, nq=nq, i=i, pjn0=pjn0):
                sc.wait_ge(s_pj.h, pjn0 + 3)
                env.sfence(sc)
                vbanks = [psv[0], psv[1], psx, psy]
                for tt in range(4):
                    ins = sc.copy(v_b[:, (nq * 4 + tt) * 256:(nq * 4 + tt + 1) * 256],
                                  vbanks[tt][:, 0:256])
                s_pe.inc(ins)

        PJB = s_pe.n

        if dbg is not None and b == dbg[0]:
            @env.block.sync
            def _(s):
                s.wait_ge(s_pe.h, PJB)
                s.dma_start(out=dbg_qb[:, 0:128], in_=q_b[:, 0:128]).then_inc(s_so.h, 16)
                s.dma_start(out=dbg_qb[:, 128:256], in_=q_b[:, S_:S_ + 128]).then_inc(s_so.h, 16)
                s.dma_start(out=dbg_qb[:, 256:384], in_=k_b[:, 0:128]).then_inc(s_so.h, 16)
                s.dma_start(out=dbg_qb[:, 384:512], in_=k_b[:, S_:S_ + 128]).then_inc(s_so.h, 16)
                s.dma_start(out=dbg_qb[:, 512:768], in_=v_b[:, 0:256]).then_inc(s_so.h, 16)
            s_so.n += 80

        @env.block.vector
        def _(v, b=b):
            v.wait_ge(s_nr.h, s_nr.n)
            s_nr.inc(v.memset(row4[0:4, 0:1], 0.0))

        nr_base = s_nr.n
        for which, srcb in ((0, q_b), (1, k_b)):
            for ch in range(NNQ):
                @env.block.scalar
                def _(sc, which=which, srcb=srcb, ch=ch):
                    sc.wait_ge(s_pe.h, PJB)
                    if not (which == 0 and ch == 0):
                        sc.wait_ge(s_nr.h, s_nr.n)
                    if which == 0 and ch == 0:
                        env.sfence(sc)
                    sc.activation(sq_scr[:, 0:512], srcb[:, 0 * S_ + ch * 512:0 * S_ + (ch + 1) * 512], AF.Square)
                    sc.activation(sq_scr[:, 512:1024], srcb[:, 1 * S_ + ch * 512:1 * S_ + (ch + 1) * 512], AF.Square)
                    s_px.inc(env.sfence(sc))

                @env.block.tensor
                def _(t, which=which, ch=ch):
                    t.wait_ge(s_px.h, s_px.n)
                    t.matmul(psx[0:4, :], hsel_sb[:, :], sq_scr[:, 0:512], start=True, stop=False)
                    s_pj.inc(t.matmul(psx[0:4, :], hsel_sb[:, :], sq_scr[:, 512:1024], start=False, stop=True))

                @env.block.vector
                def _(v, which=which, ch=ch):
                    v.wait_ge(s_pj.h, s_pj.n)
                    env.vfence(v)
                    if which == 0:
                        ins = v.tensor_copy(qn_sb[0:4, ch * 512:(ch + 1) * 512], psx[0:4, :])
                    else:
                        v.reduce_max(row4[0:4, 1:2], psx[0:4, :], axis=AX.X)
                        env.vspace(v)
                        ins = v.tensor_max(row4[0:4, 0:1], row4[0:4, 0:1], row4[0:4, 1:2])
                    s_nr.inc(ins)

        @env.block.vector
        def _(v, b=b):
            v.wait_ge(s_nr.h, s_nr.n)
            if b > 0:
                v.wait_ge(s_pt.h, pt_nm_guard[0])
            env.vspace(v)
            if b == 0:
                v.memset(negm16[0:16, :], 0.0)
            v.tensor_scalar(negm16[0:4, :], qn_sb[0:4, :], row4[0:4, 0:1], None, ALU.mult)
            s_nr.inc(env.vfence(v))

        @env.block.scalar
        def _(sc, b=b):
            sc.wait_ge(s_nr.h, s_nr.n)
            env.sfence(sc)
            sc.activation(negm16[0:4, :], negm16[0:4, :], AF.Ln)
            sc.activation(negm16[0:4, :], negm16[0:4, :], AF.Exp, scale=0.5)
            s_nr.inc(env.sfence(sc))

        @env.block.vector
        def _(v, b=b):
            v.wait_ge(s_nr.h, s_nr.n)
            env.vfence(v)
            v.tensor_scalar(negm16[0:4, :], negm16[0:4, :], -1.0,
                            row4[0:4, 3:4], ALU.mult, ALU.add)
            s_nr.inc(env.vfence(v))

        NMDONE = s_nr.n

        NB = b * 4 * NQT

        @env.block.sync
        def _(s, b=b):
            s.wait_ge(s_nr.h, NMDONE)
            if b > 0:
                s.wait_ge(s_px.h, s_px.n)
            s.wait_ge(s_pt.h, s_pt.n)
            s_pt.inc(s.dma_start(
                out=nmt16t.ap().rearrange("p (t r) -> p t r", r=16),
                in_=negm16[:, :], transpose=True), 16)
        pt_nm_guard[0] = s_pt.n

        @env.block.scalar
        def _(sc, b=b, NB=NB):
            sc.wait_ge(s_pt.h, s_pt.n)
            nmtv16 = nmt16t.ap().rearrange("p (t r) -> p t r", r=16)
            nmt4 = nmt_all.ap()[:, NB:NB + 4 * NQT].rearrange("p (q j) -> p q j", j=4)
            sc.copy(nmt4[:, :, :], nmtv16[:, :, 0:4])
            eoffv = eoff_all.ap()[:, NB:NB + 4 * NQT].rearrange("p (q j) -> p q j", j=4)
            for j in range(4):
                ins = sc.activation(eoffv[:, :, j], nmt4[:, :, j], AF.Exp,
                                    bias=offb_sb[:, j:j + 1])
            s_dn.inc(ins)

        EOFF_B = s_dn.n

        if dbg is not None and b == dbg[0]:
            @env.block.sync
            def _(s):
                s.wait_ge(s_dn.h, EOFF_B)
                s.dma_start(out=dbg_nm[:, 0:4 * NQT], in_=nmt_all[:, NB:NB + 4 * NQT]).then_inc(s_so.h, 16)
                s.dma_start(out=dbg_nm[:, 4 * NQT:8 * NQT], in_=nmt_all[:, NB:NB + 4 * NQT]).then_inc(s_so.h, 16)
                s.dma_start(out=dbg_nm[:, 8 * NQT:12 * NQT], in_=eoff_all[:, NB:NB + 4 * NQT]).then_inc(s_so.h, 16)
            s_so.n += 48

        for hp in range(2):
            sc_before_bank = s_sc.n

            @env.block.gpsimd
            def _(g, b=b, hp=hp, sc_before_bank=sc_before_bank):
                g.wait_ge(s_sc.h, sc_before_bank)
                g.wait_ge(s_pt.h, s_pt.n)
                for h in range(2):
                    hg = 2 * hp + h
                    for di in range(12):
                        s_bk.inc(g.dma_start(
                            out=bank[:, (h * 12 + di) * 512:(h * 12 + di + 1) * 512],
                            in_=banktv[hg * 12 + di]), 16)

            BKN = s_bk.n
            pending = [None]

            def flush_pending(extra_sc=None):
                fn = pending[0]
                if fn is not None:
                    pending[0] = None
                    fn(extra_sc)

            for qt in range(NQT):
                ncov = qt // 4 + 1
                flush_pending(extra_sc=s_sc.n + 1)

                for h in range(2):
                    strip = 32 * (2 * hp + h)
                    SB = psq if h == 0 else psk
                    col = NB + qt * 4 + 2 * hp + h
                    dn_guard = s_dn.n
                    for kc in range(ncov):
                        delta = qt * 128 - kc * 512
                        Sb = SB[kc % 2]

                        @env.block.tensor
                        def _(t, b=b, hp=hp, qt=qt, h=h, kc=kc, strip=strip, Sb=Sb, delta=delta):
                            t.wait_ge(s_px.h, max(0, s_px.n - 1))
                            if kc == 0:
                                t.wait_ge(s_bk.h, BKN)
                                t.wait_ge(s_pt.h, s_pt.n)
                            t.matmul(Sb[:, :], q_b[strip:strip + 32, 0 * S_ + qt * 128:0 * S_ + (qt + 1) * 128],
                                     k_b[strip:strip + 32, 0 * S_ + kc * 512:0 * S_ + (kc + 1) * 512],
                                     start=True, stop=False, tile_position=(strip, 0))
                            last = (delta > 1408) or no_inject
                            ins = t.matmul(Sb[:, :], q_b[strip:strip + 32, 1 * S_ + qt * 128:1 * S_ + (qt + 1) * 128],
                                           k_b[strip:strip + 32, 1 * S_ + kc * 512:1 * S_ + (kc + 1) * 512],
                                           start=False, stop=last, tile_position=(strip, 0))
                            if not last:
                                ins = t.matmul(Sb[:, :], jm_sb[:, :],
                                               bank[:, (h * 12 + delta // 128) * 512:(h * 12 + delta // 128 + 1) * 512],
                                               start=False, stop=True)
                            s_sc.inc(ins)

                        @env.block.scalar
                        def _(sc, qt=qt, h=h, kc=kc, Sb=Sb, col=col, dn_guard=dn_guard):
                            sc.wait_ge(s_sc.h, s_sc.n)
                            if kc == 0:
                                sc.wait_ge(s_dn.h, dn_guard)
                            ins = sc.activation(p_sb[:, kc * 512:(kc + 1) * 512], Sb[:, :], AF.Exp,
                                                bias=nmt_all[:, col:col + 1],
                                                accum_out=den_sb[:, h * NDSL + kc:h * NDSL + kc + 1])
                            s_px.inc(ins)

                    @env.block.vector
                    def _(v, h=h, ncov=ncov, col=col):
                        v.wait_ge(s_px.h, s_px.n)
                        v.wait_ge(s_dn.h, EOFF_B)
                        env.vfence(v)
                        if ncov > 1:
                            v.reduce_sum(dcol_sb[:, h:h + 1], den_sb[:, h * NDSL:h * NDSL + ncov], axis=AX.X)
                            env.vspace(v)
                            v.tensor_add(dcol_sb[:, h:h + 1], dcol_sb[:, h:h + 1], eoff_all[:, col:col + 1])
                        else:
                            v.tensor_add(dcol_sb[:, h:h + 1], den_sb[:, h * NDSL:h * NDSL + 1],
                                         eoff_all[:, col:col + 1])
                        env.vspace(v)
                        v.reciprocal(dcol_sb[:, h:h + 1], dcol_sb[:, h:h + 1])
                        ins = env.vfence(v)
                        if dbg is not None and (b, hp) == dbg:
                            v.tensor_copy(dcarch[:, qt * 8 + h * 4:qt * 8 + h * 4 + 1], dcol_sb[:, h:h + 1])
                            v.tensor_copy(dcarch[:, qt * 8 + h * 4 + 1:qt * 8 + h * 4 + 2],
                                          den_sb[:, h * NDSL:h * NDSL + 1])
                            ins = v.tensor_copy(dcarch[:, qt * 8 + h * 4 + 2:qt * 8 + h * 4 + 3],
                                                eoff_all[:, col:col + 1])
                        s_dn.inc(ins)

                    if dbg is not None and (b, hp) == dbg and qt == 0 and h == 0:
                        @env.block.sync
                        def _(s):
                            s.wait_ge(s_px.h, s_px.n)
                            s.wait_ge(s_dn.h, s_dn.n)
                            s.dma_start(out=dbg_p0[:, :], in_=p_sb[:, 0:512]).then_inc(s_so.h, 16)
                        s_so.n += 16

                    @env.block.sync
                    def _(s, b=b, hp=hp, qt=qt, h=h, ncov=ncov):
                        s.wait_ge(s_px.h, s_px.n)
                        s.wait_ge(s_dn.h, s_dn.n)
                        if not (b == 0 and hp == 0 and qt == 0 and h == 0):
                            s.wait_ge(s_av.h, s_av.n)
                        if qt == 0:
                            s_jk.inc(s.dma_start(out=jnk_dram.ap(), in_=env._junk[0:1, 0:512]), 16)
                            s.wait_ge(s_jk.h, s_jk.n)
                        s_pt.inc(s.dma_start(
                            out=pt_sb.ap()[:, h * S_:h * S_ + ncov * 512].rearrange(
                                "p (t q) -> p t q", q=128),
                            in_=p_sb[:, 0:ncov * 512], transpose=True), 16)

                if dbg is not None and (b, hp) == dbg:
                    @env.block.sync
                    def _(s, qt=qt):
                        s.wait_ge(s_dn.h, s_dn.n)
                        s.dma_start(out=dbg_dc[:, qt * 8:qt * 8 + 8], in_=dcarch[:, qt * 8:qt * 8 + 8],
                                    allow_slow_non_contiguous=True).then_inc(s_so.h, 16)
                        s.dma_start(out=dbg_ds[qt][:, 0:2 * NDSL],
                                    in_=den_sb[:, :]).then_inc(s_so.h, 16)
                        s.dma_start(out=dbg_pq[qt], in_=p_sb[:, 0:512]).then_inc(s_so.h, 16)
                        if qt == 0:
                            s.dma_start(out=dbg_pt[0], in_=pt_sb[:, 0:512]).then_inc(s_so.h, 16)
                            s.dma_start(out=dbg_pt[1], in_=pt_sb[:, S_:S_ + 512]).then_inc(s_so.h, 16)
                            s.dma_start(out=dbg_av[:, :], in_=dcarch[:, 8 * NQT:8 * NQT + 4],
                                        allow_slow_non_contiguous=True).then_inc(s_so.h, 16)
                            s_so.n += 48
                    s_so.n += 32

                @env.block.tensor
                def _(t, b=b, hp=hp, qt=qt, ncov=ncov):
                    t.wait_ge(s_pt.h, s_pt.n)
                    t.wait_ge(s_os.h, s_os.n)
                    nkt = ncov * 4
                    for kt in range(nkt):
                        t.matmul(psv[0][:, 0:64], pt_sb[:, 0 * S_ + kt * 128:0 * S_ + (kt + 1) * 128],
                                 v_b[:, kt * 256 + hp * 128:kt * 256 + hp * 128 + 64],
                                 start=(kt == 0), stop=(kt == nkt - 1))
                        ins = t.matmul(psv[1][:, 0:64], pt_sb[:, 1 * S_ + kt * 128:1 * S_ + (kt + 1) * 128],
                                       v_b[:, kt * 256 + hp * 128 + 64:kt * 256 + hp * 128 + 128],
                                       start=(kt == 0), stop=(kt == nkt - 1))
                    s_av.inc(ins)

                def make_evac(qt=qt, av_target=s_av.n, dn_target=s_dn.n, b=b, hp=hp):
                    def emit(extra_sc):
                        idx = osl_use[0]
                        osl_use[0] += 1
                        slot = (idx % 2) * 128

                        @env.block.vector
                        def _(v):
                            v.wait_ge(s_av.h, av_target)
                            if extra_sc is not None:
                                v.wait_ge(s_sc.h, extra_sc)
                            if idx >= 2:
                                v.wait_ge(s_so.h, so_hist[idx - 2])
                            if extra_sc is None or qt == 0:
                                env.vfence(v)
                            if dbg is not None and (b, hp) == dbg and qt == 0:
                                v.tensor_copy(dcarch[:, 8 * NQT:8 * NQT + 1], psv[0][:, 0:1])
                                v.tensor_copy(dcarch[:, 8 * NQT + 1:8 * NQT + 2], psv[1][:, 0:1])
                            v.tensor_scalar(osl_sb[:, slot:slot + 64], psv[0][:, 0:64],
                                            dcol_sb[:, 0:1], None, ALU.mult)
                            s_os.inc(v.tensor_scalar(osl_sb[:, slot + 64:slot + 128], psv[1][:, 0:64],
                                                     dcol_sb[:, 1:2], None, ALU.mult))

                        @env.block.sync
                        def _(s):
                            s.wait_ge(s_os.h, s_os.n)
                            s_so.inc(s.dma_start(out=oslc_v[hp, b][:, qt * 128:(qt + 1) * 128],
                                                 in_=osl_sb[:, slot:slot + 128]), 16)
                        so_hist.append(s_so.n)
                    return emit

                pending[0] = make_evac()

            flush_pending(extra_sc=None)

        if dbg is not None and b == dbg[0]:
            @env.block.sync
            def _(s):
                s.wait_ge(s_os.h, s_os.n)
                s.dma_start(out=dbg_qb2[:, 0:128], in_=q_b[:, 0:128]).then_inc(s_so.h, 16)
                s.dma_start(out=dbg_qb2[:, 128:256], in_=q_b[:, S_:S_ + 128]).then_inc(s_so.h, 16)
            s_so.n += 32

    @env.block.sync
    def _(s):
        s.wait_ge(s_so.h, s_so.n)

    return env.close()



def host_prep_B(inputs, S_=S):
    Wq, Wk, Wv = inputs["L0_attn_Wq"], inputs["L0_attn_Wk"], inputs["L0_attn_Wv"]
    LB = inputs["L0_attn_learnable_bias"]
    off = inputs["L0_attn_softmax_offset"]
    H, Dh = 32, 64
    p = np.arange(S_, dtype=np.float64)
    dgrid = np.arange(32, dtype=np.float64)
    ang = np.outer(p, 1.0 / (10000.0 ** (dgrid / 32.0)))
    cs = np.cos(ang).astype(np.float32)
    sn = np.sin(ang).astype(np.float32)
    cst = np.ascontiguousarray(np.tile(cs.T, (4, 1)))
    snt = np.ascontiguousarray(np.tile(sn.T, (4, 1)))
    hsel = np.zeros((128, 4), dtype=np.float32)
    for r in range(128):
        hsel[r, r // 32] = 1.0
    jmat = np.eye(128, dtype=np.float32)[::-1].copy()
    idm = np.eye(128, dtype=np.float32)
    maps = []
    for c in range(NCORE):
        heads = [4 * c + j for j in range(4)]
        rows_qk = [h * Dh + half * 32 + d0 for half in range(2) for h in heads for d0 in range(32)]
        rows_v = [h * Dh + d for h in heads for d in range(Dh)]
        wq = np.ascontiguousarray(Wq[rows_qk, :].T)
        wk = np.ascontiguousarray(Wk[rows_qk, :].T)
        wv = np.ascontiguousarray(Wv[rows_v, :].T)
        brev = np.zeros((4, 2432), dtype=np.float32)
        for j, h in enumerate(heads):
            u = np.arange(2432)
            dd = 1535 - u
            val = np.zeros(2432, dtype=np.float32)
            band = (dd >= 0) & (dd <= 1023)
            val[band] = LB[h, dd[band]]
            val[dd < 0] = -1e30
            brev[j] = val
        nqt_ = S_ // 128
        offs = np.tile(off[heads][None, :], (128, nqt_)).astype(np.float32)
        bankt = np.zeros((48, 128, 512), dtype=np.float32)
        u = np.arange(512)
        for hg in range(4):
            for di in range(12):
                base = 1408 - 128 * di
                for r in range(128):
                    bankt[hg * 12 + di, r, :] = brev[hg, base + r:base + r + 512]
        maps.append(dict(wq=wq, wk=wk, wv=wv, cst=cst, snt=snt, brev=brev,
                         bankt=bankt, offs=offs, hsel=hsel, jmat=jmat, idm=idm))
    return maps


def assemble_xn0(shards, S_=S):
    so = S_ // NCORE
    full = np.empty((D, B * S_), dtype=BFD)
    for c, sh in enumerate(shards):
        shv = np.asarray(sh).reshape(D, B, so)
        for b_ in range(B):
            full[:, b_ * S_ + so * c: b_ * S_ + so * (c + 1)] = shv[:, b_, :]
    return full



def build_launch_C1(T_=TOWN):
    env = Env("C1")
    nc = env.nc
    CW = 512
    NCH = T_ // CW

    oat = nc.dram_tensor("oat", [D, T_], BF16, kind="ExternalInput").ap()
    x0c = nc.dram_tensor("x0c", [D, T_], F32, kind="ExternalInput").ap()
    woT = nc.dram_tensor("woT", [D, D], F32, kind="ExternalInput").ap()
    cvw = nc.dram_tensor("cvw", [128, 2 * KT], F32, kind="ExternalInput").ap()
    x1o = nc.dram_tensor("x1o", [D, T_], F32, kind="ExternalOutput").ap()
    xco = nc.dram_tensor("xco", [D, T_], BF16, kind="ExternalOutput").ap()
    oat_t = dram_tiles(oat)
    x0_t = dram_tiles(x0c)
    wo_t = dram_tiles(woT)
    x1_t = dram_tiles(x1o)
    xc_t = dram_tiles(xco)

    wo_sb = env.sbuf("wo_sb", [128, KT * D], BF16)
    oat_sb = env.sbuf("oat_sb", [128, KT * CW], BF16)
    x0p = env.sbuf("x0p", [128, 4 * CW], F32)
    x1p = env.sbuf("x1p", [128, 4 * CW], F32)
    xbp = [env.sbuf("xb0", [128, KT * CW], BF16),
           env.sbuf("xb1", [128, KT * CW], BF16)]
    sqb = env.sbuf("sqb", [128, KT * CW], BF16)
    row = env.sbuf("row", [1, 6 * CW], F32)
    rB = env.sbuf("rB", [128, CW], BF16)
    mB = env.sbuf("mB", [128, CW], BF16)
    tmpb = env.sbuf("tmpb", [128, CW], BF16)
    xcp = env.sbuf("xcp", [128, 2 * CW], F32)
    xcb = env.sbuf("xcb", [128, 2 * CW], BF16)
    cvs = env.sbuf("cvs", [128, 2 * KT], F32)
    psm = [env.psum("c1_ps0", [128, CW]), env.psum("c1_ps1", [128, CW])]
    psA = env.psum("c1_psA", [1, CW])
    psB = env.psum("c1_psB", [1, CW])
    psR = env.psum("c1_psR", [128, CW])
    psM = env.psum("c1_psM", [128, CW])
    ones_b = env.sbuf("ones_b", [128, 1], BF16)
    ones_f = env.sbuf("ones_f", [1, 128], F32)
    env.set_junk(env.sbuf("junk", [1, 1024], F32))

    s_w = env.sem("c_w")
    s_oa = env.sem("c_oa")
    s_x0 = env.sem("c_x0")
    s_mm = env.sem("c_mm")
    s_ev = env.sem("c_ev")
    s_sx = env.sem("c_sx")
    s_sq = env.sem("c_sq")
    s_st = env.sem("c_st")
    s_rw = env.sem("c_rw")
    s_e2 = env.sem("c_e2")
    s_cv = env.sem("c_cv")
    s_sl = env.sem("c_sl")
    s_sc = env.sem("c_sc")
    ini = env.sem("ini")

    def evn(t):
        return t + 1 + t // KT

    def sln(t):
        return t + 1 + t // KT

    env.open_block()
    emit_consts(env, ones_b, ones_f)

    @env.block.gpsimd
    def _(g):
        for et in range(KT):
            s_w.inc(g.dma_start(out=wo_sb[:, et * D:(et + 1) * D], in_=wo_t[et][:, :]), 16)
    NW = s_w.n

    s_ws = env.sem("c_ws")

    @env.block.sync
    def _(s):
        s_ws.inc(s.dma_start(out=cvs[:, :], in_=cvw[:, :]), 16)
    NWS = s_ws.n

    for ci in range(NCH):
        c0, c1 = ci * CW, (ci + 1) * CW
        xb = xbp[ci % 2]
        xprev = xbp[(ci - 1) % 2]

        @env.block.gpsimd
        def _(g, ci=ci, c0=c0, c1=c1):
            if ci > 0:
                g.wait_ge(s_mm.h, 16 * ci)
            for et in range(KT):
                s_oa.inc(g.dma_start(out=oat_sb[:, et * CW:(et + 1) * CW],
                                     in_=oat_t[et][:, c0:c1]), 16)
            for dt in range(KT):
                gidx = ci * KT + dt
                if gidx >= 4:
                    g.wait_ge(s_ev.h, evn(gidx - 4))
                s_x0.inc(g.dma_start(out=x0p[:, (gidx % 4) * CW:(gidx % 4 + 1) * CW],
                                     in_=x0_t[dt][:, c0:c1]), 16)

        for dt in range(KT):
            gidx = ci * KT + dt

            @env.block.tensor
            def _(t, ci=ci, dt=dt, gidx=gidx):
                if gidx == 0:
                    t.wait_ge(s_w.h, NW)
                t.wait_ge(s_oa.h, 16 * KT * (ci + 1))
                if gidx >= 2:
                    t.wait_ge(s_ev.h, evn(gidx - 2))
                for et in range(KT):
                    ins = t.matmul(psm[gidx % 2][:, :],
                                   wo_sb[:, et * D + dt * 128:et * D + (dt + 1) * 128],
                                   oat_sb[:, et * CW:(et + 1) * CW],
                                   start=(et == 0), stop=(et == KT - 1))
                s_mm.inc(ins)

            @env.block.vector
            def _(v, ci=ci, dt=dt, gidx=gidx):
                v.wait_ge(s_mm.h, gidx + 1)
                v.wait_ge(s_x0.h, 16 * (gidx + 1))
                if gidx >= 4:
                    v.wait_ge(s_sx.h, 16 * (gidx - 3))
                if dt == 0 and ci > 0:
                    v.wait_ge(s_sq.h, ci)
                env.vfence(v)
                ins = v.tensor_add(x1p[:, (gidx % 4) * CW:(gidx % 4 + 1) * CW],
                                   psm[gidx % 2][:, :],
                                   x0p[:, (gidx % 4) * CW:(gidx % 4 + 1) * CW])
                s_ev.inc(ins)

            @env.block.sync
            def _(s, ci=ci, dt=dt, gidx=gidx, c0=c0, c1=c1):
                s.wait_ge(s_ev.h, min(evn(gidx + 1), 17 * (ci + 1)))
                s_sx.inc(s.dma_start(out=x1_t[dt][:, c0:c1],
                                     in_=x1p[:, (gidx % 4) * CW:(gidx % 4 + 1) * CW]), 16)

        @env.block.vector
        def _(v):
            s_ev.inc(env.vfence(v))

        @env.block.scalar
        def _(sc, ci=ci, xb=xb):
            for dt in range(KT):
                gidx = ci * KT + dt
                sc.wait_ge(s_ev.h, min(evn(gidx + 1), 17 * (ci + 1)))
                sc.copy(xb[:, dt * CW:(dt + 1) * CW],
                        x1p[:, (gidx % 4) * CW:(gidx % 4 + 1) * CW])
                sc.activation(sqb[:, dt * CW:(dt + 1) * CW], xb[:, dt * CW:(dt + 1) * CW],
                              AF.Square)
            s_sq.inc(env.sfence(sc))

        @env.block.tensor
        def _(t, ci=ci, xb=xb):
            t.wait_ge(ini.h, 2)
            t.wait_ge(s_sq.h, ci + 1)
            if ci >= 1:
                t.wait_ge(s_rw.h, 3 * (ci - 1) + 1)
            for k in range(KT):
                t.matmul(psA[0:1, :], ones_b[:, :], xb[:, k * CW:(k + 1) * CW],
                         start=(k == 0), stop=(k == KT - 1))
            for k in range(KT):
                ins = t.matmul(psB[0:1, :], ones_b[:, :], sqb[:, k * CW:(k + 1) * CW],
                               start=(k == 0), stop=(k == KT - 1))
            s_st.inc(ins)

        @env.block.vector
        def _(v, ci=ci):
            v.wait_ge(s_st.h, 2 * ci + 1)
            env.vfence(v)
            v.tensor_scalar(row[0:1, 0:CW], psA[0:1, :], 1.0 / D, None, ALU.mult)
            v.tensor_scalar(row[0:1, CW:2 * CW], psB[0:1, :], 1.0 / D, None, ALU.mult)
            v.tensor_mul(row[0:1, 4 * CW:5 * CW], row[0:1, 0:CW], row[0:1, 0:CW])
            v.tensor_sub(row[0:1, 2 * CW:3 * CW], row[0:1, CW:2 * CW], row[0:1, 4 * CW:5 * CW])
            v.tensor_scalar(row[0:1, 2 * CW:3 * CW], row[0:1, 2 * CW:3 * CW], EPS, None, ALU.add)
            ins = v.reciprocal(row[0:1, 2 * CW:3 * CW], row[0:1, 2 * CW:3 * CW])
            s_rw.inc(ins)
            v.wait_ge(s_rw.h, 3 * ci + 2)
            v.tensor_mul(row[0:1, 4 * CW:5 * CW], row[0:1, 0:CW], row[0:1, 3 * CW:4 * CW])
            s_rw.inc(env.vfence(v))

        @env.block.scalar
        def _(sc, ci=ci):
            sc.wait_ge(s_rw.h, 3 * ci + 1)
            env.sfence(sc)
            sc.activation(row[0:1, 3 * CW:4 * CW], row[0:1, 2 * CW:3 * CW], AF.Sqrt)
            s_rw.inc(env.sfence(sc))

        @env.block.tensor
        def _(t, ci=ci):
            t.wait_ge(s_rw.h, 3 * ci + 3)
            if ci >= 1:
                t.wait_ge(s_e2.h, 2 * ci)
            t.matmul(psR[:, :], ones_f[:, :], row[0:1, 3 * CW:4 * CW], start=True, stop=True)
            ins = t.matmul(psM[:, :], ones_f[:, :], row[0:1, 4 * CW:5 * CW], start=True, stop=True)
            s_st.inc(ins)

        @env.block.vector
        def _(v, ci=ci, xb=xb, xprev=xprev):
            v.wait_ge(s_st.h, 2 * ci + 2)
            env.vfence(v)
            if ci == 0:
                v.wait_ge(s_ws.h, NWS)
            s_e2.inc(v.tensor_copy(rB[:, :], psR[:, :]))
            s_e2.inc(v.tensor_copy(mB[:, :], psM[:, :]))
            for k in range(KT):
                gk = ci * KT + k
                if gk >= 2:
                    v.wait_ge(s_sl.h, sln(gk - 2))
                v.tensor_mul(tmpb[:, :], xb[:, k * CW:(k + 1) * CW], rB[:, :])
                v.tensor_sub(xb[:, k * CW:(k + 1) * CW], tmpb[:, :], mB[:, :])
                dstg = xcp[:, (gk % 2) * CW:(gk % 2 + 1) * CW]
                v.tensor_scalar(dstg, xb[:, k * CW:(k + 1) * CW],
                                cvs[:, 2 * k + 1:2 * k + 2], None, ALU.mult)
                if ci > 0:
                    v.tensor_scalar(tmpb[:, :], xprev[:, k * CW:(k + 1) * CW],
                                    cvs[:, 2 * k:2 * k + 1], None, ALU.mult)
                    v.tensor_add(dstg, dstg, tmpb[:, :])
                s_cv.inc(v.engine_nop())

        @env.block.scalar
        def _(sc, ci=ci):
            for k in range(KT):
                gk = ci * KT + k
                sc.wait_ge(s_cv.h, min(gk + 2, 16 * (ci + 1)))
                if k == KT - 1:
                    env.sfence(sc)
                if gk >= 2:
                    sc.wait_ge(s_sc.h, 16 * (gk - 1))
                ins = sc.activation(xcb[:, (gk % 2) * CW:(gk % 2 + 1) * CW],
                                    xcp[:, (gk % 2) * CW:(gk % 2 + 1) * CW], AF.Silu)
                s_sl.inc(ins)
            s_sl.inc(env.sfence(sc))

        @env.block.sync
        def _(s, ci=ci, c0=c0, c1=c1):
            for k in range(KT):
                gk = ci * KT + k
                s.wait_ge(s_sl.h, min(sln(gk + 1), 17 * (ci + 1)))
                s_sc.inc(s.dma_start(out=xc_t[k][:, c0:c1],
                                     in_=xcb[:, (gk % 2) * CW:(gk % 2 + 1) * CW]), 16)

    @env.block.sync
    def _(s):
        s.wait_ge(s_sx.h, s_sx.n)
        s.wait_ge(s_sc.h, s_sc.n)

    return env.close()



def build_launch_C2(T_=TOWN):
    env = Env("C2")
    nc = env.nc
    PW = 128
    CW = 4 * PW
    NPC = (T_ // 4) // PW
    DK = 1024

    xco = nc.dram_tensor("xco", [D, T_], BF16, kind="ExternalInput").ap()
    wqw = nc.dram_tensor("wqw", [D, DK], F32, kind="ExternalInput").ap()
    wkw = nc.dram_tensor("wkw", [D, DK], F32, kind="ExternalInput").ap()
    wvw = nc.dram_tensor("wvw", [D, D], F32, kind="ExternalInput").ap()
    rwm = nc.dram_tensor("rwm", [128, 32], F32, kind="ExternalInput").ap()
    cosT = nc.dram_tensor("cosT", [128, CW], F32, kind="ExternalInput").ap()
    sinT = nc.dram_tensor("sinT", [128, CW], F32, kind="ExternalInput").ap()
    hs2 = nc.dram_tensor("hs2", [128, 2], F32, kind="ExternalInput").ap()
    onrm = nc.dram_tensor("onrm", [4, T_ // 4, D], BF16, kind="ExternalOutput").ap()
    wv_bf = nc.dram_tensor("wv_bf", [D, D], BF16, kind="Internal").ap()
    xc_t = dram_tiles(xco)
    wv_t = dram_tiles(wvw)
    wvb_t = dram_tiles(wv_bf)

    PAIRS = [(i, j) for i in range(4) for j in range(i + 1)]

    wq_f = env.sbuf("wq_f", [128, KT * DK], BF16)
    wk_f = env.sbuf("wk_f", [128, KT * DK], BF16)
    wv_sl = env.sbuf("wv_sl", [128, 2 * KT * 256], BF16)
    wcast = env.sbuf("wcast", [128, 2048], BF16)
    xc_sb = env.sbuf("xc_sb", [128, KT * CW], BF16)
    qsil = env.sbuf("qsil", [128, 2 * CW], BF16)
    q_sb = env.sbuf("q_sb", [128, 8 * CW], BF16)
    k_sb = env.sbuf("k_sb", [128, 8 * CW], BF16)
    v_sb = env.sbuf("v_sb", [128, 2 * CW], BF16)
    v_tm = env.sbuf("v_tm", [128, 4 * D], BF16)
    csT = env.sbuf("csT", [128, 2 * CW], F32)
    hs2b = env.sbuf("hs2b", [128, 2], BF16)
    rws = env.sbuf("rws", [128, 32], BF16)
    rpt = env.sbuf("rpt", [128, CW], F32)
    rp2 = env.sbuf("rp2", [128, CW], F32)
    rtt = env.sbuf("rtt", [128, 8 * 64], F32)
    mmt = env.sbuf("mmt", [128, 160], F32)
    ctf = env.sbuf("ctf", [128, 160], F32)
    prodb = env.sbuf("prodb", [128, 2 * 1280], BF16)
    o_i = env.sbuf("o_i", [128, D], F32)
    o_n = env.sbuf("o_n", [128, D], BF16)
    st16 = env.sbuf("st16", [128, 8 * 16], F32)
    otmp = env.sbuf("otmp", [128, 128], F32)
    psP = [env.psum("c2_p0", [128, CW]), env.psum("c2_p1", [128, CW])]
    psR = env.psum("c2_r", [128, 128])
    psQ = env.psum("c2_q", [128, 160])
    env.set_junk(env.sbuf("junk", [1, 1024], F32))

    s_w = env.sem("d_w")
    s_wc = env.sem("d_wc")
    s_wv = env.sem("d_wv")
    s_xc = env.sem("d_xc")
    s_pj = env.sem("d_pj")
    s_pe = env.sem("d_pe")
    s_rp = env.sem("d_rp")
    s_rt = env.sem("d_rt")
    s_sg = env.sem("d_sg")
    s_rv = env.sem("d_rv")
    s_pb = env.sem("d_pb")
    s_qd = env.sem("d_qd")
    s_cm = env.sem("d_cm")
    s_vt = env.sem("d_vt")
    s_oa = env.sem("d_oa")
    s_iv = env.sem("d_iv")
    s_sq = env.sem("d_sq")
    s_so = env.sem("d_so")

    env.open_block()

    @env.block.gpsimd
    def _(g):
        for et in range(KT):
            s_w.inc(g.dma_start(out=wq_f[:, et * DK:(et + 1) * DK],
                                in_=wqw.rearrange("(k p) c -> k p c", p=128)[et]), 16)
            s_w.inc(g.dma_start(out=wk_f[:, et * DK:(et + 1) * DK],
                                in_=wkw.rearrange("(k p) c -> k p c", p=128)[et]), 16)
        s_w.inc(g.dma_start(out=rws[:, :], in_=rwm[:, :]), 16)
        s_w.inc(g.dma_start(out=hs2b[:, :], in_=hs2[:, :]), 16)
        s_w.inc(g.dma_start(out=csT[:, 0:CW], in_=cosT[:, :]), 16)
        s_w.inc(g.dma_start(out=csT[:, CW:2 * CW], in_=sinT[:, :]), 16)
        for et in range(KT):
            if et > 0:
                g.wait_ge(s_wc.h, 32 * et - 16)
            s_wc.inc(g.dma_start(out=wcast[:, :], in_=wv_t[et][:, :]), 16)
            g.wait_ge(s_wc.h, 32 * et + 16)
            s_wc.inc(g.dma_start(out=wvb_t[et][:, :], in_=wcast[:, :]), 16)
    NW = s_w.n
    NWC = s_wc.n

    for pc in range(NPC):
        p0 = pc * PW
        PJ0 = s_pj.n
        PE0 = s_pe.n
        RT0 = s_rt.n
        VT0 = s_vt.n
        QD0 = s_qd.n
        RP0 = s_rp.n

        @env.block.sync
        def _(s, pc=pc, p0=p0, PJ0=PJ0):
            if pc == 0:
                s.wait_ge(s_wc.h, NWC)
            if pc > 0:
                s.wait_ge(s_pj.h, PJ0)
            for k in range(KT):
                s_xc.inc(s.dma_start(
                    out=xc_sb[:, k * CW:(k + 1) * CW].rearrange("p (b q) -> p b q", b=4),
                    in_=xc_t[k].rearrange("p (b q) -> p b q", b=4)[:, :, p0:p0 + PW]), 16)

        for sl in range(8):
            @env.block.sync
            def _(s, pc=pc, sl=sl, PJ0=PJ0):
                if sl >= 2:
                    s.wait_ge(s_pj.h, PJ0 + 16 + 2 * (sl - 2) + 2)
                elif pc > 0:
                    s.wait_ge(s_pj.h, PJ0)
                for et in range(KT):
                    s_wv.inc(s.dma_start(
                        out=wv_sl[:, (sl % 2) * KT * 256 + et * 256:
                                  (sl % 2) * KT * 256 + (et + 1) * 256],
                        in_=wvb_t[et][:, sl * 256:(sl + 1) * 256]), 16)

            for h in ([2 * (sl - 1), 2 * (sl - 1) + 1] if sl >= 1 else []):
                @env.block.sync
                def _(s, pc=pc, h=h, PE0=PE0, VT0=VT0):
                    s.wait_ge(s_pe.h, PE0 + 16 + h + 1)
                    s_vt.inc(s.dma_start(
                        out=v_tm.ap().rearrange("p (g x) -> p g x", g=4)[:, :, h * 128:(h + 1) * 128],
                        in_=v_sb[:, (h % 2) * CW:(h % 2 + 1) * CW], transpose=True), 16)

        for h in (14, 15):
            @env.block.sync
            def _(s, pc=pc, h=h, PE0=PE0, VT0=VT0):
                s.wait_ge(s_pe.h, PE0 + 16 + h + 1)
                s_vt.inc(s.dma_start(
                    out=v_tm.ap().rearrange("p (g x) -> p g x", g=4)[:, :, h * 128:(h + 1) * 128],
                    in_=v_sb[:, (h % 2) * CW:(h % 2 + 1) * CW], transpose=True), 16)

        for gt in range(32):
            @env.block.tensor
            def _(t, pc=pc, gt=gt, PJ0=PJ0, PE0=PE0, RT0=RT0):
                if gt == 0:
                    t.wait_ge(s_w.h, NW)
                    t.wait_ge(s_xc.h, 16 * 16 * (pc + 1))
                if gt >= 2:
                    t.wait_ge(s_pe.h, PE0 + gt - 1)
                if gt >= 16:
                    sl = (gt - 16) // 2
                    t.wait_ge(s_wv.h, 16 * 16 * (8 * pc + sl + 1))
                for et in range(KT):
                    if gt < 8:
                        lhs = wq_f[:, et * DK + gt * 128:et * DK + (gt + 1) * 128]
                    elif gt < 16:
                        lhs = wk_f[:, et * DK + (gt - 8) * 128:et * DK + (gt - 7) * 128]
                    else:
                        sl = (gt - 16) // 2
                        half = (gt - 16) % 2
                        lhs = wv_sl[:, (sl % 2) * KT * 256 + et * 256 + half * 128:
                                    (sl % 2) * KT * 256 + et * 256 + (half + 1) * 128]
                    ins = t.matmul(psP[gt % 2][:, :], lhs, xc_sb[:, et * CW:(et + 1) * CW],
                                   start=(et == 0), stop=(et == KT - 1))
                s_pj.inc(ins)
                if gt >= 17:
                    h = gt - 17
                    t.wait_ge(s_pe.h, PE0 + 16 + h + 1)
                    for g in range(4):
                        ins = t.matmul(psR[:, 2 * (g * 16 + h):2 * (g * 16 + h) + 2],
                                       v_sb[:, (h % 2) * CW + g * 128:(h % 2) * CW + (g + 1) * 128],
                                       rws[:, 2 * h:2 * h + 2], start=True, stop=True)
                    s_rt.inc(ins)

            @env.block.scalar
            def _(sc, pc=pc, gt=gt, PJ0=PJ0, RP0=RP0, RT0=RT0, VT0=VT0):
                sc.wait_ge(s_pj.h, PJ0 + gt + 1)
                if gt < 16:
                    if gt >= 2:
                        sc.wait_ge(s_rp.h, RP0 + gt - 1)
                    dst = qsil[:, (gt % 2) * CW:(gt % 2 + 1) * CW]
                    fn = AF.Silu if gt < 8 else AF.Copy
                    ins = sc.activation(dst, psP[gt % 2][:, :], fn)
                else:
                    h = gt - 16
                    if h >= 2:
                        sc.wait_ge(s_rt.h, RT0 + h - 1)
                        sc.wait_ge(s_vt.h, VT0 + 16 * (h - 1))
                    ins = sc.copy(v_sb[:, (h % 2) * CW:(h % 2 + 1) * CW], psP[gt % 2][:, :])
                s_pe.inc(ins)

            if gt < 16:
                @env.block.vector
                def _(v, pc=pc, gt=gt, PE0=PE0):
                    v.wait_ge(s_pe.h, PE0 + gt + 1)
                    env.vfence(v)
                    Q = qsil[:, (gt % 2) * CW:(gt % 2 + 1) * CW]
                    dst = (q_sb if gt < 8 else k_sb)
                    tb = (gt % 8) * CW
                    cosv, sinv = csT[:, 0:CW], csT[:, CW:2 * CW]
                    v.tensor_copy(rpt[0:64, :], Q[64:128, :])
                    v.tensor_copy(rpt[64:128, :], Q[0:64, :])
                    v.tensor_mul(rp2[0:64, :], rpt[0:64, :], sinv[0:64, :])
                    v.tensor_mul(rpt[0:64, :], Q[0:64, :], cosv[0:64, :])
                    v.tensor_sub(dst[0:64, tb:tb + CW], rpt[0:64, :], rp2[0:64, :])
                    v.tensor_mul(rp2[64:128, :], rpt[64:128, :], sinv[64:128, :])
                    v.tensor_mul(rpt[64:128, :], Q[64:128, :], cosv[64:128, :])
                    ins = v.tensor_add(dst[64:128, tb:tb + CW], rpt[64:128, :], rp2[64:128, :])
                    s_rp.inc(ins)

        @env.block.tensor
        def _(t, pc=pc, PE0=PE0):
            h = 15
            t.wait_ge(s_pe.h, PE0 + 16 + h + 1)
            for g in range(4):
                ins = t.matmul(psR[:, 2 * (g * 16 + h):2 * (g * 16 + h) + 2],
                               v_sb[:, (h % 2) * CW + g * 128:(h % 2) * CW + (g + 1) * 128],
                               rws[:, 2 * h:2 * h + 2], start=True, stop=True)
            s_rt.inc(ins)

        RD, RS0, RS1, RI0, RI1, RN0, RN1, RTMP = [rtt.ap()[:, i * 64:(i + 1) * 64]
                                                  for i in range(8)]

        @env.block.vector
        def _(v, pc=pc, RT0=RT0):
            v.wait_ge(s_rt.h, RT0 + 16)
            env.vfence(v)
            l0 = psR.ap().rearrange("p (s r) -> p s r", r=2)[:, :, 0]
            l1 = psR.ap().rearrange("p (s r) -> p s r", r=2)[:, :, 1]
            v.tensor_copy(RD, l0)
            v.tensor_copy(RD, l0)
            v.tensor_copy(RTMP, l1)
            env.vspace(v)
            ins = v.tensor_sub(RD, RD, RTMP)
            s_rv.inc(env.vfence(v))

        @env.block.scalar
        def _(sc, pc=pc):
            sc.wait_ge(s_rv.h, 2 * pc + 1)
            env.sfence(sc)
            sc.activation(RS0, RD, AF.Sigmoid)
            s_sg.inc(env.sfence(sc))

        @env.block.vector
        def _(v, pc=pc):
            v.wait_ge(s_sg.h, pc + 1)
            env.vfence(v)
            v.tensor_scalar(RS1, RS0, -1.0, 1.0, ALU.mult, ALU.add)
            v.tensor_scalar(RI0, RD, 0.0, None, ALU.is_ge)
            env.vspace(v)
            v.tensor_scalar(RI1, RI0, -1.0, 1.0, ALU.mult, ALU.add)
            v.tensor_mul(RS0, RS0, RI0)
            v.tensor_mul(RS1, RS1, RI1)
            for RN, RI in ((RN0, RI0), (RN1, RI1)):
                v.tensor_copy(RN[:, 0:16], RI[:, 0:16])
                for g in range(1, 4):
                    env.vspace(v)
                    v.tensor_add(RN[:, 16 * g:16 * (g + 1)], RN[:, 16 * (g - 1):16 * g],
                                 RI[:, 16 * g:16 * (g + 1)])
            for RN in (RN0, RN1):
                env.vspace(v)
                v.tensor_scalar(RTMP, RN, 0.5, None, ALU.is_ge)
                v.tensor_scalar(RN, RN, 1.0, None, ALU.max)
                env.vspace(v)
                v.reciprocal(RN, RN)
                env.vfence(v)
                v.tensor_mul(RN, RN, RTMP)
            env.vspace(v)
            v.tensor_mul(RS0, RS0, RN0)
            v.tensor_mul(RS1, RS1, RN1)
            v.tensor_scalar(RS0, RS0, 0.125, None, ALU.mult)
            v.tensor_scalar(RS1, RS1, 0.125, None, ALU.mult)
            env.vspace(v)
            for pidx, (i, j) in enumerate(PAIRS):
                ms = mmt[:, pidx * 16:(pidx + 1) * 16]
                v.tensor_mul(ms, RS0[:, 16 * i:16 * (i + 1)], RI0[:, 16 * j:16 * (j + 1)])
                v.tensor_mul(RTMP[:, 0:16], RS1[:, 16 * i:16 * (i + 1)],
                             RI1[:, 16 * j:16 * (j + 1)])
                env.vspace(v)
                v.tensor_add(ms, ms, RTMP[:, 0:16])
            s_rv.inc(v.engine_nop())

            for qt in range(8):
                v.wait_ge(s_rp.h, RP0 + 16)
                if qt >= 2:
                    v.wait_ge(s_qd.h, QD0 + qt - 1)
                for pidx, (i, j) in enumerate(PAIRS):
                    ins = v.tensor_mul(
                        prodb[:, (qt % 2) * 1280 + pidx * 128:(qt % 2) * 1280 + (pidx + 1) * 128],
                        q_sb[:, qt * CW + i * 128:qt * CW + (i + 1) * 128],
                        k_sb[:, qt * CW + j * 128:qt * CW + (j + 1) * 128])
                s_pb.inc(ins)

        @env.block.tensor
        def _(t, pc=pc, QD0=QD0):
            for qt in range(8):
                t.wait_ge(s_pb.h, 8 * pc + qt + 1)
                if pc > 0 and qt == 0:
                    t.wait_ge(s_cm.h, pc)
                for pidx in range(len(PAIRS)):
                    ins = t.matmul(psQ[:, pidx * 16 + 2 * qt:pidx * 16 + 2 * qt + 2],
                                   prodb[:, (qt % 2) * 1280 + pidx * 128:
                                         (qt % 2) * 1280 + (pidx + 1) * 128],
                                   hs2b[:, :], start=True, stop=True)
                s_qd.inc(ins)

        @env.block.vector
        def _(v, pc=pc, QD0=QD0):
            v.wait_ge(s_qd.h, QD0 + 8)
            env.vfence(v)
            v.tensor_copy(ctf[:, :], psQ[:, :])
            v.tensor_copy(ctf[:, :], psQ[:, :])
            env.vspace(v)
            ins = v.tensor_mul(ctf[:, :], ctf[:, :], mmt[:, :])
            s_cm.inc(ins)

        for i in range(4):
            @env.block.vector
            def _(v, pc=pc, i=i, VT0=VT0):
                if i == 0:
                    v.wait_ge(s_vt.h, VT0 + 16 * 16)
                    env.vfence(v)
                if pc > 0 or i > 0:
                    v.wait_ge(s_so.h, 16 * (4 * pc + i))
                pmap = {(a, b): px for px, (a, b) in enumerate(PAIRS)}
                for h in range(16):
                    for j in range(i + 1):
                        px = pmap[(i, j)]
                        csc = ctf[:, px * 16 + h:px * 16 + h + 1]
                        vsl = v_tm[:, j * D + h * 128:j * D + (h + 1) * 128]
                        osl = o_i[:, h * 128:(h + 1) * 128]
                        if j == 0:
                            v.tensor_scalar(osl, vsl, csc, None, ALU.mult)
                        else:
                            v.tensor_scalar(otmp[:, :], vsl, csc, None, ALU.mult)
                            v.tensor_add(osl, osl, otmp[:, :])
                env.vspace(v)
                for h in range(16):
                    v.reduce_sum(st16[:, h:h + 1], o_i[:, h * 128:(h + 1) * 128], axis=AX.X)
                for h in range(16):
                    v.tensor_mul(otmp[:, :], o_i[:, h * 128:(h + 1) * 128],
                                 o_i[:, h * 128:(h + 1) * 128])
                    env.vspace(v)
                    v.reduce_sum(st16[:, 16 + h:17 + h], otmp[:, :], axis=AX.X)
                env.vspace(v)
                v.tensor_scalar(st16[:, 0:16], st16[:, 0:16], 1.0 / 128, None, ALU.mult)
                v.tensor_scalar(st16[:, 16:32], st16[:, 16:32], 1.0 / 128, None, ALU.mult)
                env.vspace(v)
                v.tensor_mul(st16[:, 32:48], st16[:, 0:16], st16[:, 0:16])
                env.vspace(v)
                v.tensor_sub(st16[:, 32:48], st16[:, 16:32], st16[:, 32:48])
                env.vspace(v)
                v.tensor_scalar(st16[:, 32:48], st16[:, 32:48], EPS, None, ALU.add)
                env.vspace(v)
                v.reciprocal(st16[:, 32:48], st16[:, 32:48])
                s_iv.inc(env.vfence(v))

            @env.block.scalar
            def _(sc, pc=pc, i=i):
                sc.wait_ge(s_iv.h, 4 * pc + i + 1)
                env.sfence(sc)
                sc.activation(st16[:, 48:64], st16[:, 32:48], AF.Sqrt)
                s_sq.inc(env.sfence(sc))

            @env.block.vector
            def _(v, pc=pc, i=i):
                v.wait_ge(s_sq.h, 4 * pc + i + 1)
                env.vfence(v)
                for h in range(16):
                    v.tensor_scalar(o_n[:, h * 128:(h + 1) * 128],
                                    o_i[:, h * 128:(h + 1) * 128],
                                    st16[:, h:h + 1], st16[:, 48 + h:49 + h],
                                    ALU.subtract, ALU.mult)
                s_oa.inc(env.vfence(v))

            @env.block.sync
            def _(s, pc=pc, i=i, p0=p0):
                s.wait_ge(s_oa.h, 4 * pc + i + 1)
                s_so.inc(s.dma_start(out=onrm[i][p0:p0 + PW, :], in_=o_n[:, :]), 16)

    @env.block.sync
    def _(s):
        s.wait_ge(s_so.h, s_so.n)

    return env.close()


def host_prep_C2(inputs):
    Wq, Wk, Wv = inputs["L0_ffn_Wq"], inputs["L0_ffn_Wk"], inputs["L0_ffn_Wv"]
    rw = inputs["L0_ffn_router_w"]
    perm = []
    for qt in range(8):
        for h in (2 * qt, 2 * qt + 1):
            perm.extend(range(h * 64, h * 64 + 32))
        for h in (2 * qt, 2 * qt + 1):
            perm.extend(range(h * 64 + 32, h * 64 + 64))
    perm = np.asarray(perm)
    wqw = np.ascontiguousarray(Wq[:, perm]).astype(np.float32)
    wkw = np.ascontiguousarray(Wk[:, perm]).astype(np.float32)
    wvw = np.ascontiguousarray(Wv).astype(np.float32)
    rwm = np.zeros((128, 32), dtype=np.float32)
    for h in range(16):
        rwm[:, 2 * h] = rw[h, :, 0]
        rwm[:, 2 * h + 1] = rw[h, :, 1]
    invf = 1.0 / (10000.0 ** (np.arange(32, dtype=np.float64) / 32.0))
    ang = np.outer(invf, np.arange(4, dtype=np.float64))
    cosT = np.zeros((128, 512), dtype=np.float32)
    sinT = np.zeros((128, 512), dtype=np.float32)
    for r in range(128):
        for g in range(4):
            cosT[r, g * 128:(g + 1) * 128] = np.cos(ang[r % 32, g])
            sinT[r, g * 128:(g + 1) * 128] = np.sin(ang[r % 32, g])
    hs2 = np.zeros((128, 2), dtype=np.float32)
    hs2[0:32, 0] = 1.0
    hs2[64:96, 0] = 1.0
    hs2[32:64, 1] = 1.0
    hs2[96:128, 1] = 1.0
    m = dict(wqw=wqw, wkw=wkw, wvw=wvw, rwm=rwm, cosT=cosT, sinT=sinT, hs2=hs2)
    return [dict(m) for _ in range(NCORE)]



def build_launch_C3(T_=TOWN):
    env = Env("C3")
    nc = env.nc
    CW = 256
    NCH = T_ // CW

    xco = nc.dram_tensor("xco", [D, T_], BF16, kind="ExternalInput").ap()
    onc = nc.dram_tensor("onc", [D, T_], BF16, kind="ExternalInput").ap()
    x1o = nc.dram_tensor("x1o", [D, T_], F32, kind="ExternalInput").ap()
    wgw = nc.dram_tensor("wgw", [D, D], F32, kind="ExternalInput").ap()
    wow = nc.dram_tensor("wow", [D, D], F32, kind="ExternalInput").ap()
    x2o = nc.dram_tensor("x2o", [D, T_], F32, kind="ExternalOutput").ap()
    xn2o = nc.dram_tensor("xn2o", [D, T_], BF16, kind="ExternalOutput").ap()
    xc_t = dram_tiles(xco)
    on_t = dram_tiles(onc)
    x1_t = dram_tiles(x1o)
    wg_t = dram_tiles(wgw)
    wo_t = dram_tiles(wow)
    x2_t = dram_tiles(x2o)
    xn_t = dram_tiles(xn2o)

    wg_sb = env.sbuf("wg_sb", [128, KT * D], BF16)
    wo_sb = env.sbuf("wo_sb", [128, KT * D], BF16)
    xc_sb = env.sbuf("xc_sb", [128, KT * CW], BF16)
    onp = env.sbuf("onp", [128, 4 * CW], BF16)
    gsil = env.sbuf("gsil", [128, 2 * CW], BF16)
    fact = env.sbuf("fact", [128, KT * CW], BF16)
    x1p = env.sbuf("x1p", [128, 4 * CW], F32)
    x2p = env.sbuf("x2p", [128, 4 * CW], F32)
    xb = env.sbuf("xb", [128, KT * CW], BF16)
    sqb = env.sbuf("sqb", [128, KT * CW], BF16)
    row = env.sbuf("row", [1, 6 * CW], F32)
    rB = env.sbuf("rB", [128, CW], BF16)
    mB = env.sbuf("mB", [128, CW], BF16)
    tmpb = env.sbuf("tmpb", [128, CW], BF16)
    xnb = env.sbuf("xnb", [128, 2 * CW], BF16)
    psG = [env.psum("c3_g0", [128, CW]), env.psum("c3_g1", [128, CW])]
    psW = [env.psum("c3_w0", [128, CW]), env.psum("c3_w1", [128, CW])]
    psA = env.psum("c3_psA", [1, CW])
    psB = env.psum("c3_psB", [1, CW])
    psR = env.psum("c3_psR", [128, CW])
    psM = env.psum("c3_psM", [128, CW])
    ones_b = env.sbuf("ones_b", [128, 1], BF16)
    ones_f = env.sbuf("ones_f", [1, 128], F32)
    env.set_junk(env.sbuf("junk", [1, 1024], F32))

    s_w = env.sem("e_w")
    s_ld = env.sem("e_ld")
    s_gm = env.sem("e_gm")
    s_gs = env.sem("e_gs")
    s_fa = env.sem("e_fa")
    s_wm = env.sem("e_wm")
    s_ev = env.sem("e_ev")
    s_sx = env.sem("e_sx")
    s_sq = env.sem("e_sq")
    s_st = env.sem("e_st")
    s_rw = env.sem("e_rw")
    s_e2 = env.sem("e_e2")
    s_nm = env.sem("e_nm")
    s_sn = env.sem("e_sn")
    ini = env.sem("ini")

    def evn(t):
        return t + 1 + t // KT

    env.open_block()
    emit_consts(env, ones_b, ones_f)

    @env.block.gpsimd
    def _(g):
        for et in range(KT):
            s_w.inc(g.dma_start(out=wg_sb[:, et * D:(et + 1) * D], in_=wg_t[et][:, :]), 16)
        for et in range(KT):
            s_w.inc(g.dma_start(out=wo_sb[:, et * D:(et + 1) * D], in_=wo_t[et][:, :]), 16)
    NW = s_w.n

    for ci in range(NCH):
        c0, c1 = ci * CW, (ci + 1) * CW

        @env.block.gpsimd
        def _(g, ci=ci, c0=c0, c1=c1):
            if ci > 0:
                g.wait_ge(s_gm.h, 16 * ci)
            for k in range(KT):
                s_ld.inc(g.dma_start(out=xc_sb[:, k * CW:(k + 1) * CW],
                                     in_=xc_t[k][:, c0:c1]), 16)
            for dt in range(KT):
                gidx = ci * KT + dt
                if gidx >= 4:
                    g.wait_ge(s_fa.h, (gidx - 4) + 1 + (gidx - 4) // KT)
                s_ld.inc(g.dma_start(out=onp[:, (gidx % 4) * CW:(gidx % 4 + 1) * CW],
                                     in_=on_t[dt][:, c0:c1]), 16)
            for dt in range(KT):
                gidx = ci * KT + dt
                if gidx >= 4:
                    g.wait_ge(s_ev.h, evn(gidx - 4))
                s_ld.inc(g.dma_start(out=x1p[:, (gidx % 4) * CW:(gidx % 4 + 1) * CW],
                                     in_=x1_t[dt][:, c0:c1]), 16)

        for dt in range(KT):
            gidx = ci * KT + dt

            @env.block.tensor
            def _(t, ci=ci, dt=dt, gidx=gidx):
                if gidx == 0:
                    t.wait_ge(s_w.h, NW)
                t.wait_ge(s_ld.h, 16 * (48 * ci + 16))
                if gidx >= 2:
                    t.wait_ge(s_gs.h, gidx - 1)
                for et in range(KT):
                    ins = t.matmul(psG[gidx % 2][:, :],
                                   wg_sb[:, et * D + dt * 128:et * D + (dt + 1) * 128],
                                   xc_sb[:, et * CW:(et + 1) * CW],
                                   start=(et == 0), stop=(et == KT - 1))
                s_gm.inc(ins)

            @env.block.scalar
            def _(sc, gidx=gidx):
                sc.wait_ge(s_gm.h, gidx + 1)
                if gidx >= 2:
                    sc.wait_ge(s_fa.h, (gidx - 2) + 1 + (gidx - 2) // KT)
                ins = sc.activation(gsil[:, (gidx % 2) * CW:(gidx % 2 + 1) * CW],
                                    psG[gidx % 2][:, :], AF.Silu)
                s_gs.inc(ins)

            @env.block.vector
            def _(v, ci=ci, dt=dt, gidx=gidx):
                v.wait_ge(s_gs.h, gidx + 1)
                v.wait_ge(s_ld.h, 16 * (48 * ci + 16 + dt + 1))
                if ci > 0 and dt == 0:
                    v.wait_ge(s_wm.h, 16 * ci)
                env.vfence(v)
                ins = v.tensor_mul(fact[:, dt * CW:(dt + 1) * CW],
                                   gsil[:, (gidx % 2) * CW:(gidx % 2 + 1) * CW],
                                   onp[:, (gidx % 4) * CW:(gidx % 4 + 1) * CW])
                s_fa.inc(ins)

        @env.block.vector
        def _(v):
            s_fa.inc(env.vfence(v))

        for dt in range(KT):
            gidx = ci * KT + dt

            @env.block.tensor
            def _(t, ci=ci, dt=dt, gidx=gidx):
                t.wait_ge(s_fa.h, 17 * (ci + 1))
                if gidx >= 2:
                    t.wait_ge(s_ev.h, evn(gidx - 2))
                for et in range(KT):
                    ins = t.matmul(psW[gidx % 2][:, :],
                                   wo_sb[:, et * D + dt * 128:et * D + (dt + 1) * 128],
                                   fact[:, et * CW:(et + 1) * CW],
                                   start=(et == 0), stop=(et == KT - 1))
                s_wm.inc(ins)

            @env.block.vector
            def _(v, ci=ci, dt=dt, gidx=gidx):
                v.wait_ge(s_wm.h, gidx + 1)
                v.wait_ge(s_ld.h, 16 * (48 * ci + 32 + dt + 1))
                if gidx >= 4:
                    v.wait_ge(s_sx.h, 16 * (gidx - 3))
                if dt == 0 and ci > 0:
                    v.wait_ge(s_sq.h, ci)
                env.vfence(v)
                ins = v.tensor_add(x2p[:, (gidx % 4) * CW:(gidx % 4 + 1) * CW],
                                   psW[gidx % 2][:, :],
                                   x1p[:, (gidx % 4) * CW:(gidx % 4 + 1) * CW])
                s_ev.inc(ins)

            @env.block.sync
            def _(s, ci=ci, dt=dt, gidx=gidx, c0=c0, c1=c1):
                s.wait_ge(s_ev.h, min(evn(gidx + 1), 17 * (ci + 1)))
                s_sx.inc(s.dma_start(out=x2_t[dt][:, c0:c1],
                                     in_=x2p[:, (gidx % 4) * CW:(gidx % 4 + 1) * CW]), 16)

        @env.block.vector
        def _(v):
            s_ev.inc(env.vfence(v))

        @env.block.scalar
        def _(sc, ci=ci):
            for dt in range(KT):
                gidx = ci * KT + dt
                sc.wait_ge(s_ev.h, min(evn(gidx + 1), 17 * (ci + 1)))
                sc.copy(xb[:, dt * CW:(dt + 1) * CW],
                        x2p[:, (gidx % 4) * CW:(gidx % 4 + 1) * CW])
                sc.activation(sqb[:, dt * CW:(dt + 1) * CW], xb[:, dt * CW:(dt + 1) * CW],
                              AF.Square)
            s_sq.inc(env.sfence(sc))

        @env.block.tensor
        def _(t, ci=ci):
            t.wait_ge(ini.h, 2)
            t.wait_ge(s_sq.h, ci + 1)
            if ci >= 1:
                t.wait_ge(s_rw.h, 3 * (ci - 1) + 1)
            for k in range(KT):
                t.matmul(psA[0:1, :], ones_b[:, :], xb[:, k * CW:(k + 1) * CW],
                         start=(k == 0), stop=(k == KT - 1))
            for k in range(KT):
                ins = t.matmul(psB[0:1, :], ones_b[:, :], sqb[:, k * CW:(k + 1) * CW],
                               start=(k == 0), stop=(k == KT - 1))
            s_st.inc(ins)

        @env.block.vector
        def _(v, ci=ci):
            v.wait_ge(s_st.h, 2 * ci + 1)
            env.vfence(v)
            v.tensor_scalar(row[0:1, 0:CW], psA[0:1, :], 1.0 / D, None, ALU.mult)
            v.tensor_scalar(row[0:1, CW:2 * CW], psB[0:1, :], 1.0 / D, None, ALU.mult)
            v.tensor_mul(row[0:1, 4 * CW:5 * CW], row[0:1, 0:CW], row[0:1, 0:CW])
            v.tensor_sub(row[0:1, 2 * CW:3 * CW], row[0:1, CW:2 * CW], row[0:1, 4 * CW:5 * CW])
            v.tensor_scalar(row[0:1, 2 * CW:3 * CW], row[0:1, 2 * CW:3 * CW], EPS, None, ALU.add)
            ins = v.reciprocal(row[0:1, 2 * CW:3 * CW], row[0:1, 2 * CW:3 * CW])
            s_rw.inc(ins)
            v.wait_ge(s_rw.h, 3 * ci + 2)
            v.tensor_mul(row[0:1, 4 * CW:5 * CW], row[0:1, 0:CW], row[0:1, 3 * CW:4 * CW])
            s_rw.inc(env.vfence(v))

        @env.block.scalar
        def _(sc, ci=ci):
            sc.wait_ge(s_rw.h, 3 * ci + 1)
            env.sfence(sc)
            sc.activation(row[0:1, 3 * CW:4 * CW], row[0:1, 2 * CW:3 * CW], AF.Sqrt)
            s_rw.inc(env.sfence(sc))

        @env.block.tensor
        def _(t, ci=ci):
            t.wait_ge(s_rw.h, 3 * ci + 3)
            if ci >= 1:
                t.wait_ge(s_e2.h, 2 * ci)
            t.matmul(psR[:, :], ones_f[:, :], row[0:1, 3 * CW:4 * CW], start=True, stop=True)
            ins = t.matmul(psM[:, :], ones_f[:, :], row[0:1, 4 * CW:5 * CW], start=True, stop=True)
            s_st.inc(ins)

        @env.block.vector
        def _(v, ci=ci):
            v.wait_ge(s_st.h, 2 * ci + 2)
            env.vfence(v)
            s_e2.inc(v.tensor_copy(rB[:, :], psR[:, :]))
            s_e2.inc(v.tensor_copy(mB[:, :], psM[:, :]))
            for k in range(KT):
                gk = ci * KT + k
                if gk >= 2:
                    v.wait_ge(s_sn.h, 16 * (gk - 1))
                v.tensor_mul(tmpb[:, :], xb[:, k * CW:(k + 1) * CW], rB[:, :])
                ins = v.tensor_sub(xnb[:, (gk % 2) * CW:(gk % 2 + 1) * CW],
                                   tmpb[:, :], mB[:, :])
                s_nm.inc(ins)
            s_nm.inc(env.vfence(v))

        @env.block.sync
        def _(s, ci=ci, c0=c0, c1=c1):
            for k in range(KT):
                gk = ci * KT + k
                s.wait_ge(s_nm.h, min(gk + 2 + gk // KT, 17 * (ci + 1)))
                s_sn.inc(s.dma_start(out=xn_t[k][:, c0:c1],
                                     in_=xnb[:, (gk % 2) * CW:(gk % 2 + 1) * CW]), 16)

    @env.block.sync
    def _(s):
        s.wait_ge(s_sx.h, s_sx.n)
        s.wait_ge(s_sn.h, s_sn.n)

    return env.close()


def host_prep_C3(inputs):
    Wg = np.ascontiguousarray(inputs["L0_ffn_Wg"]).astype(np.float32)
    Wo = np.ascontiguousarray(inputs["L0_ffn_Wout"]).astype(np.float32)
    m = dict(wgw=Wg, wow=Wo)
    return [dict(m) for _ in range(NCORE)]


def onrm_to_ch_major(onrm_arr):
    a = np.asarray(onrm_arr)
    return np.ascontiguousarray(a.transpose(2, 0, 1).reshape(D, TOWN))


def host_prep_C1(inputs, oat_full):
    Wo = inputs["L0_attn_Wo"]
    cw = inputs["L0_ffn_conv_w"]
    woT = np.ascontiguousarray(Wo.T).astype(np.float32)
    cvw = np.zeros((128, 2 * KT), dtype=np.float32)
    for k in range(KT):
        cvw[:, 2 * k] = cw[k * 128:(k + 1) * 128, 0]
        cvw[:, 2 * k + 1] = cw[k * 128:(k + 1) * 128, 1]
    hs = inputs["L0_attn_hidden_states"]
    maps = []
    for c in range(NCORE):
        cols = np.concatenate([np.arange(b_ * S + SOWN * c, b_ * S + SOWN * (c + 1))
                               for b_ in range(B)])
        oatc = np.ascontiguousarray(oat_full[:, cols])
        x0c = np.ascontiguousarray(
            hs[:, SOWN * c:SOWN * (c + 1), :].transpose(2, 0, 1).reshape(D, TOWN)
        ).astype(np.float32)
        maps.append(dict(oat=oatc, x0c=x0c, woT=woT, cvw=cvw))
    return maps


def unpack_oslc(oslc_arr, S_=S):
    o4 = np.asarray(oslc_arr).astype(np.float32).reshape(2, B, 128, S_)
    nqt = S_ // 128
    got = np.zeros((256, B * S_), dtype=np.float32)
    for hp in range(2):
        for b_ in range(B):
            arr = o4[hp, b_].reshape(128, nqt, 128)
            for h in range(2):
                j = 2 * hp + h
                got[j * 64:(j + 1) * 64, b_ * S_:(b_ + 1) * S_] = \
                    arr[:, :, h * 64:(h + 1) * 64].transpose(2, 1, 0).reshape(64, S_)
    return got



def build_launch_D():
    env = Env("DL")
    nc = env.nc
    LK = 4096
    NQT = 16
    NCH = LK // 512

    def nco(qt):
        return qt // 4 + 5

    xq = nc.dram_tensor("xq", [D, 2048], BF16, kind="ExternalInput").ap()
    xa = nc.dram_tensor("xa", [D, LK], BF16, kind="ExternalInput").ap()
    xv = nc.dram_tensor("xv", [LK, D], BF16, kind="ExternalInput").ap()
    wsw = nc.dram_tensor("wsw", [D, 256], F32, kind="ExternalInput").ap()
    waw = nc.dram_tensor("waw", [D, 256], F32, kind="ExternalInput").ap()
    brevD = nc.dram_tensor("brevD", [1, 6144], F32, kind="ExternalInput").ap()
    jmm = nc.dram_tensor("jmm", [128, 128], F32, kind="ExternalInput").ap()
    otd = nc.dram_tensor("otd", [2048, D], BF16, kind="ExternalOutput").ap()
    ptd = nc.dram_tensor("ptd", [NQT, 32, 128, 128], BF16, kind="Internal").ap()
    xa_t = dram_tiles(xa)
    xq_t = dram_tiles(xq)
    xvv = xv.rearrange("(t p) d -> t p d", p=128)

    ws_sb = env.sbuf("ws_sb", [128, KT * 256], BF16)
    wa_sb = env.sbuf("wa_sb", [128, KT * 256], BF16)
    jm_sb = env.sbuf("jm_sb", [128, 128], BF16)
    xa_st = env.sbuf("xa_st", [128, 2 * KT * 512], BF16)
    act_sb = env.sbuf("act_sb", [128, 2 * LK], BF16)
    st_sb = env.sbuf("st_sb", [128, 2 * 2048], BF16)
    sqs = env.sbuf("sqs", [128, 2 * 512], BF16)
    qn_sb = env.sbuf("qn_sb", [1, 2048], F32)
    an_sb = env.sbuf("an_sb", [1, LK + 16], F32)
    mrow = env.sbuf("mrow", [1, 2048], F32)
    negm16 = env.sbuf("negm16", [16, 2048], BF16)
    nmt16 = env.sbuf("nmt16", [128, 16 * 16], BF16)
    nmtf = env.sbuf("nmtf", [128, 16], F32)
    mskb = env.sbuf("mskb", [128, 8 * 512], BF16)
    p_sb = env.sbuf("p_sb", [128, 8 * 512], BF16)
    pt_sb = env.sbuf("pt_sb", [128, 32 * 128], BF16)
    den_sb = env.sbuf("den_sb", [128, 16], F32)
    dcol = env.sbuf("dcol", [128, 1], F32)
    vt_sb = env.sbuf("vt_sb", [128, 32 * 1024], BF16)
    ptb = act_sb
    ot_sb = p_sb
    ones_b = env.sbuf("ones_b", [128, 1], BF16)
    ones_f = env.sbuf("ones_f", [1, 128], F32)
    psA2 = [env.psum("dl_a0", [128, 512]), env.psum("dl_a1", [128, 512])]
    psS = [env.psum("dl_s0", [128, 512]), env.psum("dl_s1", [128, 512])]
    psO = [env.psum("dl_o0", [128, 512]), env.psum("dl_o1", [128, 512])]
    psN = env.psum("dl_n", [1, 512])
    env.set_junk(env.sbuf("junk", [1, 1024], F32))

    s_w = env.sem("f_w")
    s_mk = env.sem("f_mk")
    s_xa = env.sem("f_xa")
    s_pj = env.sem("f_pj")
    s_pe = env.sem("f_pe")
    s_nq = env.sem("f_nq")
    s_nr = env.sem("f_nr")
    s_pt = env.sem("f_pt")
    s_sc = env.sem("f_sc")
    s_px = env.sem("f_px")
    s_dn = env.sem("f_dn")
    s_ps = env.sem("f_ps")
    s_vl = env.sem("f_vl")
    s_pl = env.sem("f_pl")
    s_om = env.sem("f_om")
    s_oe = env.sem("f_oe")
    s_os = env.sem("f_os")
    ini = env.sem("ini")

    env.open_block()
    emit_consts(env, ones_b, ones_f)

    @env.block.gpsimd
    def _(g):
        for et in range(KT):
            s_w.inc(g.dma_start(out=ws_sb[:, et * 256:(et + 1) * 256],
                                in_=wsw.rearrange("(k p) c -> k p c", p=128)[et]), 16)
            s_w.inc(g.dma_start(out=wa_sb[:, et * 256:(et + 1) * 256],
                                in_=waw.rearrange("(k p) c -> k p c", p=128)[et]), 16)
        s_w.inc(g.dma_start(out=jm_sb[:, :], in_=jmm[:, :]), 16)
    NW = s_w.n

    PJN, PEN = [], []
    for ch in range(NCH + 4):
        is_state = ch >= NCH
        src_t = xq_t if is_state else xa_t
        co = (ch - NCH) if is_state else ch

        @env.block.sync
        def _(s, ch=ch, co=co, src_t=src_t):
            if ch >= 2:
                s.wait_ge(s_pj.h, PJN[ch - 2])
            for et in range(KT):
                s_xa.inc(s.dma_start(
                    out=xa_st[:, (ch % 2) * KT * 512 + et * 512:
                              (ch % 2) * KT * 512 + (et + 1) * 512],
                    in_=src_t[et][:, co * 512:(co + 1) * 512]), 16)

        @env.block.tensor
        def _(t, ch=ch, is_state=is_state):
            if ch == 0:
                t.wait_ge(s_w.h, NW)
            t.wait_ge(s_xa.h, 16 * KT * (ch + 1))
            if ch >= 1:
                t.wait_ge(s_pe.h, PEN[ch - 1])
            xo = (ch % 2) * KT * 512
            wsel = ws_sb if is_state else wa_sb
            for rt in range(2):
                for et in range(KT):
                    ins = t.matmul(psA2[rt][:, :],
                                   wsel[:, et * 256 + rt * 128:et * 256 + (rt + 1) * 128],
                                   xa_st[:, xo + et * 512:xo + (et + 1) * 512],
                                   start=(et == 0), stop=(et == KT - 1))
                s_pj.inc(ins)
        PJN.append(s_pj.n)

        @env.block.scalar
        def _(sc, ch=ch, co=co, is_state=is_state):
            sc.wait_ge(s_pj.h, PJN[ch])
            for rt in range(2):
                if is_state:
                    ins = sc.activation(st_sb[:, rt * 2048 + co * 512:rt * 2048 + (co + 1) * 512],
                                        psA2[rt][:, :], AF.Copy, scale=0.0625)
                else:
                    ins = sc.copy(act_sb[:, rt * LK + co * 512:rt * LK + (co + 1) * 512],
                                  psA2[rt][:, :])
            s_pe.inc(ins)
        PEN.append(s_pe.n)

    NPE = s_pe.n

    for which, nchk in ((0, 4), (1, 8)):
        for c in range(nchk):
            @env.block.scalar
            def _(sc, which=which, c=c):
                sc.wait_ge(s_pe.h, NPE)
                if not (which == 0 and c == 0):
                    sc.wait_ge(s_nq.h, s_nq.n)
                if which == 0 and c == 0:
                    env.sfence(sc)
                srcb = st_sb if which == 0 else act_sb
                W = 2048 if which == 0 else LK
                sc.activation(sqs[:, 0:512], srcb[:, 0 * W + c * 512:0 * W + (c + 1) * 512],
                              AF.Square)
                sc.activation(sqs[:, 512:1024], srcb[:, 1 * W + c * 512:1 * W + (c + 1) * 512],
                              AF.Square)
                s_nq.inc(env.sfence(sc))

            @env.block.tensor
            def _(t, which=which, c=c):
                t.wait_ge(ini.h, 2)
                t.wait_ge(s_nq.h, s_nq.n)
                if s_nr.n > 0:
                    t.wait_ge(s_nr.h, s_nr.n)
                t.matmul(psN[0:1, :], ones_b[:, :], sqs[:, 0:512], start=True, stop=False)
                ins = t.matmul(psN[0:1, :], ones_b[:, :], sqs[:, 512:1024],
                               start=False, stop=True)
                s_nq.inc(ins)

            @env.block.vector
            def _(v, which=which, c=c):
                v.wait_ge(s_nq.h, s_nq.n)
                env.vfence(v)
                dst = qn_sb if which == 0 else an_sb
                ins = v.tensor_copy(dst[0:1, c * 512:(c + 1) * 512], psN[0:1, :])
                s_nr.inc(ins)

    @env.block.vector
    def _(v):
        v.wait_ge(s_nr.h, s_nr.n)
        env.vspace(v)
        v.reduce_max(an_sb[0:1, LK:LK + 1], an_sb[0:1, 0:LK], axis=AX.X)
        env.vspace(v)
        v.memset(negm16[0:16, :], 0.0)
        v.tensor_scalar(mrow[0:1, :], qn_sb[0:1, :], an_sb[0:1, LK:LK + 1], None, ALU.mult)
        s_nr.inc(env.vfence(v))
    NM1 = s_nr.n

    @env.block.scalar
    def _(sc):
        sc.wait_ge(s_nr.h, NM1)
        env.sfence(sc)
        sc.activation(mrow[0:1, :], mrow[0:1, :], AF.Ln)
        sc.activation(mrow[0:1, :], mrow[0:1, :], AF.Exp, scale=0.5)
        s_nr.inc(env.sfence(sc))
    NM2 = s_nr.n

    @env.block.vector
    def _(v):
        v.wait_ge(s_nr.h, NM2)
        env.vfence(v)
        v.tensor_scalar(negm16[0:1, :], mrow[0:1, :], -1.0, None, ALU.mult)
        s_nr.inc(env.vfence(v))
    NM3 = s_nr.n

    @env.block.sync
    def _(s):
        s.wait_ge(s_nr.h, NM3)
        s_pt.inc(s.dma_start(out=nmt16.ap().rearrange("p (t r) -> p t r", r=16),
                             in_=negm16[:, :], transpose=True), 16)
    NPT0 = s_pt.n

    @env.block.scalar
    def _(sc):
        sc.wait_ge(s_pt.h, NPT0)
        ins = sc.copy(nmtf[:, :], nmt16.ap().rearrange("p (t r) -> p t r", r=16)[:, :, 0])
        s_px.inc(ins)
    PX0 = s_px.n

    SCN = []
    PXN = []
    DNN = []
    PTN = []
    for qt in range(NQT):
        nc_q = nco(qt)

        @env.block.gpsimd
        def _(g, qt=qt, nc_q=nc_q):
            g.wait_ge(s_pt.h, s_pt.n)
            if qt >= 1:
                g.wait_ge(s_sc.h, SCN[qt - 1])
            for kc in range(nc_q):
                srcap = AP(brevD.tensor, offset=1920 - qt * 128 + kc * 512,
                           ap=[[1, 128], [1, 512]])
                s_mk.inc(g.dma_start(
                    out=mskb[:, kc * 512:(kc + 1) * 512], in_=srcap), 16)
        MKN = s_mk.n

        for kc in range(nc_q):
            @env.block.tensor
            def _(t, qt=qt, kc=kc, MKN=MKN):
                t.wait_ge(s_mk.h, MKN)
                if kc == 0:
                    t.wait_ge(s_px.h, PX0)
                    if qt > 0:
                        t.wait_ge(s_pt.h, PTN[qt - 1])
                if s_px.n >= 2:
                    t.wait_ge(s_px.h, s_px.n - 1)
                t.matmul(psS[kc % 2][:, :], st_sb[:, 0 * 2048 + qt * 128:0 * 2048 + (qt + 1) * 128],
                         act_sb[:, 0 * LK + kc * 512:0 * LK + (kc + 1) * 512],
                         start=True, stop=False)
                t.matmul(psS[kc % 2][:, :], st_sb[:, 1 * 2048 + qt * 128:1 * 2048 + (qt + 1) * 128],
                         act_sb[:, 1 * LK + kc * 512:1 * LK + (kc + 1) * 512],
                         start=False, stop=False)
                ins = t.matmul(psS[kc % 2][:, :], jm_sb[:, :],
                               mskb[:, kc * 512:(kc + 1) * 512],
                               start=False, stop=True)
                s_sc.inc(ins)

            @env.block.scalar
            def _(sc, qt=qt, kc=kc):
                sc.wait_ge(s_sc.h, s_sc.n)
                if kc == 0 and qt > 0:
                    sc.wait_ge(s_dn.h, DNN[qt - 1])
                ins = sc.activation(p_sb[:, kc * 512:(kc + 1) * 512], psS[kc % 2][:, :],
                                    AF.Exp, bias=nmtf[:, qt:qt + 1],
                                    accum_out=den_sb[:, kc:kc + 1])
                s_px.inc(ins)
        SCN.append(s_sc.n)
        PXN.append(s_px.n)

        @env.block.vector
        def _(v, qt=qt, nc_q=nc_q):
            v.wait_ge(s_px.h, PXN[qt])
            env.vfence(v)
            v.reduce_sum(dcol[:, 0:1], den_sb[:, 0:nc_q], axis=AX.X)
            env.vspace(v)
            v.reciprocal(dcol[:, 0:1], dcol[:, 0:1])
            env.vfence(v)
            for kc in range(nc_q):
                v.tensor_scalar(p_sb[:, kc * 512:(kc + 1) * 512],
                                p_sb[:, kc * 512:(kc + 1) * 512],
                                dcol[:, 0:1], None, ALU.mult)
            s_dn.inc(env.vfence(v))
        DNN.append(s_dn.n)

        @env.block.sync
        def _(s, qt=qt, nc_q=nc_q):
            s.wait_ge(s_dn.h, DNN[qt])
            s_pt.inc(s.dma_start(
                out=pt_sb.ap()[:, 0:nc_q * 4 * 128].rearrange("p (t q) -> p t q", q=128),
                in_=p_sb[:, 0:nc_q * 512], transpose=True), 16)
            s_ps.inc(s.dma_start(
                out=ptd[qt].rearrange("t p q -> p t q")[:, 0:nc_q * 4, :],
                in_=pt_sb.ap()[:, 0:nc_q * 4 * 128].rearrange("p (t q) -> p t q", q=128)), 16)
        PTN.append(s_pt.n)
    NPS = s_ps.n

    OMN = {}
    for chh in range(2):
        @env.block.sync
        def _(s, chh=chh):
            if chh > 0:
                s.wait_ge(s_om.h, OMN[(0, NQT - 1, 1)])
            else:
                s.wait_ge(s_ps.h, NPS)
            for kt in range(32):
                s_vl.inc(s.dma_start(out=vt_sb[:, kt * 1024:(kt + 1) * 1024],
                                     in_=xvv[kt][:, chh * 1024:(chh + 1) * 1024]), 16)
        VLN = s_vl.n

        for qt in range(NQT):
            nc_q = nco(qt)

            @env.block.sync
            def _(s, chh=chh, qt=qt, nc_q=nc_q):
                if qt >= 2:
                    s.wait_ge(s_om.h, OMN[(chh, qt - 2, 1)])
                s_pl.inc(s.dma_start(
                    out=ptb.ap()[:, (qt % 2) * 4096:(qt % 2) * 4096 + nc_q * 4 * 128]
                        .rearrange("p (t q) -> p t q", q=128),
                    in_=ptd[qt].rearrange("t p q -> p t q")[:, 0:nc_q * 4, :]), 16)

            for c5 in range(2):
                @env.block.tensor
                def _(t, chh=chh, qt=qt, c5=c5, nc_q=nc_q, VLN=VLN):
                    t.wait_ge(s_vl.h, VLN)
                    t.wait_ge(s_pl.h, s_pl.n)
                    if s_oe.n >= 2:
                        t.wait_ge(s_oe.h, s_oe.n - 1)
                    for kt in range(nc_q * 4):
                        ins = t.matmul(psO[c5][:, :],
                                       ptb[:, (qt % 2) * 4096 + kt * 128:(qt % 2) * 4096 + (kt + 1) * 128],
                                       vt_sb[:, kt * 1024 + c5 * 512:kt * 1024 + (c5 + 1) * 512],
                                       start=(kt == 0), stop=(kt == nc_q * 4 - 1))
                    s_om.inc(ins)
                OMN[(chh, qt, c5)] = s_om.n

                @env.block.scalar
                def _(sc, c5=c5):
                    sc.wait_ge(s_om.h, s_om.n)
                    if s_os.n >= 32:
                        sc.wait_ge(s_os.h, s_os.n - 16)
                    ins = sc.copy(ot_sb[:, c5 * 512:(c5 + 1) * 512], psO[c5][:, :])
                    s_oe.inc(ins)

                @env.block.sync
                def _(s, chh=chh, qt=qt, c5=c5):
                    s.wait_ge(s_oe.h, s_oe.n)
                    s_os.inc(s.dma_start(
                        out=otd[qt * 128:(qt + 1) * 128,
                                chh * 1024 + c5 * 512:chh * 1024 + (c5 + 1) * 512],
                        in_=ot_sb[:, c5 * 512:(c5 + 1) * 512]), 16)
            OMN[(chh, qt, 1)] = s_om.n

    @env.block.sync
    def _(s):
        s.wait_ge(s_os.h, s_os.n)

    return env.close()


def host_prep_D(xn2_full):
    maps = []
    u = np.arange(6144)
    for c in range(NCORE):
        b, qh = c // 2, c % 2
        dd = (2047 - u) + qh * 2048
        brev = np.where(dd >= 0, 0.0, -1e30).astype(np.float32)[None, :]
        xa = np.ascontiguousarray(xn2_full[b].T)
        xq = np.ascontiguousarray(xa[:, qh * 2048:(qh + 1) * 2048])
        xv = np.ascontiguousarray(xn2_full[b])
        maps.append(dict(xq=xq, xa=xa, xv=xv, brevD=brev,
                         jmm=np.eye(128, dtype=np.float32)[::-1].copy()))
    return maps



def build_launch_E1(T_=TOWN):
    env = Env("E1")
    nc = env.nc
    CW = 512
    NCH = T_ // CW

    oat = nc.dram_tensor("oat", [D, T_], BF16, kind="ExternalInput").ap()
    x0c = nc.dram_tensor("x0c", [D, T_], F32, kind="ExternalInput").ap()
    woT = nc.dram_tensor("woT", [D, D], F32, kind="ExternalInput").ap()
    bob = nc.dram_tensor("bob", [128, KT], F32, kind="ExternalInput").ap()
    x1o = nc.dram_tensor("x1o", [D, T_], F32, kind="ExternalOutput").ap()
    xno = nc.dram_tensor("xno", [D, T_], BF16, kind="ExternalOutput").ap()
    oat_t = dram_tiles(oat)
    x0_t = dram_tiles(x0c)
    wo_t = dram_tiles(woT)
    x1_t = dram_tiles(x1o)
    xn_t = dram_tiles(xno)

    wo_sb = env.sbuf("wo_sb", [128, KT * D], BF16)
    oat_sb = env.sbuf("oat_sb", [128, KT * CW], BF16)
    x0p = env.sbuf("x0p", [128, 4 * CW], F32)
    x1p = env.sbuf("x1p", [128, 4 * CW], F32)
    xb = env.sbuf("xb", [128, KT * CW], BF16)
    sqb = env.sbuf("sqb", [128, KT * CW], BF16)
    row = env.sbuf("row", [1, 6 * CW], F32)
    rB = env.sbuf("rB", [128, CW], BF16)
    mB = env.sbuf("mB", [128, CW], BF16)
    tmpb = env.sbuf("tmpb", [128, CW], BF16)
    xnb = env.sbuf("xnb", [128, 2 * CW], BF16)
    bos = env.sbuf("bos", [128, KT], F32)
    psm = [env.psum("e1_ps0", [128, CW]), env.psum("e1_ps1", [128, CW])]
    psA = env.psum("e1_psA", [1, CW])
    psB = env.psum("e1_psB", [1, CW])
    psR = env.psum("e1_psR", [128, CW])
    psM = env.psum("e1_psM", [128, CW])
    ones_b = env.sbuf("ones_b", [128, 1], BF16)
    ones_f = env.sbuf("ones_f", [1, 128], F32)
    env.set_junk(env.sbuf("junk", [1, 1024], F32))

    s_w = env.sem("g_w")
    s_ws = env.sem("g_ws")
    s_oa = env.sem("g_oa")
    s_x0 = env.sem("g_x0")
    s_mm = env.sem("g_mm")
    s_ev = env.sem("g_ev")
    s_sx = env.sem("g_sx")
    s_sq = env.sem("g_sq")
    s_st = env.sem("g_st")
    s_rw = env.sem("g_rw")
    s_e2 = env.sem("g_e2")
    s_nm = env.sem("g_nm")
    s_sn = env.sem("g_sn")
    ini = env.sem("ini")

    def evn(t):
        return t + 1 + t // KT

    env.open_block()
    emit_consts(env, ones_b, ones_f)

    @env.block.gpsimd
    def _(g):
        for et in range(KT):
            s_w.inc(g.dma_start(out=wo_sb[:, et * D:(et + 1) * D], in_=wo_t[et][:, :]), 16)
    NW = s_w.n

    @env.block.sync
    def _(s):
        s_ws.inc(s.dma_start(out=bos[:, :], in_=bob[:, :]), 16)
    NWS = s_ws.n

    for ci in range(NCH):
        c0, c1 = ci * CW, (ci + 1) * CW

        @env.block.gpsimd
        def _(g, ci=ci, c0=c0, c1=c1):
            if ci > 0:
                g.wait_ge(s_mm.h, 16 * ci)
            for et in range(KT):
                s_oa.inc(g.dma_start(out=oat_sb[:, et * CW:(et + 1) * CW],
                                     in_=oat_t[et][:, c0:c1]), 16)
            for dt in range(KT):
                gidx = ci * KT + dt
                if gidx >= 4:
                    g.wait_ge(s_ev.h, evn(gidx - 4))
                s_x0.inc(g.dma_start(out=x0p[:, (gidx % 4) * CW:(gidx % 4 + 1) * CW],
                                     in_=x0_t[dt][:, c0:c1]), 16)

        for dt in range(KT):
            gidx = ci * KT + dt

            @env.block.tensor
            def _(t, ci=ci, dt=dt, gidx=gidx):
                if gidx == 0:
                    t.wait_ge(s_w.h, NW)
                t.wait_ge(s_oa.h, 16 * KT * (ci + 1))
                if gidx >= 2:
                    t.wait_ge(s_ev.h, evn(gidx - 2))
                for et in range(KT):
                    ins = t.matmul(psm[gidx % 2][:, :],
                                   wo_sb[:, et * D + dt * 128:et * D + (dt + 1) * 128],
                                   oat_sb[:, et * CW:(et + 1) * CW],
                                   start=(et == 0), stop=(et == KT - 1))
                s_mm.inc(ins)

            @env.block.vector
            def _(v, ci=ci, dt=dt, gidx=gidx):
                v.wait_ge(s_mm.h, gidx + 1)
                v.wait_ge(s_x0.h, 16 * (gidx + 1))
                if gidx == 0:
                    v.wait_ge(s_ws.h, NWS)
                if gidx >= 4:
                    v.wait_ge(s_sx.h, 16 * (gidx - 3))
                if dt == 0 and ci > 0:
                    v.wait_ge(s_sq.h, ci)
                env.vfence(v)
                sl = x1p[:, (gidx % 4) * CW:(gidx % 4 + 1) * CW]
                v.tensor_add(sl, psm[gidx % 2][:, :], x0p[:, (gidx % 4) * CW:(gidx % 4 + 1) * CW])
                ins = v.tensor_scalar(sl, sl, bos[:, dt:dt + 1], None, ALU.add)
                s_ev.inc(ins)

            @env.block.sync
            def _(s, ci=ci, dt=dt, gidx=gidx, c0=c0, c1=c1):
                s.wait_ge(s_ev.h, min(evn(gidx + 1), 17 * (ci + 1)))
                s_sx.inc(s.dma_start(out=x1_t[dt][:, c0:c1],
                                     in_=x1p[:, (gidx % 4) * CW:(gidx % 4 + 1) * CW]), 16)

        @env.block.vector
        def _(v):
            s_ev.inc(env.vfence(v))

        @env.block.scalar
        def _(sc, ci=ci):
            for dt in range(KT):
                gidx = ci * KT + dt
                sc.wait_ge(s_ev.h, min(evn(gidx + 1), 17 * (ci + 1)))
                sc.copy(xb[:, dt * CW:(dt + 1) * CW],
                        x1p[:, (gidx % 4) * CW:(gidx % 4 + 1) * CW])
                sc.activation(sqb[:, dt * CW:(dt + 1) * CW], xb[:, dt * CW:(dt + 1) * CW],
                              AF.Square)
            s_sq.inc(env.sfence(sc))

        @env.block.tensor
        def _(t, ci=ci):
            t.wait_ge(ini.h, 2)
            t.wait_ge(s_sq.h, ci + 1)
            if ci >= 1:
                t.wait_ge(s_rw.h, 3 * (ci - 1) + 1)
            for k in range(KT):
                t.matmul(psA[0:1, :], ones_b[:, :], xb[:, k * CW:(k + 1) * CW],
                         start=(k == 0), stop=(k == KT - 1))
            for k in range(KT):
                ins = t.matmul(psB[0:1, :], ones_b[:, :], sqb[:, k * CW:(k + 1) * CW],
                               start=(k == 0), stop=(k == KT - 1))
            s_st.inc(ins)

        @env.block.vector
        def _(v, ci=ci):
            v.wait_ge(s_st.h, 2 * ci + 1)
            env.vfence(v)
            v.tensor_scalar(row[0:1, 0:CW], psA[0:1, :], 1.0 / D, None, ALU.mult)
            v.tensor_scalar(row[0:1, CW:2 * CW], psB[0:1, :], 1.0 / D, None, ALU.mult)
            v.tensor_mul(row[0:1, 4 * CW:5 * CW], row[0:1, 0:CW], row[0:1, 0:CW])
            v.tensor_sub(row[0:1, 2 * CW:3 * CW], row[0:1, CW:2 * CW], row[0:1, 4 * CW:5 * CW])
            v.tensor_scalar(row[0:1, 2 * CW:3 * CW], row[0:1, 2 * CW:3 * CW], EPS, None, ALU.add)
            ins = v.reciprocal(row[0:1, 2 * CW:3 * CW], row[0:1, 2 * CW:3 * CW])
            s_rw.inc(ins)
            v.wait_ge(s_rw.h, 3 * ci + 2)
            v.tensor_mul(row[0:1, 4 * CW:5 * CW], row[0:1, 0:CW], row[0:1, 3 * CW:4 * CW])
            s_rw.inc(env.vfence(v))

        @env.block.scalar
        def _(sc, ci=ci):
            sc.wait_ge(s_rw.h, 3 * ci + 1)
            env.sfence(sc)
            sc.activation(row[0:1, 3 * CW:4 * CW], row[0:1, 2 * CW:3 * CW], AF.Sqrt)
            s_rw.inc(env.sfence(sc))

        @env.block.tensor
        def _(t, ci=ci):
            t.wait_ge(s_rw.h, 3 * ci + 3)
            if ci >= 1:
                t.wait_ge(s_e2.h, 2 * ci)
            t.matmul(psR[:, :], ones_f[:, :], row[0:1, 3 * CW:4 * CW], start=True, stop=True)
            ins = t.matmul(psM[:, :], ones_f[:, :], row[0:1, 4 * CW:5 * CW], start=True, stop=True)
            s_st.inc(ins)

        @env.block.vector
        def _(v, ci=ci):
            v.wait_ge(s_st.h, 2 * ci + 2)
            env.vfence(v)
            s_e2.inc(v.tensor_copy(rB[:, :], psR[:, :]))
            s_e2.inc(v.tensor_copy(mB[:, :], psM[:, :]))
            for k in range(KT):
                gk = ci * KT + k
                if gk >= 2:
                    v.wait_ge(s_sn.h, 16 * (gk - 1))
                v.tensor_mul(tmpb[:, :], xb[:, k * CW:(k + 1) * CW], rB[:, :])
                ins = v.tensor_sub(xnb[:, (gk % 2) * CW:(gk % 2 + 1) * CW],
                                   tmpb[:, :], mB[:, :])
                s_nm.inc(ins)
            s_nm.inc(env.vfence(v))

        @env.block.sync
        def _(s, ci=ci, c0=c0, c1=c1):
            for k in range(KT):
                gk = ci * KT + k
                s.wait_ge(s_nm.h, min(gk + 2 + gk // KT, 17 * (ci + 1)))
                s_sn.inc(s.dma_start(out=xn_t[k][:, c0:c1],
                                     in_=xnb[:, (gk % 2) * CW:(gk % 2 + 1) * CW]), 16)

    @env.block.sync
    def _(s):
        s.wait_ge(s_sx.h, s_sx.n)
        s.wait_ge(s_sn.h, s_sn.n)

    return env.close()


def host_prep_E1(inputs, oat1_full):
    Wo = inputs["L1_attn_Wo"]
    bo = inputs["L1_attn_bo"]
    woT = np.ascontiguousarray(Wo.T).astype(np.float32)
    bob = np.ascontiguousarray(bo.reshape(KT, 128).T).astype(np.float32)
    maps = []
    for c in range(NCORE):
        cols = np.concatenate([np.arange(b_ * S + SOWN * c, b_ * S + SOWN * (c + 1))
                               for b_ in range(B)])
        oatc = np.ascontiguousarray(oat1_full[:, cols])
        maps.append(dict(oat=oatc, woT=woT, bob=bob))
    return maps



def build_launch_E2a(T_=TOWN):
    env = Env("E2a")
    nc = env.nc
    CW = 512
    NCH = T_ // CW
    IN8 = 8192

    xn3 = nc.dram_tensor("xn3", [D, T_], BF16, kind="ExternalInput").ap()
    wupT = nc.dram_tensor("wupT", [D, IN8], F32, kind="ExternalInput").ap()
    wgtT = nc.dram_tensor("wgtT", [D, IN8], F32, kind="ExternalInput").ap()
    supm = nc.dram_tensor("supm", [128, KT], F32, kind="ExternalInput").ap()
    wsub = nc.dram_tensor("wsub", [128, 64], F32, kind="ExternalInput").ap()
    sudb = nc.dram_tensor("sudb", [128, 64], F32, kind="ExternalInput").ap()
    wsgb = nc.dram_tensor("wsgb", [128, 64], F32, kind="ExternalInput").ap()
    hmat = nc.dram_tensor("hmat", [128, 128], F32, kind="ExternalInput").ap()
    h2d = nc.dram_tensor("h2d", [IN8, T_], BF16, kind="ExternalOutput").ap()
    xn_t = dram_tiles(xn3)
    wup_t = dram_tiles(wupT)
    wgt_t = dram_tiles(wgtT)
    h2_t = h2d.rearrange("(o p) t -> o p t", p=128)

    AR1 = env.sbuf("AR1", [128, 32768], BF16)
    ARf = AR1.ap().bitcast(F32)
    yb = env.sbuf("yb", [128, KT * T_], BF16)
    xch = env.sbuf("xch", [128, KT * CW], BF16)
    wub = env.sbuf("wub", [128, 2 * KT * 128], BF16)
    wgf = env.sbuf("wgf", [128, 2 * KT * 128], F32)
    wgb = env.sbuf("wgb", [128, 2 * KT * 128], BF16)
    sgp = env.sbuf("sgp", [128, 2 * CW], BF16)
    ftmp = env.sbuf("ftmp", [128, CW], F32)
    h2blk = env.sbuf("h2blk", [128, 2 * T_], BF16)
    sup = env.sbuf("sup", [128, KT], F32)
    wsu_s = env.sbuf("wsu_s", [128, 64], F32)
    sud_s = env.sbuf("sud_s", [128, 64], F32)
    wsg_s = env.sbuf("wsg_s", [128, 64], F32)
    hm_sb = env.sbuf("hm_sb", [128, 128], BF16)
    psH = [env.psum("ea_h0", [128, CW]), env.psum("ea_h1", [128, CW])]
    psU = [env.psum("ea_u0", [128, CW]), env.psum("ea_u1", [128, CW])]
    psG = [env.psum("ea_g0", [128, CW]), env.psum("ea_g1", [128, CW])]
    env.set_junk(env.sbuf("junk", [1, 1024], F32))

    s_w = env.sem("h_w")
    s_iv = env.sem("h_iv")
    s_xc = env.sem("h_xc")
    s_su = env.sem("h_su")
    s_hm = env.sem("h_hm")
    s_he = env.sem("h_he")
    s_yc = env.sem("h_yc")
    s_wu = env.sem("h_wu")
    s_tq = env.sem("h_tq")
    s_gc = env.sem("h_gc")
    s_um = env.sem("h_um")
    s_sg = env.sem("h_sg")
    s_hv = env.sem("h_hv")
    s_hs = env.sem("h_hs")

    env.open_block()

    @env.block.gpsimd
    def _(g):
        s_w.inc(g.dma_start(out=hm_sb[:, :], in_=hmat[:, :]), 16)
        s_w.inc(g.dma_start(out=sup[:, :], in_=supm[:, :]), 16)
        s_w.inc(g.dma_start(out=wsu_s[:, :], in_=wsub[:, :]), 16)
        s_w.inc(g.dma_start(out=sud_s[:, :], in_=sudb[:, :]), 16)
        s_w.inc(g.dma_start(out=wsg_s[:, :], in_=wsgb[:, :]), 16)
    NW = s_w.n

    @env.block.vector
    def _(v):
        v.wait_ge(s_w.h, NW)
        env.vfence(v)
        v.tensor_mul(wsu_s[:, :], wsu_s[:, :], sud_s[:, :])
        env.vspace(v)
        v.tensor_scalar(wsu_s[:, :], wsu_s[:, :], 1.0 / np.sqrt(2048.0), None, ALU.mult)
        v.tensor_scalar(wsg_s[:, :], wsg_s[:, :], 1.0 / np.sqrt(2048.0), None, ALU.mult)
        s_iv.inc(env.vfence(v))

    for c in range(NCH):
        @env.block.sync
        def _(s, c=c):
            if c > 0:
                s.wait_ge(s_hm.h, 16 * c)
            for k in range(KT):
                s_xc.inc(s.dma_start(out=xch[:, k * CW:(k + 1) * CW],
                                     in_=xn_t[k][:, c * CW:(c + 1) * CW]), 16)

        @env.block.vector
        def _(v, c=c):
            v.wait_ge(s_xc.h, 16 * 16 * (c + 1))
            if c == 0:
                v.wait_ge(s_iv.h, 1)
            env.vfence(v)
            for k in range(KT):
                v.tensor_scalar(xch[:, k * CW:(k + 1) * CW], xch[:, k * CW:(k + 1) * CW],
                                sup[:, k:k + 1], None, ALU.mult)
            s_su.inc(env.vfence(v))

        @env.block.tensor
        def _(t, c=c):
            t.wait_ge(s_su.h, c + 1)
            for k in range(KT):
                if 16 * c + k >= 2:
                    t.wait_ge(s_he.h, 16 * c + k - 1)
                ins = t.matmul(psH[k % 2][:, :], hm_sb[:, :], xch[:, k * CW:(k + 1) * CW],
                               start=True, stop=True)
                s_hm.inc(ins)

        @env.block.scalar
        def _(sc, c=c):
            for k in range(KT):
                sc.wait_ge(s_hm.h, 16 * c + k + 1)
                ins = sc.copy(ARf[:, k * CW:(k + 1) * CW], psH[k % 2][:, :])
                s_he.inc(ins)

        @env.block.vector
        def _(v, c=c):
            v.wait_ge(s_he.h, 16 * (c + 1))
            env.vfence(v)
            A = [ARf[:, k * CW:(k + 1) * CW] for k in range(KT)]
            Bv = [ARf[:, (16 + k) * CW:(17 + k) * CW] for k in range(KT)]
            for li, s in enumerate((1, 2, 4, 8)):
                src, dst = (A, Bv) if li % 2 == 0 else (Bv, A)
                if li == 3:
                    dst = [yb.ap()[:, k * T_ + c * CW:k * T_ + (c + 1) * CW]
                           for k in range(KT)]
                for t in range(KT):
                    if t & s == 0:
                        v.tensor_add(dst[t], src[t], src[t + s])
                    else:
                        v.tensor_sub(dst[t], src[t - s], src[t])
            s_yc.inc(env.vfence(v))

    for ot in range(64):
        @env.block.gpsimd
        def _(g, ot=ot):
            if ot >= 2:
                g.wait_ge(s_um.h, 4 * (ot - 1))
            for et in range(KT):
                s_wu.inc(g.dma_start(
                    out=wub[:, (ot % 2) * 2048 + et * 128:(ot % 2) * 2048 + (et + 1) * 128],
                    in_=wup_t[et][:, ot * 128:(ot + 1) * 128]), 16)

        @env.block.scalar
        def _(sc, ot=ot):
            def issue_loads(o2):
                for et in range(KT):
                    s_tq.inc(sc.dma_start(
                        out=wgf[:, (o2 % 2) * 2048 + et * 128:(o2 % 2) * 2048 + (et + 1) * 128],
                        in_=wgt_t[et][:, o2 * 128:(o2 + 1) * 128]), 16)
            if ot == 0:
                issue_loads(0)
                issue_loads(1)
            sc.wait_ge(s_tq.h, 16 * 16 * (ot + 1))
            if ot >= 2:
                sc.wait_ge(s_um.h, 4 * (ot - 1))
            for et in range(KT):
                sl = slice((ot % 2) * 2048 + et * 128, (ot % 2) * 2048 + (et + 1) * 128)
                sc.copy(wgb[:, sl], wgf[:, sl])
            s_gc.inc(env.sfence(sc))
            if ot + 2 < 64:
                issue_loads(ot + 2)

        for c in range(NCH):
            uc = ot * 4 + c

            @env.block.tensor
            def _(t, ot=ot, c=c, uc=uc):
                if uc == 0:
                    t.wait_ge(s_yc.h, NCH)
                t.wait_ge(s_wu.h, 16 * 16 * (ot + 1))
                t.wait_ge(s_gc.h, ot + 1)
                if uc >= 2:
                    t.wait_ge(s_hv.h, uc - 1)
                for et in range(KT):
                    t.matmul(psU[uc % 2][:, :],
                             wub[:, (ot % 2) * 2048 + et * 128:(ot % 2) * 2048 + (et + 1) * 128],
                             yb[:, et * T_ + c * CW:et * T_ + (c + 1) * CW],
                             start=(et == 0), stop=(et == KT - 1))
                for et in range(KT):
                    ins = t.matmul(psG[uc % 2][:, :],
                                   wgb[:, (ot % 2) * 2048 + et * 128:(ot % 2) * 2048 + (et + 1) * 128],
                                   yb[:, et * T_ + c * CW:et * T_ + (c + 1) * CW],
                                   start=(et == 0), stop=(et == KT - 1))
                s_um.inc(ins)

            @env.block.scalar
            def _(sc, ot=ot, uc=uc):
                sc.wait_ge(s_um.h, uc + 1)
                if uc >= 2:
                    sc.wait_ge(s_hv.h, uc - 1)
                ins = sc.activation(sgp[:, (uc % 2) * CW:(uc % 2 + 1) * CW],
                                    psG[uc % 2][:, :], AF.Silu,
                                    scale=wsg_s[:, ot:ot + 1])
                s_sg.inc(ins)

            @env.block.vector
            def _(v, ot=ot, c=c, uc=uc):
                v.wait_ge(s_sg.h, uc + 1)
                if c == 0 and ot >= 2:
                    v.wait_ge(s_hs.h, 16 * (ot - 1))
                env.vfence(v)
                v.tensor_scalar(ftmp[:, :], psU[uc % 2][:, :], wsu_s[:, ot:ot + 1],
                                None, ALU.mult)
                ins = v.tensor_mul(h2blk[:, (ot % 2) * T_ + c * CW:
                                         (ot % 2) * T_ + (c + 1) * CW],
                                   ftmp[:, :], sgp[:, (uc % 2) * CW:(uc % 2 + 1) * CW])
                if c == NCH - 1:
                    ins = env.vfence(v)
                s_hv.inc(ins)

        @env.block.sync
        def _(s, ot=ot):
            s.wait_ge(s_hv.h, 4 * (ot + 1))
            s_hs.inc(s.dma_start(out=h2_t[ot],
                                 in_=h2blk[:, (ot % 2) * T_:(ot % 2 + 1) * T_]), 16)

    @env.block.sync
    def _(s):
        s.wait_ge(s_hs.h, s_hs.n)

    return env.close()



def build_launch_E2b(T_=TOWN):
    env = Env("E2b")
    nc = env.nc
    CW = 1024
    NCH = T_ // CW
    IN8 = 8192
    NOT = 64

    h2d = nc.dram_tensor("h2d", [IN8, T_], BF16, kind="ExternalInput").ap()
    x3c = nc.dram_tensor("x3c", [D, T_], F32, kind="ExternalInput").ap()
    wdnT = nc.dram_tensor("wdnT", [IN8, D], F32, kind="ExternalInput").ap()
    wsdb = nc.dram_tensor("wsdb", [128, KT], F32, kind="ExternalInput").ap()
    hmat = nc.dram_tensor("hmat", [128, 128], F32, kind="ExternalInput").ap()
    x4o = nc.dram_tensor("x4o", [D, T_], F32, kind="ExternalOutput").ap()
    h2_t = h2d.rearrange("(o p) t -> o p t", p=128)
    wdn_t = wdnT.rearrange("(o p) t -> o p t", p=128)
    x3_t = dram_tiles(x3c)
    x4_t = dram_tiles(x4o)

    Z = env.sbuf("Z", [128, NOT * CW], BF16)
    tb = env.sbuf("tb", [128, CW], BF16)
    wdb = env.sbuf("wdb", [128, 2 * NOT * 128], BF16)
    x3p = env.sbuf("x3p", [128, 4 * 512], F32)
    x4p = env.sbuf("x4p", [128, 4 * 512], F32)
    ftmp = env.sbuf("ftmp", [128, 512], F32)
    wsd_s = env.sbuf("wsd_s", [128, KT], F32)
    hm_sb = env.sbuf("hm_sb", [128, 128], BF16)
    psH = [env.psum("eb_h0", [128, 512]), env.psum("eb_h1", [128, 512])]
    psD = [env.psum("eb_d0", [128, 512]), env.psum("eb_d1", [128, 512])]
    env.set_junk(env.sbuf("junk", [1, 1024], F32))

    s_w = env.sem("i_w")
    s_iv = env.sem("i_iv")
    s_zl = env.sem("i_zl")
    s_bf = env.sem("i_bf")
    s_hm = env.sem("i_hm")
    s_he = env.sem("i_he")
    s_wd = env.sem("i_wd")
    s_dm = env.sem("i_dm")
    s_de = env.sem("i_de")
    s_xl = env.sem("i_xl")
    s_xs = env.sem("i_xs")

    env.open_block()

    @env.block.gpsimd
    def _(g):
        s_w.inc(g.dma_start(out=hm_sb[:, :], in_=hmat[:, :]), 16)
        s_w.inc(g.dma_start(out=wsd_s[:, :], in_=wsdb[:, :]), 16)
    NW = s_w.n

    @env.block.vector
    def _(v):
        v.wait_ge(s_w.h, NW)
        env.vfence(v)
        v.tensor_scalar(wsd_s[:, :], wsd_s[:, :], 1.0 / np.sqrt(8192.0), None, ALU.mult)
        s_iv.inc(env.vfence(v))

    for cc in range(NCH):
        DM0 = s_dm.n

        @env.block.sync
        def _(s, cc=cc, DM0=DM0):
            if cc > 0:
                s.wait_ge(s_dm.h, DM0)
            for t in range(NOT):
                s_zl.inc(s.dma_start(out=Z[:, t * CW:(t + 1) * CW],
                                     in_=h2_t[t][:, cc * CW:(cc + 1) * CW]), 16)

        @env.block.vector
        def _(v, cc=cc):
            v.wait_ge(s_zl.h, 16 * NOT * (cc + 1))
            env.vfence(v)
            for s in (1, 2, 4, 8, 16, 32):
                for t in range(NOT):
                    if t & s == 0:
                        u = t + s
                        v.tensor_sub(tb[:, :], Z[:, t * CW:(t + 1) * CW],
                                     Z[:, u * CW:(u + 1) * CW])
                        v.tensor_add(Z[:, t * CW:(t + 1) * CW],
                                     Z[:, t * CW:(t + 1) * CW], Z[:, u * CW:(u + 1) * CW])
                        v.tensor_copy(Z[:, u * CW:(u + 1) * CW], tb[:, :])
            s_bf.inc(env.vfence(v))

        @env.block.tensor
        def _(t, cc=cc):
            t.wait_ge(s_bf.h, cc + 1)
            if cc == 0:
                t.wait_ge(s_iv.h, 1)
            for i in range(2 * NOT):
                if 128 * cc + i >= 2:
                    t.wait_ge(s_he.h, 129 * cc + i - 1)
                tt_ = i // 2
                th = i % 2
                ins = t.matmul(psH[i % 2][:, :], hm_sb[:, :],
                               Z[:, tt_ * CW + th * 512:tt_ * CW + (th + 1) * 512],
                               start=True, stop=True)
                s_hm.inc(ins)

        @env.block.scalar
        def _(sc, cc=cc):
            for i in range(2 * NOT):
                sc.wait_ge(s_hm.h, 128 * cc + i + 1)
                tt_ = i // 2
                th = i % 2
                ins = sc.copy(Z[:, tt_ * CW + th * 512:tt_ * CW + (th + 1) * 512],
                              psH[i % 2][:, :])
                s_he.inc(ins)
            s_he.inc(env.sfence(sc))

        for dt in range(KT):
            gd = cc * KT + dt

            @env.block.gpsimd
            def _(g, cc=cc, dt=dt, gd=gd):
                if gd >= 2:
                    g.wait_ge(s_dm.h, 2 * (gd - 1))
                for it in range(NOT):
                    s_wd.inc(g.dma_start(
                        out=wdb[:, (gd % 2) * 8192 + it * 128:(gd % 2) * 8192 + (it + 1) * 128],
                        in_=wdn_t[it][:, dt * 128:(dt + 1) * 128]), 16)

            @env.block.scalar
            def _(sc, cc=cc, dt=dt, gd=gd):
                if gd >= 2:
                    sc.wait_ge(s_de.h, 3 * (gd - 1))
                for th in range(2):
                    s_xl.inc(sc.dma_start(
                        out=x3p[:, ((2 * gd + th) % 4) * 512:((2 * gd + th) % 4 + 1) * 512],
                        in_=x3_t[dt][:, cc * CW + th * 512:cc * CW + (th + 1) * 512]), 16)

            @env.block.tensor
            def _(t, cc=cc, dt=dt, gd=gd):
                t.wait_ge(s_he.h, 129 * (cc + 1))
                t.wait_ge(s_wd.h, 16 * NOT * (gd + 1))
                for th in range(2):
                    if gd >= 1:
                        t.wait_ge(s_de.h, 3 * (gd - 1) + th + 1)
                    for it in range(NOT):
                        ins = t.matmul(psD[th][:, :],
                                       wdb[:, (gd % 2) * 8192 + it * 128:(gd % 2) * 8192 + (it + 1) * 128],
                                       Z[:, it * CW + th * 512:it * CW + (th + 1) * 512],
                                       start=(it == 0), stop=(it == NOT - 1))
                    s_dm.inc(ins)

            @env.block.vector
            def _(v, cc=cc, dt=dt, gd=gd):
                v.wait_ge(s_dm.h, 2 * gd + 1)
                v.wait_ge(s_xl.h, 32 * (gd + 1))
                if gd >= 2:
                    v.wait_ge(s_xs.h, 32 * (gd - 1))
                env.vfence(v)
                for th in range(2):
                    if th == 1:
                        v.wait_ge(s_dm.h, 2 * gd + 2)
                    v.tensor_scalar(ftmp[:, :], psD[th][:, :], wsd_s[:, dt:dt + 1],
                                    None, ALU.mult)
                    ins = v.tensor_add(x4p[:, ((2 * gd + th) % 4) * 512:((2 * gd + th) % 4 + 1) * 512],
                                       ftmp[:, :], x3p[:, ((2 * gd + th) % 4) * 512:((2 * gd + th) % 4 + 1) * 512])
                    s_de.inc(ins)
                s_de.inc(env.vfence(v))

            @env.block.sync
            def _(s, cc=cc, dt=dt, gd=gd):
                s.wait_ge(s_de.h, 3 * (gd + 1))
                for th in range(2):
                    s_xs.inc(s.dma_start(
                        out=x4_t[dt][:, cc * CW + th * 512:cc * CW + (th + 1) * 512],
                        in_=x4p[:, ((2 * gd + th) % 4) * 512:((2 * gd + th) % 4 + 1) * 512]), 16)

    @env.block.sync
    def _(s):
        s.wait_ge(s_xs.h, s_xs.n)

    return env.close()


def host_prep_E2(inputs):
    W_up, W_gate, W_down = inputs["L1_ffn_W_up"], inputs["L1_ffn_W_gate"], inputs["L1_ffn_W_down"]
    su_ug, su_dp = inputs["L1_ffn_SU_ug"], inputs["L1_ffn_SU_dp"]
    ws_ug, ws_dp = inputs["L1_ffn_Wscale_ug"], inputs["L1_ffn_Wscale_dp"]
    wupT = np.ascontiguousarray(W_up.T).astype(np.float32)
    wgtT = np.ascontiguousarray(W_gate.T).astype(np.float32)
    wdnT = np.ascontiguousarray(W_down.T).astype(np.float32)
    supm = np.ascontiguousarray(su_ug.reshape(KT, 128).T).astype(np.float32)
    wsub = np.ascontiguousarray(ws_ug[:8192].reshape(64, 128).T).astype(np.float32)
    sudb = np.ascontiguousarray(su_dp.reshape(64, 128).T).astype(np.float32)
    wsgb = np.ascontiguousarray(ws_ug[8192:].reshape(64, 128).T).astype(np.float32)
    wsdb = np.ascontiguousarray(ws_dp.reshape(KT, 128).T).astype(np.float32)
    i = np.arange(128)
    hmat = np.where((i[:, None] & i[None, :] & 0).astype(bool), 0, 0)
    pc = np.zeros((128, 128), dtype=np.int64)
    for a in range(128):
        for b_ in range(128):
            pc[a, b_] = bin(a & b_).count("1")
    hmat = np.where(pc % 2 == 0, 1.0, -1.0).astype(np.float32)
    mA = dict(wupT=wupT, wgtT=wgtT, supm=supm, wsub=wsub, sudb=sudb, wsgb=wsgb,
              hmat=hmat)
    mB = dict(wdnT=wdnT, wsdb=wsdb, hmat=hmat)
    return [dict(mA) for _ in range(NCORE)], [dict(mB) for _ in range(NCORE)]



def _shard_ch_major(x):
    out = []
    for c in range(NCORE):
        xs = x[:, SOWN * c:SOWN * (c + 1), :]
        out.append(np.ascontiguousarray(xs.transpose(2, 0, 1).reshape(D, TOWN)))
    return out


def kernel(**inputs):
    hs = np.asarray(inputs["L0_attn_hidden_states"], dtype=np.float32)

    ncA = build_launch_A()
    shards = _shard_ch_major(hs)
    rA = run_bass_kernel_spmd(ncA, [dict(x=s) for s in shards], core_ids=list(range(NCORE)))
    xn0_full = assemble_xn0([np.asarray(rA.results[c]["xn"]) for c in range(NCORE)])

    ncB = build_launch_B()
    mapsB = host_prep_B(inputs)
    for m in mapsB:
        m["xn0"] = xn0_full
    rB = run_bass_kernel_spmd(ncB, mapsB, core_ids=list(range(NCORE)))
    oat_full = np.zeros((D, B * S), dtype=BFD)
    for c in range(NCORE):
        oat_full[c * 256:(c + 1) * 256] = unpack_oslc(rB.results[c]["oslc"]).astype(BFD)

    ncC1 = build_launch_C1()
    mapsC1 = host_prep_C1(inputs, oat_full)
    rC1 = run_bass_kernel_spmd(ncC1, mapsC1, core_ids=list(range(NCORE)))
    x1s = [np.asarray(rC1.results[c]["x1o"]) for c in range(NCORE)]
    xcs = [np.asarray(rC1.results[c]["xco"]) for c in range(NCORE)]

    ncC2 = build_launch_C2()
    mapsC2 = host_prep_C2(inputs)
    for c in range(NCORE):
        mapsC2[c]["xco"] = xcs[c]
    rC2 = run_bass_kernel_spmd(ncC2, mapsC2, core_ids=list(range(NCORE)))

    ncC3 = build_launch_C3()
    mapsC3 = host_prep_C3(inputs)
    for c in range(NCORE):
        mapsC3[c]["xco"] = xcs[c]
        mapsC3[c]["onc"] = onrm_to_ch_major(np.asarray(rC2.results[c]["onrm"]))
        mapsC3[c]["x1o"] = x1s[c]
    rC3 = run_bass_kernel_spmd(ncC3, mapsC3, core_ids=list(range(NCORE)))
    x2s = [np.asarray(rC3.results[c]["x2o"]) for c in range(NCORE)]
    xn2s = [np.asarray(rC3.results[c]["xn2o"]) for c in range(NCORE)]

    xn2_full = np.zeros((B, S, D), dtype=BFD)
    for c in range(NCORE):
        blk = np.asarray(xn2s[c]).reshape(D, B, SOWN)
        for b_ in range(B):
            xn2_full[b_, SOWN * c:SOWN * (c + 1), :] = blk[:, b_, :].T
    ncD = build_launch_D()
    mapsD = host_prep_D(xn2_full)
    Ws = np.ascontiguousarray(inputs["L1_attn_Ws"]).astype(np.float32)
    Wa = np.ascontiguousarray(inputs["L1_attn_Wa"]).astype(np.float32)
    for m in mapsD:
        m["wsw"] = Ws
        m["waw"] = Wa
    rD = run_bass_kernel_spmd(ncD, mapsD, core_ids=list(range(NCORE)))

    oat1_full = np.zeros((D, B * S), dtype=BFD)
    for c in range(NCORE):
        b_, qh = c // 2, c % 2
        ot = np.asarray(rD.results[c]["otd"])
        oat1_full[:, b_ * S + qh * 2048:b_ * S + (qh + 1) * 2048] = ot.T
    ncE1 = build_launch_E1()
    mapsE1 = host_prep_E1(inputs, oat1_full)
    for c in range(NCORE):
        mapsE1[c]["x0c"] = x2s[c]
    rE1 = run_bass_kernel_spmd(ncE1, mapsE1, core_ids=list(range(NCORE)))
    x3s = [np.asarray(rE1.results[c]["x1o"]) for c in range(NCORE)]
    xn3s = [np.asarray(rE1.results[c]["xno"]) for c in range(NCORE)]

    mA, mB = host_prep_E2(inputs)
    ncE2a = build_launch_E2a()
    for c in range(NCORE):
        mA[c]["xn3"] = xn3s[c]
    rE2a = run_bass_kernel_spmd(ncE2a, mA, core_ids=list(range(NCORE)))
    ncE2b = build_launch_E2b()
    for c in range(NCORE):
        mB[c]["h2d"] = np.asarray(rE2a.results[c]["h2d"])
        mB[c]["x3c"] = x3s[c]
    rE2b = run_bass_kernel_spmd(ncE2b, mB, core_ids=list(range(NCORE)))

    out = np.zeros((B, S, D), dtype=np.float32)
    for c in range(NCORE):
        x4 = np.asarray(rE2b.results[c]["x4o"]).reshape(D, B, SOWN)
        for b_ in range(B):
            out[b_, SOWN * c:SOWN * (c + 1), :] = x4[:, b_, :].T
    return out


# version 03
# speedup vs baseline: 1.0691x; 1.0691x over previous
import numpy as np
import ml_dtypes
from contextlib import ExitStack

import concourse.bass as bass
import concourse.mybir as mybir
from concourse.ap import AP
from concourse.bass_utils import run_bass_kernel_spmd

F32 = mybir.dt.float32
BF16 = mybir.dt.bfloat16
AF = mybir.ActivationFunctionType
ALU = mybir.AluOpType
AX = mybir.AxisListType

NCORE = 8
B, S, D = 4, 4096, 2048
SOWN = S // NCORE
TOWN = B * SOWN
KT = D // 128
EPS = 1e-5

BFD = ml_dtypes.bfloat16


class Sem:

    def __init__(self, handle):
        self.h = handle
        self.n = 0

    def inc(self, ins, k=1):
        ins.then_inc(self.h, k)
        self.n += k
        return self.n


class Env:

    def __init__(self, name):
        self.nc = bass.Bass("TRN2", target_bir_lowering=False, detect_race_conditions=False)
        self.ctx = ExitStack()
        self.block = None
        self.sems = {}

    def sem(self, name):
        if name not in self.sems:
            self.sems[name] = Sem(self.ctx.enter_context(self.nc.semaphore(name)))
        return self.sems[name]

    def sbuf(self, name, shape, dtype):
        return self.ctx.enter_context(self.nc.sbuf_tensor(name, list(shape), dtype))

    def psum(self, name, shape, dtype=F32):
        return self.ctx.enter_context(self.nc.psum_tensor(name, list(shape), dtype))

    def open_block(self):
        self.block = self.ctx.enter_context(self.nc.Block())

    def set_fence(self, out_ap, lhs_ap, rhs_ap):
        self._f_out, self._f_lhs, self._f_rhs = out_ap, lhs_ap, rhs_ap

    def set_junk(self, junk):
        self._junk = junk

    FENCE_N = 4

    def vfence(self, v):
        v.memset(self._junk[0:1, 0:512], 0.0)
        for _ in range(self.FENCE_N):
            ins = v.tensor_copy(self._junk[0:1, 512:1024], self._junk[0:1, 0:512])
        return ins

    def sfence(self, sc):
        sc.memzero(self._junk[0:1, 0:512])
        for _ in range(self.FENCE_N):
            ins = sc.copy(self._junk[0:1, 512:1024], self._junk[0:1, 0:512])
        return ins

    def vspace(self, v):
        v.memset(self._junk[0:1, 0:64], 0.0)
        return v.tensor_copy(self._junk[0:1, 64:128], self._junk[0:1, 0:64])

    def gfence(self, g):
        g.memset(self._junk[0:1, 0:512], 0.0)
        for _ in range(self.FENCE_N):
            ins = g.tensor_copy(self._junk[0:1, 512:1024], self._junk[0:1, 0:512])
        return ins

    def close(self):
        self.ctx.close()
        return self.nc



def emit_consts(env, ones_b, ones_f):
    ini = env.sem("ini")

    @env.block.gpsimd
    def _(g):
        g.memset(ones_b[:, :], 1.0)
        g.memset(ones_f[:, :], 1.0)
        ini.inc(env.gfence(g))
        ini.inc(g.engine_nop())


def emit_ln(env, src_view, dst_view, bufs, T, CH=512, tag=""):
    nchunk = T // CH
    xb, sq = bufs["xb"], bufs["sq"]
    row, rB, mB, tmp = bufs["row"], bufs["rB"], bufs["mB"], bufs["tmp"]
    ones_b, ones_f = bufs["ones_b"], bufs["ones_f"]
    psA, psB, psR, psM = bufs["psA"], bufs["psB"], bufs["psR"], bufs["psM"]

    ldp = [env.sem("ln_ld0" + tag), env.sem("ln_ld1" + tag)]
    cv = env.sem("ln_cv" + tag)
    st1 = env.sem("ln_st1" + tag)
    rw = env.sem("ln_rw" + tag)
    ev = env.sem("ln_ev" + tag)
    nm = env.sem("ln_nm" + tag)
    sop = [env.sem("ln_so0" + tag), env.sem("ln_so1" + tag)]
    ini = env.sem("ini")

    ld0 = [ldp[0].n, ldp[1].n]
    so0 = [sop[0].n, sop[1].n]
    cv0, st10, rw0, ev0, nm0 = cv.n, st1.n, rw.n, ev.n, nm.n

    @env.block.gpsimd
    def _(g):
        for ci in range(nchunk):
            bsl = (ci % 2) * KT * CH
            if ci >= 2:
                g.wait_ge(nm.h, nm0 + KT * (ci - 1))
            for k in range(KT):
                ldp[ci % 2].inc(g.dma_start(out=xb[:, bsl + k * CH:bsl + (k + 1) * CH],
                                            in_=src_view[k][:, ci * CH:(ci + 1) * CH]), 16)

    @env.block.scalar
    def _(sc):
        for ci in range(nchunk):
            bsl = (ci % 2) * KT * CH
            sc.wait_ge(ldp[ci % 2].h, ld0[ci % 2] + 16 * KT * (ci // 2 + 1))
            if ci >= 1:
                sc.wait_ge(st1.h, st10 + 2 * (ci - 1) + 1)
            for k in range(KT):
                sc.activation(sq[:, k * CH:(k + 1) * CH],
                              xb[:, bsl + k * CH:bsl + (k + 1) * CH], AF.Square)
            cv.inc(env.sfence(sc))
            sc.wait_ge(rw.h, rw0 + 3 * ci + 1)
            env.sfence(sc)
            sc.activation(row[0:1, 3 * CH:4 * CH], row[0:1, 2 * CH:3 * CH], AF.Sqrt)
            rw.inc(env.sfence(sc))

    @env.block.vector
    def _(v):
        for ci in range(nchunk):
            bsl = (ci % 2) * KT * CH
            v.wait_ge(st1.h, st10 + 2 * ci + 1)
            env.vfence(v)
            v.tensor_scalar(row[0:1, 0:CH], psA[0:1, :], 1.0 / D, None, ALU.mult)
            v.tensor_scalar(row[0:1, CH:2 * CH], psB[0:1, :], 1.0 / D, None, ALU.mult)
            v.tensor_mul(row[0:1, 4 * CH:5 * CH], row[0:1, 0:CH], row[0:1, 0:CH])
            v.tensor_sub(row[0:1, 2 * CH:3 * CH], row[0:1, CH:2 * CH], row[0:1, 4 * CH:5 * CH])
            v.tensor_scalar(row[0:1, 2 * CH:3 * CH], row[0:1, 2 * CH:3 * CH], EPS, None, ALU.add)
            ins = v.reciprocal(row[0:1, 2 * CH:3 * CH], row[0:1, 2 * CH:3 * CH])
            rw.inc(ins)
            v.wait_ge(rw.h, rw0 + 3 * ci + 2)
            v.tensor_mul(row[0:1, 4 * CH:5 * CH], row[0:1, 0:CH], row[0:1, 3 * CH:4 * CH])
            rw.inc(env.vfence(v))
            v.wait_ge(st1.h, st10 + 2 * ci + 2)
            env.vfence(v)
            ins = v.tensor_copy(rB[:, :], psR[:, :])
            ev.inc(ins)
            ins = v.tensor_copy(mB[:, :], psM[:, :])
            ev.inc(ins)
            if ci >= 2:
                v.wait_ge(sop[ci % 2].h, so0[ci % 2] + 16 * KT * (ci // 2))
            for k in range(KT):
                v.tensor_mul(tmp[:, 0:CH], xb[:, bsl + k * CH:bsl + (k + 1) * CH], rB[:, :])
                ins = v.tensor_sub(bufs["xn"][:, bsl + k * CH:bsl + (k + 1) * CH],
                                   tmp[:, 0:CH], mB[:, :])
                nm.inc(ins)

    @env.block.tensor
    def _(t):
        t.wait_ge(ini.h, 2)
        for ci in range(nchunk):
            bsl = (ci % 2) * KT * CH
            t.wait_ge(ldp[ci % 2].h, ld0[ci % 2] + 16 * KT * (ci // 2 + 1))
            t.wait_ge(cv.h, cv0 + ci + 1)
            if ci >= 1:
                t.wait_ge(rw.h, rw0 + 3 * (ci - 1) + 1)
            for k in range(KT):
                t.matmul(psA[0:1, 0:CH], ones_b[:, :], xb[:, bsl + k * CH:bsl + (k + 1) * CH],
                         start=(k == 0), stop=(k == KT - 1))
            for k in range(KT):
                ins = t.matmul(psB[0:1, 0:CH], ones_b[:, :], sq[:, k * CH:(k + 1) * CH],
                               start=(k == 0), stop=(k == KT - 1))
            st1.inc(ins)
            t.wait_ge(rw.h, rw0 + 3 * ci + 3)
            if ci >= 1:
                t.wait_ge(ev.h, ev0 + 2 * ci)
            t.matmul(psR[:, 0:CH], ones_f[:, :], row[0:1, 3 * CH:4 * CH], start=True, stop=True)
            ins = t.matmul(psM[:, 0:CH], ones_f[:, :], row[0:1, 4 * CH:5 * CH], start=True, stop=True)
            st1.inc(ins)

    @env.block.sync
    def _(s):
        for ci in range(nchunk):
            bsl = (ci % 2) * KT * CH
            s.wait_ge(nm.h, nm0 + KT * (ci + 1))
            for k in range(KT):
                sop[ci % 2].inc(s.dma_start(out=dst_view[k][:, ci * CH:(ci + 1) * CH],
                                            in_=bufs["xn"][:, bsl + k * CH:bsl + (k + 1) * CH]), 16)


def ln_bufs(env, CH=512, tag=""):
    return dict(
        xb=env.sbuf("ln_xb" + tag, [128, 2 * KT * CH], BF16),
        sq=env.sbuf("ln_sq" + tag, [128, KT * CH], BF16),
        row=env.sbuf("ln_row" + tag, [1, 6 * CH], F32),
        rB=env.sbuf("ln_rB" + tag, [128, CH], BF16),
        mB=env.sbuf("ln_mB" + tag, [128, CH], BF16),
        tmp=env.sbuf("ln_tmp" + tag, [128, CH], BF16),
        xn=env.sbuf("ln_xn" + tag, [128, 2 * KT * CH], BF16),
        psA=env.psum("ln_psA" + tag, [1, CH]),
        psB=env.psum("ln_psB" + tag, [1, CH]),
        psR=env.psum("ln_psR" + tag, [128, CH]),
        psM=env.psum("ln_psM" + tag, [128, CH]),
    )


def dram_tiles(ap_2d):
    v = ap_2d.rearrange("(k p) t -> k p t", p=128)
    return [v[k] for k in range(KT)]



def build_launch_A():
    env = Env("A")
    nc = env.nc
    x = nc.dram_tensor("x", [D, TOWN], F32, kind="ExternalInput").ap()
    xn = nc.dram_tensor("xn", [D, TOWN], BF16, kind="ExternalOutput").ap()
    bufs = ln_bufs(env)
    bufs["ones_b"] = env.sbuf("ones_b", [128, 1], BF16)
    bufs["ones_f"] = env.sbuf("ones_f", [1, 128], F32)
    env.set_junk(env.sbuf("junk", [1, 1024], F32))
    env.open_block()
    emit_consts(env, bufs["ones_b"], bufs["ones_f"])
    emit_ln(env, dram_tiles(x), dram_tiles(xn), bufs, TOWN)
    so0, so1 = env.sem("ln_so0"), env.sem("ln_so1")

    @env.block.sync
    def _(s):
        s.wait_ge(so0.h, so0.n)
        s.wait_ge(so1.h, so1.n)

    return env.close()




def build_launch_B(S_=S, dbg=None, no_inject=False):
    env = Env("B")
    nc = env.nc
    NQT = S_ // 128
    NKC = S_ // 512
    NNQ = S_ // 512
    BT = B * S_
    NDSL = NKC + 4

    xn0 = nc.dram_tensor("xn0", [D, BT], BF16, kind="ExternalInput").ap()
    wq = nc.dram_tensor("wq", [D, 256], F32, kind="ExternalInput").ap()
    wk = nc.dram_tensor("wk", [D, 256], F32, kind="ExternalInput").ap()
    wv = nc.dram_tensor("wv", [D, 256], F32, kind="ExternalInput").ap()
    cst = nc.dram_tensor("cst", [128, S_], F32, kind="ExternalInput").ap()
    snt = nc.dram_tensor("snt", [128, S_], F32, kind="ExternalInput").ap()
    brev = nc.dram_tensor("brev", [4, 2432], F32, kind="ExternalInput").ap()
    bankt = nc.dram_tensor("bankt", [48, 128, 512], F32, kind="ExternalInput").ap()
    banktv = [bankt[i] for i in range(48)]
    offs = nc.dram_tensor("offs", [128, 4 * (S_ // 128)], F32, kind="ExternalInput").ap()
    hsel = nc.dram_tensor("hsel", [128, 4], F32, kind="ExternalInput").ap()
    jmat = nc.dram_tensor("jmat", [128, 128], F32, kind="ExternalInput").ap()
    idm = nc.dram_tensor("idm", [128, 128], F32, kind="ExternalInput").ap()
    oslc = nc.dram_tensor("oslc", [2, B, 128, (S_ // 128) * 128], BF16, kind="ExternalOutput").ap()
    oslc_v = oslc
    if dbg is not None:
        dbg_nm = nc.dram_tensor("dbg_nm", [128, 3 * 4 * (S_ // 128)], F32, kind="ExternalOutput").ap()
        dbg_dc = nc.dram_tensor("dbg_dc", [128, 8 * (S_ // 128)], F32, kind="ExternalOutput").ap()
        dbg_ds = nc.dram_tensor("dbg_ds", [(S_ // 128), 128, 2 * (S_ // 512 + 4)], F32, kind="ExternalOutput").ap()
        dbg_pq = nc.dram_tensor("dbg_pq", [(S_ // 128), 128, 512], BF16, kind="ExternalOutput").ap()
        dbg_pt = nc.dram_tensor("dbg_pt", [2, 128, 512], BF16, kind="ExternalOutput").ap()
        dbg_av = nc.dram_tensor("dbg_av", [128, 4], F32, kind="ExternalOutput").ap()
        dbg_p0 = nc.dram_tensor("dbg_p0", [128, 512], BF16, kind="ExternalOutput").ap()
        dbg_qb = nc.dram_tensor("dbg_qb", [128, 768], BF16, kind="ExternalOutput").ap()
        dbg_qb2 = nc.dram_tensor("dbg_qb2", [128, 256], BF16, kind="ExternalOutput").ap()

    CQ = 512
    xn_st = env.sbuf("xn_st", [128, 2 * KT * CQ], BF16)
    cs_st = env.sbuf("cs_st", [128, 2 * 2 * CQ], F32)
    rp_scr = env.sbuf("rp_scr", [128, 2 * CQ], F32)
    q_b = env.sbuf("q_b", [128, 2 * S_], BF16)
    k_b = env.sbuf("k_b", [128, 2 * S_], BF16)
    v_b = env.sbuf("v_b", [128, (S_ // 128) * 256], BF16)
    wq_sb = env.sbuf("wq_sb", [128, KT * 256], BF16)
    wk_sb = env.sbuf("wk_sb", [128, KT * 256], BF16)
    wv_sb = env.sbuf("wv_sb", [128, KT * 256], BF16)
    bank = env.sbuf("bank", [128, 24 * 512], BF16)
    jm_sb = env.sbuf("jm_sb", [128, 128], BF16)
    id4 = env.sbuf("id4", [4, 4], BF16)
    hsel_sb = env.sbuf("hsel_sb", [128, 4], BF16)
    p_sb = env.sbuf("p_sb", [128, 2 * S_], BF16)
    pt_sb = env.sbuf("pt_sb", [128, 2 * S_], BF16)
    osl_sb = env.sbuf("osl_sb", [128, 2 * 128], BF16)
    sq_scr = env.sbuf("sq_scr", [128, 2 * 512], BF16)
    brev_f = sq_scr.ap().bitcast(F32)[0:4, 0:512]
    qn_sb = env.sbuf("qn_sb", [4, S_], BF16)
    negm16 = env.sbuf("negm16", [16, S_], BF16)
    nmt16t = env.sbuf("nmt16t", [128, (S_ // 128) * 16], BF16)
    nmt_all = env.sbuf("nmt_all", [128, 16 * (S_ // 128)], F32)
    eoff_all = env.sbuf("eoff_all", [128, 16 * (S_ // 128)], F32)
    offb_sb = env.sbuf("offb_sb", [128, 4 * (S_ // 128)], F32)
    den_sb = env.sbuf("den_sb", [128, 2 * NDSL], F32)
    dcol_sb = env.sbuf("dcol_sb", [128, 2], F32)
    if dbg is not None:
        dcarch = env.sbuf("dcarch", [128, 8 * (S_ // 128) + 8], F32)
    row4 = env.sbuf("row4", [4, 8], F32)
    ones1 = env.sbuf("ones1", [1, 128], F32)
    ones_b1 = env.sbuf("ones_b1", [128, 1], BF16)

    psq = [env.psum(f"psq{i}", [128, 512]) for i in range(2)]
    psk = [env.psum(f"psk{i}", [128, 512]) for i in range(2)]
    psv = [env.psum(f"psv{i}", [128, 512]) for i in range(2)]
    psx = env.psum("psx", [128, 512])
    psy = env.psum("psy", [128, 512])
    psx_bf = psx.ap().bitcast(BF16)

    s_w = env.sem("b_wg")
    s_ws = env.sem("b_ws")
    s_wk = env.sem("b_wk")
    s_xn = [env.sem("b_xn0"), env.sem("b_xn1")]
    s_cs = [env.sem("b_cs0"), env.sem("b_cs1")]
    s_pj = env.sem("b_pj")
    s_pe = env.sem("b_pe")
    s_nr = env.sem("b_nr")
    s_bk = env.sem("b_bk")
    s_sc = env.sem("b_sc")
    s_px = env.sem("b_px")
    s_pt = env.sem("b_pt")
    s_av = env.sem("b_av")
    s_dn = env.sem("b_dn")
    s_os = env.sem("b_os")
    s_so = env.sem("b_so")
    s_jk = env.sem("b_jk")
    ini = env.sem("ini")
    jnk_dram = nc.dram_tensor("b_jnkd", [1, 512], F32, kind="Internal")

    env.set_junk(env.sbuf("junk", [1, 1024], F32))
    env.open_block()
    emit_consts(env, ones_b1, ones1)
    if dbg is not None:
        @env.block.gpsimd
        def _(g):
            g.memset(den_sb[:, :], 0.0)
            g.memset(dcol_sb[:, :], 0.0)
            g.memset(dcarch[:, :], 0.0)
            env.sem("ini").inc(g.engine_nop())

    xnv = xn0.rearrange("(k p) t -> k p t", p=128)
    wqv = wq.rearrange("(k p) m -> k p m", p=128)
    wkv = wk.rearrange("(k p) m -> k p m", p=128)
    wvv = wv.rearrange("(k p) m -> k p m", p=128)

    @env.block.gpsimd
    def _(g):
        for k in range(KT):
            s_w.inc(g.dma_start(out=wq_sb[:, k * 256:(k + 1) * 256], in_=wqv[k]), 16)
            s_w.inc(g.dma_start(out=wk_sb[:, k * 256:(k + 1) * 256], in_=wkv[k]), 16)
            s_w.inc(g.dma_start(out=wv_sb[:, k * 256:(k + 1) * 256], in_=wvv[k]), 16)
        s_w.inc(g.dma_start(out=jm_sb[:, :], in_=jmat[:, :]), 16)
        s_w.inc(g.dma_start(out=hsel_sb[:, :], in_=hsel[:, :]), 16)
        s_w.inc(g.dma_start(out=id4[:, :], in_=idm[0:4, 0:4]), 16)

    @env.block.sync
    def _(s):
        s_ws.inc(s.dma_start(out=offb_sb[:, :], in_=offs[:, :]), 16)

    NW1 = s_w.n
    NWS = s_ws.n

    @env.block.scalar
    def _(sc):
        sc.wait_ge(s_w.h, NW1)
        s_wk.inc(sc.mul(wq_sb[:, :], wq_sb[:, :], 0.125))

    NW = s_w.n

    for mchunk in range(5):
        cw = min(512, 2432 - mchunk * 512)

        @env.block.sync
        def _(s, mchunk=mchunk, cw=cw):
            if mchunk > 0:
                s.wait_ge(s_nr.h, s_nr.n)
            s_ws.inc(s.dma_start(out=brev_f[:, 0:cw], in_=brev[:, mchunk * 512:mchunk * 512 + cw]), 16)

        @env.block.vector
        def _(v, mchunk=mchunk, cw=cw):
            v.wait_ge(s_ws.h, s_ws.n)
            v.reduce_max(row4[0:4, 4:5], brev_f[:, 0:cw], axis=AX.X)
            env.vspace(v)
            if mchunk == 0:
                ins = v.tensor_copy(row4[0:4, 2:3], row4[0:4, 4:5])
            else:
                ins = v.tensor_max(row4[0:4, 2:3], row4[0:4, 2:3], row4[0:4, 4:5])
            s_nr.inc(ins)

    @env.block.vector
    def _(v):
        v.wait_ge(s_nr.h, s_nr.n)
        env.vspace(v)
        s_nr.inc(v.tensor_scalar(row4[0:4, 3:4], row4[0:4, 2:3], -1.0, None, ALU.mult))

    pt_nm_guard = [0]
    pu = [0]
    pt_after = []
    osl_use = [0]
    so_hist = []
    for b in range(B):
        for nq in range(NNQ):
            i = b * NNQ + nq
            buf = i % 2
            pjn0 = s_pj.n

            @env.block.scalar
            def _(sc, b=b, nq=nq, buf=buf, i=i):
                if i >= 2:
                    sc.wait_ge(s_pe.h, (i - 1) * 3)
                for k in range(KT):
                    s_xn[buf].inc(sc.dma_start(
                        out=xn_st[:, buf * KT * CQ + k * CQ:buf * KT * CQ + (k + 1) * CQ],
                        in_=xnv[k][:, b * S_ + nq * CQ:b * S_ + (nq + 1) * CQ]), 16)

            @env.block.sync
            def _(s, b=b, nq=nq, buf=buf, i=i):
                if i >= 2:
                    s.wait_ge(s_pe.h, (i - 1) * 3)
                s_cs[buf].inc(s.dma_start(out=cs_st[:, buf * 2 * CQ:buf * 2 * CQ + CQ],
                                          in_=cst[:, nq * CQ:(nq + 1) * CQ]), 16)
                s_cs[buf].inc(s.dma_start(out=cs_st[:, buf * 2 * CQ + CQ:(buf + 1) * 2 * CQ],
                                          in_=snt[:, nq * CQ:(nq + 1) * CQ]), 16)

            @env.block.tensor
            def _(t, b=b, nq=nq, buf=buf, i=i):
                t.wait_ge(s_w.h, NW)
                t.wait_ge(s_ws.h, NWS)
                t.wait_ge(s_wk.h, 1)
                t.wait_ge(s_xn[buf].h, 16 * KT * (i // 2 + 1))
                if nq == 0:
                    t.wait_ge(s_os.h, s_os.n)
                    t.wait_ge(s_dn.h, s_dn.n)
                if i >= 1:
                    t.wait_ge(s_pe.h, i * 3)
                xo = buf * KT * CQ
                for mt in range(2):
                    for c in range(KT):
                        ins = t.matmul(psq[mt][:, :],
                                       wq_sb[:, c * 256 + mt * 128:c * 256 + (mt + 1) * 128],
                                       xn_st[:, xo + c * CQ:xo + (c + 1) * CQ],
                                       start=(c == 0), stop=(c == KT - 1))
                s_pj.inc(ins)
                for mt in range(2):
                    for c in range(KT):
                        ins = t.matmul(psk[mt][:, :],
                                       wk_sb[:, c * 256 + mt * 128:c * 256 + (mt + 1) * 128],
                                       xn_st[:, xo + c * CQ:xo + (c + 1) * CQ],
                                       start=(c == 0), stop=(c == KT - 1))
                s_pj.inc(ins)
                vbanks = [psv[0], psv[1], psx, psy]
                for tt in range(4):
                    for c in range(KT):
                        ins = t.matmul(vbanks[tt][:, 0:256],
                                       xn_st[:, xo + c * CQ + tt * 128:xo + c * CQ + tt * 128 + 128],
                                       wv_sb[:, c * 256:(c + 1) * 256],
                                       start=(c == 0), stop=(c == KT - 1))
                s_pj.inc(ins)

            @env.block.vector
            def _(v, b=b, nq=nq, buf=buf, i=i, pjn0=pjn0):
                co = buf * 2 * CQ
                cc, ss = cs_st[:, co:co + CQ], cs_st[:, co + CQ:co + 2 * CQ]
                v.wait_ge(s_cs[buf].h, 32 * (i // 2 + 1))
                for which, ps2, dstb in ((0, psq, q_b), (1, psk, k_b)):
                    v.wait_ge(s_pj.h, pjn0 + (3 if nq == 0 else which + 1))
                    env.vfence(v)
                    t1, t2 = rp_scr[:, 0:CQ], rp_scr[:, CQ:2 * CQ]
                    if nq == 0:
                        v.tensor_copy(t1, ps2[0][:, :])
                        v.tensor_copy(t2, ps2[1][:, :])
                    v.tensor_mul(t1, ps2[0][:, :], cc)
                    v.tensor_mul(t2, ps2[1][:, :], ss)
                    v.tensor_sub(dstb[:, 0 * S_ + nq * CQ:0 * S_ + (nq + 1) * CQ], t1, t2)
                    v.tensor_mul(t1, ps2[0][:, :], ss)
                    v.tensor_mul(t2, ps2[1][:, :], cc)
                    ins = v.tensor_add(dstb[:, 1 * S_ + nq * CQ:1 * S_ + (nq + 1) * CQ], t1, t2)
                    s_pe.inc(ins)

            @env.block.scalar
            def _(sc, b=b, nq=nq, i=i, pjn0=pjn0):
                sc.wait_ge(s_pj.h, pjn0 + 3)
                env.sfence(sc)
                vbanks = [psv[0], psv[1], psx, psy]
                for tt in range(4):
                    ins = sc.copy(v_b[:, (nq * 4 + tt) * 256:(nq * 4 + tt + 1) * 256],
                                  vbanks[tt][:, 0:256])
                s_pe.inc(ins)

        PJB = s_pe.n

        if dbg is not None and b == dbg[0]:
            @env.block.sync
            def _(s):
                s.wait_ge(s_pe.h, PJB)
                s.dma_start(out=dbg_qb[:, 0:128], in_=q_b[:, 0:128]).then_inc(s_so.h, 16)
                s.dma_start(out=dbg_qb[:, 128:256], in_=q_b[:, S_:S_ + 128]).then_inc(s_so.h, 16)
                s.dma_start(out=dbg_qb[:, 256:384], in_=k_b[:, 0:128]).then_inc(s_so.h, 16)
                s.dma_start(out=dbg_qb[:, 384:512], in_=k_b[:, S_:S_ + 128]).then_inc(s_so.h, 16)
                s.dma_start(out=dbg_qb[:, 512:768], in_=v_b[:, 0:256]).then_inc(s_so.h, 16)
            s_so.n += 80

        @env.block.vector
        def _(v, b=b):
            v.wait_ge(s_nr.h, s_nr.n)
            s_nr.inc(v.memset(row4[0:4, 0:1], 0.0))

        nr_base = s_nr.n
        for which, srcb in ((0, q_b), (1, k_b)):
            for ch in range(NNQ):
                @env.block.scalar
                def _(sc, which=which, srcb=srcb, ch=ch):
                    sc.wait_ge(s_pe.h, PJB)
                    if not (which == 0 and ch == 0):
                        sc.wait_ge(s_nr.h, s_nr.n)
                    if which == 0 and ch == 0:
                        env.sfence(sc)
                    sc.activation(sq_scr[:, 0:512], srcb[:, 0 * S_ + ch * 512:0 * S_ + (ch + 1) * 512], AF.Square)
                    sc.activation(sq_scr[:, 512:1024], srcb[:, 1 * S_ + ch * 512:1 * S_ + (ch + 1) * 512], AF.Square)
                    s_px.inc(env.sfence(sc))

                @env.block.tensor
                def _(t, which=which, ch=ch):
                    t.wait_ge(s_px.h, s_px.n)
                    t.matmul(psx[0:4, :], hsel_sb[:, :], sq_scr[:, 0:512], start=True, stop=False)
                    s_pj.inc(t.matmul(psx[0:4, :], hsel_sb[:, :], sq_scr[:, 512:1024], start=False, stop=True))

                @env.block.vector
                def _(v, which=which, ch=ch):
                    v.wait_ge(s_pj.h, s_pj.n)
                    env.vfence(v)
                    if which == 0:
                        ins = v.tensor_copy(qn_sb[0:4, ch * 512:(ch + 1) * 512], psx[0:4, :])
                    else:
                        v.reduce_max(row4[0:4, 1:2], psx[0:4, :], axis=AX.X)
                        env.vspace(v)
                        ins = v.tensor_max(row4[0:4, 0:1], row4[0:4, 0:1], row4[0:4, 1:2])
                    s_nr.inc(ins)

        @env.block.vector
        def _(v, b=b):
            v.wait_ge(s_nr.h, s_nr.n)
            if b > 0:
                v.wait_ge(s_pt.h, pt_nm_guard[0])
            env.vspace(v)
            if b == 0:
                v.memset(negm16[0:16, :], 0.0)
            v.tensor_scalar(negm16[0:4, :], qn_sb[0:4, :], row4[0:4, 0:1], None, ALU.mult)
            s_nr.inc(env.vfence(v))

        @env.block.scalar
        def _(sc, b=b):
            sc.wait_ge(s_nr.h, s_nr.n)
            env.sfence(sc)
            sc.activation(negm16[0:4, :], negm16[0:4, :], AF.Ln)
            sc.activation(negm16[0:4, :], negm16[0:4, :], AF.Exp, scale=0.5)
            s_nr.inc(env.sfence(sc))

        @env.block.vector
        def _(v, b=b):
            v.wait_ge(s_nr.h, s_nr.n)
            env.vfence(v)
            v.tensor_scalar(negm16[0:4, :], negm16[0:4, :], -1.0,
                            row4[0:4, 3:4], ALU.mult, ALU.add)
            s_nr.inc(env.vfence(v))

        NMDONE = s_nr.n

        NB = b * 4 * NQT

        @env.block.sync
        def _(s, b=b):
            s.wait_ge(s_nr.h, NMDONE)
            if b > 0:
                s.wait_ge(s_px.h, s_px.n)
            s.wait_ge(s_pt.h, s_pt.n)
            s_pt.inc(s.dma_start(
                out=nmt16t.ap().rearrange("p (t r) -> p t r", r=16),
                in_=negm16[:, :], transpose=True), 16)
        pt_nm_guard[0] = s_pt.n

        @env.block.scalar
        def _(sc, b=b, NB=NB):
            sc.wait_ge(s_pt.h, s_pt.n)
            nmtv16 = nmt16t.ap().rearrange("p (t r) -> p t r", r=16)
            nmt4 = nmt_all.ap()[:, NB:NB + 4 * NQT].rearrange("p (q j) -> p q j", j=4)
            sc.copy(nmt4[:, :, :], nmtv16[:, :, 0:4])
            eoffv = eoff_all.ap()[:, NB:NB + 4 * NQT].rearrange("p (q j) -> p q j", j=4)
            for j in range(4):
                ins = sc.activation(eoffv[:, :, j], nmt4[:, :, j], AF.Exp,
                                    bias=offb_sb[:, j:j + 1])
            s_dn.inc(ins)

        EOFF_B = s_dn.n

        if dbg is not None and b == dbg[0]:
            @env.block.sync
            def _(s):
                s.wait_ge(s_dn.h, EOFF_B)
                s.dma_start(out=dbg_nm[:, 0:4 * NQT], in_=nmt_all[:, NB:NB + 4 * NQT]).then_inc(s_so.h, 16)
                s.dma_start(out=dbg_nm[:, 4 * NQT:8 * NQT], in_=nmt_all[:, NB:NB + 4 * NQT]).then_inc(s_so.h, 16)
                s.dma_start(out=dbg_nm[:, 8 * NQT:12 * NQT], in_=eoff_all[:, NB:NB + 4 * NQT]).then_inc(s_so.h, 16)
            s_so.n += 48

        for hp in range(2):
            sc_before_bank = s_sc.n

            @env.block.gpsimd
            def _(g, b=b, hp=hp, sc_before_bank=sc_before_bank):
                g.wait_ge(s_sc.h, sc_before_bank)
                g.wait_ge(s_pt.h, s_pt.n)
                for h in range(2):
                    hg = 2 * hp + h
                    for di in range(12):
                        s_bk.inc(g.dma_start(
                            out=bank[:, (h * 12 + di) * 512:(h * 12 + di + 1) * 512],
                            in_=banktv[hg * 12 + di]), 16)

            BKN = s_bk.n
            pending = [None]

            def flush_pending(extra_sc=None):
                fn = pending[0]
                if fn is not None:
                    pending[0] = None
                    fn(extra_sc)

            for qt in range(NQT):
                ncov = qt // 4 + 1
                flush_pending(extra_sc=s_sc.n + 1)

                for h in range(2):
                    strip = 32 * (2 * hp + h)
                    SB = psq if h == 0 else psk
                    col = NB + qt * 4 + 2 * hp + h
                    dn_guard = s_dn.n
                    pbase = (pu[0] % 2) * S_
                    pslot_wait = pt_after[pu[0] - 2] if pu[0] >= 2 else 0
                    for kc in range(ncov):
                        delta = qt * 128 - kc * 512
                        Sb = SB[kc % 2]

                        @env.block.tensor
                        def _(t, b=b, hp=hp, qt=qt, h=h, kc=kc, strip=strip, Sb=Sb, delta=delta,
                              pslot_wait=pslot_wait):
                            t.wait_ge(s_px.h, max(0, s_px.n - 1))
                            if kc == 0:
                                t.wait_ge(s_bk.h, BKN)
                                if pslot_wait:
                                    t.wait_ge(s_pt.h, pslot_wait)
                            t.matmul(Sb[:, :], q_b[strip:strip + 32, 0 * S_ + qt * 128:0 * S_ + (qt + 1) * 128],
                                     k_b[strip:strip + 32, 0 * S_ + kc * 512:0 * S_ + (kc + 1) * 512],
                                     start=True, stop=False, tile_position=(strip, 0))
                            last = (delta > 1408) or no_inject
                            ins = t.matmul(Sb[:, :], q_b[strip:strip + 32, 1 * S_ + qt * 128:1 * S_ + (qt + 1) * 128],
                                           k_b[strip:strip + 32, 1 * S_ + kc * 512:1 * S_ + (kc + 1) * 512],
                                           start=False, stop=last, tile_position=(strip, 0))
                            if not last:
                                ins = t.matmul(Sb[:, :], jm_sb[:, :],
                                               bank[:, (h * 12 + delta // 128) * 512:(h * 12 + delta // 128 + 1) * 512],
                                               start=False, stop=True)
                            s_sc.inc(ins)

                        @env.block.scalar
                        def _(sc, qt=qt, h=h, kc=kc, Sb=Sb, col=col, dn_guard=dn_guard, pbase=pbase):
                            sc.wait_ge(s_sc.h, s_sc.n)
                            if kc == 0:
                                sc.wait_ge(s_dn.h, dn_guard)
                            ins = sc.activation(p_sb[:, pbase + kc * 512:pbase + (kc + 1) * 512],
                                                Sb[:, :], AF.Exp,
                                                bias=nmt_all[:, col:col + 1],
                                                accum_out=den_sb[:, h * NDSL + kc:h * NDSL + kc + 1])
                            s_px.inc(ins)

                    @env.block.vector
                    def _(v, h=h, ncov=ncov, col=col):
                        v.wait_ge(s_px.h, s_px.n)
                        v.wait_ge(s_dn.h, EOFF_B)
                        env.vfence(v)
                        if ncov > 1:
                            v.reduce_sum(dcol_sb[:, h:h + 1], den_sb[:, h * NDSL:h * NDSL + ncov], axis=AX.X)
                            env.vspace(v)
                            v.tensor_add(dcol_sb[:, h:h + 1], dcol_sb[:, h:h + 1], eoff_all[:, col:col + 1])
                        else:
                            v.tensor_add(dcol_sb[:, h:h + 1], den_sb[:, h * NDSL:h * NDSL + 1],
                                         eoff_all[:, col:col + 1])
                        env.vspace(v)
                        v.reciprocal(dcol_sb[:, h:h + 1], dcol_sb[:, h:h + 1])
                        ins = env.vfence(v)
                        if dbg is not None and (b, hp) == dbg:
                            v.tensor_copy(dcarch[:, qt * 8 + h * 4:qt * 8 + h * 4 + 1], dcol_sb[:, h:h + 1])
                            v.tensor_copy(dcarch[:, qt * 8 + h * 4 + 1:qt * 8 + h * 4 + 2],
                                          den_sb[:, h * NDSL:h * NDSL + 1])
                            ins = v.tensor_copy(dcarch[:, qt * 8 + h * 4 + 2:qt * 8 + h * 4 + 3],
                                                eoff_all[:, col:col + 1])
                        s_dn.inc(ins)

                    if dbg is not None and (b, hp) == dbg and qt == 0 and h == 0:
                        @env.block.sync
                        def _(s):
                            s.wait_ge(s_px.h, s_px.n)
                            s.wait_ge(s_dn.h, s_dn.n)
                            s.dma_start(out=dbg_p0[:, :], in_=p_sb[:, 0:512]).then_inc(s_so.h, 16)
                        s_so.n += 16

                    @env.block.sync
                    def _(s, b=b, hp=hp, qt=qt, h=h, ncov=ncov, pbase=pbase):
                        s.wait_ge(s_px.h, s_px.n)
                        s.wait_ge(s_dn.h, s_dn.n)
                        if not (b == 0 and hp == 0 and qt == 0 and h == 0):
                            s.wait_ge(s_av.h, s_av.n)
                        if qt == 0:
                            s_jk.inc(s.dma_start(out=jnk_dram.ap(), in_=env._junk[0:1, 0:512]), 16)
                            s.wait_ge(s_jk.h, s_jk.n)
                        s_pt.inc(s.dma_start(
                            out=pt_sb.ap()[:, h * S_:h * S_ + ncov * 512].rearrange(
                                "p (t q) -> p t q", q=128),
                            in_=p_sb[:, pbase:pbase + ncov * 512], transpose=True), 16)
                    pt_after.append(s_pt.n)
                    pu[0] += 1

                if dbg is not None and (b, hp) == dbg:
                    @env.block.sync
                    def _(s, qt=qt):
                        s.wait_ge(s_dn.h, s_dn.n)
                        s.dma_start(out=dbg_dc[:, qt * 8:qt * 8 + 8], in_=dcarch[:, qt * 8:qt * 8 + 8],
                                    allow_slow_non_contiguous=True).then_inc(s_so.h, 16)
                        s.dma_start(out=dbg_ds[qt][:, 0:2 * NDSL],
                                    in_=den_sb[:, :]).then_inc(s_so.h, 16)
                        s.dma_start(out=dbg_pq[qt], in_=p_sb[:, 0:512]).then_inc(s_so.h, 16)
                        if qt == 0:
                            s.dma_start(out=dbg_pt[0], in_=pt_sb[:, 0:512]).then_inc(s_so.h, 16)
                            s.dma_start(out=dbg_pt[1], in_=pt_sb[:, S_:S_ + 512]).then_inc(s_so.h, 16)
                            s.dma_start(out=dbg_av[:, :], in_=dcarch[:, 8 * NQT:8 * NQT + 4],
                                        allow_slow_non_contiguous=True).then_inc(s_so.h, 16)
                            s_so.n += 48
                    s_so.n += 32

                @env.block.tensor
                def _(t, b=b, hp=hp, qt=qt, ncov=ncov):
                    t.wait_ge(s_pt.h, s_pt.n)
                    t.wait_ge(s_os.h, s_os.n)
                    nkt = ncov * 4
                    for kt in range(nkt):
                        t.matmul(psv[0][:, 0:64], pt_sb[:, 0 * S_ + kt * 128:0 * S_ + (kt + 1) * 128],
                                 v_b[:, kt * 256 + hp * 128:kt * 256 + hp * 128 + 64],
                                 start=(kt == 0), stop=(kt == nkt - 1))
                        ins = t.matmul(psv[1][:, 0:64], pt_sb[:, 1 * S_ + kt * 128:1 * S_ + (kt + 1) * 128],
                                       v_b[:, kt * 256 + hp * 128 + 64:kt * 256 + hp * 128 + 128],
                                       start=(kt == 0), stop=(kt == nkt - 1))
                    s_av.inc(ins)

                def make_evac(qt=qt, av_target=s_av.n, dn_target=s_dn.n, b=b, hp=hp):
                    def emit(extra_sc):
                        idx = osl_use[0]
                        osl_use[0] += 1
                        slot = (idx % 2) * 128

                        @env.block.vector
                        def _(v):
                            v.wait_ge(s_av.h, av_target)
                            if extra_sc is not None:
                                v.wait_ge(s_sc.h, extra_sc)
                            if idx >= 2:
                                v.wait_ge(s_so.h, so_hist[idx - 2])
                            if extra_sc is None or qt == 0:
                                env.vfence(v)
                            if dbg is not None and (b, hp) == dbg and qt == 0:
                                v.tensor_copy(dcarch[:, 8 * NQT:8 * NQT + 1], psv[0][:, 0:1])
                                v.tensor_copy(dcarch[:, 8 * NQT + 1:8 * NQT + 2], psv[1][:, 0:1])
                            v.tensor_scalar(osl_sb[:, slot:slot + 64], psv[0][:, 0:64],
                                            dcol_sb[:, 0:1], None, ALU.mult)
                            s_os.inc(v.tensor_scalar(osl_sb[:, slot + 64:slot + 128], psv[1][:, 0:64],
                                                     dcol_sb[:, 1:2], None, ALU.mult))

                        @env.block.sync
                        def _(s):
                            s.wait_ge(s_os.h, s_os.n)
                            s_so.inc(s.dma_start(out=oslc_v[hp, b][:, qt * 128:(qt + 1) * 128],
                                                 in_=osl_sb[:, slot:slot + 128]), 16)
                        so_hist.append(s_so.n)
                    return emit

                pending[0] = make_evac()

            flush_pending(extra_sc=None)

        if dbg is not None and b == dbg[0]:
            @env.block.sync
            def _(s):
                s.wait_ge(s_os.h, s_os.n)
                s.dma_start(out=dbg_qb2[:, 0:128], in_=q_b[:, 0:128]).then_inc(s_so.h, 16)
                s.dma_start(out=dbg_qb2[:, 128:256], in_=q_b[:, S_:S_ + 128]).then_inc(s_so.h, 16)
            s_so.n += 32

    @env.block.sync
    def _(s):
        s.wait_ge(s_so.h, s_so.n)

    return env.close()



def host_prep_B(inputs, S_=S):
    Wq, Wk, Wv = inputs["L0_attn_Wq"], inputs["L0_attn_Wk"], inputs["L0_attn_Wv"]
    LB = inputs["L0_attn_learnable_bias"]
    off = inputs["L0_attn_softmax_offset"]
    H, Dh = 32, 64
    p = np.arange(S_, dtype=np.float64)
    dgrid = np.arange(32, dtype=np.float64)
    ang = np.outer(p, 1.0 / (10000.0 ** (dgrid / 32.0)))
    cs = np.cos(ang).astype(np.float32)
    sn = np.sin(ang).astype(np.float32)
    cst = np.ascontiguousarray(np.tile(cs.T, (4, 1)))
    snt = np.ascontiguousarray(np.tile(sn.T, (4, 1)))
    hsel = np.zeros((128, 4), dtype=np.float32)
    for r in range(128):
        hsel[r, r // 32] = 1.0
    jmat = np.eye(128, dtype=np.float32)[::-1].copy()
    idm = np.eye(128, dtype=np.float32)
    maps = []
    for c in range(NCORE):
        heads = [4 * c + j for j in range(4)]
        rows_qk = [h * Dh + half * 32 + d0 for half in range(2) for h in heads for d0 in range(32)]
        rows_v = [h * Dh + d for h in heads for d in range(Dh)]
        wq = np.ascontiguousarray(Wq[rows_qk, :].T)
        wk = np.ascontiguousarray(Wk[rows_qk, :].T)
        wv = np.ascontiguousarray(Wv[rows_v, :].T)
        brev = np.zeros((4, 2432), dtype=np.float32)
        for j, h in enumerate(heads):
            u = np.arange(2432)
            dd = 1535 - u
            val = np.zeros(2432, dtype=np.float32)
            band = (dd >= 0) & (dd <= 1023)
            val[band] = LB[h, dd[band]]
            val[dd < 0] = -1e30
            brev[j] = val
        nqt_ = S_ // 128
        offs = np.tile(off[heads][None, :], (128, nqt_)).astype(np.float32)
        bankt = np.zeros((48, 128, 512), dtype=np.float32)
        u = np.arange(512)
        for hg in range(4):
            for di in range(12):
                base = 1408 - 128 * di
                for r in range(128):
                    bankt[hg * 12 + di, r, :] = brev[hg, base + r:base + r + 512]
        maps.append(dict(wq=wq, wk=wk, wv=wv, cst=cst, snt=snt, brev=brev,
                         bankt=bankt, offs=offs, hsel=hsel, jmat=jmat, idm=idm))
    return maps


def assemble_xn0(shards, S_=S):
    so = S_ // NCORE
    full = np.empty((D, B * S_), dtype=BFD)
    for c, sh in enumerate(shards):
        shv = np.asarray(sh).reshape(D, B, so)
        for b_ in range(B):
            full[:, b_ * S_ + so * c: b_ * S_ + so * (c + 1)] = shv[:, b_, :]
    return full



def build_launch_C1(T_=TOWN):
    env = Env("C1")
    nc = env.nc
    CW = 512
    NCH = T_ // CW

    oat = nc.dram_tensor("oat", [D, T_], BF16, kind="ExternalInput").ap()
    x0c = nc.dram_tensor("x0c", [D, T_], F32, kind="ExternalInput").ap()
    woT = nc.dram_tensor("woT", [D, D], F32, kind="ExternalInput").ap()
    cvw = nc.dram_tensor("cvw", [128, 2 * KT], F32, kind="ExternalInput").ap()
    x1o = nc.dram_tensor("x1o", [D, T_], F32, kind="ExternalOutput").ap()
    xco = nc.dram_tensor("xco", [D, T_], BF16, kind="ExternalOutput").ap()
    oat_t = dram_tiles(oat)
    x0_t = dram_tiles(x0c)
    wo_t = dram_tiles(woT)
    x1_t = dram_tiles(x1o)
    xc_t = dram_tiles(xco)

    wo_sb = env.sbuf("wo_sb", [128, KT * D], BF16)
    oat_sb = env.sbuf("oat_sb", [128, KT * CW], BF16)
    x0p = env.sbuf("x0p", [128, 4 * CW], F32)
    x1p = env.sbuf("x1p", [128, 4 * CW], F32)
    xbp = [env.sbuf("xb0", [128, KT * CW], BF16),
           env.sbuf("xb1", [128, KT * CW], BF16)]
    sqb = env.sbuf("sqb", [128, KT * CW], BF16)
    row = env.sbuf("row", [1, 6 * CW], F32)
    rB = env.sbuf("rB", [128, CW], BF16)
    mB = env.sbuf("mB", [128, CW], BF16)
    tmpb = env.sbuf("tmpb", [128, CW], BF16)
    xcp = env.sbuf("xcp", [128, 2 * CW], F32)
    xcb = env.sbuf("xcb", [128, 2 * CW], BF16)
    cvs = env.sbuf("cvs", [128, 2 * KT], F32)
    psm = [env.psum("c1_ps0", [128, CW]), env.psum("c1_ps1", [128, CW])]
    psA = env.psum("c1_psA", [1, CW])
    psB = env.psum("c1_psB", [1, CW])
    psR = env.psum("c1_psR", [128, CW])
    psM = env.psum("c1_psM", [128, CW])
    ones_b = env.sbuf("ones_b", [128, 1], BF16)
    ones_f = env.sbuf("ones_f", [1, 128], F32)
    env.set_junk(env.sbuf("junk", [1, 1024], F32))

    s_w = env.sem("c_w")
    s_oa = env.sem("c_oa")
    s_x0 = env.sem("c_x0")
    s_mm = env.sem("c_mm")
    s_ev = env.sem("c_ev")
    s_sx = env.sem("c_sx")
    s_sq = env.sem("c_sq")
    s_st = env.sem("c_st")
    s_rw = env.sem("c_rw")
    s_e2 = env.sem("c_e2")
    s_cv = env.sem("c_cv")
    s_sl = env.sem("c_sl")
    s_sc = env.sem("c_sc")
    ini = env.sem("ini")

    def evn(t):
        return t + 1 + t // KT

    def sln(t):
        return t + 1 + t // KT

    env.open_block()
    emit_consts(env, ones_b, ones_f)

    @env.block.gpsimd
    def _(g):
        for et in range(KT):
            s_w.inc(g.dma_start(out=wo_sb[:, et * D:(et + 1) * D], in_=wo_t[et][:, :]), 16)
    NW = s_w.n

    s_ws = env.sem("c_ws")

    @env.block.sync
    def _(s):
        s_ws.inc(s.dma_start(out=cvs[:, :], in_=cvw[:, :]), 16)
    NWS = s_ws.n

    for ci in range(NCH):
        c0, c1 = ci * CW, (ci + 1) * CW
        xb = xbp[ci % 2]
        xprev = xbp[(ci - 1) % 2]

        @env.block.gpsimd
        def _(g, ci=ci, c0=c0, c1=c1):
            if ci > 0:
                g.wait_ge(s_mm.h, 16 * ci)
            for et in range(KT):
                s_oa.inc(g.dma_start(out=oat_sb[:, et * CW:(et + 1) * CW],
                                     in_=oat_t[et][:, c0:c1]), 16)
            for dt in range(KT):
                gidx = ci * KT + dt
                if gidx >= 4:
                    g.wait_ge(s_ev.h, evn(gidx - 4))
                s_x0.inc(g.dma_start(out=x0p[:, (gidx % 4) * CW:(gidx % 4 + 1) * CW],
                                     in_=x0_t[dt][:, c0:c1]), 16)

        for dt in range(KT):
            gidx = ci * KT + dt

            @env.block.tensor
            def _(t, ci=ci, dt=dt, gidx=gidx):
                if gidx == 0:
                    t.wait_ge(s_w.h, NW)
                t.wait_ge(s_oa.h, 16 * KT * (ci + 1))
                if gidx >= 2:
                    t.wait_ge(s_ev.h, evn(gidx - 2))
                for et in range(KT):
                    ins = t.matmul(psm[gidx % 2][:, :],
                                   wo_sb[:, et * D + dt * 128:et * D + (dt + 1) * 128],
                                   oat_sb[:, et * CW:(et + 1) * CW],
                                   start=(et == 0), stop=(et == KT - 1))
                s_mm.inc(ins)

            @env.block.vector
            def _(v, ci=ci, dt=dt, gidx=gidx):
                v.wait_ge(s_mm.h, gidx + 1)
                v.wait_ge(s_x0.h, 16 * (gidx + 1))
                if gidx >= 4:
                    v.wait_ge(s_sx.h, 16 * (gidx - 3))
                if dt == 0 and ci > 0:
                    v.wait_ge(s_sq.h, ci)
                env.vfence(v)
                ins = v.tensor_add(x1p[:, (gidx % 4) * CW:(gidx % 4 + 1) * CW],
                                   psm[gidx % 2][:, :],
                                   x0p[:, (gidx % 4) * CW:(gidx % 4 + 1) * CW])
                s_ev.inc(ins)

            @env.block.sync
            def _(s, ci=ci, dt=dt, gidx=gidx, c0=c0, c1=c1):
                s.wait_ge(s_ev.h, min(evn(gidx + 1), 17 * (ci + 1)))
                s_sx.inc(s.dma_start(out=x1_t[dt][:, c0:c1],
                                     in_=x1p[:, (gidx % 4) * CW:(gidx % 4 + 1) * CW]), 16)

        @env.block.vector
        def _(v):
            s_ev.inc(env.vfence(v))

        @env.block.scalar
        def _(sc, ci=ci, xb=xb):
            for dt in range(KT):
                gidx = ci * KT + dt
                sc.wait_ge(s_ev.h, min(evn(gidx + 1), 17 * (ci + 1)))
                sc.copy(xb[:, dt * CW:(dt + 1) * CW],
                        x1p[:, (gidx % 4) * CW:(gidx % 4 + 1) * CW])
                sc.activation(sqb[:, dt * CW:(dt + 1) * CW], xb[:, dt * CW:(dt + 1) * CW],
                              AF.Square)
            s_sq.inc(env.sfence(sc))

        @env.block.tensor
        def _(t, ci=ci, xb=xb):
            t.wait_ge(ini.h, 2)
            t.wait_ge(s_sq.h, ci + 1)
            if ci >= 1:
                t.wait_ge(s_rw.h, 3 * (ci - 1) + 1)
            for k in range(KT):
                t.matmul(psA[0:1, :], ones_b[:, :], xb[:, k * CW:(k + 1) * CW],
                         start=(k == 0), stop=(k == KT - 1))
            for k in range(KT):
                ins = t.matmul(psB[0:1, :], ones_b[:, :], sqb[:, k * CW:(k + 1) * CW],
                               start=(k == 0), stop=(k == KT - 1))
            s_st.inc(ins)

        @env.block.vector
        def _(v, ci=ci):
            v.wait_ge(s_st.h, 2 * ci + 1)
            env.vfence(v)
            v.tensor_scalar(row[0:1, 0:CW], psA[0:1, :], 1.0 / D, None, ALU.mult)
            v.tensor_scalar(row[0:1, CW:2 * CW], psB[0:1, :], 1.0 / D, None, ALU.mult)
            v.tensor_mul(row[0:1, 4 * CW:5 * CW], row[0:1, 0:CW], row[0:1, 0:CW])
            v.tensor_sub(row[0:1, 2 * CW:3 * CW], row[0:1, CW:2 * CW], row[0:1, 4 * CW:5 * CW])
            v.tensor_scalar(row[0:1, 2 * CW:3 * CW], row[0:1, 2 * CW:3 * CW], EPS, None, ALU.add)
            ins = v.reciprocal(row[0:1, 2 * CW:3 * CW], row[0:1, 2 * CW:3 * CW])
            s_rw.inc(ins)
            v.wait_ge(s_rw.h, 3 * ci + 2)
            v.tensor_mul(row[0:1, 4 * CW:5 * CW], row[0:1, 0:CW], row[0:1, 3 * CW:4 * CW])
            s_rw.inc(env.vfence(v))

        @env.block.scalar
        def _(sc, ci=ci):
            sc.wait_ge(s_rw.h, 3 * ci + 1)
            env.sfence(sc)
            sc.activation(row[0:1, 3 * CW:4 * CW], row[0:1, 2 * CW:3 * CW], AF.Sqrt)
            s_rw.inc(env.sfence(sc))

        @env.block.tensor
        def _(t, ci=ci):
            t.wait_ge(s_rw.h, 3 * ci + 3)
            if ci >= 1:
                t.wait_ge(s_e2.h, 2 * ci)
            t.matmul(psR[:, :], ones_f[:, :], row[0:1, 3 * CW:4 * CW], start=True, stop=True)
            ins = t.matmul(psM[:, :], ones_f[:, :], row[0:1, 4 * CW:5 * CW], start=True, stop=True)
            s_st.inc(ins)

        @env.block.vector
        def _(v, ci=ci, xb=xb, xprev=xprev):
            v.wait_ge(s_st.h, 2 * ci + 2)
            env.vfence(v)
            if ci == 0:
                v.wait_ge(s_ws.h, NWS)
            s_e2.inc(v.tensor_copy(rB[:, :], psR[:, :]))
            s_e2.inc(v.tensor_copy(mB[:, :], psM[:, :]))
            for k in range(KT):
                gk = ci * KT + k
                if gk >= 2:
                    v.wait_ge(s_sl.h, sln(gk - 2))
                v.tensor_mul(tmpb[:, :], xb[:, k * CW:(k + 1) * CW], rB[:, :])
                v.tensor_sub(xb[:, k * CW:(k + 1) * CW], tmpb[:, :], mB[:, :])
                dstg = xcp[:, (gk % 2) * CW:(gk % 2 + 1) * CW]
                v.tensor_scalar(dstg, xb[:, k * CW:(k + 1) * CW],
                                cvs[:, 2 * k + 1:2 * k + 2], None, ALU.mult)
                if ci > 0:
                    v.tensor_scalar(tmpb[:, :], xprev[:, k * CW:(k + 1) * CW],
                                    cvs[:, 2 * k:2 * k + 1], None, ALU.mult)
                    v.tensor_add(dstg, dstg, tmpb[:, :])
                s_cv.inc(v.engine_nop())

        @env.block.scalar
        def _(sc, ci=ci):
            for k in range(KT):
                gk = ci * KT + k
                sc.wait_ge(s_cv.h, min(gk + 2, 16 * (ci + 1)))
                if k == KT - 1:
                    env.sfence(sc)
                if gk >= 2:
                    sc.wait_ge(s_sc.h, 16 * (gk - 1))
                ins = sc.activation(xcb[:, (gk % 2) * CW:(gk % 2 + 1) * CW],
                                    xcp[:, (gk % 2) * CW:(gk % 2 + 1) * CW], AF.Silu)
                s_sl.inc(ins)
            s_sl.inc(env.sfence(sc))

        @env.block.sync
        def _(s, ci=ci, c0=c0, c1=c1):
            for k in range(KT):
                gk = ci * KT + k
                s.wait_ge(s_sl.h, min(sln(gk + 1), 17 * (ci + 1)))
                s_sc.inc(s.dma_start(out=xc_t[k][:, c0:c1],
                                     in_=xcb[:, (gk % 2) * CW:(gk % 2 + 1) * CW]), 16)

    @env.block.sync
    def _(s):
        s.wait_ge(s_sx.h, s_sx.n)
        s.wait_ge(s_sc.h, s_sc.n)

    return env.close()



def build_launch_C2(T_=TOWN):
    env = Env("C2")
    nc = env.nc
    PW = 128
    CW = 4 * PW
    NPC = (T_ // 4) // PW
    DK = 1024

    xco = nc.dram_tensor("xco", [D, T_], BF16, kind="ExternalInput").ap()
    wqw = nc.dram_tensor("wqw", [D, DK], F32, kind="ExternalInput").ap()
    wkw = nc.dram_tensor("wkw", [D, DK], F32, kind="ExternalInput").ap()
    wvw = nc.dram_tensor("wvw", [D, D], F32, kind="ExternalInput").ap()
    rwm = nc.dram_tensor("rwm", [128, 32], F32, kind="ExternalInput").ap()
    cosT = nc.dram_tensor("cosT", [128, CW], F32, kind="ExternalInput").ap()
    sinT = nc.dram_tensor("sinT", [128, CW], F32, kind="ExternalInput").ap()
    hs2 = nc.dram_tensor("hs2", [128, 2], F32, kind="ExternalInput").ap()
    onrm = nc.dram_tensor("onrm", [4, T_ // 4, D], BF16, kind="ExternalOutput").ap()
    wv_bf = nc.dram_tensor("wv_bf", [D, D], BF16, kind="Internal").ap()
    xc_t = dram_tiles(xco)
    wv_t = dram_tiles(wvw)
    wvb_t = dram_tiles(wv_bf)

    PAIRS = [(i, j) for i in range(4) for j in range(i + 1)]

    wq_f = env.sbuf("wq_f", [128, KT * DK], BF16)
    wk_f = env.sbuf("wk_f", [128, KT * DK], BF16)
    wv_sl = env.sbuf("wv_sl", [128, 2 * KT * 256], BF16)
    wcast = env.sbuf("wcast", [128, 2048], BF16)
    xc_sb = env.sbuf("xc_sb", [128, KT * CW], BF16)
    qsil = env.sbuf("qsil", [128, 2 * CW], BF16)
    q_sb = env.sbuf("q_sb", [128, 8 * CW], BF16)
    k_sb = env.sbuf("k_sb", [128, 8 * CW], BF16)
    v_sb = env.sbuf("v_sb", [128, 2 * CW], BF16)
    v_tm = env.sbuf("v_tm", [128, 4 * D], BF16)
    csT = env.sbuf("csT", [128, 2 * CW], F32)
    hs2b = env.sbuf("hs2b", [128, 2], BF16)
    rws = env.sbuf("rws", [128, 32], BF16)
    rpt = env.sbuf("rpt", [128, CW], F32)
    rp2 = env.sbuf("rp2", [128, CW], F32)
    rtt = env.sbuf("rtt", [128, 8 * 64], F32)
    mmt = env.sbuf("mmt", [128, 160], F32)
    ctf = env.sbuf("ctf", [128, 160], F32)
    prodb = env.sbuf("prodb", [128, 2 * 1280], BF16)
    o_i = env.sbuf("o_i", [128, D], F32)
    o_n = env.sbuf("o_n", [128, D], BF16)
    st16 = env.sbuf("st16", [128, 8 * 16], F32)
    otmp = env.sbuf("otmp", [128, 128], F32)
    psP = [env.psum("c2_p0", [128, CW]), env.psum("c2_p1", [128, CW])]
    psR = env.psum("c2_r", [128, 128])
    psQ = env.psum("c2_q", [128, 160])
    env.set_junk(env.sbuf("junk", [1, 1024], F32))

    s_w = env.sem("d_w")
    s_wc = env.sem("d_wc")
    s_wv = env.sem("d_wv")
    s_xc = env.sem("d_xc")
    s_pj = env.sem("d_pj")
    s_pe = env.sem("d_pe")
    s_rp = env.sem("d_rp")
    s_rt = env.sem("d_rt")
    s_sg = env.sem("d_sg")
    s_rv = env.sem("d_rv")
    s_pb = env.sem("d_pb")
    s_qd = env.sem("d_qd")
    s_cm = env.sem("d_cm")
    s_vt = env.sem("d_vt")
    s_oa = env.sem("d_oa")
    s_iv = env.sem("d_iv")
    s_sq = env.sem("d_sq")
    s_so = env.sem("d_so")

    env.open_block()

    @env.block.gpsimd
    def _(g):
        for et in range(KT):
            s_w.inc(g.dma_start(out=wq_f[:, et * DK:(et + 1) * DK],
                                in_=wqw.rearrange("(k p) c -> k p c", p=128)[et]), 16)
            s_w.inc(g.dma_start(out=wk_f[:, et * DK:(et + 1) * DK],
                                in_=wkw.rearrange("(k p) c -> k p c", p=128)[et]), 16)
        s_w.inc(g.dma_start(out=rws[:, :], in_=rwm[:, :]), 16)
        s_w.inc(g.dma_start(out=hs2b[:, :], in_=hs2[:, :]), 16)
        s_w.inc(g.dma_start(out=csT[:, 0:CW], in_=cosT[:, :]), 16)
        s_w.inc(g.dma_start(out=csT[:, CW:2 * CW], in_=sinT[:, :]), 16)
        for et in range(KT):
            if et > 0:
                g.wait_ge(s_wc.h, 32 * et - 16)
            s_wc.inc(g.dma_start(out=wcast[:, :], in_=wv_t[et][:, :]), 16)
            g.wait_ge(s_wc.h, 32 * et + 16)
            s_wc.inc(g.dma_start(out=wvb_t[et][:, :], in_=wcast[:, :]), 16)
    NW = s_w.n
    NWC = s_wc.n

    for pc in range(NPC):
        p0 = pc * PW
        PJ0 = s_pj.n
        PE0 = s_pe.n
        RT0 = s_rt.n
        VT0 = s_vt.n
        QD0 = s_qd.n
        RP0 = s_rp.n

        @env.block.sync
        def _(s, pc=pc, p0=p0, PJ0=PJ0):
            if pc == 0:
                s.wait_ge(s_wc.h, NWC)
            if pc > 0:
                s.wait_ge(s_pj.h, PJ0)
            for k in range(KT):
                s_xc.inc(s.dma_start(
                    out=xc_sb[:, k * CW:(k + 1) * CW].rearrange("p (b q) -> p b q", b=4),
                    in_=xc_t[k].rearrange("p (b q) -> p b q", b=4)[:, :, p0:p0 + PW]), 16)

        for sl in range(8):
            @env.block.sync
            def _(s, pc=pc, sl=sl, PJ0=PJ0):
                if sl >= 2:
                    s.wait_ge(s_pj.h, PJ0 + 16 + 2 * (sl - 2) + 2)
                elif pc > 0:
                    s.wait_ge(s_pj.h, PJ0)
                for et in range(KT):
                    s_wv.inc(s.dma_start(
                        out=wv_sl[:, (sl % 2) * KT * 256 + et * 256:
                                  (sl % 2) * KT * 256 + (et + 1) * 256],
                        in_=wvb_t[et][:, sl * 256:(sl + 1) * 256]), 16)

            for h in ([2 * (sl - 1), 2 * (sl - 1) + 1] if sl >= 1 else []):
                @env.block.sync
                def _(s, pc=pc, h=h, PE0=PE0, VT0=VT0):
                    s.wait_ge(s_pe.h, PE0 + 16 + h + 1)
                    s_vt.inc(s.dma_start(
                        out=v_tm.ap().rearrange("p (g x) -> p g x", g=4)[:, :, h * 128:(h + 1) * 128],
                        in_=v_sb[:, (h % 2) * CW:(h % 2 + 1) * CW], transpose=True), 16)

        for h in (14, 15):
            @env.block.sync
            def _(s, pc=pc, h=h, PE0=PE0, VT0=VT0):
                s.wait_ge(s_pe.h, PE0 + 16 + h + 1)
                s_vt.inc(s.dma_start(
                    out=v_tm.ap().rearrange("p (g x) -> p g x", g=4)[:, :, h * 128:(h + 1) * 128],
                    in_=v_sb[:, (h % 2) * CW:(h % 2 + 1) * CW], transpose=True), 16)

        for gt in range(32):
            @env.block.tensor
            def _(t, pc=pc, gt=gt, PJ0=PJ0, PE0=PE0, RT0=RT0):
                if gt == 0:
                    t.wait_ge(s_w.h, NW)
                    t.wait_ge(s_xc.h, 16 * 16 * (pc + 1))
                if gt >= 2:
                    t.wait_ge(s_pe.h, PE0 + gt - 1)
                if gt >= 16:
                    sl = (gt - 16) // 2
                    t.wait_ge(s_wv.h, 16 * 16 * (8 * pc + sl + 1))
                for et in range(KT):
                    if gt < 8:
                        lhs = wq_f[:, et * DK + gt * 128:et * DK + (gt + 1) * 128]
                    elif gt < 16:
                        lhs = wk_f[:, et * DK + (gt - 8) * 128:et * DK + (gt - 7) * 128]
                    else:
                        sl = (gt - 16) // 2
                        half = (gt - 16) % 2
                        lhs = wv_sl[:, (sl % 2) * KT * 256 + et * 256 + half * 128:
                                    (sl % 2) * KT * 256 + et * 256 + (half + 1) * 128]
                    ins = t.matmul(psP[gt % 2][:, :], lhs, xc_sb[:, et * CW:(et + 1) * CW],
                                   start=(et == 0), stop=(et == KT - 1))
                s_pj.inc(ins)
                if gt >= 17:
                    h = gt - 17
                    t.wait_ge(s_pe.h, PE0 + 16 + h + 1)
                    for g in range(4):
                        ins = t.matmul(psR[:, 2 * (g * 16 + h):2 * (g * 16 + h) + 2],
                                       v_sb[:, (h % 2) * CW + g * 128:(h % 2) * CW + (g + 1) * 128],
                                       rws[:, 2 * h:2 * h + 2], start=True, stop=True)
                    s_rt.inc(ins)

            @env.block.scalar
            def _(sc, pc=pc, gt=gt, PJ0=PJ0, RP0=RP0, RT0=RT0, VT0=VT0):
                sc.wait_ge(s_pj.h, PJ0 + gt + 1)
                if gt < 16:
                    if gt >= 2:
                        sc.wait_ge(s_rp.h, RP0 + gt - 1)
                    dst = qsil[:, (gt % 2) * CW:(gt % 2 + 1) * CW]
                    fn = AF.Silu if gt < 8 else AF.Copy
                    ins = sc.activation(dst, psP[gt % 2][:, :], fn)
                else:
                    h = gt - 16
                    if h >= 2:
                        sc.wait_ge(s_rt.h, RT0 + h - 1)
                        sc.wait_ge(s_vt.h, VT0 + 16 * (h - 1))
                    ins = sc.copy(v_sb[:, (h % 2) * CW:(h % 2 + 1) * CW], psP[gt % 2][:, :])
                s_pe.inc(ins)

            if gt < 16:
                @env.block.vector
                def _(v, pc=pc, gt=gt, PE0=PE0):
                    v.wait_ge(s_pe.h, PE0 + gt + 1)
                    env.vfence(v)
                    Q = qsil[:, (gt % 2) * CW:(gt % 2 + 1) * CW]
                    dst = (q_sb if gt < 8 else k_sb)
                    tb = (gt % 8) * CW
                    cosv, sinv = csT[:, 0:CW], csT[:, CW:2 * CW]
                    v.tensor_copy(rpt[0:64, :], Q[64:128, :])
                    v.tensor_copy(rpt[64:128, :], Q[0:64, :])
                    v.tensor_mul(rp2[0:64, :], rpt[0:64, :], sinv[0:64, :])
                    v.tensor_mul(rpt[0:64, :], Q[0:64, :], cosv[0:64, :])
                    v.tensor_sub(dst[0:64, tb:tb + CW], rpt[0:64, :], rp2[0:64, :])
                    v.tensor_mul(rp2[64:128, :], rpt[64:128, :], sinv[64:128, :])
                    v.tensor_mul(rpt[64:128, :], Q[64:128, :], cosv[64:128, :])
                    ins = v.tensor_add(dst[64:128, tb:tb + CW], rpt[64:128, :], rp2[64:128, :])
                    s_rp.inc(ins)

        @env.block.tensor
        def _(t, pc=pc, PE0=PE0):
            h = 15
            t.wait_ge(s_pe.h, PE0 + 16 + h + 1)
            for g in range(4):
                ins = t.matmul(psR[:, 2 * (g * 16 + h):2 * (g * 16 + h) + 2],
                               v_sb[:, (h % 2) * CW + g * 128:(h % 2) * CW + (g + 1) * 128],
                               rws[:, 2 * h:2 * h + 2], start=True, stop=True)
            s_rt.inc(ins)

        RD, RS0, RS1, RI0, RI1, RN0, RN1, RTMP = [rtt.ap()[:, i * 64:(i + 1) * 64]
                                                  for i in range(8)]

        @env.block.vector
        def _(v, pc=pc, RT0=RT0):
            v.wait_ge(s_rt.h, RT0 + 16)
            env.vfence(v)
            l0 = psR.ap().rearrange("p (s r) -> p s r", r=2)[:, :, 0]
            l1 = psR.ap().rearrange("p (s r) -> p s r", r=2)[:, :, 1]
            v.tensor_copy(RD, l0)
            v.tensor_copy(RD, l0)
            v.tensor_copy(RTMP, l1)
            env.vspace(v)
            ins = v.tensor_sub(RD, RD, RTMP)
            s_rv.inc(env.vfence(v))

        @env.block.scalar
        def _(sc, pc=pc):
            sc.wait_ge(s_rv.h, 2 * pc + 1)
            env.sfence(sc)
            sc.activation(RS0, RD, AF.Sigmoid)
            s_sg.inc(env.sfence(sc))

        @env.block.vector
        def _(v, pc=pc):
            v.wait_ge(s_sg.h, pc + 1)
            env.vfence(v)
            v.tensor_scalar(RS1, RS0, -1.0, 1.0, ALU.mult, ALU.add)
            v.tensor_scalar(RI0, RD, 0.0, None, ALU.is_ge)
            env.vspace(v)
            v.tensor_scalar(RI1, RI0, -1.0, 1.0, ALU.mult, ALU.add)
            v.tensor_mul(RS0, RS0, RI0)
            v.tensor_mul(RS1, RS1, RI1)
            for RN, RI in ((RN0, RI0), (RN1, RI1)):
                v.tensor_copy(RN[:, 0:16], RI[:, 0:16])
                for g in range(1, 4):
                    env.vspace(v)
                    v.tensor_add(RN[:, 16 * g:16 * (g + 1)], RN[:, 16 * (g - 1):16 * g],
                                 RI[:, 16 * g:16 * (g + 1)])
            for RN in (RN0, RN1):
                env.vspace(v)
                v.tensor_scalar(RTMP, RN, 0.5, None, ALU.is_ge)
                v.tensor_scalar(RN, RN, 1.0, None, ALU.max)
                env.vspace(v)
                v.reciprocal(RN, RN)
                env.vfence(v)
                v.tensor_mul(RN, RN, RTMP)
            env.vspace(v)
            v.tensor_mul(RS0, RS0, RN0)
            v.tensor_mul(RS1, RS1, RN1)
            v.tensor_scalar(RS0, RS0, 0.125, None, ALU.mult)
            v.tensor_scalar(RS1, RS1, 0.125, None, ALU.mult)
            env.vspace(v)
            for pidx, (i, j) in enumerate(PAIRS):
                ms = mmt[:, pidx * 16:(pidx + 1) * 16]
                v.tensor_mul(ms, RS0[:, 16 * i:16 * (i + 1)], RI0[:, 16 * j:16 * (j + 1)])
                v.tensor_mul(RTMP[:, 0:16], RS1[:, 16 * i:16 * (i + 1)],
                             RI1[:, 16 * j:16 * (j + 1)])
                env.vspace(v)
                v.tensor_add(ms, ms, RTMP[:, 0:16])
            s_rv.inc(v.engine_nop())

            for qt in range(8):
                v.wait_ge(s_rp.h, RP0 + 16)
                if qt >= 2:
                    v.wait_ge(s_qd.h, QD0 + qt - 1)
                for pidx, (i, j) in enumerate(PAIRS):
                    ins = v.tensor_mul(
                        prodb[:, (qt % 2) * 1280 + pidx * 128:(qt % 2) * 1280 + (pidx + 1) * 128],
                        q_sb[:, qt * CW + i * 128:qt * CW + (i + 1) * 128],
                        k_sb[:, qt * CW + j * 128:qt * CW + (j + 1) * 128])
                s_pb.inc(ins)

        @env.block.tensor
        def _(t, pc=pc, QD0=QD0):
            for qt in range(8):
                t.wait_ge(s_pb.h, 8 * pc + qt + 1)
                if pc > 0 and qt == 0:
                    t.wait_ge(s_cm.h, pc)
                for pidx in range(len(PAIRS)):
                    ins = t.matmul(psQ[:, pidx * 16 + 2 * qt:pidx * 16 + 2 * qt + 2],
                                   prodb[:, (qt % 2) * 1280 + pidx * 128:
                                         (qt % 2) * 1280 + (pidx + 1) * 128],
                                   hs2b[:, :], start=True, stop=True)
                s_qd.inc(ins)

        @env.block.vector
        def _(v, pc=pc, QD0=QD0):
            v.wait_ge(s_qd.h, QD0 + 8)
            env.vfence(v)
            v.tensor_copy(ctf[:, :], psQ[:, :])
            v.tensor_copy(ctf[:, :], psQ[:, :])
            env.vspace(v)
            ins = v.tensor_mul(ctf[:, :], ctf[:, :], mmt[:, :])
            s_cm.inc(ins)

        for i in range(4):
            @env.block.vector
            def _(v, pc=pc, i=i, VT0=VT0):
                if i == 0:
                    v.wait_ge(s_vt.h, VT0 + 16 * 16)
                    env.vfence(v)
                if pc > 0 or i > 0:
                    v.wait_ge(s_so.h, 16 * (4 * pc + i))
                pmap = {(a, b): px for px, (a, b) in enumerate(PAIRS)}
                for h in range(16):
                    for j in range(i + 1):
                        px = pmap[(i, j)]
                        csc = ctf[:, px * 16 + h:px * 16 + h + 1]
                        vsl = v_tm[:, j * D + h * 128:j * D + (h + 1) * 128]
                        osl = o_i[:, h * 128:(h + 1) * 128]
                        if j == 0:
                            v.tensor_scalar(osl, vsl, csc, None, ALU.mult)
                        else:
                            v.tensor_scalar(otmp[:, :], vsl, csc, None, ALU.mult)
                            v.tensor_add(osl, osl, otmp[:, :])
                env.vspace(v)
                for h in range(16):
                    v.reduce_sum(st16[:, h:h + 1], o_i[:, h * 128:(h + 1) * 128], axis=AX.X)
                for h in range(16):
                    v.tensor_mul(otmp[:, :], o_i[:, h * 128:(h + 1) * 128],
                                 o_i[:, h * 128:(h + 1) * 128])
                    env.vspace(v)
                    v.reduce_sum(st16[:, 16 + h:17 + h], otmp[:, :], axis=AX.X)
                env.vspace(v)
                v.tensor_scalar(st16[:, 0:16], st16[:, 0:16], 1.0 / 128, None, ALU.mult)
                v.tensor_scalar(st16[:, 16:32], st16[:, 16:32], 1.0 / 128, None, ALU.mult)
                env.vspace(v)
                v.tensor_mul(st16[:, 32:48], st16[:, 0:16], st16[:, 0:16])
                env.vspace(v)
                v.tensor_sub(st16[:, 32:48], st16[:, 16:32], st16[:, 32:48])
                env.vspace(v)
                v.tensor_scalar(st16[:, 32:48], st16[:, 32:48], EPS, None, ALU.add)
                env.vspace(v)
                v.reciprocal(st16[:, 32:48], st16[:, 32:48])
                s_iv.inc(env.vfence(v))

            @env.block.scalar
            def _(sc, pc=pc, i=i):
                sc.wait_ge(s_iv.h, 4 * pc + i + 1)
                env.sfence(sc)
                sc.activation(st16[:, 48:64], st16[:, 32:48], AF.Sqrt)
                s_sq.inc(env.sfence(sc))

            @env.block.vector
            def _(v, pc=pc, i=i):
                v.wait_ge(s_sq.h, 4 * pc + i + 1)
                env.vfence(v)
                for h in range(16):
                    v.tensor_scalar(o_n[:, h * 128:(h + 1) * 128],
                                    o_i[:, h * 128:(h + 1) * 128],
                                    st16[:, h:h + 1], st16[:, 48 + h:49 + h],
                                    ALU.subtract, ALU.mult)
                s_oa.inc(env.vfence(v))

            @env.block.sync
            def _(s, pc=pc, i=i, p0=p0):
                s.wait_ge(s_oa.h, 4 * pc + i + 1)
                s_so.inc(s.dma_start(out=onrm[i][p0:p0 + PW, :], in_=o_n[:, :]), 16)

    @env.block.sync
    def _(s):
        s.wait_ge(s_so.h, s_so.n)

    return env.close()


def host_prep_C2(inputs):
    Wq, Wk, Wv = inputs["L0_ffn_Wq"], inputs["L0_ffn_Wk"], inputs["L0_ffn_Wv"]
    rw = inputs["L0_ffn_router_w"]
    perm = []
    for qt in range(8):
        for h in (2 * qt, 2 * qt + 1):
            perm.extend(range(h * 64, h * 64 + 32))
        for h in (2 * qt, 2 * qt + 1):
            perm.extend(range(h * 64 + 32, h * 64 + 64))
    perm = np.asarray(perm)
    wqw = np.ascontiguousarray(Wq[:, perm]).astype(np.float32)
    wkw = np.ascontiguousarray(Wk[:, perm]).astype(np.float32)
    wvw = np.ascontiguousarray(Wv).astype(np.float32)
    rwm = np.zeros((128, 32), dtype=np.float32)
    for h in range(16):
        rwm[:, 2 * h] = rw[h, :, 0]
        rwm[:, 2 * h + 1] = rw[h, :, 1]
    invf = 1.0 / (10000.0 ** (np.arange(32, dtype=np.float64) / 32.0))
    ang = np.outer(invf, np.arange(4, dtype=np.float64))
    cosT = np.zeros((128, 512), dtype=np.float32)
    sinT = np.zeros((128, 512), dtype=np.float32)
    for r in range(128):
        for g in range(4):
            cosT[r, g * 128:(g + 1) * 128] = np.cos(ang[r % 32, g])
            sinT[r, g * 128:(g + 1) * 128] = np.sin(ang[r % 32, g])
    hs2 = np.zeros((128, 2), dtype=np.float32)
    hs2[0:32, 0] = 1.0
    hs2[64:96, 0] = 1.0
    hs2[32:64, 1] = 1.0
    hs2[96:128, 1] = 1.0
    m = dict(wqw=wqw, wkw=wkw, wvw=wvw, rwm=rwm, cosT=cosT, sinT=sinT, hs2=hs2)
    return [dict(m) for _ in range(NCORE)]



def build_launch_C3(T_=TOWN):
    env = Env("C3")
    nc = env.nc
    CW = 256
    NCH = T_ // CW

    xco = nc.dram_tensor("xco", [D, T_], BF16, kind="ExternalInput").ap()
    onc = nc.dram_tensor("onc", [D, T_], BF16, kind="ExternalInput").ap()
    x1o = nc.dram_tensor("x1o", [D, T_], F32, kind="ExternalInput").ap()
    wgw = nc.dram_tensor("wgw", [D, D], F32, kind="ExternalInput").ap()
    wow = nc.dram_tensor("wow", [D, D], F32, kind="ExternalInput").ap()
    x2o = nc.dram_tensor("x2o", [D, T_], F32, kind="ExternalOutput").ap()
    xn2o = nc.dram_tensor("xn2o", [D, T_], BF16, kind="ExternalOutput").ap()
    xc_t = dram_tiles(xco)
    on_t = dram_tiles(onc)
    x1_t = dram_tiles(x1o)
    wg_t = dram_tiles(wgw)
    wo_t = dram_tiles(wow)
    x2_t = dram_tiles(x2o)
    xn_t = dram_tiles(xn2o)

    wg_sb = env.sbuf("wg_sb", [128, KT * D], BF16)
    wo_sb = env.sbuf("wo_sb", [128, KT * D], BF16)
    xc_sb = env.sbuf("xc_sb", [128, KT * CW], BF16)
    onp = env.sbuf("onp", [128, 4 * CW], BF16)
    gsil = env.sbuf("gsil", [128, 2 * CW], BF16)
    fact = env.sbuf("fact", [128, KT * CW], BF16)
    x1p = env.sbuf("x1p", [128, 4 * CW], F32)
    x2p = env.sbuf("x2p", [128, 4 * CW], F32)
    xb = env.sbuf("xb", [128, KT * CW], BF16)
    sqb = env.sbuf("sqb", [128, KT * CW], BF16)
    row = env.sbuf("row", [1, 6 * CW], F32)
    rB = env.sbuf("rB", [128, CW], BF16)
    mB = env.sbuf("mB", [128, CW], BF16)
    tmpb = env.sbuf("tmpb", [128, CW], BF16)
    xnb = env.sbuf("xnb", [128, 2 * CW], BF16)
    psG = [env.psum("c3_g0", [128, CW]), env.psum("c3_g1", [128, CW])]
    psW = [env.psum("c3_w0", [128, CW]), env.psum("c3_w1", [128, CW])]
    psA = env.psum("c3_psA", [1, CW])
    psB = env.psum("c3_psB", [1, CW])
    psR = env.psum("c3_psR", [128, CW])
    psM = env.psum("c3_psM", [128, CW])
    ones_b = env.sbuf("ones_b", [128, 1], BF16)
    ones_f = env.sbuf("ones_f", [1, 128], F32)
    env.set_junk(env.sbuf("junk", [1, 1024], F32))

    s_w = env.sem("e_w")
    s_ld = env.sem("e_ld")
    s_gm = env.sem("e_gm")
    s_gs = env.sem("e_gs")
    s_fa = env.sem("e_fa")
    s_wm = env.sem("e_wm")
    s_ev = env.sem("e_ev")
    s_sx = env.sem("e_sx")
    s_sq = env.sem("e_sq")
    s_st = env.sem("e_st")
    s_rw = env.sem("e_rw")
    s_e2 = env.sem("e_e2")
    s_nm = env.sem("e_nm")
    s_sn = env.sem("e_sn")
    ini = env.sem("ini")

    def evn(t):
        return t + 1 + t // KT

    env.open_block()
    emit_consts(env, ones_b, ones_f)

    @env.block.gpsimd
    def _(g):
        for et in range(KT):
            s_w.inc(g.dma_start(out=wg_sb[:, et * D:(et + 1) * D], in_=wg_t[et][:, :]), 16)
        for et in range(KT):
            s_w.inc(g.dma_start(out=wo_sb[:, et * D:(et + 1) * D], in_=wo_t[et][:, :]), 16)
    NW = s_w.n

    for ci in range(NCH):
        c0, c1 = ci * CW, (ci + 1) * CW

        @env.block.gpsimd
        def _(g, ci=ci, c0=c0, c1=c1):
            if ci > 0:
                g.wait_ge(s_gm.h, 16 * ci)
            for k in range(KT):
                s_ld.inc(g.dma_start(out=xc_sb[:, k * CW:(k + 1) * CW],
                                     in_=xc_t[k][:, c0:c1]), 16)
            for dt in range(KT):
                gidx = ci * KT + dt
                if gidx >= 4:
                    g.wait_ge(s_fa.h, (gidx - 4) + 1 + (gidx - 4) // KT)
                s_ld.inc(g.dma_start(out=onp[:, (gidx % 4) * CW:(gidx % 4 + 1) * CW],
                                     in_=on_t[dt][:, c0:c1]), 16)
            for dt in range(KT):
                gidx = ci * KT + dt
                if gidx >= 4:
                    g.wait_ge(s_ev.h, evn(gidx - 4))
                s_ld.inc(g.dma_start(out=x1p[:, (gidx % 4) * CW:(gidx % 4 + 1) * CW],
                                     in_=x1_t[dt][:, c0:c1]), 16)

        for dt in range(KT):
            gidx = ci * KT + dt

            @env.block.tensor
            def _(t, ci=ci, dt=dt, gidx=gidx):
                if gidx == 0:
                    t.wait_ge(s_w.h, NW)
                t.wait_ge(s_ld.h, 16 * (48 * ci + 16))
                if gidx >= 2:
                    t.wait_ge(s_gs.h, gidx - 1)
                for et in range(KT):
                    ins = t.matmul(psG[gidx % 2][:, :],
                                   wg_sb[:, et * D + dt * 128:et * D + (dt + 1) * 128],
                                   xc_sb[:, et * CW:(et + 1) * CW],
                                   start=(et == 0), stop=(et == KT - 1))
                s_gm.inc(ins)

            @env.block.scalar
            def _(sc, gidx=gidx):
                sc.wait_ge(s_gm.h, gidx + 1)
                if gidx >= 2:
                    sc.wait_ge(s_fa.h, (gidx - 2) + 1 + (gidx - 2) // KT)
                ins = sc.activation(gsil[:, (gidx % 2) * CW:(gidx % 2 + 1) * CW],
                                    psG[gidx % 2][:, :], AF.Silu)
                s_gs.inc(ins)

            @env.block.vector
            def _(v, ci=ci, dt=dt, gidx=gidx):
                v.wait_ge(s_gs.h, gidx + 1)
                v.wait_ge(s_ld.h, 16 * (48 * ci + 16 + dt + 1))
                if ci > 0 and dt == 0:
                    v.wait_ge(s_wm.h, 16 * ci)
                env.vfence(v)
                ins = v.tensor_mul(fact[:, dt * CW:(dt + 1) * CW],
                                   gsil[:, (gidx % 2) * CW:(gidx % 2 + 1) * CW],
                                   onp[:, (gidx % 4) * CW:(gidx % 4 + 1) * CW])
                s_fa.inc(ins)

        @env.block.vector
        def _(v):
            s_fa.inc(env.vfence(v))

        for dt in range(KT):
            gidx = ci * KT + dt

            @env.block.tensor
            def _(t, ci=ci, dt=dt, gidx=gidx):
                t.wait_ge(s_fa.h, 17 * (ci + 1))
                if gidx >= 2:
                    t.wait_ge(s_ev.h, evn(gidx - 2))
                for et in range(KT):
                    ins = t.matmul(psW[gidx % 2][:, :],
                                   wo_sb[:, et * D + dt * 128:et * D + (dt + 1) * 128],
                                   fact[:, et * CW:(et + 1) * CW],
                                   start=(et == 0), stop=(et == KT - 1))
                s_wm.inc(ins)

            @env.block.vector
            def _(v, ci=ci, dt=dt, gidx=gidx):
                v.wait_ge(s_wm.h, gidx + 1)
                v.wait_ge(s_ld.h, 16 * (48 * ci + 32 + dt + 1))
                if gidx >= 4:
                    v.wait_ge(s_sx.h, 16 * (gidx - 3))
                if dt == 0 and ci > 0:
                    v.wait_ge(s_sq.h, ci)
                env.vfence(v)
                ins = v.tensor_add(x2p[:, (gidx % 4) * CW:(gidx % 4 + 1) * CW],
                                   psW[gidx % 2][:, :],
                                   x1p[:, (gidx % 4) * CW:(gidx % 4 + 1) * CW])
                s_ev.inc(ins)

            @env.block.sync
            def _(s, ci=ci, dt=dt, gidx=gidx, c0=c0, c1=c1):
                s.wait_ge(s_ev.h, min(evn(gidx + 1), 17 * (ci + 1)))
                s_sx.inc(s.dma_start(out=x2_t[dt][:, c0:c1],
                                     in_=x2p[:, (gidx % 4) * CW:(gidx % 4 + 1) * CW]), 16)

        @env.block.vector
        def _(v):
            s_ev.inc(env.vfence(v))

        @env.block.scalar
        def _(sc, ci=ci):
            for dt in range(KT):
                gidx = ci * KT + dt
                sc.wait_ge(s_ev.h, min(evn(gidx + 1), 17 * (ci + 1)))
                sc.copy(xb[:, dt * CW:(dt + 1) * CW],
                        x2p[:, (gidx % 4) * CW:(gidx % 4 + 1) * CW])
                sc.activation(sqb[:, dt * CW:(dt + 1) * CW], xb[:, dt * CW:(dt + 1) * CW],
                              AF.Square)
            s_sq.inc(env.sfence(sc))

        @env.block.tensor
        def _(t, ci=ci):
            t.wait_ge(ini.h, 2)
            t.wait_ge(s_sq.h, ci + 1)
            if ci >= 1:
                t.wait_ge(s_rw.h, 3 * (ci - 1) + 1)
            for k in range(KT):
                t.matmul(psA[0:1, :], ones_b[:, :], xb[:, k * CW:(k + 1) * CW],
                         start=(k == 0), stop=(k == KT - 1))
            for k in range(KT):
                ins = t.matmul(psB[0:1, :], ones_b[:, :], sqb[:, k * CW:(k + 1) * CW],
                               start=(k == 0), stop=(k == KT - 1))
            s_st.inc(ins)

        @env.block.vector
        def _(v, ci=ci):
            v.wait_ge(s_st.h, 2 * ci + 1)
            env.vfence(v)
            v.tensor_scalar(row[0:1, 0:CW], psA[0:1, :], 1.0 / D, None, ALU.mult)
            v.tensor_scalar(row[0:1, CW:2 * CW], psB[0:1, :], 1.0 / D, None, ALU.mult)
            v.tensor_mul(row[0:1, 4 * CW:5 * CW], row[0:1, 0:CW], row[0:1, 0:CW])
            v.tensor_sub(row[0:1, 2 * CW:3 * CW], row[0:1, CW:2 * CW], row[0:1, 4 * CW:5 * CW])
            v.tensor_scalar(row[0:1, 2 * CW:3 * CW], row[0:1, 2 * CW:3 * CW], EPS, None, ALU.add)
            ins = v.reciprocal(row[0:1, 2 * CW:3 * CW], row[0:1, 2 * CW:3 * CW])
            s_rw.inc(ins)
            v.wait_ge(s_rw.h, 3 * ci + 2)
            v.tensor_mul(row[0:1, 4 * CW:5 * CW], row[0:1, 0:CW], row[0:1, 3 * CW:4 * CW])
            s_rw.inc(env.vfence(v))

        @env.block.scalar
        def _(sc, ci=ci):
            sc.wait_ge(s_rw.h, 3 * ci + 1)
            env.sfence(sc)
            sc.activation(row[0:1, 3 * CW:4 * CW], row[0:1, 2 * CW:3 * CW], AF.Sqrt)
            s_rw.inc(env.sfence(sc))

        @env.block.tensor
        def _(t, ci=ci):
            t.wait_ge(s_rw.h, 3 * ci + 3)
            if ci >= 1:
                t.wait_ge(s_e2.h, 2 * ci)
            t.matmul(psR[:, :], ones_f[:, :], row[0:1, 3 * CW:4 * CW], start=True, stop=True)
            ins = t.matmul(psM[:, :], ones_f[:, :], row[0:1, 4 * CW:5 * CW], start=True, stop=True)
            s_st.inc(ins)

        @env.block.vector
        def _(v, ci=ci):
            v.wait_ge(s_st.h, 2 * ci + 2)
            env.vfence(v)
            s_e2.inc(v.tensor_copy(rB[:, :], psR[:, :]))
            s_e2.inc(v.tensor_copy(mB[:, :], psM[:, :]))
            for k in range(KT):
                gk = ci * KT + k
                if gk >= 2:
                    v.wait_ge(s_sn.h, 16 * (gk - 1))
                v.tensor_mul(tmpb[:, :], xb[:, k * CW:(k + 1) * CW], rB[:, :])
                ins = v.tensor_sub(xnb[:, (gk % 2) * CW:(gk % 2 + 1) * CW],
                                   tmpb[:, :], mB[:, :])
                s_nm.inc(ins)
            s_nm.inc(env.vfence(v))

        @env.block.sync
        def _(s, ci=ci, c0=c0, c1=c1):
            for k in range(KT):
                gk = ci * KT + k
                s.wait_ge(s_nm.h, min(gk + 2 + gk // KT, 17 * (ci + 1)))
                s_sn.inc(s.dma_start(out=xn_t[k][:, c0:c1],
                                     in_=xnb[:, (gk % 2) * CW:(gk % 2 + 1) * CW]), 16)

    @env.block.sync
    def _(s):
        s.wait_ge(s_sx.h, s_sx.n)
        s.wait_ge(s_sn.h, s_sn.n)

    return env.close()


def host_prep_C3(inputs):
    Wg = np.ascontiguousarray(inputs["L0_ffn_Wg"]).astype(np.float32)
    Wo = np.ascontiguousarray(inputs["L0_ffn_Wout"]).astype(np.float32)
    m = dict(wgw=Wg, wow=Wo)
    return [dict(m) for _ in range(NCORE)]


def onrm_to_ch_major(onrm_arr):
    a = np.asarray(onrm_arr)
    return np.ascontiguousarray(a.transpose(2, 0, 1).reshape(D, TOWN))


def host_prep_C1(inputs, oat_full):
    Wo = inputs["L0_attn_Wo"]
    cw = inputs["L0_ffn_conv_w"]
    woT = np.ascontiguousarray(Wo.T).astype(np.float32)
    cvw = np.zeros((128, 2 * KT), dtype=np.float32)
    for k in range(KT):
        cvw[:, 2 * k] = cw[k * 128:(k + 1) * 128, 0]
        cvw[:, 2 * k + 1] = cw[k * 128:(k + 1) * 128, 1]
    hs = inputs["L0_attn_hidden_states"]
    maps = []
    for c in range(NCORE):
        cols = np.concatenate([np.arange(b_ * S + SOWN * c, b_ * S + SOWN * (c + 1))
                               for b_ in range(B)])
        oatc = np.ascontiguousarray(oat_full[:, cols])
        x0c = np.ascontiguousarray(
            hs[:, SOWN * c:SOWN * (c + 1), :].transpose(2, 0, 1).reshape(D, TOWN)
        ).astype(np.float32)
        maps.append(dict(oat=oatc, x0c=x0c, woT=woT, cvw=cvw))
    return maps


def unpack_oslc(oslc_arr, S_=S):
    o4 = np.asarray(oslc_arr).astype(np.float32).reshape(2, B, 128, S_)
    nqt = S_ // 128
    got = np.zeros((256, B * S_), dtype=np.float32)
    for hp in range(2):
        for b_ in range(B):
            arr = o4[hp, b_].reshape(128, nqt, 128)
            for h in range(2):
                j = 2 * hp + h
                got[j * 64:(j + 1) * 64, b_ * S_:(b_ + 1) * S_] = \
                    arr[:, :, h * 64:(h + 1) * 64].transpose(2, 1, 0).reshape(64, S_)
    return got



def build_launch_D():
    env = Env("DL")
    nc = env.nc
    LK = 4096
    NQT = 16
    NCH = LK // 512

    def nco(qt):
        return qt // 4 + 5

    xq = nc.dram_tensor("xq", [D, 2048], BF16, kind="ExternalInput").ap()
    xa = nc.dram_tensor("xa", [D, LK], BF16, kind="ExternalInput").ap()
    xv = nc.dram_tensor("xv", [LK, D], BF16, kind="ExternalInput").ap()
    wsw = nc.dram_tensor("wsw", [D, 256], F32, kind="ExternalInput").ap()
    waw = nc.dram_tensor("waw", [D, 256], F32, kind="ExternalInput").ap()
    brevD = nc.dram_tensor("brevD", [1, 6144], F32, kind="ExternalInput").ap()
    jmm = nc.dram_tensor("jmm", [128, 128], F32, kind="ExternalInput").ap()
    otd = nc.dram_tensor("otd", [2048, D], BF16, kind="ExternalOutput").ap()
    ptd = nc.dram_tensor("ptd", [NQT, 32, 128, 128], BF16, kind="Internal").ap()
    xa_t = dram_tiles(xa)
    xq_t = dram_tiles(xq)
    xvv = xv.rearrange("(t p) d -> t p d", p=128)

    ws_sb = env.sbuf("ws_sb", [128, KT * 256], BF16)
    wa_sb = env.sbuf("wa_sb", [128, KT * 256], BF16)
    jm_sb = env.sbuf("jm_sb", [128, 128], BF16)
    xa_st = env.sbuf("xa_st", [128, 2 * KT * 512], BF16)
    act_sb = env.sbuf("act_sb", [128, 2 * LK], BF16)
    st_sb = env.sbuf("st_sb", [128, 2 * 2048], BF16)
    sqs = env.sbuf("sqs", [128, 2 * 512], BF16)
    qn_sb = env.sbuf("qn_sb", [1, 2048], F32)
    an_sb = env.sbuf("an_sb", [1, LK + 16], F32)
    mrow = env.sbuf("mrow", [1, 2048], F32)
    negm16 = env.sbuf("negm16", [16, 2048], BF16)
    nmt16 = env.sbuf("nmt16", [128, 16 * 16], BF16)
    nmtf = env.sbuf("nmtf", [128, 16], F32)
    mskb = env.sbuf("mskb", [128, 8 * 512], BF16)
    p_sb = env.sbuf("p_sb", [128, 8 * 512], BF16)
    pt_sb = env.sbuf("pt_sb", [128, 32 * 128], BF16)
    den_sb = env.sbuf("den_sb", [128, 16], F32)
    dcol = env.sbuf("dcol", [128, 1], F32)
    vt_sb = env.sbuf("vt_sb", [128, 32 * 1024], BF16)
    ptb = act_sb
    ot_sb = p_sb
    ones_b = env.sbuf("ones_b", [128, 1], BF16)
    ones_f = env.sbuf("ones_f", [1, 128], F32)
    psA2 = [env.psum("dl_a0", [128, 512]), env.psum("dl_a1", [128, 512])]
    psS = [env.psum("dl_s0", [128, 512]), env.psum("dl_s1", [128, 512])]
    psO = [env.psum("dl_o0", [128, 512]), env.psum("dl_o1", [128, 512])]
    psN = env.psum("dl_n", [1, 512])
    env.set_junk(env.sbuf("junk", [1, 1024], F32))

    s_w = env.sem("f_w")
    s_mk = env.sem("f_mk")
    s_xa = env.sem("f_xa")
    s_pj = env.sem("f_pj")
    s_pe = env.sem("f_pe")
    s_nq = env.sem("f_nq")
    s_nr = env.sem("f_nr")
    s_pt = env.sem("f_pt")
    s_sc = env.sem("f_sc")
    s_px = env.sem("f_px")
    s_dn = env.sem("f_dn")
    s_ps = env.sem("f_ps")
    s_vl = env.sem("f_vl")
    s_pl = env.sem("f_pl")
    s_om = env.sem("f_om")
    s_oe = env.sem("f_oe")
    s_os = env.sem("f_os")
    ini = env.sem("ini")

    env.open_block()
    emit_consts(env, ones_b, ones_f)

    @env.block.gpsimd
    def _(g):
        for et in range(KT):
            s_w.inc(g.dma_start(out=ws_sb[:, et * 256:(et + 1) * 256],
                                in_=wsw.rearrange("(k p) c -> k p c", p=128)[et]), 16)
            s_w.inc(g.dma_start(out=wa_sb[:, et * 256:(et + 1) * 256],
                                in_=waw.rearrange("(k p) c -> k p c", p=128)[et]), 16)
        s_w.inc(g.dma_start(out=jm_sb[:, :], in_=jmm[:, :]), 16)
    NW = s_w.n

    PJN, PEN = [], []
    for ch in range(NCH + 4):
        is_state = ch >= NCH
        src_t = xq_t if is_state else xa_t
        co = (ch - NCH) if is_state else ch

        @env.block.sync
        def _(s, ch=ch, co=co, src_t=src_t):
            if ch >= 2:
                s.wait_ge(s_pj.h, PJN[ch - 2])
            for et in range(KT):
                s_xa.inc(s.dma_start(
                    out=xa_st[:, (ch % 2) * KT * 512 + et * 512:
                              (ch % 2) * KT * 512 + (et + 1) * 512],
                    in_=src_t[et][:, co * 512:(co + 1) * 512]), 16)

        @env.block.tensor
        def _(t, ch=ch, is_state=is_state):
            if ch == 0:
                t.wait_ge(s_w.h, NW)
            t.wait_ge(s_xa.h, 16 * KT * (ch + 1))
            if ch >= 1:
                t.wait_ge(s_pe.h, PEN[ch - 1])
            xo = (ch % 2) * KT * 512
            wsel = ws_sb if is_state else wa_sb
            for rt in range(2):
                for et in range(KT):
                    ins = t.matmul(psA2[rt][:, :],
                                   wsel[:, et * 256 + rt * 128:et * 256 + (rt + 1) * 128],
                                   xa_st[:, xo + et * 512:xo + (et + 1) * 512],
                                   start=(et == 0), stop=(et == KT - 1))
                s_pj.inc(ins)
        PJN.append(s_pj.n)

        @env.block.scalar
        def _(sc, ch=ch, co=co, is_state=is_state):
            sc.wait_ge(s_pj.h, PJN[ch])
            for rt in range(2):
                if is_state:
                    ins = sc.activation(st_sb[:, rt * 2048 + co * 512:rt * 2048 + (co + 1) * 512],
                                        psA2[rt][:, :], AF.Copy, scale=0.0625)
                else:
                    ins = sc.copy(act_sb[:, rt * LK + co * 512:rt * LK + (co + 1) * 512],
                                  psA2[rt][:, :])
            s_pe.inc(ins)
        PEN.append(s_pe.n)

    NPE = s_pe.n

    for which, nchk in ((0, 4), (1, 8)):
        for c in range(nchk):
            @env.block.scalar
            def _(sc, which=which, c=c):
                sc.wait_ge(s_pe.h, NPE)
                if not (which == 0 and c == 0):
                    sc.wait_ge(s_nq.h, s_nq.n)
                if which == 0 and c == 0:
                    env.sfence(sc)
                srcb = st_sb if which == 0 else act_sb
                W = 2048 if which == 0 else LK
                sc.activation(sqs[:, 0:512], srcb[:, 0 * W + c * 512:0 * W + (c + 1) * 512],
                              AF.Square)
                sc.activation(sqs[:, 512:1024], srcb[:, 1 * W + c * 512:1 * W + (c + 1) * 512],
                              AF.Square)
                s_nq.inc(env.sfence(sc))

            @env.block.tensor
            def _(t, which=which, c=c):
                t.wait_ge(ini.h, 2)
                t.wait_ge(s_nq.h, s_nq.n)
                if s_nr.n > 0:
                    t.wait_ge(s_nr.h, s_nr.n)
                t.matmul(psN[0:1, :], ones_b[:, :], sqs[:, 0:512], start=True, stop=False)
                ins = t.matmul(psN[0:1, :], ones_b[:, :], sqs[:, 512:1024],
                               start=False, stop=True)
                s_nq.inc(ins)

            @env.block.vector
            def _(v, which=which, c=c):
                v.wait_ge(s_nq.h, s_nq.n)
                env.vfence(v)
                dst = qn_sb if which == 0 else an_sb
                ins = v.tensor_copy(dst[0:1, c * 512:(c + 1) * 512], psN[0:1, :])
                s_nr.inc(ins)

    @env.block.vector
    def _(v):
        v.wait_ge(s_nr.h, s_nr.n)
        env.vspace(v)
        v.reduce_max(an_sb[0:1, LK:LK + 1], an_sb[0:1, 0:LK], axis=AX.X)
        env.vspace(v)
        v.memset(negm16[0:16, :], 0.0)
        v.tensor_scalar(mrow[0:1, :], qn_sb[0:1, :], an_sb[0:1, LK:LK + 1], None, ALU.mult)
        s_nr.inc(env.vfence(v))
    NM1 = s_nr.n

    @env.block.scalar
    def _(sc):
        sc.wait_ge(s_nr.h, NM1)
        env.sfence(sc)
        sc.activation(mrow[0:1, :], mrow[0:1, :], AF.Ln)
        sc.activation(mrow[0:1, :], mrow[0:1, :], AF.Exp, scale=0.5)
        s_nr.inc(env.sfence(sc))
    NM2 = s_nr.n

    @env.block.vector
    def _(v):
        v.wait_ge(s_nr.h, NM2)
        env.vfence(v)
        v.tensor_scalar(negm16[0:1, :], mrow[0:1, :], -1.0, None, ALU.mult)
        s_nr.inc(env.vfence(v))
    NM3 = s_nr.n

    @env.block.sync
    def _(s):
        s.wait_ge(s_nr.h, NM3)
        s_pt.inc(s.dma_start(out=nmt16.ap().rearrange("p (t r) -> p t r", r=16),
                             in_=negm16[:, :], transpose=True), 16)
    NPT0 = s_pt.n

    @env.block.scalar
    def _(sc):
        sc.wait_ge(s_pt.h, NPT0)
        ins = sc.copy(nmtf[:, :], nmt16.ap().rearrange("p (t r) -> p t r", r=16)[:, :, 0])
        s_px.inc(ins)
    PX0 = s_px.n

    SCN = []
    PXN = []
    DNN = []
    PTN = []
    for qt in range(NQT):
        nc_q = nco(qt)

        @env.block.gpsimd
        def _(g, qt=qt, nc_q=nc_q):
            g.wait_ge(s_pt.h, s_pt.n)
            if qt >= 1:
                g.wait_ge(s_sc.h, SCN[qt - 1])
            for kc in range(nc_q):
                srcap = AP(brevD.tensor, offset=1920 - qt * 128 + kc * 512,
                           ap=[[1, 128], [1, 512]])
                s_mk.inc(g.dma_start(
                    out=mskb[:, kc * 512:(kc + 1) * 512], in_=srcap), 16)
        MKN = s_mk.n

        for kc in range(nc_q):
            @env.block.tensor
            def _(t, qt=qt, kc=kc, MKN=MKN):
                t.wait_ge(s_mk.h, MKN)
                if kc == 0:
                    t.wait_ge(s_px.h, PX0)
                    if qt > 0:
                        t.wait_ge(s_pt.h, PTN[qt - 1])
                if s_px.n >= 2:
                    t.wait_ge(s_px.h, s_px.n - 1)
                t.matmul(psS[kc % 2][:, :], st_sb[:, 0 * 2048 + qt * 128:0 * 2048 + (qt + 1) * 128],
                         act_sb[:, 0 * LK + kc * 512:0 * LK + (kc + 1) * 512],
                         start=True, stop=False)
                t.matmul(psS[kc % 2][:, :], st_sb[:, 1 * 2048 + qt * 128:1 * 2048 + (qt + 1) * 128],
                         act_sb[:, 1 * LK + kc * 512:1 * LK + (kc + 1) * 512],
                         start=False, stop=False)
                ins = t.matmul(psS[kc % 2][:, :], jm_sb[:, :],
                               mskb[:, kc * 512:(kc + 1) * 512],
                               start=False, stop=True)
                s_sc.inc(ins)

            @env.block.scalar
            def _(sc, qt=qt, kc=kc):
                sc.wait_ge(s_sc.h, s_sc.n)
                if kc == 0 and qt > 0:
                    sc.wait_ge(s_dn.h, DNN[qt - 1])
                ins = sc.activation(p_sb[:, kc * 512:(kc + 1) * 512], psS[kc % 2][:, :],
                                    AF.Exp, bias=nmtf[:, qt:qt + 1],
                                    accum_out=den_sb[:, kc:kc + 1])
                s_px.inc(ins)
        SCN.append(s_sc.n)
        PXN.append(s_px.n)

        @env.block.vector
        def _(v, qt=qt, nc_q=nc_q):
            v.wait_ge(s_px.h, PXN[qt])
            env.vfence(v)
            v.reduce_sum(dcol[:, 0:1], den_sb[:, 0:nc_q], axis=AX.X)
            env.vspace(v)
            v.reciprocal(dcol[:, 0:1], dcol[:, 0:1])
            env.vfence(v)
            for kc in range(nc_q):
                v.tensor_scalar(p_sb[:, kc * 512:(kc + 1) * 512],
                                p_sb[:, kc * 512:(kc + 1) * 512],
                                dcol[:, 0:1], None, ALU.mult)
            s_dn.inc(env.vfence(v))
        DNN.append(s_dn.n)

        @env.block.sync
        def _(s, qt=qt, nc_q=nc_q):
            s.wait_ge(s_dn.h, DNN[qt])
            s_pt.inc(s.dma_start(
                out=pt_sb.ap()[:, 0:nc_q * 4 * 128].rearrange("p (t q) -> p t q", q=128),
                in_=p_sb[:, 0:nc_q * 512], transpose=True), 16)
            s_ps.inc(s.dma_start(
                out=ptd[qt].rearrange("t p q -> p t q")[:, 0:nc_q * 4, :],
                in_=pt_sb.ap()[:, 0:nc_q * 4 * 128].rearrange("p (t q) -> p t q", q=128)), 16)
        PTN.append(s_pt.n)
    NPS = s_ps.n

    OMN = {}
    for chh in range(2):
        @env.block.sync
        def _(s, chh=chh):
            if chh > 0:
                s.wait_ge(s_om.h, OMN[(0, NQT - 1, 1)])
            else:
                s.wait_ge(s_ps.h, NPS)
            for kt in range(32):
                s_vl.inc(s.dma_start(out=vt_sb[:, kt * 1024:(kt + 1) * 1024],
                                     in_=xvv[kt][:, chh * 1024:(chh + 1) * 1024]), 16)
        VLN = s_vl.n

        for qt in range(NQT):
            nc_q = nco(qt)

            @env.block.sync
            def _(s, chh=chh, qt=qt, nc_q=nc_q):
                if qt >= 2:
                    s.wait_ge(s_om.h, OMN[(chh, qt - 2, 1)])
                s_pl.inc(s.dma_start(
                    out=ptb.ap()[:, (qt % 2) * 4096:(qt % 2) * 4096 + nc_q * 4 * 128]
                        .rearrange("p (t q) -> p t q", q=128),
                    in_=ptd[qt].rearrange("t p q -> p t q")[:, 0:nc_q * 4, :]), 16)

            for c5 in range(2):
                @env.block.tensor
                def _(t, chh=chh, qt=qt, c5=c5, nc_q=nc_q, VLN=VLN):
                    t.wait_ge(s_vl.h, VLN)
                    t.wait_ge(s_pl.h, s_pl.n)
                    if s_oe.n >= 2:
                        t.wait_ge(s_oe.h, s_oe.n - 1)
                    for kt in range(nc_q * 4):
                        ins = t.matmul(psO[c5][:, :],
                                       ptb[:, (qt % 2) * 4096 + kt * 128:(qt % 2) * 4096 + (kt + 1) * 128],
                                       vt_sb[:, kt * 1024 + c5 * 512:kt * 1024 + (c5 + 1) * 512],
                                       start=(kt == 0), stop=(kt == nc_q * 4 - 1))
                    s_om.inc(ins)
                OMN[(chh, qt, c5)] = s_om.n

                @env.block.scalar
                def _(sc, c5=c5):
                    sc.wait_ge(s_om.h, s_om.n)
                    if s_os.n >= 32:
                        sc.wait_ge(s_os.h, s_os.n - 16)
                    ins = sc.copy(ot_sb[:, c5 * 512:(c5 + 1) * 512], psO[c5][:, :])
                    s_oe.inc(ins)

                @env.block.sync
                def _(s, chh=chh, qt=qt, c5=c5):
                    s.wait_ge(s_oe.h, s_oe.n)
                    s_os.inc(s.dma_start(
                        out=otd[qt * 128:(qt + 1) * 128,
                                chh * 1024 + c5 * 512:chh * 1024 + (c5 + 1) * 512],
                        in_=ot_sb[:, c5 * 512:(c5 + 1) * 512]), 16)
            OMN[(chh, qt, 1)] = s_om.n

    @env.block.sync
    def _(s):
        s.wait_ge(s_os.h, s_os.n)

    return env.close()


def host_prep_D(xn2_full):
    maps = []
    u = np.arange(6144)
    for c in range(NCORE):
        b, qh = c // 2, c % 2
        dd = (2047 - u) + qh * 2048
        brev = np.where(dd >= 0, 0.0, -1e30).astype(np.float32)[None, :]
        xa = np.ascontiguousarray(xn2_full[b].T)
        xq = np.ascontiguousarray(xa[:, qh * 2048:(qh + 1) * 2048])
        xv = np.ascontiguousarray(xn2_full[b])
        maps.append(dict(xq=xq, xa=xa, xv=xv, brevD=brev,
                         jmm=np.eye(128, dtype=np.float32)[::-1].copy()))
    return maps



def build_launch_E1(T_=TOWN):
    env = Env("E1")
    nc = env.nc
    CW = 512
    NCH = T_ // CW

    oat = nc.dram_tensor("oat", [D, T_], BF16, kind="ExternalInput").ap()
    x0c = nc.dram_tensor("x0c", [D, T_], F32, kind="ExternalInput").ap()
    woT = nc.dram_tensor("woT", [D, D], F32, kind="ExternalInput").ap()
    bob = nc.dram_tensor("bob", [128, KT], F32, kind="ExternalInput").ap()
    x1o = nc.dram_tensor("x1o", [D, T_], F32, kind="ExternalOutput").ap()
    xno = nc.dram_tensor("xno", [D, T_], BF16, kind="ExternalOutput").ap()
    oat_t = dram_tiles(oat)
    x0_t = dram_tiles(x0c)
    wo_t = dram_tiles(woT)
    x1_t = dram_tiles(x1o)
    xn_t = dram_tiles(xno)

    wo_sb = env.sbuf("wo_sb", [128, KT * D], BF16)
    oat_sb = env.sbuf("oat_sb", [128, KT * CW], BF16)
    x0p = env.sbuf("x0p", [128, 4 * CW], F32)
    x1p = env.sbuf("x1p", [128, 4 * CW], F32)
    xb = env.sbuf("xb", [128, KT * CW], BF16)
    sqb = env.sbuf("sqb", [128, KT * CW], BF16)
    row = env.sbuf("row", [1, 6 * CW], F32)
    rB = env.sbuf("rB", [128, CW], BF16)
    mB = env.sbuf("mB", [128, CW], BF16)
    tmpb = env.sbuf("tmpb", [128, CW], BF16)
    xnb = env.sbuf("xnb", [128, 2 * CW], BF16)
    bos = env.sbuf("bos", [128, KT], F32)
    psm = [env.psum("e1_ps0", [128, CW]), env.psum("e1_ps1", [128, CW])]
    psA = env.psum("e1_psA", [1, CW])
    psB = env.psum("e1_psB", [1, CW])
    psR = env.psum("e1_psR", [128, CW])
    psM = env.psum("e1_psM", [128, CW])
    ones_b = env.sbuf("ones_b", [128, 1], BF16)
    ones_f = env.sbuf("ones_f", [1, 128], F32)
    env.set_junk(env.sbuf("junk", [1, 1024], F32))

    s_w = env.sem("g_w")
    s_ws = env.sem("g_ws")
    s_oa = env.sem("g_oa")
    s_x0 = env.sem("g_x0")
    s_mm = env.sem("g_mm")
    s_ev = env.sem("g_ev")
    s_sx = env.sem("g_sx")
    s_sq = env.sem("g_sq")
    s_st = env.sem("g_st")
    s_rw = env.sem("g_rw")
    s_e2 = env.sem("g_e2")
    s_nm = env.sem("g_nm")
    s_sn = env.sem("g_sn")
    ini = env.sem("ini")

    def evn(t):
        return t + 1 + t // KT

    env.open_block()
    emit_consts(env, ones_b, ones_f)

    @env.block.gpsimd
    def _(g):
        for et in range(KT):
            s_w.inc(g.dma_start(out=wo_sb[:, et * D:(et + 1) * D], in_=wo_t[et][:, :]), 16)
    NW = s_w.n

    @env.block.sync
    def _(s):
        s_ws.inc(s.dma_start(out=bos[:, :], in_=bob[:, :]), 16)
    NWS = s_ws.n

    for ci in range(NCH):
        c0, c1 = ci * CW, (ci + 1) * CW

        @env.block.gpsimd
        def _(g, ci=ci, c0=c0, c1=c1):
            if ci > 0:
                g.wait_ge(s_mm.h, 16 * ci)
            for et in range(KT):
                s_oa.inc(g.dma_start(out=oat_sb[:, et * CW:(et + 1) * CW],
                                     in_=oat_t[et][:, c0:c1]), 16)
            for dt in range(KT):
                gidx = ci * KT + dt
                if gidx >= 4:
                    g.wait_ge(s_ev.h, evn(gidx - 4))
                s_x0.inc(g.dma_start(out=x0p[:, (gidx % 4) * CW:(gidx % 4 + 1) * CW],
                                     in_=x0_t[dt][:, c0:c1]), 16)

        for dt in range(KT):
            gidx = ci * KT + dt

            @env.block.tensor
            def _(t, ci=ci, dt=dt, gidx=gidx):
                if gidx == 0:
                    t.wait_ge(s_w.h, NW)
                t.wait_ge(s_oa.h, 16 * KT * (ci + 1))
                if gidx >= 2:
                    t.wait_ge(s_ev.h, evn(gidx - 2))
                for et in range(KT):
                    ins = t.matmul(psm[gidx % 2][:, :],
                                   wo_sb[:, et * D + dt * 128:et * D + (dt + 1) * 128],
                                   oat_sb[:, et * CW:(et + 1) * CW],
                                   start=(et == 0), stop=(et == KT - 1))
                s_mm.inc(ins)

            @env.block.vector
            def _(v, ci=ci, dt=dt, gidx=gidx):
                v.wait_ge(s_mm.h, gidx + 1)
                v.wait_ge(s_x0.h, 16 * (gidx + 1))
                if gidx == 0:
                    v.wait_ge(s_ws.h, NWS)
                if gidx >= 4:
                    v.wait_ge(s_sx.h, 16 * (gidx - 3))
                if dt == 0 and ci > 0:
                    v.wait_ge(s_sq.h, ci)
                env.vfence(v)
                sl = x1p[:, (gidx % 4) * CW:(gidx % 4 + 1) * CW]
                v.tensor_add(sl, psm[gidx % 2][:, :], x0p[:, (gidx % 4) * CW:(gidx % 4 + 1) * CW])
                ins = v.tensor_scalar(sl, sl, bos[:, dt:dt + 1], None, ALU.add)
                s_ev.inc(ins)

            @env.block.sync
            def _(s, ci=ci, dt=dt, gidx=gidx, c0=c0, c1=c1):
                s.wait_ge(s_ev.h, min(evn(gidx + 1), 17 * (ci + 1)))
                s_sx.inc(s.dma_start(out=x1_t[dt][:, c0:c1],
                                     in_=x1p[:, (gidx % 4) * CW:(gidx % 4 + 1) * CW]), 16)

        @env.block.vector
        def _(v):
            s_ev.inc(env.vfence(v))

        @env.block.scalar
        def _(sc, ci=ci):
            for dt in range(KT):
                gidx = ci * KT + dt
                sc.wait_ge(s_ev.h, min(evn(gidx + 1), 17 * (ci + 1)))
                sc.copy(xb[:, dt * CW:(dt + 1) * CW],
                        x1p[:, (gidx % 4) * CW:(gidx % 4 + 1) * CW])
                sc.activation(sqb[:, dt * CW:(dt + 1) * CW], xb[:, dt * CW:(dt + 1) * CW],
                              AF.Square)
            s_sq.inc(env.sfence(sc))

        @env.block.tensor
        def _(t, ci=ci):
            t.wait_ge(ini.h, 2)
            t.wait_ge(s_sq.h, ci + 1)
            if ci >= 1:
                t.wait_ge(s_rw.h, 3 * (ci - 1) + 1)
            for k in range(KT):
                t.matmul(psA[0:1, :], ones_b[:, :], xb[:, k * CW:(k + 1) * CW],
                         start=(k == 0), stop=(k == KT - 1))
            for k in range(KT):
                ins = t.matmul(psB[0:1, :], ones_b[:, :], sqb[:, k * CW:(k + 1) * CW],
                               start=(k == 0), stop=(k == KT - 1))
            s_st.inc(ins)

        @env.block.vector
        def _(v, ci=ci):
            v.wait_ge(s_st.h, 2 * ci + 1)
            env.vfence(v)
            v.tensor_scalar(row[0:1, 0:CW], psA[0:1, :], 1.0 / D, None, ALU.mult)
            v.tensor_scalar(row[0:1, CW:2 * CW], psB[0:1, :], 1.0 / D, None, ALU.mult)
            v.tensor_mul(row[0:1, 4 * CW:5 * CW], row[0:1, 0:CW], row[0:1, 0:CW])
            v.tensor_sub(row[0:1, 2 * CW:3 * CW], row[0:1, CW:2 * CW], row[0:1, 4 * CW:5 * CW])
            v.tensor_scalar(row[0:1, 2 * CW:3 * CW], row[0:1, 2 * CW:3 * CW], EPS, None, ALU.add)
            ins = v.reciprocal(row[0:1, 2 * CW:3 * CW], row[0:1, 2 * CW:3 * CW])
            s_rw.inc(ins)
            v.wait_ge(s_rw.h, 3 * ci + 2)
            v.tensor_mul(row[0:1, 4 * CW:5 * CW], row[0:1, 0:CW], row[0:1, 3 * CW:4 * CW])
            s_rw.inc(env.vfence(v))

        @env.block.scalar
        def _(sc, ci=ci):
            sc.wait_ge(s_rw.h, 3 * ci + 1)
            env.sfence(sc)
            sc.activation(row[0:1, 3 * CW:4 * CW], row[0:1, 2 * CW:3 * CW], AF.Sqrt)
            s_rw.inc(env.sfence(sc))

        @env.block.tensor
        def _(t, ci=ci):
            t.wait_ge(s_rw.h, 3 * ci + 3)
            if ci >= 1:
                t.wait_ge(s_e2.h, 2 * ci)
            t.matmul(psR[:, :], ones_f[:, :], row[0:1, 3 * CW:4 * CW], start=True, stop=True)
            ins = t.matmul(psM[:, :], ones_f[:, :], row[0:1, 4 * CW:5 * CW], start=True, stop=True)
            s_st.inc(ins)

        @env.block.vector
        def _(v, ci=ci):
            v.wait_ge(s_st.h, 2 * ci + 2)
            env.vfence(v)
            s_e2.inc(v.tensor_copy(rB[:, :], psR[:, :]))
            s_e2.inc(v.tensor_copy(mB[:, :], psM[:, :]))
            for k in range(KT):
                gk = ci * KT + k
                if gk >= 2:
                    v.wait_ge(s_sn.h, 16 * (gk - 1))
                v.tensor_mul(tmpb[:, :], xb[:, k * CW:(k + 1) * CW], rB[:, :])
                ins = v.tensor_sub(xnb[:, (gk % 2) * CW:(gk % 2 + 1) * CW],
                                   tmpb[:, :], mB[:, :])
                s_nm.inc(ins)
            s_nm.inc(env.vfence(v))

        @env.block.sync
        def _(s, ci=ci, c0=c0, c1=c1):
            for k in range(KT):
                gk = ci * KT + k
                s.wait_ge(s_nm.h, min(gk + 2 + gk // KT, 17 * (ci + 1)))
                s_sn.inc(s.dma_start(out=xn_t[k][:, c0:c1],
                                     in_=xnb[:, (gk % 2) * CW:(gk % 2 + 1) * CW]), 16)

    @env.block.sync
    def _(s):
        s.wait_ge(s_sx.h, s_sx.n)
        s.wait_ge(s_sn.h, s_sn.n)

    return env.close()


def host_prep_E1(inputs, oat1_full):
    Wo = inputs["L1_attn_Wo"]
    bo = inputs["L1_attn_bo"]
    woT = np.ascontiguousarray(Wo.T).astype(np.float32)
    bob = np.ascontiguousarray(bo.reshape(KT, 128).T).astype(np.float32)
    maps = []
    for c in range(NCORE):
        cols = np.concatenate([np.arange(b_ * S + SOWN * c, b_ * S + SOWN * (c + 1))
                               for b_ in range(B)])
        oatc = np.ascontiguousarray(oat1_full[:, cols])
        maps.append(dict(oat=oatc, woT=woT, bob=bob))
    return maps



def build_launch_E2a(T_=TOWN):
    env = Env("E2a")
    nc = env.nc
    CW = 512
    NCH = T_ // CW
    IN8 = 8192

    xn3 = nc.dram_tensor("xn3", [D, T_], BF16, kind="ExternalInput").ap()
    wupT = nc.dram_tensor("wupT", [D, IN8], F32, kind="ExternalInput").ap()
    wgtT = nc.dram_tensor("wgtT", [D, IN8], F32, kind="ExternalInput").ap()
    supm = nc.dram_tensor("supm", [128, KT], F32, kind="ExternalInput").ap()
    wsub = nc.dram_tensor("wsub", [128, 64], F32, kind="ExternalInput").ap()
    sudb = nc.dram_tensor("sudb", [128, 64], F32, kind="ExternalInput").ap()
    wsgb = nc.dram_tensor("wsgb", [128, 64], F32, kind="ExternalInput").ap()
    hmat = nc.dram_tensor("hmat", [128, 128], F32, kind="ExternalInput").ap()
    h2d = nc.dram_tensor("h2d", [IN8, T_], BF16, kind="ExternalOutput").ap()
    xn_t = dram_tiles(xn3)
    wup_t = dram_tiles(wupT)
    wgt_t = dram_tiles(wgtT)
    h2_t = h2d.rearrange("(o p) t -> o p t", p=128)

    AR1 = env.sbuf("AR1", [128, 32768], BF16)
    ARf = AR1.ap().bitcast(F32)
    yb = env.sbuf("yb", [128, KT * T_], BF16)
    xch = env.sbuf("xch", [128, KT * CW], BF16)
    wub = env.sbuf("wub", [128, 2 * KT * 128], BF16)
    wgf = env.sbuf("wgf", [128, 2 * KT * 128], F32)
    wgb = env.sbuf("wgb", [128, 2 * KT * 128], BF16)
    sgp = env.sbuf("sgp", [128, 2 * CW], BF16)
    ftmp = env.sbuf("ftmp", [128, CW], F32)
    h2blk = env.sbuf("h2blk", [128, 2 * T_], BF16)
    sup = env.sbuf("sup", [128, KT], F32)
    wsu_s = env.sbuf("wsu_s", [128, 64], F32)
    sud_s = env.sbuf("sud_s", [128, 64], F32)
    wsg_s = env.sbuf("wsg_s", [128, 64], F32)
    hm_sb = env.sbuf("hm_sb", [128, 128], BF16)
    psH = [env.psum("ea_h0", [128, CW]), env.psum("ea_h1", [128, CW])]
    psU = [env.psum("ea_u0", [128, CW]), env.psum("ea_u1", [128, CW])]
    psG = [env.psum("ea_g0", [128, CW]), env.psum("ea_g1", [128, CW])]
    env.set_junk(env.sbuf("junk", [1, 1024], F32))

    s_w = env.sem("h_w")
    s_iv = env.sem("h_iv")
    s_xc = env.sem("h_xc")
    s_su = env.sem("h_su")
    s_hm = env.sem("h_hm")
    s_he = env.sem("h_he")
    s_yc = env.sem("h_yc")
    s_wu = env.sem("h_wu")
    s_tq = env.sem("h_tq")
    s_gc = env.sem("h_gc")
    s_um = env.sem("h_um")
    s_sg = env.sem("h_sg")
    s_hv = env.sem("h_hv")
    s_hs = env.sem("h_hs")

    env.open_block()

    @env.block.gpsimd
    def _(g):
        s_w.inc(g.dma_start(out=hm_sb[:, :], in_=hmat[:, :]), 16)
        s_w.inc(g.dma_start(out=sup[:, :], in_=supm[:, :]), 16)
        s_w.inc(g.dma_start(out=wsu_s[:, :], in_=wsub[:, :]), 16)
        s_w.inc(g.dma_start(out=sud_s[:, :], in_=sudb[:, :]), 16)
        s_w.inc(g.dma_start(out=wsg_s[:, :], in_=wsgb[:, :]), 16)
    NW = s_w.n

    @env.block.vector
    def _(v):
        v.wait_ge(s_w.h, NW)
        env.vfence(v)
        v.tensor_mul(wsu_s[:, :], wsu_s[:, :], sud_s[:, :])
        env.vspace(v)
        v.tensor_scalar(wsu_s[:, :], wsu_s[:, :], 1.0 / np.sqrt(2048.0), None, ALU.mult)
        v.tensor_scalar(wsg_s[:, :], wsg_s[:, :], 1.0 / np.sqrt(2048.0), None, ALU.mult)
        s_iv.inc(env.vfence(v))

    for c in range(NCH):
        @env.block.sync
        def _(s, c=c):
            if c > 0:
                s.wait_ge(s_hm.h, 16 * c)
            for k in range(KT):
                s_xc.inc(s.dma_start(out=xch[:, k * CW:(k + 1) * CW],
                                     in_=xn_t[k][:, c * CW:(c + 1) * CW]), 16)

        @env.block.vector
        def _(v, c=c):
            v.wait_ge(s_xc.h, 16 * 16 * (c + 1))
            if c == 0:
                v.wait_ge(s_iv.h, 1)
            env.vfence(v)
            for k in range(KT):
                v.tensor_scalar(xch[:, k * CW:(k + 1) * CW], xch[:, k * CW:(k + 1) * CW],
                                sup[:, k:k + 1], None, ALU.mult)
            s_su.inc(env.vfence(v))

        @env.block.tensor
        def _(t, c=c):
            t.wait_ge(s_su.h, c + 1)
            for k in range(KT):
                if 16 * c + k >= 2:
                    t.wait_ge(s_he.h, 16 * c + k - 1)
                ins = t.matmul(psH[k % 2][:, :], hm_sb[:, :], xch[:, k * CW:(k + 1) * CW],
                               start=True, stop=True)
                s_hm.inc(ins)

        @env.block.scalar
        def _(sc, c=c):
            for k in range(KT):
                sc.wait_ge(s_hm.h, 16 * c + k + 1)
                ins = sc.copy(ARf[:, k * CW:(k + 1) * CW], psH[k % 2][:, :])
                s_he.inc(ins)

        @env.block.vector
        def _(v, c=c):
            v.wait_ge(s_he.h, 16 * (c + 1))
            env.vfence(v)
            A = [ARf[:, k * CW:(k + 1) * CW] for k in range(KT)]
            Bv = [ARf[:, (16 + k) * CW:(17 + k) * CW] for k in range(KT)]
            for li, s in enumerate((1, 2, 4, 8)):
                src, dst = (A, Bv) if li % 2 == 0 else (Bv, A)
                if li == 3:
                    dst = [yb.ap()[:, k * T_ + c * CW:k * T_ + (c + 1) * CW]
                           for k in range(KT)]
                for t in range(KT):
                    if t & s == 0:
                        v.tensor_add(dst[t], src[t], src[t + s])
                    else:
                        v.tensor_sub(dst[t], src[t - s], src[t])
            s_yc.inc(env.vfence(v))

    for ot in range(64):
        @env.block.gpsimd
        def _(g, ot=ot):
            if ot >= 2:
                g.wait_ge(s_um.h, 4 * (ot - 1))
            for et in range(KT):
                s_wu.inc(g.dma_start(
                    out=wub[:, (ot % 2) * 2048 + et * 128:(ot % 2) * 2048 + (et + 1) * 128],
                    in_=wup_t[et][:, ot * 128:(ot + 1) * 128]), 16)

        @env.block.scalar
        def _(sc, ot=ot):
            def issue_loads(o2):
                for et in range(KT):
                    s_tq.inc(sc.dma_start(
                        out=wgf[:, (o2 % 2) * 2048 + et * 128:(o2 % 2) * 2048 + (et + 1) * 128],
                        in_=wgt_t[et][:, o2 * 128:(o2 + 1) * 128]), 16)
            if ot == 0:
                issue_loads(0)
                issue_loads(1)
            sc.wait_ge(s_tq.h, 16 * 16 * (ot + 1))
            if ot >= 2:
                sc.wait_ge(s_um.h, 4 * (ot - 1))
            for et in range(KT):
                sl = slice((ot % 2) * 2048 + et * 128, (ot % 2) * 2048 + (et + 1) * 128)
                sc.copy(wgb[:, sl], wgf[:, sl])
            s_gc.inc(env.sfence(sc))
            if ot + 2 < 64:
                issue_loads(ot + 2)

        for c in range(NCH):
            uc = ot * 4 + c

            @env.block.tensor
            def _(t, ot=ot, c=c, uc=uc):
                if uc == 0:
                    t.wait_ge(s_yc.h, NCH)
                t.wait_ge(s_wu.h, 16 * 16 * (ot + 1))
                t.wait_ge(s_gc.h, ot + 1)
                if uc >= 2:
                    t.wait_ge(s_hv.h, uc - 1)
                for et in range(KT):
                    t.matmul(psU[uc % 2][:, :],
                             wub[:, (ot % 2) * 2048 + et * 128:(ot % 2) * 2048 + (et + 1) * 128],
                             yb[:, et * T_ + c * CW:et * T_ + (c + 1) * CW],
                             start=(et == 0), stop=(et == KT - 1))
                for et in range(KT):
                    ins = t.matmul(psG[uc % 2][:, :],
                                   wgb[:, (ot % 2) * 2048 + et * 128:(ot % 2) * 2048 + (et + 1) * 128],
                                   yb[:, et * T_ + c * CW:et * T_ + (c + 1) * CW],
                                   start=(et == 0), stop=(et == KT - 1))
                s_um.inc(ins)

            @env.block.scalar
            def _(sc, ot=ot, uc=uc):
                sc.wait_ge(s_um.h, uc + 1)
                if uc >= 2:
                    sc.wait_ge(s_hv.h, uc - 1)
                ins = sc.activation(sgp[:, (uc % 2) * CW:(uc % 2 + 1) * CW],
                                    psG[uc % 2][:, :], AF.Silu,
                                    scale=wsg_s[:, ot:ot + 1])
                s_sg.inc(ins)

            @env.block.vector
            def _(v, ot=ot, c=c, uc=uc):
                v.wait_ge(s_sg.h, uc + 1)
                if c == 0 and ot >= 2:
                    v.wait_ge(s_hs.h, 16 * (ot - 1))
                env.vfence(v)
                v.tensor_scalar(ftmp[:, :], psU[uc % 2][:, :], wsu_s[:, ot:ot + 1],
                                None, ALU.mult)
                ins = v.tensor_mul(h2blk[:, (ot % 2) * T_ + c * CW:
                                         (ot % 2) * T_ + (c + 1) * CW],
                                   ftmp[:, :], sgp[:, (uc % 2) * CW:(uc % 2 + 1) * CW])
                if c == NCH - 1:
                    ins = env.vfence(v)
                s_hv.inc(ins)

        @env.block.sync
        def _(s, ot=ot):
            s.wait_ge(s_hv.h, 4 * (ot + 1))
            s_hs.inc(s.dma_start(out=h2_t[ot],
                                 in_=h2blk[:, (ot % 2) * T_:(ot % 2 + 1) * T_]), 16)

    @env.block.sync
    def _(s):
        s.wait_ge(s_hs.h, s_hs.n)

    return env.close()



def build_launch_E2b(T_=TOWN):
    env = Env("E2b")
    nc = env.nc
    CW = 1024
    NCH = T_ // CW
    IN8 = 8192
    NOT = 64

    h2d = nc.dram_tensor("h2d", [IN8, T_], BF16, kind="ExternalInput").ap()
    x3c = nc.dram_tensor("x3c", [D, T_], F32, kind="ExternalInput").ap()
    wdnT = nc.dram_tensor("wdnT", [IN8, D], F32, kind="ExternalInput").ap()
    wsdb = nc.dram_tensor("wsdb", [128, KT], F32, kind="ExternalInput").ap()
    hmat = nc.dram_tensor("hmat", [128, 128], F32, kind="ExternalInput").ap()
    x4o = nc.dram_tensor("x4o", [D, T_], F32, kind="ExternalOutput").ap()
    h2_t = h2d.rearrange("(o p) t -> o p t", p=128)
    wdn_t = wdnT.rearrange("(o p) t -> o p t", p=128)
    x3_t = dram_tiles(x3c)
    x4_t = dram_tiles(x4o)

    Z = env.sbuf("Z", [128, NOT * CW], BF16)
    tb = env.sbuf("tb", [128, CW], BF16)
    wdb = env.sbuf("wdb", [128, 2 * NOT * 128], BF16)
    x3p = env.sbuf("x3p", [128, 4 * 512], F32)
    x4p = env.sbuf("x4p", [128, 4 * 512], F32)
    ftmp = env.sbuf("ftmp", [128, 512], F32)
    wsd_s = env.sbuf("wsd_s", [128, KT], F32)
    hm_sb = env.sbuf("hm_sb", [128, 128], BF16)
    psH = [env.psum("eb_h0", [128, 512]), env.psum("eb_h1", [128, 512])]
    psD = [env.psum("eb_d0", [128, 512]), env.psum("eb_d1", [128, 512])]
    env.set_junk(env.sbuf("junk", [1, 1024], F32))

    s_w = env.sem("i_w")
    s_iv = env.sem("i_iv")
    s_zl = env.sem("i_zl")
    s_bf = env.sem("i_bf")
    s_hm = env.sem("i_hm")
    s_he = env.sem("i_he")
    s_wd = env.sem("i_wd")
    s_dm = env.sem("i_dm")
    s_de = env.sem("i_de")
    s_xl = env.sem("i_xl")
    s_xs = env.sem("i_xs")

    env.open_block()

    @env.block.gpsimd
    def _(g):
        s_w.inc(g.dma_start(out=hm_sb[:, :], in_=hmat[:, :]), 16)
        s_w.inc(g.dma_start(out=wsd_s[:, :], in_=wsdb[:, :]), 16)
    NW = s_w.n

    @env.block.vector
    def _(v):
        v.wait_ge(s_w.h, NW)
        env.vfence(v)
        v.tensor_scalar(wsd_s[:, :], wsd_s[:, :], 1.0 / np.sqrt(8192.0), None, ALU.mult)
        s_iv.inc(env.vfence(v))

    for cc in range(NCH):
        DM0 = s_dm.n

        @env.block.sync
        def _(s, cc=cc, DM0=DM0):
            if cc > 0:
                s.wait_ge(s_dm.h, DM0)
            for t in range(NOT):
                s_zl.inc(s.dma_start(out=Z[:, t * CW:(t + 1) * CW],
                                     in_=h2_t[t][:, cc * CW:(cc + 1) * CW]), 16)

        @env.block.vector
        def _(v, cc=cc):
            v.wait_ge(s_zl.h, 16 * NOT * (cc + 1))
            env.vfence(v)
            for s in (1, 2, 4, 8, 16, 32):
                for t in range(NOT):
                    if t & s == 0:
                        u = t + s
                        v.tensor_sub(tb[:, :], Z[:, t * CW:(t + 1) * CW],
                                     Z[:, u * CW:(u + 1) * CW])
                        v.tensor_add(Z[:, t * CW:(t + 1) * CW],
                                     Z[:, t * CW:(t + 1) * CW], Z[:, u * CW:(u + 1) * CW])
                        v.tensor_copy(Z[:, u * CW:(u + 1) * CW], tb[:, :])
            s_bf.inc(env.vfence(v))

        @env.block.tensor
        def _(t, cc=cc):
            t.wait_ge(s_bf.h, cc + 1)
            if cc == 0:
                t.wait_ge(s_iv.h, 1)
            for i in range(2 * NOT):
                if 128 * cc + i >= 2:
                    t.wait_ge(s_he.h, 129 * cc + i - 1)
                tt_ = i // 2
                th = i % 2
                ins = t.matmul(psH[i % 2][:, :], hm_sb[:, :],
                               Z[:, tt_ * CW + th * 512:tt_ * CW + (th + 1) * 512],
                               start=True, stop=True)
                s_hm.inc(ins)

        @env.block.scalar
        def _(sc, cc=cc):
            for i in range(2 * NOT):
                sc.wait_ge(s_hm.h, 128 * cc + i + 1)
                tt_ = i // 2
                th = i % 2
                ins = sc.copy(Z[:, tt_ * CW + th * 512:tt_ * CW + (th + 1) * 512],
                              psH[i % 2][:, :])
                s_he.inc(ins)
            s_he.inc(env.sfence(sc))

        for dt in range(KT):
            gd = cc * KT + dt

            @env.block.gpsimd
            def _(g, cc=cc, dt=dt, gd=gd):
                if gd >= 2:
                    g.wait_ge(s_dm.h, 2 * (gd - 1))
                for it in range(NOT):
                    s_wd.inc(g.dma_start(
                        out=wdb[:, (gd % 2) * 8192 + it * 128:(gd % 2) * 8192 + (it + 1) * 128],
                        in_=wdn_t[it][:, dt * 128:(dt + 1) * 128]), 16)

            @env.block.scalar
            def _(sc, cc=cc, dt=dt, gd=gd):
                if gd >= 2:
                    sc.wait_ge(s_de.h, 3 * (gd - 1))
                for th in range(2):
                    s_xl.inc(sc.dma_start(
                        out=x3p[:, ((2 * gd + th) % 4) * 512:((2 * gd + th) % 4 + 1) * 512],
                        in_=x3_t[dt][:, cc * CW + th * 512:cc * CW + (th + 1) * 512]), 16)

            @env.block.tensor
            def _(t, cc=cc, dt=dt, gd=gd):
                t.wait_ge(s_he.h, 129 * (cc + 1))
                t.wait_ge(s_wd.h, 16 * NOT * (gd + 1))
                for th in range(2):
                    if gd >= 1:
                        t.wait_ge(s_de.h, 3 * (gd - 1) + th + 1)
                    for it in range(NOT):
                        ins = t.matmul(psD[th][:, :],
                                       wdb[:, (gd % 2) * 8192 + it * 128:(gd % 2) * 8192 + (it + 1) * 128],
                                       Z[:, it * CW + th * 512:it * CW + (th + 1) * 512],
                                       start=(it == 0), stop=(it == NOT - 1))
                    s_dm.inc(ins)

            @env.block.vector
            def _(v, cc=cc, dt=dt, gd=gd):
                v.wait_ge(s_dm.h, 2 * gd + 1)
                v.wait_ge(s_xl.h, 32 * (gd + 1))
                if gd >= 2:
                    v.wait_ge(s_xs.h, 32 * (gd - 1))
                env.vfence(v)
                for th in range(2):
                    if th == 1:
                        v.wait_ge(s_dm.h, 2 * gd + 2)
                    v.tensor_scalar(ftmp[:, :], psD[th][:, :], wsd_s[:, dt:dt + 1],
                                    None, ALU.mult)
                    ins = v.tensor_add(x4p[:, ((2 * gd + th) % 4) * 512:((2 * gd + th) % 4 + 1) * 512],
                                       ftmp[:, :], x3p[:, ((2 * gd + th) % 4) * 512:((2 * gd + th) % 4 + 1) * 512])
                    s_de.inc(ins)
                s_de.inc(env.vfence(v))

            @env.block.sync
            def _(s, cc=cc, dt=dt, gd=gd):
                s.wait_ge(s_de.h, 3 * (gd + 1))
                for th in range(2):
                    s_xs.inc(s.dma_start(
                        out=x4_t[dt][:, cc * CW + th * 512:cc * CW + (th + 1) * 512],
                        in_=x4p[:, ((2 * gd + th) % 4) * 512:((2 * gd + th) % 4 + 1) * 512]), 16)

    @env.block.sync
    def _(s):
        s.wait_ge(s_xs.h, s_xs.n)

    return env.close()


def host_prep_E2(inputs):
    W_up, W_gate, W_down = inputs["L1_ffn_W_up"], inputs["L1_ffn_W_gate"], inputs["L1_ffn_W_down"]
    su_ug, su_dp = inputs["L1_ffn_SU_ug"], inputs["L1_ffn_SU_dp"]
    ws_ug, ws_dp = inputs["L1_ffn_Wscale_ug"], inputs["L1_ffn_Wscale_dp"]
    wupT = np.ascontiguousarray(W_up.T).astype(np.float32)
    wgtT = np.ascontiguousarray(W_gate.T).astype(np.float32)
    wdnT = np.ascontiguousarray(W_down.T).astype(np.float32)
    supm = np.ascontiguousarray(su_ug.reshape(KT, 128).T).astype(np.float32)
    wsub = np.ascontiguousarray(ws_ug[:8192].reshape(64, 128).T).astype(np.float32)
    sudb = np.ascontiguousarray(su_dp.reshape(64, 128).T).astype(np.float32)
    wsgb = np.ascontiguousarray(ws_ug[8192:].reshape(64, 128).T).astype(np.float32)
    wsdb = np.ascontiguousarray(ws_dp.reshape(KT, 128).T).astype(np.float32)
    i = np.arange(128)
    hmat = np.where((i[:, None] & i[None, :] & 0).astype(bool), 0, 0)
    pc = np.zeros((128, 128), dtype=np.int64)
    for a in range(128):
        for b_ in range(128):
            pc[a, b_] = bin(a & b_).count("1")
    hmat = np.where(pc % 2 == 0, 1.0, -1.0).astype(np.float32)
    mA = dict(wupT=wupT, wgtT=wgtT, supm=supm, wsub=wsub, sudb=sudb, wsgb=wsgb,
              hmat=hmat)
    mB = dict(wdnT=wdnT, wsdb=wsdb, hmat=hmat)
    return [dict(mA) for _ in range(NCORE)], [dict(mB) for _ in range(NCORE)]



def _shard_ch_major(x):
    out = []
    for c in range(NCORE):
        xs = x[:, SOWN * c:SOWN * (c + 1), :]
        out.append(np.ascontiguousarray(xs.transpose(2, 0, 1).reshape(D, TOWN)))
    return out


def kernel(**inputs):
    hs = np.asarray(inputs["L0_attn_hidden_states"], dtype=np.float32)

    ncA = build_launch_A()
    shards = _shard_ch_major(hs)
    rA = run_bass_kernel_spmd(ncA, [dict(x=s) for s in shards], core_ids=list(range(NCORE)))
    xn0_full = assemble_xn0([np.asarray(rA.results[c]["xn"]) for c in range(NCORE)])

    ncB = build_launch_B()
    mapsB = host_prep_B(inputs)
    for m in mapsB:
        m["xn0"] = xn0_full
    rB = run_bass_kernel_spmd(ncB, mapsB, core_ids=list(range(NCORE)))
    oat_full = np.zeros((D, B * S), dtype=BFD)
    for c in range(NCORE):
        oat_full[c * 256:(c + 1) * 256] = unpack_oslc(rB.results[c]["oslc"]).astype(BFD)

    ncC1 = build_launch_C1()
    mapsC1 = host_prep_C1(inputs, oat_full)
    rC1 = run_bass_kernel_spmd(ncC1, mapsC1, core_ids=list(range(NCORE)))
    x1s = [np.asarray(rC1.results[c]["x1o"]) for c in range(NCORE)]
    xcs = [np.asarray(rC1.results[c]["xco"]) for c in range(NCORE)]

    ncC2 = build_launch_C2()
    mapsC2 = host_prep_C2(inputs)
    for c in range(NCORE):
        mapsC2[c]["xco"] = xcs[c]
    rC2 = run_bass_kernel_spmd(ncC2, mapsC2, core_ids=list(range(NCORE)))

    ncC3 = build_launch_C3()
    mapsC3 = host_prep_C3(inputs)
    for c in range(NCORE):
        mapsC3[c]["xco"] = xcs[c]
        mapsC3[c]["onc"] = onrm_to_ch_major(np.asarray(rC2.results[c]["onrm"]))
        mapsC3[c]["x1o"] = x1s[c]
    rC3 = run_bass_kernel_spmd(ncC3, mapsC3, core_ids=list(range(NCORE)))
    x2s = [np.asarray(rC3.results[c]["x2o"]) for c in range(NCORE)]
    xn2s = [np.asarray(rC3.results[c]["xn2o"]) for c in range(NCORE)]

    xn2_full = np.zeros((B, S, D), dtype=BFD)
    for c in range(NCORE):
        blk = np.asarray(xn2s[c]).reshape(D, B, SOWN)
        for b_ in range(B):
            xn2_full[b_, SOWN * c:SOWN * (c + 1), :] = blk[:, b_, :].T
    ncD = build_launch_D()
    mapsD = host_prep_D(xn2_full)
    Ws = np.ascontiguousarray(inputs["L1_attn_Ws"]).astype(np.float32)
    Wa = np.ascontiguousarray(inputs["L1_attn_Wa"]).astype(np.float32)
    for m in mapsD:
        m["wsw"] = Ws
        m["waw"] = Wa
    rD = run_bass_kernel_spmd(ncD, mapsD, core_ids=list(range(NCORE)))

    oat1_full = np.zeros((D, B * S), dtype=BFD)
    for c in range(NCORE):
        b_, qh = c // 2, c % 2
        ot = np.asarray(rD.results[c]["otd"])
        oat1_full[:, b_ * S + qh * 2048:b_ * S + (qh + 1) * 2048] = ot.T
    ncE1 = build_launch_E1()
    mapsE1 = host_prep_E1(inputs, oat1_full)
    for c in range(NCORE):
        mapsE1[c]["x0c"] = x2s[c]
    rE1 = run_bass_kernel_spmd(ncE1, mapsE1, core_ids=list(range(NCORE)))
    x3s = [np.asarray(rE1.results[c]["x1o"]) for c in range(NCORE)]
    xn3s = [np.asarray(rE1.results[c]["xno"]) for c in range(NCORE)]

    mA, mB = host_prep_E2(inputs)
    ncE2a = build_launch_E2a()
    for c in range(NCORE):
        mA[c]["xn3"] = xn3s[c]
    rE2a = run_bass_kernel_spmd(ncE2a, mA, core_ids=list(range(NCORE)))
    ncE2b = build_launch_E2b()
    for c in range(NCORE):
        mB[c]["h2d"] = np.asarray(rE2a.results[c]["h2d"])
        mB[c]["x3c"] = x3s[c]
    rE2b = run_bass_kernel_spmd(ncE2b, mB, core_ids=list(range(NCORE)))

    out = np.zeros((B, S, D), dtype=np.float32)
    for c in range(NCORE):
        x4 = np.asarray(rE2b.results[c]["x4o"]).reshape(D, B, SOWN)
        for b_ in range(B):
            out[b_, SOWN * c:SOWN * (c + 1), :] = x4[:, b_, :].T
    return out


# version 04 (final)
# speedup vs baseline: 1.0799x; 1.0102x over previous
import numpy as np
import ml_dtypes
from contextlib import ExitStack

import concourse.bass as bass
import concourse.mybir as mybir
from concourse.ap import AP
from concourse.bass_utils import run_bass_kernel_spmd

F32 = mybir.dt.float32
BF16 = mybir.dt.bfloat16
AF = mybir.ActivationFunctionType
ALU = mybir.AluOpType
AX = mybir.AxisListType

NCORE = 8
B, S, D = 4, 4096, 2048
SOWN = S // NCORE
TOWN = B * SOWN
KT = D // 128
EPS = 1e-5

BFD = ml_dtypes.bfloat16


class Sem:

    def __init__(self, handle):
        self.h = handle
        self.n = 0

    def inc(self, ins, k=1):
        ins.then_inc(self.h, k)
        self.n += k
        return self.n


class Env:

    def __init__(self, name):
        self.nc = bass.Bass("TRN2", target_bir_lowering=False, detect_race_conditions=False)
        self.ctx = ExitStack()
        self.block = None
        self.sems = {}

    def sem(self, name):
        if name not in self.sems:
            self.sems[name] = Sem(self.ctx.enter_context(self.nc.semaphore(name)))
        return self.sems[name]

    def sbuf(self, name, shape, dtype):
        return self.ctx.enter_context(self.nc.sbuf_tensor(name, list(shape), dtype))

    def psum(self, name, shape, dtype=F32):
        return self.ctx.enter_context(self.nc.psum_tensor(name, list(shape), dtype))

    def open_block(self):
        self.block = self.ctx.enter_context(self.nc.Block())

    def set_fence(self, out_ap, lhs_ap, rhs_ap):
        self._f_out, self._f_lhs, self._f_rhs = out_ap, lhs_ap, rhs_ap

    def set_junk(self, junk):
        self._junk = junk

    FENCE_N = 4

    def vfence(self, v):
        v.memset(self._junk[0:1, 0:512], 0.0)
        for _ in range(self.FENCE_N):
            ins = v.tensor_copy(self._junk[0:1, 512:1024], self._junk[0:1, 0:512])
        return ins

    def sfence(self, sc):
        sc.memzero(self._junk[0:1, 0:512])
        for _ in range(self.FENCE_N):
            ins = sc.copy(self._junk[0:1, 512:1024], self._junk[0:1, 0:512])
        return ins

    def vspace(self, v):
        v.memset(self._junk[0:1, 0:64], 0.0)
        return v.tensor_copy(self._junk[0:1, 64:128], self._junk[0:1, 0:64])

    def gfence(self, g):
        g.memset(self._junk[0:1, 0:512], 0.0)
        for _ in range(self.FENCE_N):
            ins = g.tensor_copy(self._junk[0:1, 512:1024], self._junk[0:1, 0:512])
        return ins

    def close(self):
        self.ctx.close()
        return self.nc



def emit_consts(env, ones_b, ones_f):
    ini = env.sem("ini")

    @env.block.gpsimd
    def _(g):
        g.memset(ones_b[:, :], 1.0)
        g.memset(ones_f[:, :], 1.0)
        ini.inc(env.gfence(g))
        ini.inc(g.engine_nop())


def emit_ln(env, src_view, dst_view, bufs, T, CH=512, tag=""):
    nchunk = T // CH
    xb, sq = bufs["xb"], bufs["sq"]
    row, rB, mB, tmp = bufs["row"], bufs["rB"], bufs["mB"], bufs["tmp"]
    ones_b, ones_f = bufs["ones_b"], bufs["ones_f"]
    psA, psB, psR, psM = bufs["psA"], bufs["psB"], bufs["psR"], bufs["psM"]

    ldp = [env.sem("ln_ld0" + tag), env.sem("ln_ld1" + tag)]
    cv = env.sem("ln_cv" + tag)
    st1 = env.sem("ln_st1" + tag)
    rw = env.sem("ln_rw" + tag)
    ev = env.sem("ln_ev" + tag)
    nm = env.sem("ln_nm" + tag)
    sop = [env.sem("ln_so0" + tag), env.sem("ln_so1" + tag)]
    ini = env.sem("ini")

    ld0 = [ldp[0].n, ldp[1].n]
    so0 = [sop[0].n, sop[1].n]
    cv0, st10, rw0, ev0, nm0 = cv.n, st1.n, rw.n, ev.n, nm.n

    @env.block.gpsimd
    def _(g):
        for ci in range(nchunk):
            bsl = (ci % 2) * KT * CH
            if ci >= 2:
                g.wait_ge(nm.h, nm0 + KT * (ci - 1))
            for k in range(KT):
                ldp[ci % 2].inc(g.dma_start(out=xb[:, bsl + k * CH:bsl + (k + 1) * CH],
                                            in_=src_view[k][:, ci * CH:(ci + 1) * CH]), 16)

    @env.block.scalar
    def _(sc):
        for ci in range(nchunk):
            bsl = (ci % 2) * KT * CH
            sc.wait_ge(ldp[ci % 2].h, ld0[ci % 2] + 16 * KT * (ci // 2 + 1))
            if ci >= 1:
                sc.wait_ge(st1.h, st10 + 2 * (ci - 1) + 1)
            for k in range(KT):
                sc.activation(sq[:, k * CH:(k + 1) * CH],
                              xb[:, bsl + k * CH:bsl + (k + 1) * CH], AF.Square)
            cv.inc(env.sfence(sc))
            sc.wait_ge(rw.h, rw0 + 3 * ci + 1)
            env.sfence(sc)
            sc.activation(row[0:1, 3 * CH:4 * CH], row[0:1, 2 * CH:3 * CH], AF.Sqrt)
            rw.inc(env.sfence(sc))

    @env.block.vector
    def _(v):
        for ci in range(nchunk):
            bsl = (ci % 2) * KT * CH
            v.wait_ge(st1.h, st10 + 2 * ci + 1)
            env.vfence(v)
            v.tensor_scalar(row[0:1, 0:CH], psA[0:1, :], 1.0 / D, None, ALU.mult)
            v.tensor_scalar(row[0:1, CH:2 * CH], psB[0:1, :], 1.0 / D, None, ALU.mult)
            v.tensor_mul(row[0:1, 4 * CH:5 * CH], row[0:1, 0:CH], row[0:1, 0:CH])
            v.tensor_sub(row[0:1, 2 * CH:3 * CH], row[0:1, CH:2 * CH], row[0:1, 4 * CH:5 * CH])
            v.tensor_scalar(row[0:1, 2 * CH:3 * CH], row[0:1, 2 * CH:3 * CH], EPS, None, ALU.add)
            ins = v.reciprocal(row[0:1, 2 * CH:3 * CH], row[0:1, 2 * CH:3 * CH])
            rw.inc(ins)
            v.wait_ge(rw.h, rw0 + 3 * ci + 2)
            v.tensor_mul(row[0:1, 4 * CH:5 * CH], row[0:1, 0:CH], row[0:1, 3 * CH:4 * CH])
            rw.inc(env.vfence(v))
            v.wait_ge(st1.h, st10 + 2 * ci + 2)
            env.vfence(v)
            ins = v.tensor_copy(rB[:, :], psR[:, :])
            ev.inc(ins)
            ins = v.tensor_copy(mB[:, :], psM[:, :])
            ev.inc(ins)
            if ci >= 2:
                v.wait_ge(sop[ci % 2].h, so0[ci % 2] + 16 * KT * (ci // 2))
            for k in range(KT):
                v.tensor_mul(tmp[:, 0:CH], xb[:, bsl + k * CH:bsl + (k + 1) * CH], rB[:, :])
                ins = v.tensor_sub(bufs["xn"][:, bsl + k * CH:bsl + (k + 1) * CH],
                                   tmp[:, 0:CH], mB[:, :])
                nm.inc(ins)

    @env.block.tensor
    def _(t):
        t.wait_ge(ini.h, 2)
        for ci in range(nchunk):
            bsl = (ci % 2) * KT * CH
            t.wait_ge(ldp[ci % 2].h, ld0[ci % 2] + 16 * KT * (ci // 2 + 1))
            t.wait_ge(cv.h, cv0 + ci + 1)
            if ci >= 1:
                t.wait_ge(rw.h, rw0 + 3 * (ci - 1) + 1)
            for k in range(KT):
                t.matmul(psA[0:1, 0:CH], ones_b[:, :], xb[:, bsl + k * CH:bsl + (k + 1) * CH],
                         start=(k == 0), stop=(k == KT - 1))
            for k in range(KT):
                ins = t.matmul(psB[0:1, 0:CH], ones_b[:, :], sq[:, k * CH:(k + 1) * CH],
                               start=(k == 0), stop=(k == KT - 1))
            st1.inc(ins)
            t.wait_ge(rw.h, rw0 + 3 * ci + 3)
            if ci >= 1:
                t.wait_ge(ev.h, ev0 + 2 * ci)
            t.matmul(psR[:, 0:CH], ones_f[:, :], row[0:1, 3 * CH:4 * CH], start=True, stop=True)
            ins = t.matmul(psM[:, 0:CH], ones_f[:, :], row[0:1, 4 * CH:5 * CH], start=True, stop=True)
            st1.inc(ins)

    @env.block.sync
    def _(s):
        for ci in range(nchunk):
            bsl = (ci % 2) * KT * CH
            s.wait_ge(nm.h, nm0 + KT * (ci + 1))
            for k in range(KT):
                sop[ci % 2].inc(s.dma_start(out=dst_view[k][:, ci * CH:(ci + 1) * CH],
                                            in_=bufs["xn"][:, bsl + k * CH:bsl + (k + 1) * CH]), 16)


def ln_bufs(env, CH=512, tag=""):
    return dict(
        xb=env.sbuf("ln_xb" + tag, [128, 2 * KT * CH], BF16),
        sq=env.sbuf("ln_sq" + tag, [128, KT * CH], BF16),
        row=env.sbuf("ln_row" + tag, [1, 6 * CH], F32),
        rB=env.sbuf("ln_rB" + tag, [128, CH], BF16),
        mB=env.sbuf("ln_mB" + tag, [128, CH], BF16),
        tmp=env.sbuf("ln_tmp" + tag, [128, CH], BF16),
        xn=env.sbuf("ln_xn" + tag, [128, 2 * KT * CH], BF16),
        psA=env.psum("ln_psA" + tag, [1, CH]),
        psB=env.psum("ln_psB" + tag, [1, CH]),
        psR=env.psum("ln_psR" + tag, [128, CH]),
        psM=env.psum("ln_psM" + tag, [128, CH]),
    )


def dram_tiles(ap_2d):
    v = ap_2d.rearrange("(k p) t -> k p t", p=128)
    return [v[k] for k in range(KT)]



def build_launch_A():
    env = Env("A")
    nc = env.nc
    x = nc.dram_tensor("x", [D, TOWN], F32, kind="ExternalInput").ap()
    xn = nc.dram_tensor("xn", [D, TOWN], BF16, kind="ExternalOutput").ap()
    bufs = ln_bufs(env)
    bufs["ones_b"] = env.sbuf("ones_b", [128, 1], BF16)
    bufs["ones_f"] = env.sbuf("ones_f", [1, 128], F32)
    env.set_junk(env.sbuf("junk", [1, 1024], F32))
    env.open_block()
    emit_consts(env, bufs["ones_b"], bufs["ones_f"])
    emit_ln(env, dram_tiles(x), dram_tiles(xn), bufs, TOWN)
    so0, so1 = env.sem("ln_so0"), env.sem("ln_so1")

    @env.block.sync
    def _(s):
        s.wait_ge(so0.h, so0.n)
        s.wait_ge(so1.h, so1.n)

    return env.close()




def build_launch_B(S_=S, dbg=None, no_inject=False):
    env = Env("B")
    nc = env.nc
    NQT = S_ // 128
    NKC = S_ // 512
    NNQ = S_ // 512
    BT = B * S_
    NDSL = NKC + 4

    xn0 = nc.dram_tensor("xn0", [D, BT], BF16, kind="ExternalInput").ap()
    wq = nc.dram_tensor("wq", [D, 256], F32, kind="ExternalInput").ap()
    wk = nc.dram_tensor("wk", [D, 256], F32, kind="ExternalInput").ap()
    wv = nc.dram_tensor("wv", [D, 256], F32, kind="ExternalInput").ap()
    cst = nc.dram_tensor("cst", [128, S_], F32, kind="ExternalInput").ap()
    snt = nc.dram_tensor("snt", [128, S_], F32, kind="ExternalInput").ap()
    brev = nc.dram_tensor("brev", [4, 2432], F32, kind="ExternalInput").ap()
    bankt = nc.dram_tensor("bankt", [48, 128, 512], F32, kind="ExternalInput").ap()
    banktv = [bankt[i] for i in range(48)]
    offs = nc.dram_tensor("offs", [128, 4 * (S_ // 128)], F32, kind="ExternalInput").ap()
    hsel = nc.dram_tensor("hsel", [128, 4], F32, kind="ExternalInput").ap()
    jmat = nc.dram_tensor("jmat", [128, 128], F32, kind="ExternalInput").ap()
    idm = nc.dram_tensor("idm", [128, 128], F32, kind="ExternalInput").ap()
    oslc = nc.dram_tensor("oslc", [2, B, 128, (S_ // 128) * 128], BF16, kind="ExternalOutput").ap()
    oslc_v = oslc
    if dbg is not None:
        dbg_nm = nc.dram_tensor("dbg_nm", [128, 3 * 4 * (S_ // 128)], F32, kind="ExternalOutput").ap()
        dbg_dc = nc.dram_tensor("dbg_dc", [128, 8 * (S_ // 128)], F32, kind="ExternalOutput").ap()
        dbg_ds = nc.dram_tensor("dbg_ds", [(S_ // 128), 128, 2 * (S_ // 512 + 4)], F32, kind="ExternalOutput").ap()
        dbg_pq = nc.dram_tensor("dbg_pq", [(S_ // 128), 128, 512], BF16, kind="ExternalOutput").ap()
        dbg_pt = nc.dram_tensor("dbg_pt", [2, 128, 512], BF16, kind="ExternalOutput").ap()
        dbg_av = nc.dram_tensor("dbg_av", [128, 4], F32, kind="ExternalOutput").ap()
        dbg_p0 = nc.dram_tensor("dbg_p0", [128, 512], BF16, kind="ExternalOutput").ap()
        dbg_qb = nc.dram_tensor("dbg_qb", [128, 768], BF16, kind="ExternalOutput").ap()
        dbg_qb2 = nc.dram_tensor("dbg_qb2", [128, 256], BF16, kind="ExternalOutput").ap()

    CQ = 512
    xn_st = env.sbuf("xn_st", [128, 2 * KT * CQ], BF16)
    cs_st = env.sbuf("cs_st", [128, 2 * 2 * CQ], F32)
    rp_scr = env.sbuf("rp_scr", [128, 2 * CQ], F32)
    q_b = env.sbuf("q_b", [128, 2 * S_], BF16)
    k_b = env.sbuf("k_b", [128, 2 * S_], BF16)
    v_b = env.sbuf("v_b", [128, (S_ // 128) * 256], BF16)
    wq_sb = env.sbuf("wq_sb", [128, KT * 256], BF16)
    wk_sb = env.sbuf("wk_sb", [128, KT * 256], BF16)
    wv_sb = env.sbuf("wv_sb", [128, KT * 256], BF16)
    bank = env.sbuf("bank", [128, 24 * 512], BF16)
    jm_sb = env.sbuf("jm_sb", [128, 128], BF16)
    id4 = env.sbuf("id4", [4, 4], BF16)
    hsel_sb = env.sbuf("hsel_sb", [128, 4], BF16)
    p_sb = env.sbuf("p_sb", [128, 2 * S_], BF16)
    pt_sb = env.sbuf("pt_sb", [128, 2 * S_], BF16)
    osl_sb = env.sbuf("osl_sb", [128, 2 * 128], BF16)
    sq_scr = env.sbuf("sq_scr", [128, 2 * 512], BF16)
    brev_f = sq_scr.ap().bitcast(F32)[0:4, 0:512]
    qn_sb = env.sbuf("qn_sb", [4, S_], BF16)
    negm16 = env.sbuf("negm16", [16, S_], BF16)
    nmt16t = env.sbuf("nmt16t", [128, (S_ // 128) * 16], BF16)
    nmt_all = env.sbuf("nmt_all", [128, 16 * (S_ // 128)], F32)
    eoff_all = env.sbuf("eoff_all", [128, 16 * (S_ // 128)], F32)
    offb_sb = env.sbuf("offb_sb", [128, 4 * (S_ // 128)], F32)
    den_sb = env.sbuf("den_sb", [128, 2 * NDSL], F32)
    dcol_sb = env.sbuf("dcol_sb", [128, 2], F32)
    if dbg is not None:
        dcarch = env.sbuf("dcarch", [128, 8 * (S_ // 128) + 8], F32)
    row4 = env.sbuf("row4", [4, 8], F32)
    ones1 = env.sbuf("ones1", [1, 128], F32)
    ones_b1 = env.sbuf("ones_b1", [128, 1], BF16)

    psq = [env.psum(f"psq{i}", [128, 512]) for i in range(2)]
    psk = [env.psum(f"psk{i}", [128, 512]) for i in range(2)]
    psv = [env.psum(f"psv{i}", [128, 512]) for i in range(2)]
    psx = env.psum("psx", [128, 512])
    psy = env.psum("psy", [128, 512])
    psx_bf = psx.ap().bitcast(BF16)

    s_w = env.sem("b_wg")
    s_ws = env.sem("b_ws")
    s_wk = env.sem("b_wk")
    s_xn = [env.sem("b_xn0"), env.sem("b_xn1")]
    s_cs = [env.sem("b_cs0"), env.sem("b_cs1")]
    s_pj = env.sem("b_pj")
    s_pe = env.sem("b_pe")
    s_nr = env.sem("b_nr")
    s_bk = env.sem("b_bk")
    s_sc = env.sem("b_sc")
    s_px = env.sem("b_px")
    s_pt = env.sem("b_pt")
    s_av = env.sem("b_av")
    s_dn = env.sem("b_dn")
    s_os = env.sem("b_os")
    s_so = env.sem("b_so")
    s_jk = env.sem("b_jk")
    ini = env.sem("ini")
    jnk_dram = nc.dram_tensor("b_jnkd", [1, 512], F32, kind="Internal")

    env.set_junk(env.sbuf("junk", [1, 1024], F32))
    env.open_block()
    emit_consts(env, ones_b1, ones1)
    if dbg is not None:
        @env.block.gpsimd
        def _(g):
            g.memset(den_sb[:, :], 0.0)
            g.memset(dcol_sb[:, :], 0.0)
            g.memset(dcarch[:, :], 0.0)
            env.sem("ini").inc(g.engine_nop())

    xnv = xn0.rearrange("(k p) t -> k p t", p=128)
    wqv = wq.rearrange("(k p) m -> k p m", p=128)
    wkv = wk.rearrange("(k p) m -> k p m", p=128)
    wvv = wv.rearrange("(k p) m -> k p m", p=128)

    @env.block.gpsimd
    def _(g):
        for k in range(KT):
            s_w.inc(g.dma_start(out=wq_sb[:, k * 256:(k + 1) * 256], in_=wqv[k]), 16)
            s_w.inc(g.dma_start(out=wk_sb[:, k * 256:(k + 1) * 256], in_=wkv[k]), 16)
            s_w.inc(g.dma_start(out=wv_sb[:, k * 256:(k + 1) * 256], in_=wvv[k]), 16)
        s_w.inc(g.dma_start(out=jm_sb[:, :], in_=jmat[:, :]), 16)
        s_w.inc(g.dma_start(out=hsel_sb[:, :], in_=hsel[:, :]), 16)
        s_w.inc(g.dma_start(out=id4[:, :], in_=idm[0:4, 0:4]), 16)

    @env.block.sync
    def _(s):
        s_ws.inc(s.dma_start(out=offb_sb[:, :], in_=offs[:, :]), 16)

    NW1 = s_w.n
    NWS = s_ws.n

    @env.block.scalar
    def _(sc):
        sc.wait_ge(s_w.h, NW1)
        s_wk.inc(sc.mul(wq_sb[:, :], wq_sb[:, :], 0.125))

    NW = s_w.n

    for mchunk in range(5):
        cw = min(512, 2432 - mchunk * 512)

        @env.block.sync
        def _(s, mchunk=mchunk, cw=cw):
            if mchunk > 0:
                s.wait_ge(s_nr.h, s_nr.n)
            s_ws.inc(s.dma_start(out=brev_f[:, 0:cw], in_=brev[:, mchunk * 512:mchunk * 512 + cw]), 16)

        @env.block.vector
        def _(v, mchunk=mchunk, cw=cw):
            v.wait_ge(s_ws.h, s_ws.n)
            v.reduce_max(row4[0:4, 4:5], brev_f[:, 0:cw], axis=AX.X)
            env.vspace(v)
            if mchunk == 0:
                ins = v.tensor_copy(row4[0:4, 2:3], row4[0:4, 4:5])
            else:
                ins = v.tensor_max(row4[0:4, 2:3], row4[0:4, 2:3], row4[0:4, 4:5])
            s_nr.inc(ins)

    @env.block.vector
    def _(v):
        v.wait_ge(s_nr.h, s_nr.n)
        env.vspace(v)
        s_nr.inc(v.tensor_scalar(row4[0:4, 3:4], row4[0:4, 2:3], -1.0, None, ALU.mult))

    pt_nm_guard = [0]
    av_slice = {0: 0, 1: 0}
    pu = [0]
    pt_after = []
    osl_use = [0]
    so_hist = []
    for b in range(B):
        for nq in range(NNQ):
            i = b * NNQ + nq
            buf = i % 2
            pjn0 = s_pj.n

            @env.block.scalar
            def _(sc, b=b, nq=nq, buf=buf, i=i):
                if i >= 2:
                    sc.wait_ge(s_pe.h, (i - 1) * 3)
                for k in range(KT):
                    s_xn[buf].inc(sc.dma_start(
                        out=xn_st[:, buf * KT * CQ + k * CQ:buf * KT * CQ + (k + 1) * CQ],
                        in_=xnv[k][:, b * S_ + nq * CQ:b * S_ + (nq + 1) * CQ]), 16)

            @env.block.sync
            def _(s, b=b, nq=nq, buf=buf, i=i):
                if i >= 2:
                    s.wait_ge(s_pe.h, (i - 1) * 3)
                s_cs[buf].inc(s.dma_start(out=cs_st[:, buf * 2 * CQ:buf * 2 * CQ + CQ],
                                          in_=cst[:, nq * CQ:(nq + 1) * CQ]), 16)
                s_cs[buf].inc(s.dma_start(out=cs_st[:, buf * 2 * CQ + CQ:(buf + 1) * 2 * CQ],
                                          in_=snt[:, nq * CQ:(nq + 1) * CQ]), 16)

            @env.block.tensor
            def _(t, b=b, nq=nq, buf=buf, i=i):
                t.wait_ge(s_w.h, NW)
                t.wait_ge(s_ws.h, NWS)
                t.wait_ge(s_wk.h, 1)
                t.wait_ge(s_xn[buf].h, 16 * KT * (i // 2 + 1))
                if nq == 0:
                    t.wait_ge(s_os.h, s_os.n)
                    t.wait_ge(s_dn.h, s_dn.n)
                if i >= 1:
                    t.wait_ge(s_pe.h, i * 3)
                xo = buf * KT * CQ
                for mt in range(2):
                    for c in range(KT):
                        ins = t.matmul(psq[mt][:, :],
                                       wq_sb[:, c * 256 + mt * 128:c * 256 + (mt + 1) * 128],
                                       xn_st[:, xo + c * CQ:xo + (c + 1) * CQ],
                                       start=(c == 0), stop=(c == KT - 1))
                s_pj.inc(ins)
                for mt in range(2):
                    for c in range(KT):
                        ins = t.matmul(psk[mt][:, :],
                                       wk_sb[:, c * 256 + mt * 128:c * 256 + (mt + 1) * 128],
                                       xn_st[:, xo + c * CQ:xo + (c + 1) * CQ],
                                       start=(c == 0), stop=(c == KT - 1))
                s_pj.inc(ins)
                vbanks = [psv[0], psv[1], psx, psy]
                for tt in range(4):
                    for c in range(KT):
                        ins = t.matmul(vbanks[tt][:, 0:256],
                                       xn_st[:, xo + c * CQ + tt * 128:xo + c * CQ + tt * 128 + 128],
                                       wv_sb[:, c * 256:(c + 1) * 256],
                                       start=(c == 0), stop=(c == KT - 1))
                s_pj.inc(ins)

            @env.block.vector
            def _(v, b=b, nq=nq, buf=buf, i=i, pjn0=pjn0):
                co = buf * 2 * CQ
                cc, ss = cs_st[:, co:co + CQ], cs_st[:, co + CQ:co + 2 * CQ]
                v.wait_ge(s_cs[buf].h, 32 * (i // 2 + 1))
                for which, ps2, dstb in ((0, psq, q_b), (1, psk, k_b)):
                    v.wait_ge(s_pj.h, pjn0 + (3 if nq == 0 else which + 1))
                    env.vfence(v)
                    t1, t2 = rp_scr[:, 0:CQ], rp_scr[:, CQ:2 * CQ]
                    if nq == 0:
                        v.tensor_copy(t1, ps2[0][:, :])
                        v.tensor_copy(t2, ps2[1][:, :])
                    v.tensor_mul(t1, ps2[0][:, :], cc)
                    v.tensor_mul(t2, ps2[1][:, :], ss)
                    v.tensor_sub(dstb[:, 0 * S_ + nq * CQ:0 * S_ + (nq + 1) * CQ], t1, t2)
                    v.tensor_mul(t1, ps2[0][:, :], ss)
                    v.tensor_mul(t2, ps2[1][:, :], cc)
                    ins = v.tensor_add(dstb[:, 1 * S_ + nq * CQ:1 * S_ + (nq + 1) * CQ], t1, t2)
                    s_pe.inc(ins)

            @env.block.scalar
            def _(sc, b=b, nq=nq, i=i, pjn0=pjn0):
                sc.wait_ge(s_pj.h, pjn0 + 3)
                env.sfence(sc)
                vbanks = [psv[0], psv[1], psx, psy]
                for tt in range(4):
                    ins = sc.copy(v_b[:, (nq * 4 + tt) * 256:(nq * 4 + tt + 1) * 256],
                                  vbanks[tt][:, 0:256])
                s_pe.inc(ins)

        PJB = s_pe.n

        if dbg is not None and b == dbg[0]:
            @env.block.sync
            def _(s):
                s.wait_ge(s_pe.h, PJB)
                s.dma_start(out=dbg_qb[:, 0:128], in_=q_b[:, 0:128]).then_inc(s_so.h, 16)
                s.dma_start(out=dbg_qb[:, 128:256], in_=q_b[:, S_:S_ + 128]).then_inc(s_so.h, 16)
                s.dma_start(out=dbg_qb[:, 256:384], in_=k_b[:, 0:128]).then_inc(s_so.h, 16)
                s.dma_start(out=dbg_qb[:, 384:512], in_=k_b[:, S_:S_ + 128]).then_inc(s_so.h, 16)
                s.dma_start(out=dbg_qb[:, 512:768], in_=v_b[:, 0:256]).then_inc(s_so.h, 16)
            s_so.n += 80

        @env.block.vector
        def _(v, b=b):
            v.wait_ge(s_nr.h, s_nr.n)
            s_nr.inc(v.memset(row4[0:4, 0:1], 0.0))

        nr_base = s_nr.n
        for which, srcb in ((0, q_b), (1, k_b)):
            for ch in range(NNQ):
                @env.block.scalar
                def _(sc, which=which, srcb=srcb, ch=ch):
                    sc.wait_ge(s_pe.h, PJB)
                    if not (which == 0 and ch == 0):
                        sc.wait_ge(s_nr.h, s_nr.n)
                    if which == 0 and ch == 0:
                        env.sfence(sc)
                    sc.activation(sq_scr[:, 0:512], srcb[:, 0 * S_ + ch * 512:0 * S_ + (ch + 1) * 512], AF.Square)
                    sc.activation(sq_scr[:, 512:1024], srcb[:, 1 * S_ + ch * 512:1 * S_ + (ch + 1) * 512], AF.Square)
                    s_px.inc(env.sfence(sc))

                @env.block.tensor
                def _(t, which=which, ch=ch):
                    t.wait_ge(s_px.h, s_px.n)
                    t.matmul(psx[0:4, :], hsel_sb[:, :], sq_scr[:, 0:512], start=True, stop=False)
                    s_pj.inc(t.matmul(psx[0:4, :], hsel_sb[:, :], sq_scr[:, 512:1024], start=False, stop=True))

                @env.block.vector
                def _(v, which=which, ch=ch):
                    v.wait_ge(s_pj.h, s_pj.n)
                    env.vfence(v)
                    if which == 0:
                        ins = v.tensor_copy(qn_sb[0:4, ch * 512:(ch + 1) * 512], psx[0:4, :])
                    else:
                        v.reduce_max(row4[0:4, 1:2], psx[0:4, :], axis=AX.X)
                        env.vspace(v)
                        ins = v.tensor_max(row4[0:4, 0:1], row4[0:4, 0:1], row4[0:4, 1:2])
                    s_nr.inc(ins)

        @env.block.vector
        def _(v, b=b):
            v.wait_ge(s_nr.h, s_nr.n)
            if b > 0:
                v.wait_ge(s_pt.h, pt_nm_guard[0])
            env.vspace(v)
            if b == 0:
                v.memset(negm16[0:16, :], 0.0)
            v.tensor_scalar(negm16[0:4, :], qn_sb[0:4, :], row4[0:4, 0:1], None, ALU.mult)
            s_nr.inc(env.vfence(v))

        @env.block.scalar
        def _(sc, b=b):
            sc.wait_ge(s_nr.h, s_nr.n)
            env.sfence(sc)
            sc.activation(negm16[0:4, :], negm16[0:4, :], AF.Ln)
            sc.activation(negm16[0:4, :], negm16[0:4, :], AF.Exp, scale=0.5)
            s_nr.inc(env.sfence(sc))

        @env.block.vector
        def _(v, b=b):
            v.wait_ge(s_nr.h, s_nr.n)
            env.vfence(v)
            v.tensor_scalar(negm16[0:4, :], negm16[0:4, :], -1.0,
                            row4[0:4, 3:4], ALU.mult, ALU.add)
            s_nr.inc(env.vfence(v))

        NMDONE = s_nr.n

        NB = b * 4 * NQT

        @env.block.sync
        def _(s, b=b):
            s.wait_ge(s_nr.h, NMDONE)
            if b > 0:
                s.wait_ge(s_px.h, s_px.n)
            s.wait_ge(s_pt.h, s_pt.n)
            s_pt.inc(s.dma_start(
                out=nmt16t.ap().rearrange("p (t r) -> p t r", r=16),
                in_=negm16[:, :], transpose=True), 16)
        pt_nm_guard[0] = s_pt.n

        @env.block.scalar
        def _(sc, b=b, NB=NB):
            sc.wait_ge(s_pt.h, s_pt.n)
            nmtv16 = nmt16t.ap().rearrange("p (t r) -> p t r", r=16)
            nmt4 = nmt_all.ap()[:, NB:NB + 4 * NQT].rearrange("p (q j) -> p q j", j=4)
            sc.copy(nmt4[:, :, :], nmtv16[:, :, 0:4])
            eoffv = eoff_all.ap()[:, NB:NB + 4 * NQT].rearrange("p (q j) -> p q j", j=4)
            for j in range(4):
                ins = sc.activation(eoffv[:, :, j], nmt4[:, :, j], AF.Exp,
                                    bias=offb_sb[:, j:j + 1])
            s_dn.inc(ins)

        EOFF_B = s_dn.n

        if dbg is not None and b == dbg[0]:
            @env.block.sync
            def _(s):
                s.wait_ge(s_dn.h, EOFF_B)
                s.dma_start(out=dbg_nm[:, 0:4 * NQT], in_=nmt_all[:, NB:NB + 4 * NQT]).then_inc(s_so.h, 16)
                s.dma_start(out=dbg_nm[:, 4 * NQT:8 * NQT], in_=nmt_all[:, NB:NB + 4 * NQT]).then_inc(s_so.h, 16)
                s.dma_start(out=dbg_nm[:, 8 * NQT:12 * NQT], in_=eoff_all[:, NB:NB + 4 * NQT]).then_inc(s_so.h, 16)
            s_so.n += 48

        for hp in range(2):
            sc_before_bank = s_sc.n

            @env.block.gpsimd
            def _(g, b=b, hp=hp, sc_before_bank=sc_before_bank):
                g.wait_ge(s_sc.h, sc_before_bank)
                g.wait_ge(s_pt.h, s_pt.n)
                for h in range(2):
                    hg = 2 * hp + h
                    for di in range(12):
                        s_bk.inc(g.dma_start(
                            out=bank[:, (h * 12 + di) * 512:(h * 12 + di + 1) * 512],
                            in_=banktv[hg * 12 + di]), 16)

            BKN = s_bk.n
            pending = [None]

            def flush_pending(extra_sc=None):
                fn = pending[0]
                if fn is not None:
                    pending[0] = None
                    fn(extra_sc)

            for qt in range(NQT):
                ncov = qt // 4 + 1
                pt_h = {}
                flush_pending(extra_sc=s_sc.n + 1)

                for h in range(2):
                    strip = 32 * (2 * hp + h)
                    SB = psq if h == 0 else psk
                    col = NB + qt * 4 + 2 * hp + h
                    dn_guard = s_dn.n
                    pbase = (pu[0] % 2) * S_
                    pslot_wait = pt_after[pu[0] - 2] if pu[0] >= 2 else 0
                    for kc in range(ncov):
                        delta = qt * 128 - kc * 512
                        Sb = SB[kc % 2]

                        @env.block.tensor
                        def _(t, b=b, hp=hp, qt=qt, h=h, kc=kc, strip=strip, Sb=Sb, delta=delta,
                              pslot_wait=pslot_wait):
                            t.wait_ge(s_px.h, max(0, s_px.n - 1))
                            if kc == 0:
                                t.wait_ge(s_bk.h, BKN)
                                if pslot_wait:
                                    t.wait_ge(s_pt.h, pslot_wait)
                            t.matmul(Sb[:, :], q_b[strip:strip + 32, 0 * S_ + qt * 128:0 * S_ + (qt + 1) * 128],
                                     k_b[strip:strip + 32, 0 * S_ + kc * 512:0 * S_ + (kc + 1) * 512],
                                     start=True, stop=False, tile_position=(strip, 0))
                            last = (delta > 1408) or no_inject
                            ins = t.matmul(Sb[:, :], q_b[strip:strip + 32, 1 * S_ + qt * 128:1 * S_ + (qt + 1) * 128],
                                           k_b[strip:strip + 32, 1 * S_ + kc * 512:1 * S_ + (kc + 1) * 512],
                                           start=False, stop=last, tile_position=(strip, 0))
                            if not last:
                                ins = t.matmul(Sb[:, :], jm_sb[:, :],
                                               bank[:, (h * 12 + delta // 128) * 512:(h * 12 + delta // 128 + 1) * 512],
                                               start=False, stop=True)
                            s_sc.inc(ins)

                        @env.block.scalar
                        def _(sc, qt=qt, h=h, kc=kc, Sb=Sb, col=col, dn_guard=dn_guard, pbase=pbase):
                            sc.wait_ge(s_sc.h, s_sc.n)
                            if kc == 0:
                                sc.wait_ge(s_dn.h, dn_guard)
                            ins = sc.activation(p_sb[:, pbase + kc * 512:pbase + (kc + 1) * 512],
                                                Sb[:, :], AF.Exp,
                                                bias=nmt_all[:, col:col + 1],
                                                accum_out=den_sb[:, h * NDSL + kc:h * NDSL + kc + 1])
                            s_px.inc(ins)

                    @env.block.vector
                    def _(v, h=h, ncov=ncov, col=col):
                        v.wait_ge(s_px.h, s_px.n)
                        v.wait_ge(s_dn.h, EOFF_B)
                        env.vfence(v)
                        if ncov > 1:
                            v.reduce_sum(dcol_sb[:, h:h + 1], den_sb[:, h * NDSL:h * NDSL + ncov], axis=AX.X)
                            env.vspace(v)
                            v.tensor_add(dcol_sb[:, h:h + 1], dcol_sb[:, h:h + 1], eoff_all[:, col:col + 1])
                        else:
                            v.tensor_add(dcol_sb[:, h:h + 1], den_sb[:, h * NDSL:h * NDSL + 1],
                                         eoff_all[:, col:col + 1])
                        env.vspace(v)
                        v.reciprocal(dcol_sb[:, h:h + 1], dcol_sb[:, h:h + 1])
                        ins = env.vfence(v)
                        if dbg is not None and (b, hp) == dbg:
                            v.tensor_copy(dcarch[:, qt * 8 + h * 4:qt * 8 + h * 4 + 1], dcol_sb[:, h:h + 1])
                            v.tensor_copy(dcarch[:, qt * 8 + h * 4 + 1:qt * 8 + h * 4 + 2],
                                          den_sb[:, h * NDSL:h * NDSL + 1])
                            ins = v.tensor_copy(dcarch[:, qt * 8 + h * 4 + 2:qt * 8 + h * 4 + 3],
                                                eoff_all[:, col:col + 1])
                        s_dn.inc(ins)

                    if dbg is not None and (b, hp) == dbg and qt == 0 and h == 0:
                        @env.block.sync
                        def _(s):
                            s.wait_ge(s_px.h, s_px.n)
                            s.wait_ge(s_dn.h, s_dn.n)
                            s.dma_start(out=dbg_p0[:, :], in_=p_sb[:, 0:512]).then_inc(s_so.h, 16)
                        s_so.n += 16

                    @env.block.sync
                    def _(s, b=b, hp=hp, qt=qt, h=h, ncov=ncov, pbase=pbase):
                        s.wait_ge(s_px.h, s_px.n)
                        s.wait_ge(s_dn.h, s_dn.n)
                        av_need = av_slice[h]
                        if av_need:
                            s.wait_ge(s_av.h, av_need)
                        if qt == 0:
                            s_jk.inc(s.dma_start(out=jnk_dram.ap(), in_=env._junk[0:1, 0:512]), 16)
                            s.wait_ge(s_jk.h, s_jk.n)
                        s_pt.inc(s.dma_start(
                            out=pt_sb.ap()[:, h * S_:h * S_ + ncov * 512].rearrange(
                                "p (t q) -> p t q", q=128),
                            in_=p_sb[:, pbase:pbase + ncov * 512], transpose=True), 16)
                    pt_after.append(s_pt.n)
                    pt_h[h] = s_pt.n
                    pu[0] += 1

                if dbg is not None and (b, hp) == dbg:
                    @env.block.sync
                    def _(s, qt=qt):
                        s.wait_ge(s_dn.h, s_dn.n)
                        s.dma_start(out=dbg_dc[:, qt * 8:qt * 8 + 8], in_=dcarch[:, qt * 8:qt * 8 + 8],
                                    allow_slow_non_contiguous=True).then_inc(s_so.h, 16)
                        s.dma_start(out=dbg_ds[qt][:, 0:2 * NDSL],
                                    in_=den_sb[:, :]).then_inc(s_so.h, 16)
                        s.dma_start(out=dbg_pq[qt], in_=p_sb[:, 0:512]).then_inc(s_so.h, 16)
                        if qt == 0:
                            s.dma_start(out=dbg_pt[0], in_=pt_sb[:, 0:512]).then_inc(s_so.h, 16)
                            s.dma_start(out=dbg_pt[1], in_=pt_sb[:, S_:S_ + 512]).then_inc(s_so.h, 16)
                            s.dma_start(out=dbg_av[:, :], in_=dcarch[:, 8 * NQT:8 * NQT + 4],
                                        allow_slow_non_contiguous=True).then_inc(s_so.h, 16)
                            s_so.n += 48
                    s_so.n += 32

                for ah in range(2):
                    @env.block.tensor
                    def _(t, b=b, hp=hp, qt=qt, ncov=ncov, ah=ah, PTH=pt_h[ah]):
                        t.wait_ge(s_pt.h, PTH)
                        if ah == 0:
                            t.wait_ge(s_os.h, s_os.n)
                        nkt = ncov * 4
                        for kt in range(nkt):
                            ins = t.matmul(psv[ah][:, 0:64],
                                           pt_sb[:, ah * S_ + kt * 128:ah * S_ + (kt + 1) * 128],
                                           v_b[:, kt * 256 + hp * 128 + ah * 64:
                                               kt * 256 + hp * 128 + ah * 64 + 64],
                                           start=(kt == 0), stop=(kt == nkt - 1))
                        s_av.inc(ins)
                    av_slice[ah] = s_av.n

                def make_evac(qt=qt, av_target=s_av.n, dn_target=s_dn.n, b=b, hp=hp):
                    def emit(extra_sc):
                        idx = osl_use[0]
                        osl_use[0] += 1
                        slot = (idx % 2) * 128

                        @env.block.vector
                        def _(v):
                            v.wait_ge(s_av.h, av_target)
                            if extra_sc is not None:
                                v.wait_ge(s_sc.h, extra_sc)
                            if idx >= 2:
                                v.wait_ge(s_so.h, so_hist[idx - 2])
                            if extra_sc is None or qt == 0:
                                env.vfence(v)
                            if dbg is not None and (b, hp) == dbg and qt == 0:
                                v.tensor_copy(dcarch[:, 8 * NQT:8 * NQT + 1], psv[0][:, 0:1])
                                v.tensor_copy(dcarch[:, 8 * NQT + 1:8 * NQT + 2], psv[1][:, 0:1])
                            v.tensor_scalar(osl_sb[:, slot:slot + 64], psv[0][:, 0:64],
                                            dcol_sb[:, 0:1], None, ALU.mult)
                            s_os.inc(v.tensor_scalar(osl_sb[:, slot + 64:slot + 128], psv[1][:, 0:64],
                                                     dcol_sb[:, 1:2], None, ALU.mult))

                        @env.block.sync
                        def _(s):
                            s.wait_ge(s_os.h, s_os.n)
                            s_so.inc(s.dma_start(out=oslc_v[hp, b][:, qt * 128:(qt + 1) * 128],
                                                 in_=osl_sb[:, slot:slot + 128]), 16)
                        so_hist.append(s_so.n)
                    return emit

                pending[0] = make_evac()

            flush_pending(extra_sc=None)

        if dbg is not None and b == dbg[0]:
            @env.block.sync
            def _(s):
                s.wait_ge(s_os.h, s_os.n)
                s.dma_start(out=dbg_qb2[:, 0:128], in_=q_b[:, 0:128]).then_inc(s_so.h, 16)
                s.dma_start(out=dbg_qb2[:, 128:256], in_=q_b[:, S_:S_ + 128]).then_inc(s_so.h, 16)
            s_so.n += 32

    @env.block.sync
    def _(s):
        s.wait_ge(s_so.h, s_so.n)

    return env.close()



def host_prep_B(inputs, S_=S):
    Wq, Wk, Wv = inputs["L0_attn_Wq"], inputs["L0_attn_Wk"], inputs["L0_attn_Wv"]
    LB = inputs["L0_attn_learnable_bias"]
    off = inputs["L0_attn_softmax_offset"]
    H, Dh = 32, 64
    p = np.arange(S_, dtype=np.float64)
    dgrid = np.arange(32, dtype=np.float64)
    ang = np.outer(p, 1.0 / (10000.0 ** (dgrid / 32.0)))
    cs = np.cos(ang).astype(np.float32)
    sn = np.sin(ang).astype(np.float32)
    cst = np.ascontiguousarray(np.tile(cs.T, (4, 1)))
    snt = np.ascontiguousarray(np.tile(sn.T, (4, 1)))
    hsel = np.zeros((128, 4), dtype=np.float32)
    for r in range(128):
        hsel[r, r // 32] = 1.0
    jmat = np.eye(128, dtype=np.float32)[::-1].copy()
    idm = np.eye(128, dtype=np.float32)
    maps = []
    for c in range(NCORE):
        heads = [4 * c + j for j in range(4)]
        rows_qk = [h * Dh + half * 32 + d0 for half in range(2) for h in heads for d0 in range(32)]
        rows_v = [h * Dh + d for h in heads for d in range(Dh)]
        wq = np.ascontiguousarray(Wq[rows_qk, :].T)
        wk = np.ascontiguousarray(Wk[rows_qk, :].T)
        wv = np.ascontiguousarray(Wv[rows_v, :].T)
        brev = np.zeros((4, 2432), dtype=np.float32)
        for j, h in enumerate(heads):
            u = np.arange(2432)
            dd = 1535 - u
            val = np.zeros(2432, dtype=np.float32)
            band = (dd >= 0) & (dd <= 1023)
            val[band] = LB[h, dd[band]]
            val[dd < 0] = -1e30
            brev[j] = val
        nqt_ = S_ // 128
        offs = np.tile(off[heads][None, :], (128, nqt_)).astype(np.float32)
        bankt = np.zeros((48, 128, 512), dtype=np.float32)
        u = np.arange(512)
        for hg in range(4):
            for di in range(12):
                base = 1408 - 128 * di
                for r in range(128):
                    bankt[hg * 12 + di, r, :] = brev[hg, base + r:base + r + 512]
        maps.append(dict(wq=wq, wk=wk, wv=wv, cst=cst, snt=snt, brev=brev,
                         bankt=bankt, offs=offs, hsel=hsel, jmat=jmat, idm=idm))
    return maps


def assemble_xn0(shards, S_=S):
    so = S_ // NCORE
    full = np.empty((D, B * S_), dtype=BFD)
    for c, sh in enumerate(shards):
        shv = np.asarray(sh).reshape(D, B, so)
        for b_ in range(B):
            full[:, b_ * S_ + so * c: b_ * S_ + so * (c + 1)] = shv[:, b_, :]
    return full



def build_launch_C1(T_=TOWN):
    env = Env("C1")
    nc = env.nc
    CW = 512
    NCH = T_ // CW

    oat = nc.dram_tensor("oat", [D, T_], BF16, kind="ExternalInput").ap()
    x0c = nc.dram_tensor("x0c", [D, T_], F32, kind="ExternalInput").ap()
    woT = nc.dram_tensor("woT", [D, D], F32, kind="ExternalInput").ap()
    cvw = nc.dram_tensor("cvw", [128, 2 * KT], F32, kind="ExternalInput").ap()
    x1o = nc.dram_tensor("x1o", [D, T_], F32, kind="ExternalOutput").ap()
    xco = nc.dram_tensor("xco", [D, T_], BF16, kind="ExternalOutput").ap()
    oat_t = dram_tiles(oat)
    x0_t = dram_tiles(x0c)
    wo_t = dram_tiles(woT)
    x1_t = dram_tiles(x1o)
    xc_t = dram_tiles(xco)

    wo_sb = env.sbuf("wo_sb", [128, KT * D], BF16)
    oat_sb = env.sbuf("oat_sb", [128, KT * CW], BF16)
    x0p = env.sbuf("x0p", [128, 4 * CW], F32)
    x1p = env.sbuf("x1p", [128, 4 * CW], F32)
    xbp = [env.sbuf("xb0", [128, KT * CW], BF16),
           env.sbuf("xb1", [128, KT * CW], BF16)]
    sqb = env.sbuf("sqb", [128, KT * CW], BF16)
    row = env.sbuf("row", [1, 6 * CW], F32)
    rB = env.sbuf("rB", [128, CW], BF16)
    mB = env.sbuf("mB", [128, CW], BF16)
    tmpb = env.sbuf("tmpb", [128, CW], BF16)
    xcp = env.sbuf("xcp", [128, 2 * CW], F32)
    xcb = env.sbuf("xcb", [128, 2 * CW], BF16)
    cvs = env.sbuf("cvs", [128, 2 * KT], F32)
    psm = [env.psum("c1_ps0", [128, CW]), env.psum("c1_ps1", [128, CW])]
    psA = env.psum("c1_psA", [1, CW])
    psB = env.psum("c1_psB", [1, CW])
    psR = env.psum("c1_psR", [128, CW])
    psM = env.psum("c1_psM", [128, CW])
    ones_b = env.sbuf("ones_b", [128, 1], BF16)
    ones_f = env.sbuf("ones_f", [1, 128], F32)
    env.set_junk(env.sbuf("junk", [1, 1024], F32))

    s_w = env.sem("c_w")
    s_oa = env.sem("c_oa")
    s_x0 = env.sem("c_x0")
    s_mm = env.sem("c_mm")
    s_ev = env.sem("c_ev")
    s_sx = env.sem("c_sx")
    s_sq = env.sem("c_sq")
    s_st = env.sem("c_st")
    s_rw = env.sem("c_rw")
    s_e2 = env.sem("c_e2")
    s_cv = env.sem("c_cv")
    s_sl = env.sem("c_sl")
    s_sc = env.sem("c_sc")
    ini = env.sem("ini")

    def evn(t):
        return t + 1 + t // KT

    def sln(t):
        return t + 1 + t // KT

    env.open_block()
    emit_consts(env, ones_b, ones_f)

    @env.block.gpsimd
    def _(g):
        for et in range(KT):
            s_w.inc(g.dma_start(out=wo_sb[:, et * D:(et + 1) * D], in_=wo_t[et][:, :]), 16)
    NW = s_w.n

    s_ws = env.sem("c_ws")

    @env.block.sync
    def _(s):
        s_ws.inc(s.dma_start(out=cvs[:, :], in_=cvw[:, :]), 16)
    NWS = s_ws.n

    for ci in range(NCH):
        c0, c1 = ci * CW, (ci + 1) * CW
        xb = xbp[ci % 2]
        xprev = xbp[(ci - 1) % 2]

        @env.block.gpsimd
        def _(g, ci=ci, c0=c0, c1=c1):
            if ci > 0:
                g.wait_ge(s_mm.h, 16 * ci)
            for et in range(KT):
                s_oa.inc(g.dma_start(out=oat_sb[:, et * CW:(et + 1) * CW],
                                     in_=oat_t[et][:, c0:c1]), 16)
            for dt in range(KT):
                gidx = ci * KT + dt
                if gidx >= 4:
                    g.wait_ge(s_ev.h, evn(gidx - 4))
                s_x0.inc(g.dma_start(out=x0p[:, (gidx % 4) * CW:(gidx % 4 + 1) * CW],
                                     in_=x0_t[dt][:, c0:c1]), 16)

        for dt in range(KT):
            gidx = ci * KT + dt

            @env.block.tensor
            def _(t, ci=ci, dt=dt, gidx=gidx):
                if gidx == 0:
                    t.wait_ge(s_w.h, NW)
                t.wait_ge(s_oa.h, 16 * KT * (ci + 1))
                if gidx >= 2:
                    t.wait_ge(s_ev.h, evn(gidx - 2))
                for et in range(KT):
                    ins = t.matmul(psm[gidx % 2][:, :],
                                   wo_sb[:, et * D + dt * 128:et * D + (dt + 1) * 128],
                                   oat_sb[:, et * CW:(et + 1) * CW],
                                   start=(et == 0), stop=(et == KT - 1))
                s_mm.inc(ins)

            @env.block.vector
            def _(v, ci=ci, dt=dt, gidx=gidx):
                v.wait_ge(s_mm.h, gidx + 1)
                v.wait_ge(s_x0.h, 16 * (gidx + 1))
                if gidx >= 4:
                    v.wait_ge(s_sx.h, 16 * (gidx - 3))
                if dt == 0 and ci > 0:
                    v.wait_ge(s_sq.h, ci)
                env.vfence(v)
                ins = v.tensor_add(x1p[:, (gidx % 4) * CW:(gidx % 4 + 1) * CW],
                                   psm[gidx % 2][:, :],
                                   x0p[:, (gidx % 4) * CW:(gidx % 4 + 1) * CW])
                s_ev.inc(ins)

            @env.block.sync
            def _(s, ci=ci, dt=dt, gidx=gidx, c0=c0, c1=c1):
                s.wait_ge(s_ev.h, min(evn(gidx + 1), 17 * (ci + 1)))
                s_sx.inc(s.dma_start(out=x1_t[dt][:, c0:c1],
                                     in_=x1p[:, (gidx % 4) * CW:(gidx % 4 + 1) * CW]), 16)

        @env.block.vector
        def _(v):
            s_ev.inc(env.vfence(v))

        @env.block.scalar
        def _(sc, ci=ci, xb=xb):
            for dt in range(KT):
                gidx = ci * KT + dt
                sc.wait_ge(s_ev.h, min(evn(gidx + 1), 17 * (ci + 1)))
                sc.copy(xb[:, dt * CW:(dt + 1) * CW],
                        x1p[:, (gidx % 4) * CW:(gidx % 4 + 1) * CW])
                sc.activation(sqb[:, dt * CW:(dt + 1) * CW], xb[:, dt * CW:(dt + 1) * CW],
                              AF.Square)
            s_sq.inc(env.sfence(sc))

        @env.block.tensor
        def _(t, ci=ci, xb=xb):
            t.wait_ge(ini.h, 2)
            t.wait_ge(s_sq.h, ci + 1)
            if ci >= 1:
                t.wait_ge(s_rw.h, 3 * (ci - 1) + 1)
            for k in range(KT):
                t.matmul(psA[0:1, :], ones_b[:, :], xb[:, k * CW:(k + 1) * CW],
                         start=(k == 0), stop=(k == KT - 1))
            for k in range(KT):
                ins = t.matmul(psB[0:1, :], ones_b[:, :], sqb[:, k * CW:(k + 1) * CW],
                               start=(k == 0), stop=(k == KT - 1))
            s_st.inc(ins)

        @env.block.vector
        def _(v, ci=ci):
            v.wait_ge(s_st.h, 2 * ci + 1)
            env.vfence(v)
            v.tensor_scalar(row[0:1, 0:CW], psA[0:1, :], 1.0 / D, None, ALU.mult)
            v.tensor_scalar(row[0:1, CW:2 * CW], psB[0:1, :], 1.0 / D, None, ALU.mult)
            v.tensor_mul(row[0:1, 4 * CW:5 * CW], row[0:1, 0:CW], row[0:1, 0:CW])
            v.tensor_sub(row[0:1, 2 * CW:3 * CW], row[0:1, CW:2 * CW], row[0:1, 4 * CW:5 * CW])
            v.tensor_scalar(row[0:1, 2 * CW:3 * CW], row[0:1, 2 * CW:3 * CW], EPS, None, ALU.add)
            ins = v.reciprocal(row[0:1, 2 * CW:3 * CW], row[0:1, 2 * CW:3 * CW])
            s_rw.inc(ins)
            v.wait_ge(s_rw.h, 3 * ci + 2)
            v.tensor_mul(row[0:1, 4 * CW:5 * CW], row[0:1, 0:CW], row[0:1, 3 * CW:4 * CW])
            s_rw.inc(env.vfence(v))

        @env.block.scalar
        def _(sc, ci=ci):
            sc.wait_ge(s_rw.h, 3 * ci + 1)
            env.sfence(sc)
            sc.activation(row[0:1, 3 * CW:4 * CW], row[0:1, 2 * CW:3 * CW], AF.Sqrt)
            s_rw.inc(env.sfence(sc))

        @env.block.tensor
        def _(t, ci=ci):
            t.wait_ge(s_rw.h, 3 * ci + 3)
            if ci >= 1:
                t.wait_ge(s_e2.h, 2 * ci)
            t.matmul(psR[:, :], ones_f[:, :], row[0:1, 3 * CW:4 * CW], start=True, stop=True)
            ins = t.matmul(psM[:, :], ones_f[:, :], row[0:1, 4 * CW:5 * CW], start=True, stop=True)
            s_st.inc(ins)

        @env.block.vector
        def _(v, ci=ci, xb=xb, xprev=xprev):
            v.wait_ge(s_st.h, 2 * ci + 2)
            env.vfence(v)
            if ci == 0:
                v.wait_ge(s_ws.h, NWS)
            s_e2.inc(v.tensor_copy(rB[:, :], psR[:, :]))
            s_e2.inc(v.tensor_copy(mB[:, :], psM[:, :]))
            for k in range(KT):
                gk = ci * KT + k
                if gk >= 2:
                    v.wait_ge(s_sl.h, sln(gk - 2))
                v.tensor_mul(tmpb[:, :], xb[:, k * CW:(k + 1) * CW], rB[:, :])
                v.tensor_sub(xb[:, k * CW:(k + 1) * CW], tmpb[:, :], mB[:, :])
                dstg = xcp[:, (gk % 2) * CW:(gk % 2 + 1) * CW]
                v.tensor_scalar(dstg, xb[:, k * CW:(k + 1) * CW],
                                cvs[:, 2 * k + 1:2 * k + 2], None, ALU.mult)
                if ci > 0:
                    v.tensor_scalar(tmpb[:, :], xprev[:, k * CW:(k + 1) * CW],
                                    cvs[:, 2 * k:2 * k + 1], None, ALU.mult)
                    v.tensor_add(dstg, dstg, tmpb[:, :])
                s_cv.inc(v.engine_nop())

        @env.block.scalar
        def _(sc, ci=ci):
            for k in range(KT):
                gk = ci * KT + k
                sc.wait_ge(s_cv.h, min(gk + 2, 16 * (ci + 1)))
                if k == KT - 1:
                    env.sfence(sc)
                if gk >= 2:
                    sc.wait_ge(s_sc.h, 16 * (gk - 1))
                ins = sc.activation(xcb[:, (gk % 2) * CW:(gk % 2 + 1) * CW],
                                    xcp[:, (gk % 2) * CW:(gk % 2 + 1) * CW], AF.Silu)
                s_sl.inc(ins)
            s_sl.inc(env.sfence(sc))

        @env.block.sync
        def _(s, ci=ci, c0=c0, c1=c1):
            for k in range(KT):
                gk = ci * KT + k
                s.wait_ge(s_sl.h, min(sln(gk + 1), 17 * (ci + 1)))
                s_sc.inc(s.dma_start(out=xc_t[k][:, c0:c1],
                                     in_=xcb[:, (gk % 2) * CW:(gk % 2 + 1) * CW]), 16)

    @env.block.sync
    def _(s):
        s.wait_ge(s_sx.h, s_sx.n)
        s.wait_ge(s_sc.h, s_sc.n)

    return env.close()



def build_launch_C2(T_=TOWN):
    env = Env("C2")
    nc = env.nc
    PW = 128
    CW = 4 * PW
    NPC = (T_ // 4) // PW
    DK = 1024

    xco = nc.dram_tensor("xco", [D, T_], BF16, kind="ExternalInput").ap()
    wqw = nc.dram_tensor("wqw", [D, DK], F32, kind="ExternalInput").ap()
    wkw = nc.dram_tensor("wkw", [D, DK], F32, kind="ExternalInput").ap()
    wvw = nc.dram_tensor("wvw", [D, D], F32, kind="ExternalInput").ap()
    rwm = nc.dram_tensor("rwm", [128, 32], F32, kind="ExternalInput").ap()
    cosT = nc.dram_tensor("cosT", [128, CW], F32, kind="ExternalInput").ap()
    sinT = nc.dram_tensor("sinT", [128, CW], F32, kind="ExternalInput").ap()
    hs2 = nc.dram_tensor("hs2", [128, 2], F32, kind="ExternalInput").ap()
    onrm = nc.dram_tensor("onrm", [4, T_ // 4, D], BF16, kind="ExternalOutput").ap()
    wv_bf = nc.dram_tensor("wv_bf", [D, D], BF16, kind="Internal").ap()
    xc_t = dram_tiles(xco)
    wv_t = dram_tiles(wvw)
    wvb_t = dram_tiles(wv_bf)

    PAIRS = [(i, j) for i in range(4) for j in range(i + 1)]

    wq_f = env.sbuf("wq_f", [128, KT * DK], BF16)
    wk_f = env.sbuf("wk_f", [128, KT * DK], BF16)
    wv_sl = env.sbuf("wv_sl", [128, 2 * KT * 256], BF16)
    wcast = env.sbuf("wcast", [128, 2048], BF16)
    xc_sb = env.sbuf("xc_sb", [128, KT * CW], BF16)
    qsil = env.sbuf("qsil", [128, 2 * CW], BF16)
    q_sb = env.sbuf("q_sb", [128, 8 * CW], BF16)
    k_sb = env.sbuf("k_sb", [128, 8 * CW], BF16)
    v_sb = env.sbuf("v_sb", [128, 2 * CW], BF16)
    v_tm = env.sbuf("v_tm", [128, 4 * D], BF16)
    csT = env.sbuf("csT", [128, 2 * CW], F32)
    hs2b = env.sbuf("hs2b", [128, 2], BF16)
    rws = env.sbuf("rws", [128, 32], BF16)
    rpt = env.sbuf("rpt", [128, CW], F32)
    rp2 = env.sbuf("rp2", [128, CW], F32)
    rtt = env.sbuf("rtt", [128, 8 * 64], F32)
    mmt = env.sbuf("mmt", [128, 160], F32)
    ctf = env.sbuf("ctf", [128, 160], F32)
    prodb = env.sbuf("prodb", [128, 2 * 1280], BF16)
    o_i = env.sbuf("o_i", [128, D], F32)
    o_n = env.sbuf("o_n", [128, D], BF16)
    st16 = env.sbuf("st16", [128, 8 * 16], F32)
    otmp = env.sbuf("otmp", [128, 128], F32)
    psP = [env.psum("c2_p0", [128, CW]), env.psum("c2_p1", [128, CW])]
    psR = env.psum("c2_r", [128, 128])
    psQ = env.psum("c2_q", [128, 160])
    env.set_junk(env.sbuf("junk", [1, 1024], F32))

    s_w = env.sem("d_w")
    s_wc = env.sem("d_wc")
    s_wv = env.sem("d_wv")
    s_xc = env.sem("d_xc")
    s_pj = env.sem("d_pj")
    s_pe = env.sem("d_pe")
    s_rp = env.sem("d_rp")
    s_rt = env.sem("d_rt")
    s_sg = env.sem("d_sg")
    s_rv = env.sem("d_rv")
    s_pb = env.sem("d_pb")
    s_qd = env.sem("d_qd")
    s_cm = env.sem("d_cm")
    s_vt = env.sem("d_vt")
    s_oa = env.sem("d_oa")
    s_iv = env.sem("d_iv")
    s_sq = env.sem("d_sq")
    s_so = env.sem("d_so")

    env.open_block()

    @env.block.gpsimd
    def _(g):
        for et in range(KT):
            s_w.inc(g.dma_start(out=wq_f[:, et * DK:(et + 1) * DK],
                                in_=wqw.rearrange("(k p) c -> k p c", p=128)[et]), 16)
            s_w.inc(g.dma_start(out=wk_f[:, et * DK:(et + 1) * DK],
                                in_=wkw.rearrange("(k p) c -> k p c", p=128)[et]), 16)
        s_w.inc(g.dma_start(out=rws[:, :], in_=rwm[:, :]), 16)
        s_w.inc(g.dma_start(out=hs2b[:, :], in_=hs2[:, :]), 16)
        s_w.inc(g.dma_start(out=csT[:, 0:CW], in_=cosT[:, :]), 16)
        s_w.inc(g.dma_start(out=csT[:, CW:2 * CW], in_=sinT[:, :]), 16)
        for et in range(KT):
            if et > 0:
                g.wait_ge(s_wc.h, 32 * et - 16)
            s_wc.inc(g.dma_start(out=wcast[:, :], in_=wv_t[et][:, :]), 16)
            g.wait_ge(s_wc.h, 32 * et + 16)
            s_wc.inc(g.dma_start(out=wvb_t[et][:, :], in_=wcast[:, :]), 16)
    NW = s_w.n
    NWC = s_wc.n

    for pc in range(NPC):
        p0 = pc * PW
        PJ0 = s_pj.n
        PE0 = s_pe.n
        RT0 = s_rt.n
        VT0 = s_vt.n
        QD0 = s_qd.n
        RP0 = s_rp.n

        @env.block.sync
        def _(s, pc=pc, p0=p0, PJ0=PJ0):
            if pc == 0:
                s.wait_ge(s_wc.h, NWC)
            if pc > 0:
                s.wait_ge(s_pj.h, PJ0)
            for k in range(KT):
                s_xc.inc(s.dma_start(
                    out=xc_sb[:, k * CW:(k + 1) * CW].rearrange("p (b q) -> p b q", b=4),
                    in_=xc_t[k].rearrange("p (b q) -> p b q", b=4)[:, :, p0:p0 + PW]), 16)

        for sl in range(8):
            @env.block.sync
            def _(s, pc=pc, sl=sl, PJ0=PJ0):
                if sl >= 2:
                    s.wait_ge(s_pj.h, PJ0 + 16 + 2 * (sl - 2) + 2)
                elif pc > 0:
                    s.wait_ge(s_pj.h, PJ0)
                for et in range(KT):
                    s_wv.inc(s.dma_start(
                        out=wv_sl[:, (sl % 2) * KT * 256 + et * 256:
                                  (sl % 2) * KT * 256 + (et + 1) * 256],
                        in_=wvb_t[et][:, sl * 256:(sl + 1) * 256]), 16)

            for h in ([2 * (sl - 1), 2 * (sl - 1) + 1] if sl >= 1 else []):
                @env.block.sync
                def _(s, pc=pc, h=h, PE0=PE0, VT0=VT0):
                    s.wait_ge(s_pe.h, PE0 + 16 + h + 1)
                    s_vt.inc(s.dma_start(
                        out=v_tm.ap().rearrange("p (g x) -> p g x", g=4)[:, :, h * 128:(h + 1) * 128],
                        in_=v_sb[:, (h % 2) * CW:(h % 2 + 1) * CW], transpose=True), 16)

        for h in (14, 15):
            @env.block.sync
            def _(s, pc=pc, h=h, PE0=PE0, VT0=VT0):
                s.wait_ge(s_pe.h, PE0 + 16 + h + 1)
                s_vt.inc(s.dma_start(
                    out=v_tm.ap().rearrange("p (g x) -> p g x", g=4)[:, :, h * 128:(h + 1) * 128],
                    in_=v_sb[:, (h % 2) * CW:(h % 2 + 1) * CW], transpose=True), 16)

        for gt in range(32):
            @env.block.tensor
            def _(t, pc=pc, gt=gt, PJ0=PJ0, PE0=PE0, RT0=RT0):
                if gt == 0:
                    t.wait_ge(s_w.h, NW)
                    t.wait_ge(s_xc.h, 16 * 16 * (pc + 1))
                if gt >= 2:
                    t.wait_ge(s_pe.h, PE0 + gt - 1)
                if gt >= 16:
                    sl = (gt - 16) // 2
                    t.wait_ge(s_wv.h, 16 * 16 * (8 * pc + sl + 1))
                for et in range(KT):
                    if gt < 8:
                        lhs = wq_f[:, et * DK + gt * 128:et * DK + (gt + 1) * 128]
                    elif gt < 16:
                        lhs = wk_f[:, et * DK + (gt - 8) * 128:et * DK + (gt - 7) * 128]
                    else:
                        sl = (gt - 16) // 2
                        half = (gt - 16) % 2
                        lhs = wv_sl[:, (sl % 2) * KT * 256 + et * 256 + half * 128:
                                    (sl % 2) * KT * 256 + et * 256 + (half + 1) * 128]
                    ins = t.matmul(psP[gt % 2][:, :], lhs, xc_sb[:, et * CW:(et + 1) * CW],
                                   start=(et == 0), stop=(et == KT - 1))
                s_pj.inc(ins)
                if gt >= 17:
                    h = gt - 17
                    t.wait_ge(s_pe.h, PE0 + 16 + h + 1)
                    for g in range(4):
                        ins = t.matmul(psR[:, 2 * (g * 16 + h):2 * (g * 16 + h) + 2],
                                       v_sb[:, (h % 2) * CW + g * 128:(h % 2) * CW + (g + 1) * 128],
                                       rws[:, 2 * h:2 * h + 2], start=True, stop=True)
                    s_rt.inc(ins)

            @env.block.scalar
            def _(sc, pc=pc, gt=gt, PJ0=PJ0, RP0=RP0, RT0=RT0, VT0=VT0):
                sc.wait_ge(s_pj.h, PJ0 + gt + 1)
                if gt < 16:
                    if gt >= 2:
                        sc.wait_ge(s_rp.h, RP0 + gt - 1)
                    dst = qsil[:, (gt % 2) * CW:(gt % 2 + 1) * CW]
                    fn = AF.Silu if gt < 8 else AF.Copy
                    ins = sc.activation(dst, psP[gt % 2][:, :], fn)
                else:
                    h = gt - 16
                    if h >= 2:
                        sc.wait_ge(s_rt.h, RT0 + h - 1)
                        sc.wait_ge(s_vt.h, VT0 + 16 * (h - 1))
                    ins = sc.copy(v_sb[:, (h % 2) * CW:(h % 2 + 1) * CW], psP[gt % 2][:, :])
                s_pe.inc(ins)

            if gt < 16:
                @env.block.vector
                def _(v, pc=pc, gt=gt, PE0=PE0):
                    v.wait_ge(s_pe.h, PE0 + gt + 1)
                    env.vfence(v)
                    Q = qsil[:, (gt % 2) * CW:(gt % 2 + 1) * CW]
                    dst = (q_sb if gt < 8 else k_sb)
                    tb = (gt % 8) * CW
                    cosv, sinv = csT[:, 0:CW], csT[:, CW:2 * CW]
                    v.tensor_copy(rpt[0:64, :], Q[64:128, :])
                    v.tensor_copy(rpt[64:128, :], Q[0:64, :])
                    v.tensor_mul(rp2[0:64, :], rpt[0:64, :], sinv[0:64, :])
                    v.tensor_mul(rpt[0:64, :], Q[0:64, :], cosv[0:64, :])
                    v.tensor_sub(dst[0:64, tb:tb + CW], rpt[0:64, :], rp2[0:64, :])
                    v.tensor_mul(rp2[64:128, :], rpt[64:128, :], sinv[64:128, :])
                    v.tensor_mul(rpt[64:128, :], Q[64:128, :], cosv[64:128, :])
                    ins = v.tensor_add(dst[64:128, tb:tb + CW], rpt[64:128, :], rp2[64:128, :])
                    s_rp.inc(ins)

        @env.block.tensor
        def _(t, pc=pc, PE0=PE0):
            h = 15
            t.wait_ge(s_pe.h, PE0 + 16 + h + 1)
            for g in range(4):
                ins = t.matmul(psR[:, 2 * (g * 16 + h):2 * (g * 16 + h) + 2],
                               v_sb[:, (h % 2) * CW + g * 128:(h % 2) * CW + (g + 1) * 128],
                               rws[:, 2 * h:2 * h + 2], start=True, stop=True)
            s_rt.inc(ins)

        RD, RS0, RS1, RI0, RI1, RN0, RN1, RTMP = [rtt.ap()[:, i * 64:(i + 1) * 64]
                                                  for i in range(8)]

        @env.block.vector
        def _(v, pc=pc, RT0=RT0):
            v.wait_ge(s_rt.h, RT0 + 16)
            env.vfence(v)
            l0 = psR.ap().rearrange("p (s r) -> p s r", r=2)[:, :, 0]
            l1 = psR.ap().rearrange("p (s r) -> p s r", r=2)[:, :, 1]
            v.tensor_copy(RD, l0)
            v.tensor_copy(RD, l0)
            v.tensor_copy(RTMP, l1)
            env.vspace(v)
            ins = v.tensor_sub(RD, RD, RTMP)
            s_rv.inc(env.vfence(v))

        @env.block.scalar
        def _(sc, pc=pc):
            sc.wait_ge(s_rv.h, 2 * pc + 1)
            env.sfence(sc)
            sc.activation(RS0, RD, AF.Sigmoid)
            s_sg.inc(env.sfence(sc))

        @env.block.vector
        def _(v, pc=pc):
            v.wait_ge(s_sg.h, pc + 1)
            env.vfence(v)
            v.tensor_scalar(RS1, RS0, -1.0, 1.0, ALU.mult, ALU.add)
            v.tensor_scalar(RI0, RD, 0.0, None, ALU.is_ge)
            env.vspace(v)
            v.tensor_scalar(RI1, RI0, -1.0, 1.0, ALU.mult, ALU.add)
            v.tensor_mul(RS0, RS0, RI0)
            v.tensor_mul(RS1, RS1, RI1)
            for RN, RI in ((RN0, RI0), (RN1, RI1)):
                v.tensor_copy(RN[:, 0:16], RI[:, 0:16])
                for g in range(1, 4):
                    env.vspace(v)
                    v.tensor_add(RN[:, 16 * g:16 * (g + 1)], RN[:, 16 * (g - 1):16 * g],
                                 RI[:, 16 * g:16 * (g + 1)])
            for RN in (RN0, RN1):
                env.vspace(v)
                v.tensor_scalar(RTMP, RN, 0.5, None, ALU.is_ge)
                v.tensor_scalar(RN, RN, 1.0, None, ALU.max)
                env.vspace(v)
                v.reciprocal(RN, RN)
                env.vfence(v)
                v.tensor_mul(RN, RN, RTMP)
            env.vspace(v)
            v.tensor_mul(RS0, RS0, RN0)
            v.tensor_mul(RS1, RS1, RN1)
            v.tensor_scalar(RS0, RS0, 0.125, None, ALU.mult)
            v.tensor_scalar(RS1, RS1, 0.125, None, ALU.mult)
            env.vspace(v)
            for pidx, (i, j) in enumerate(PAIRS):
                ms = mmt[:, pidx * 16:(pidx + 1) * 16]
                v.tensor_mul(ms, RS0[:, 16 * i:16 * (i + 1)], RI0[:, 16 * j:16 * (j + 1)])
                v.tensor_mul(RTMP[:, 0:16], RS1[:, 16 * i:16 * (i + 1)],
                             RI1[:, 16 * j:16 * (j + 1)])
                env.vspace(v)
                v.tensor_add(ms, ms, RTMP[:, 0:16])
            s_rv.inc(v.engine_nop())

            for qt in range(8):
                v.wait_ge(s_rp.h, RP0 + 16)
                if qt >= 2:
                    v.wait_ge(s_qd.h, QD0 + qt - 1)
                for pidx, (i, j) in enumerate(PAIRS):
                    ins = v.tensor_mul(
                        prodb[:, (qt % 2) * 1280 + pidx * 128:(qt % 2) * 1280 + (pidx + 1) * 128],
                        q_sb[:, qt * CW + i * 128:qt * CW + (i + 1) * 128],
                        k_sb[:, qt * CW + j * 128:qt * CW + (j + 1) * 128])
                s_pb.inc(ins)

        @env.block.tensor
        def _(t, pc=pc, QD0=QD0):
            for qt in range(8):
                t.wait_ge(s_pb.h, 8 * pc + qt + 1)
                if pc > 0 and qt == 0:
                    t.wait_ge(s_cm.h, pc)
                for pidx in range(len(PAIRS)):
                    ins = t.matmul(psQ[:, pidx * 16 + 2 * qt:pidx * 16 + 2 * qt + 2],
                                   prodb[:, (qt % 2) * 1280 + pidx * 128:
                                         (qt % 2) * 1280 + (pidx + 1) * 128],
                                   hs2b[:, :], start=True, stop=True)
                s_qd.inc(ins)

        @env.block.vector
        def _(v, pc=pc, QD0=QD0):
            v.wait_ge(s_qd.h, QD0 + 8)
            env.vfence(v)
            v.tensor_copy(ctf[:, :], psQ[:, :])
            v.tensor_copy(ctf[:, :], psQ[:, :])
            env.vspace(v)
            ins = v.tensor_mul(ctf[:, :], ctf[:, :], mmt[:, :])
            s_cm.inc(ins)

        for i in range(4):
            @env.block.vector
            def _(v, pc=pc, i=i, VT0=VT0):
                if i == 0:
                    v.wait_ge(s_vt.h, VT0 + 16 * 16)
                    env.vfence(v)
                if pc > 0 or i > 0:
                    v.wait_ge(s_so.h, 16 * (4 * pc + i))
                pmap = {(a, b): px for px, (a, b) in enumerate(PAIRS)}
                for h in range(16):
                    for j in range(i + 1):
                        px = pmap[(i, j)]
                        csc = ctf[:, px * 16 + h:px * 16 + h + 1]
                        vsl = v_tm[:, j * D + h * 128:j * D + (h + 1) * 128]
                        osl = o_i[:, h * 128:(h + 1) * 128]
                        if j == 0:
                            v.tensor_scalar(osl, vsl, csc, None, ALU.mult)
                        else:
                            v.tensor_scalar(otmp[:, :], vsl, csc, None, ALU.mult)
                            v.tensor_add(osl, osl, otmp[:, :])
                env.vspace(v)
                for h in range(16):
                    v.reduce_sum(st16[:, h:h + 1], o_i[:, h * 128:(h + 1) * 128], axis=AX.X)
                for h in range(16):
                    v.tensor_mul(otmp[:, :], o_i[:, h * 128:(h + 1) * 128],
                                 o_i[:, h * 128:(h + 1) * 128])
                    env.vspace(v)
                    v.reduce_sum(st16[:, 16 + h:17 + h], otmp[:, :], axis=AX.X)
                env.vspace(v)
                v.tensor_scalar(st16[:, 0:16], st16[:, 0:16], 1.0 / 128, None, ALU.mult)
                v.tensor_scalar(st16[:, 16:32], st16[:, 16:32], 1.0 / 128, None, ALU.mult)
                env.vspace(v)
                v.tensor_mul(st16[:, 32:48], st16[:, 0:16], st16[:, 0:16])
                env.vspace(v)
                v.tensor_sub(st16[:, 32:48], st16[:, 16:32], st16[:, 32:48])
                env.vspace(v)
                v.tensor_scalar(st16[:, 32:48], st16[:, 32:48], EPS, None, ALU.add)
                env.vspace(v)
                v.reciprocal(st16[:, 32:48], st16[:, 32:48])
                s_iv.inc(env.vfence(v))

            @env.block.scalar
            def _(sc, pc=pc, i=i):
                sc.wait_ge(s_iv.h, 4 * pc + i + 1)
                env.sfence(sc)
                sc.activation(st16[:, 48:64], st16[:, 32:48], AF.Sqrt)
                s_sq.inc(env.sfence(sc))

            @env.block.vector
            def _(v, pc=pc, i=i):
                v.wait_ge(s_sq.h, 4 * pc + i + 1)
                env.vfence(v)
                for h in range(16):
                    v.tensor_scalar(o_n[:, h * 128:(h + 1) * 128],
                                    o_i[:, h * 128:(h + 1) * 128],
                                    st16[:, h:h + 1], st16[:, 48 + h:49 + h],
                                    ALU.subtract, ALU.mult)
                s_oa.inc(env.vfence(v))

            @env.block.sync
            def _(s, pc=pc, i=i, p0=p0):
                s.wait_ge(s_oa.h, 4 * pc + i + 1)
                s_so.inc(s.dma_start(out=onrm[i][p0:p0 + PW, :], in_=o_n[:, :]), 16)

    @env.block.sync
    def _(s):
        s.wait_ge(s_so.h, s_so.n)

    return env.close()


def host_prep_C2(inputs):
    Wq, Wk, Wv = inputs["L0_ffn_Wq"], inputs["L0_ffn_Wk"], inputs["L0_ffn_Wv"]
    rw = inputs["L0_ffn_router_w"]
    perm = []
    for qt in range(8):
        for h in (2 * qt, 2 * qt + 1):
            perm.extend(range(h * 64, h * 64 + 32))
        for h in (2 * qt, 2 * qt + 1):
            perm.extend(range(h * 64 + 32, h * 64 + 64))
    perm = np.asarray(perm)
    wqw = np.ascontiguousarray(Wq[:, perm]).astype(np.float32)
    wkw = np.ascontiguousarray(Wk[:, perm]).astype(np.float32)
    wvw = np.ascontiguousarray(Wv).astype(np.float32)
    rwm = np.zeros((128, 32), dtype=np.float32)
    for h in range(16):
        rwm[:, 2 * h] = rw[h, :, 0]
        rwm[:, 2 * h + 1] = rw[h, :, 1]
    invf = 1.0 / (10000.0 ** (np.arange(32, dtype=np.float64) / 32.0))
    ang = np.outer(invf, np.arange(4, dtype=np.float64))
    cosT = np.zeros((128, 512), dtype=np.float32)
    sinT = np.zeros((128, 512), dtype=np.float32)
    for r in range(128):
        for g in range(4):
            cosT[r, g * 128:(g + 1) * 128] = np.cos(ang[r % 32, g])
            sinT[r, g * 128:(g + 1) * 128] = np.sin(ang[r % 32, g])
    hs2 = np.zeros((128, 2), dtype=np.float32)
    hs2[0:32, 0] = 1.0
    hs2[64:96, 0] = 1.0
    hs2[32:64, 1] = 1.0
    hs2[96:128, 1] = 1.0
    m = dict(wqw=wqw, wkw=wkw, wvw=wvw, rwm=rwm, cosT=cosT, sinT=sinT, hs2=hs2)
    return [dict(m) for _ in range(NCORE)]



def build_launch_C3(T_=TOWN):
    env = Env("C3")
    nc = env.nc
    CW = 256
    NCH = T_ // CW

    xco = nc.dram_tensor("xco", [D, T_], BF16, kind="ExternalInput").ap()
    onc = nc.dram_tensor("onc", [D, T_], BF16, kind="ExternalInput").ap()
    x1o = nc.dram_tensor("x1o", [D, T_], F32, kind="ExternalInput").ap()
    wgw = nc.dram_tensor("wgw", [D, D], F32, kind="ExternalInput").ap()
    wow = nc.dram_tensor("wow", [D, D], F32, kind="ExternalInput").ap()
    x2o = nc.dram_tensor("x2o", [D, T_], F32, kind="ExternalOutput").ap()
    xn2o = nc.dram_tensor("xn2o", [D, T_], BF16, kind="ExternalOutput").ap()
    xc_t = dram_tiles(xco)
    on_t = dram_tiles(onc)
    x1_t = dram_tiles(x1o)
    wg_t = dram_tiles(wgw)
    wo_t = dram_tiles(wow)
    x2_t = dram_tiles(x2o)
    xn_t = dram_tiles(xn2o)

    wg_sb = env.sbuf("wg_sb", [128, KT * D], BF16)
    wo_sb = env.sbuf("wo_sb", [128, KT * D], BF16)
    xc_sb = env.sbuf("xc_sb", [128, KT * CW], BF16)
    onp = env.sbuf("onp", [128, 4 * CW], BF16)
    gsil = env.sbuf("gsil", [128, 2 * CW], BF16)
    fact = env.sbuf("fact", [128, KT * CW], BF16)
    x1p = env.sbuf("x1p", [128, 4 * CW], F32)
    x2p = env.sbuf("x2p", [128, 4 * CW], F32)
    xb = env.sbuf("xb", [128, KT * CW], BF16)
    sqb = env.sbuf("sqb", [128, KT * CW], BF16)
    row = env.sbuf("row", [1, 6 * CW], F32)
    rB = env.sbuf("rB", [128, CW], BF16)
    mB = env.sbuf("mB", [128, CW], BF16)
    tmpb = env.sbuf("tmpb", [128, CW], BF16)
    xnb = env.sbuf("xnb", [128, 2 * CW], BF16)
    psG = [env.psum("c3_g0", [128, CW]), env.psum("c3_g1", [128, CW])]
    psW = [env.psum("c3_w0", [128, CW]), env.psum("c3_w1", [128, CW])]
    psA = env.psum("c3_psA", [1, CW])
    psB = env.psum("c3_psB", [1, CW])
    psR = env.psum("c3_psR", [128, CW])
    psM = env.psum("c3_psM", [128, CW])
    ones_b = env.sbuf("ones_b", [128, 1], BF16)
    ones_f = env.sbuf("ones_f", [1, 128], F32)
    env.set_junk(env.sbuf("junk", [1, 1024], F32))

    s_w = env.sem("e_w")
    s_ld = env.sem("e_ld")
    s_gm = env.sem("e_gm")
    s_gs = env.sem("e_gs")
    s_fa = env.sem("e_fa")
    s_wm = env.sem("e_wm")
    s_ev = env.sem("e_ev")
    s_sx = env.sem("e_sx")
    s_sq = env.sem("e_sq")
    s_st = env.sem("e_st")
    s_rw = env.sem("e_rw")
    s_e2 = env.sem("e_e2")
    s_nm = env.sem("e_nm")
    s_sn = env.sem("e_sn")
    ini = env.sem("ini")

    def evn(t):
        return t + 1 + t // KT

    env.open_block()
    emit_consts(env, ones_b, ones_f)

    @env.block.gpsimd
    def _(g):
        for et in range(KT):
            s_w.inc(g.dma_start(out=wg_sb[:, et * D:(et + 1) * D], in_=wg_t[et][:, :]), 16)
        for et in range(KT):
            s_w.inc(g.dma_start(out=wo_sb[:, et * D:(et + 1) * D], in_=wo_t[et][:, :]), 16)
    NW = s_w.n

    for ci in range(NCH):
        c0, c1 = ci * CW, (ci + 1) * CW

        @env.block.gpsimd
        def _(g, ci=ci, c0=c0, c1=c1):
            if ci > 0:
                g.wait_ge(s_gm.h, 16 * ci)
            for k in range(KT):
                s_ld.inc(g.dma_start(out=xc_sb[:, k * CW:(k + 1) * CW],
                                     in_=xc_t[k][:, c0:c1]), 16)
            for dt in range(KT):
                gidx = ci * KT + dt
                if gidx >= 4:
                    g.wait_ge(s_fa.h, (gidx - 4) + 1 + (gidx - 4) // KT)
                s_ld.inc(g.dma_start(out=onp[:, (gidx % 4) * CW:(gidx % 4 + 1) * CW],
                                     in_=on_t[dt][:, c0:c1]), 16)
            for dt in range(KT):
                gidx = ci * KT + dt
                if gidx >= 4:
                    g.wait_ge(s_ev.h, evn(gidx - 4))
                s_ld.inc(g.dma_start(out=x1p[:, (gidx % 4) * CW:(gidx % 4 + 1) * CW],
                                     in_=x1_t[dt][:, c0:c1]), 16)

        for dt in range(KT):
            gidx = ci * KT + dt

            @env.block.tensor
            def _(t, ci=ci, dt=dt, gidx=gidx):
                if gidx == 0:
                    t.wait_ge(s_w.h, NW)
                t.wait_ge(s_ld.h, 16 * (48 * ci + 16))
                if gidx >= 2:
                    t.wait_ge(s_gs.h, gidx - 1)
                for et in range(KT):
                    ins = t.matmul(psG[gidx % 2][:, :],
                                   wg_sb[:, et * D + dt * 128:et * D + (dt + 1) * 128],
                                   xc_sb[:, et * CW:(et + 1) * CW],
                                   start=(et == 0), stop=(et == KT - 1))
                s_gm.inc(ins)

            @env.block.scalar
            def _(sc, gidx=gidx):
                sc.wait_ge(s_gm.h, gidx + 1)
                if gidx >= 2:
                    sc.wait_ge(s_fa.h, (gidx - 2) + 1 + (gidx - 2) // KT)
                ins = sc.activation(gsil[:, (gidx % 2) * CW:(gidx % 2 + 1) * CW],
                                    psG[gidx % 2][:, :], AF.Silu)
                s_gs.inc(ins)

            @env.block.vector
            def _(v, ci=ci, dt=dt, gidx=gidx):
                v.wait_ge(s_gs.h, gidx + 1)
                v.wait_ge(s_ld.h, 16 * (48 * ci + 16 + dt + 1))
                if ci > 0 and dt == 0:
                    v.wait_ge(s_wm.h, 16 * ci)
                env.vfence(v)
                ins = v.tensor_mul(fact[:, dt * CW:(dt + 1) * CW],
                                   gsil[:, (gidx % 2) * CW:(gidx % 2 + 1) * CW],
                                   onp[:, (gidx % 4) * CW:(gidx % 4 + 1) * CW])
                s_fa.inc(ins)

        @env.block.vector
        def _(v):
            s_fa.inc(env.vfence(v))

        for dt in range(KT):
            gidx = ci * KT + dt

            @env.block.tensor
            def _(t, ci=ci, dt=dt, gidx=gidx):
                t.wait_ge(s_fa.h, 17 * (ci + 1))
                if gidx >= 2:
                    t.wait_ge(s_ev.h, evn(gidx - 2))
                for et in range(KT):
                    ins = t.matmul(psW[gidx % 2][:, :],
                                   wo_sb[:, et * D + dt * 128:et * D + (dt + 1) * 128],
                                   fact[:, et * CW:(et + 1) * CW],
                                   start=(et == 0), stop=(et == KT - 1))
                s_wm.inc(ins)

            @env.block.vector
            def _(v, ci=ci, dt=dt, gidx=gidx):
                v.wait_ge(s_wm.h, gidx + 1)
                v.wait_ge(s_ld.h, 16 * (48 * ci + 32 + dt + 1))
                if gidx >= 4:
                    v.wait_ge(s_sx.h, 16 * (gidx - 3))
                if dt == 0 and ci > 0:
                    v.wait_ge(s_sq.h, ci)
                env.vfence(v)
                ins = v.tensor_add(x2p[:, (gidx % 4) * CW:(gidx % 4 + 1) * CW],
                                   psW[gidx % 2][:, :],
                                   x1p[:, (gidx % 4) * CW:(gidx % 4 + 1) * CW])
                s_ev.inc(ins)

            @env.block.sync
            def _(s, ci=ci, dt=dt, gidx=gidx, c0=c0, c1=c1):
                s.wait_ge(s_ev.h, min(evn(gidx + 1), 17 * (ci + 1)))
                s_sx.inc(s.dma_start(out=x2_t[dt][:, c0:c1],
                                     in_=x2p[:, (gidx % 4) * CW:(gidx % 4 + 1) * CW]), 16)

        @env.block.vector
        def _(v):
            s_ev.inc(env.vfence(v))

        @env.block.scalar
        def _(sc, ci=ci):
            for dt in range(KT):
                gidx = ci * KT + dt
                sc.wait_ge(s_ev.h, min(evn(gidx + 1), 17 * (ci + 1)))
                sc.copy(xb[:, dt * CW:(dt + 1) * CW],
                        x2p[:, (gidx % 4) * CW:(gidx % 4 + 1) * CW])
                sc.activation(sqb[:, dt * CW:(dt + 1) * CW], xb[:, dt * CW:(dt + 1) * CW],
                              AF.Square)
            s_sq.inc(env.sfence(sc))

        @env.block.tensor
        def _(t, ci=ci):
            t.wait_ge(ini.h, 2)
            t.wait_ge(s_sq.h, ci + 1)
            if ci >= 1:
                t.wait_ge(s_rw.h, 3 * (ci - 1) + 1)
            for k in range(KT):
                t.matmul(psA[0:1, :], ones_b[:, :], xb[:, k * CW:(k + 1) * CW],
                         start=(k == 0), stop=(k == KT - 1))
            for k in range(KT):
                ins = t.matmul(psB[0:1, :], ones_b[:, :], sqb[:, k * CW:(k + 1) * CW],
                               start=(k == 0), stop=(k == KT - 1))
            s_st.inc(ins)

        @env.block.vector
        def _(v, ci=ci):
            v.wait_ge(s_st.h, 2 * ci + 1)
            env.vfence(v)
            v.tensor_scalar(row[0:1, 0:CW], psA[0:1, :], 1.0 / D, None, ALU.mult)
            v.tensor_scalar(row[0:1, CW:2 * CW], psB[0:1, :], 1.0 / D, None, ALU.mult)
            v.tensor_mul(row[0:1, 4 * CW:5 * CW], row[0:1, 0:CW], row[0:1, 0:CW])
            v.tensor_sub(row[0:1, 2 * CW:3 * CW], row[0:1, CW:2 * CW], row[0:1, 4 * CW:5 * CW])
            v.tensor_scalar(row[0:1, 2 * CW:3 * CW], row[0:1, 2 * CW:3 * CW], EPS, None, ALU.add)
            ins = v.reciprocal(row[0:1, 2 * CW:3 * CW], row[0:1, 2 * CW:3 * CW])
            s_rw.inc(ins)
            v.wait_ge(s_rw.h, 3 * ci + 2)
            v.tensor_mul(row[0:1, 4 * CW:5 * CW], row[0:1, 0:CW], row[0:1, 3 * CW:4 * CW])
            s_rw.inc(env.vfence(v))

        @env.block.scalar
        def _(sc, ci=ci):
            sc.wait_ge(s_rw.h, 3 * ci + 1)
            env.sfence(sc)
            sc.activation(row[0:1, 3 * CW:4 * CW], row[0:1, 2 * CW:3 * CW], AF.Sqrt)
            s_rw.inc(env.sfence(sc))

        @env.block.tensor
        def _(t, ci=ci):
            t.wait_ge(s_rw.h, 3 * ci + 3)
            if ci >= 1:
                t.wait_ge(s_e2.h, 2 * ci)
            t.matmul(psR[:, :], ones_f[:, :], row[0:1, 3 * CW:4 * CW], start=True, stop=True)
            ins = t.matmul(psM[:, :], ones_f[:, :], row[0:1, 4 * CW:5 * CW], start=True, stop=True)
            s_st.inc(ins)

        @env.block.vector
        def _(v, ci=ci):
            v.wait_ge(s_st.h, 2 * ci + 2)
            env.vfence(v)
            s_e2.inc(v.tensor_copy(rB[:, :], psR[:, :]))
            s_e2.inc(v.tensor_copy(mB[:, :], psM[:, :]))
            for k in range(KT):
                gk = ci * KT + k
                if gk >= 2:
                    v.wait_ge(s_sn.h, 16 * (gk - 1))
                v.tensor_mul(tmpb[:, :], xb[:, k * CW:(k + 1) * CW], rB[:, :])
                ins = v.tensor_sub(xnb[:, (gk % 2) * CW:(gk % 2 + 1) * CW],
                                   tmpb[:, :], mB[:, :])
                s_nm.inc(ins)
            s_nm.inc(env.vfence(v))

        @env.block.sync
        def _(s, ci=ci, c0=c0, c1=c1):
            for k in range(KT):
                gk = ci * KT + k
                s.wait_ge(s_nm.h, min(gk + 2 + gk // KT, 17 * (ci + 1)))
                s_sn.inc(s.dma_start(out=xn_t[k][:, c0:c1],
                                     in_=xnb[:, (gk % 2) * CW:(gk % 2 + 1) * CW]), 16)

    @env.block.sync
    def _(s):
        s.wait_ge(s_sx.h, s_sx.n)
        s.wait_ge(s_sn.h, s_sn.n)

    return env.close()


def host_prep_C3(inputs):
    Wg = np.ascontiguousarray(inputs["L0_ffn_Wg"]).astype(np.float32)
    Wo = np.ascontiguousarray(inputs["L0_ffn_Wout"]).astype(np.float32)
    m = dict(wgw=Wg, wow=Wo)
    return [dict(m) for _ in range(NCORE)]


def onrm_to_ch_major(onrm_arr):
    a = np.asarray(onrm_arr)
    return np.ascontiguousarray(a.transpose(2, 0, 1).reshape(D, TOWN))


def host_prep_C1(inputs, oat_full):
    Wo = inputs["L0_attn_Wo"]
    cw = inputs["L0_ffn_conv_w"]
    woT = np.ascontiguousarray(Wo.T).astype(np.float32)
    cvw = np.zeros((128, 2 * KT), dtype=np.float32)
    for k in range(KT):
        cvw[:, 2 * k] = cw[k * 128:(k + 1) * 128, 0]
        cvw[:, 2 * k + 1] = cw[k * 128:(k + 1) * 128, 1]
    hs = inputs["L0_attn_hidden_states"]
    maps = []
    for c in range(NCORE):
        cols = np.concatenate([np.arange(b_ * S + SOWN * c, b_ * S + SOWN * (c + 1))
                               for b_ in range(B)])
        oatc = np.ascontiguousarray(oat_full[:, cols])
        x0c = np.ascontiguousarray(
            hs[:, SOWN * c:SOWN * (c + 1), :].transpose(2, 0, 1).reshape(D, TOWN)
        ).astype(np.float32)
        maps.append(dict(oat=oatc, x0c=x0c, woT=woT, cvw=cvw))
    return maps


def unpack_oslc(oslc_arr, S_=S):
    o4 = np.asarray(oslc_arr).astype(np.float32).reshape(2, B, 128, S_)
    nqt = S_ // 128
    got = np.zeros((256, B * S_), dtype=np.float32)
    for hp in range(2):
        for b_ in range(B):
            arr = o4[hp, b_].reshape(128, nqt, 128)
            for h in range(2):
                j = 2 * hp + h
                got[j * 64:(j + 1) * 64, b_ * S_:(b_ + 1) * S_] = \
                    arr[:, :, h * 64:(h + 1) * 64].transpose(2, 1, 0).reshape(64, S_)
    return got



def build_launch_D():
    env = Env("DL")
    nc = env.nc
    LK = 4096
    NQT = 16
    NCH = LK // 512

    def nco(qt):
        return qt // 4 + 5

    xq = nc.dram_tensor("xq", [D, 2048], BF16, kind="ExternalInput").ap()
    xa = nc.dram_tensor("xa", [D, LK], BF16, kind="ExternalInput").ap()
    xv = nc.dram_tensor("xv", [LK, D], BF16, kind="ExternalInput").ap()
    wsw = nc.dram_tensor("wsw", [D, 256], F32, kind="ExternalInput").ap()
    waw = nc.dram_tensor("waw", [D, 256], F32, kind="ExternalInput").ap()
    brevD = nc.dram_tensor("brevD", [1, 6144], F32, kind="ExternalInput").ap()
    jmm = nc.dram_tensor("jmm", [128, 128], F32, kind="ExternalInput").ap()
    otd = nc.dram_tensor("otd", [2048, D], BF16, kind="ExternalOutput").ap()
    ptd = nc.dram_tensor("ptd", [NQT, 32, 128, 128], BF16, kind="Internal").ap()
    xa_t = dram_tiles(xa)
    xq_t = dram_tiles(xq)
    xvv = xv.rearrange("(t p) d -> t p d", p=128)

    ws_sb = env.sbuf("ws_sb", [128, KT * 256], BF16)
    wa_sb = env.sbuf("wa_sb", [128, KT * 256], BF16)
    jm_sb = env.sbuf("jm_sb", [128, 128], BF16)
    xa_st = env.sbuf("xa_st", [128, 2 * KT * 512], BF16)
    act_sb = env.sbuf("act_sb", [128, 2 * LK], BF16)
    st_sb = env.sbuf("st_sb", [128, 2 * 2048], BF16)
    sqs = env.sbuf("sqs", [128, 2 * 512], BF16)
    qn_sb = env.sbuf("qn_sb", [1, 2048], F32)
    an_sb = env.sbuf("an_sb", [1, LK + 16], F32)
    mrow = env.sbuf("mrow", [1, 2048], F32)
    negm16 = env.sbuf("negm16", [16, 2048], BF16)
    nmt16 = env.sbuf("nmt16", [128, 16 * 16], BF16)
    nmtf = env.sbuf("nmtf", [128, 16], F32)
    mskb = env.sbuf("mskb", [128, 8 * 512], BF16)
    p_sb = env.sbuf("p_sb", [128, 8 * 512], BF16)
    pt_sb = env.sbuf("pt_sb", [128, 32 * 128], BF16)
    den_sb = env.sbuf("den_sb", [128, 16], F32)
    dcol = env.sbuf("dcol", [128, 1], F32)
    vt_sb = env.sbuf("vt_sb", [128, 32 * 1024], BF16)
    ptb = act_sb
    ot_sb = p_sb
    ones_b = env.sbuf("ones_b", [128, 1], BF16)
    ones_f = env.sbuf("ones_f", [1, 128], F32)
    psA2 = [env.psum("dl_a0", [128, 512]), env.psum("dl_a1", [128, 512])]
    psS = [env.psum("dl_s0", [128, 512]), env.psum("dl_s1", [128, 512])]
    psO = [env.psum("dl_o0", [128, 512]), env.psum("dl_o1", [128, 512])]
    psN = env.psum("dl_n", [1, 512])
    env.set_junk(env.sbuf("junk", [1, 1024], F32))

    s_w = env.sem("f_w")
    s_mk = env.sem("f_mk")
    s_xa = env.sem("f_xa")
    s_pj = env.sem("f_pj")
    s_pe = env.sem("f_pe")
    s_nq = env.sem("f_nq")
    s_nr = env.sem("f_nr")
    s_pt = env.sem("f_pt")
    s_sc = env.sem("f_sc")
    s_px = env.sem("f_px")
    s_dn = env.sem("f_dn")
    s_ps = env.sem("f_ps")
    s_vl = env.sem("f_vl")
    s_pl = env.sem("f_pl")
    s_om = env.sem("f_om")
    s_oe = env.sem("f_oe")
    s_os = env.sem("f_os")
    ini = env.sem("ini")

    env.open_block()
    emit_consts(env, ones_b, ones_f)

    @env.block.gpsimd
    def _(g):
        for et in range(KT):
            s_w.inc(g.dma_start(out=ws_sb[:, et * 256:(et + 1) * 256],
                                in_=wsw.rearrange("(k p) c -> k p c", p=128)[et]), 16)
            s_w.inc(g.dma_start(out=wa_sb[:, et * 256:(et + 1) * 256],
                                in_=waw.rearrange("(k p) c -> k p c", p=128)[et]), 16)
        s_w.inc(g.dma_start(out=jm_sb[:, :], in_=jmm[:, :]), 16)
    NW = s_w.n

    PJN, PEN = [], []
    for ch in range(NCH + 4):
        is_state = ch >= NCH
        src_t = xq_t if is_state else xa_t
        co = (ch - NCH) if is_state else ch

        @env.block.sync
        def _(s, ch=ch, co=co, src_t=src_t):
            if ch >= 2:
                s.wait_ge(s_pj.h, PJN[ch - 2])
            for et in range(KT):
                s_xa.inc(s.dma_start(
                    out=xa_st[:, (ch % 2) * KT * 512 + et * 512:
                              (ch % 2) * KT * 512 + (et + 1) * 512],
                    in_=src_t[et][:, co * 512:(co + 1) * 512]), 16)

        @env.block.tensor
        def _(t, ch=ch, is_state=is_state):
            if ch == 0:
                t.wait_ge(s_w.h, NW)
            t.wait_ge(s_xa.h, 16 * KT * (ch + 1))
            if ch >= 1:
                t.wait_ge(s_pe.h, PEN[ch - 1])
            xo = (ch % 2) * KT * 512
            wsel = ws_sb if is_state else wa_sb
            for rt in range(2):
                for et in range(KT):
                    ins = t.matmul(psA2[rt][:, :],
                                   wsel[:, et * 256 + rt * 128:et * 256 + (rt + 1) * 128],
                                   xa_st[:, xo + et * 512:xo + (et + 1) * 512],
                                   start=(et == 0), stop=(et == KT - 1))
                s_pj.inc(ins)
        PJN.append(s_pj.n)

        @env.block.scalar
        def _(sc, ch=ch, co=co, is_state=is_state):
            sc.wait_ge(s_pj.h, PJN[ch])
            for rt in range(2):
                if is_state:
                    ins = sc.activation(st_sb[:, rt * 2048 + co * 512:rt * 2048 + (co + 1) * 512],
                                        psA2[rt][:, :], AF.Copy, scale=0.0625)
                else:
                    ins = sc.copy(act_sb[:, rt * LK + co * 512:rt * LK + (co + 1) * 512],
                                  psA2[rt][:, :])
            s_pe.inc(ins)
        PEN.append(s_pe.n)

    NPE = s_pe.n

    for which, nchk in ((0, 4), (1, 8)):
        for c in range(nchk):
            @env.block.scalar
            def _(sc, which=which, c=c):
                sc.wait_ge(s_pe.h, NPE)
                if not (which == 0 and c == 0):
                    sc.wait_ge(s_nq.h, s_nq.n)
                if which == 0 and c == 0:
                    env.sfence(sc)
                srcb = st_sb if which == 0 else act_sb
                W = 2048 if which == 0 else LK
                sc.activation(sqs[:, 0:512], srcb[:, 0 * W + c * 512:0 * W + (c + 1) * 512],
                              AF.Square)
                sc.activation(sqs[:, 512:1024], srcb[:, 1 * W + c * 512:1 * W + (c + 1) * 512],
                              AF.Square)
                s_nq.inc(env.sfence(sc))

            @env.block.tensor
            def _(t, which=which, c=c):
                t.wait_ge(ini.h, 2)
                t.wait_ge(s_nq.h, s_nq.n)
                if s_nr.n > 0:
                    t.wait_ge(s_nr.h, s_nr.n)
                t.matmul(psN[0:1, :], ones_b[:, :], sqs[:, 0:512], start=True, stop=False)
                ins = t.matmul(psN[0:1, :], ones_b[:, :], sqs[:, 512:1024],
                               start=False, stop=True)
                s_nq.inc(ins)

            @env.block.vector
            def _(v, which=which, c=c):
                v.wait_ge(s_nq.h, s_nq.n)
                env.vfence(v)
                dst = qn_sb if which == 0 else an_sb
                ins = v.tensor_copy(dst[0:1, c * 512:(c + 1) * 512], psN[0:1, :])
                s_nr.inc(ins)

    @env.block.vector
    def _(v):
        v.wait_ge(s_nr.h, s_nr.n)
        env.vspace(v)
        v.reduce_max(an_sb[0:1, LK:LK + 1], an_sb[0:1, 0:LK], axis=AX.X)
        env.vspace(v)
        v.memset(negm16[0:16, :], 0.0)
        v.tensor_scalar(mrow[0:1, :], qn_sb[0:1, :], an_sb[0:1, LK:LK + 1], None, ALU.mult)
        s_nr.inc(env.vfence(v))
    NM1 = s_nr.n

    @env.block.scalar
    def _(sc):
        sc.wait_ge(s_nr.h, NM1)
        env.sfence(sc)
        sc.activation(mrow[0:1, :], mrow[0:1, :], AF.Ln)
        sc.activation(mrow[0:1, :], mrow[0:1, :], AF.Exp, scale=0.5)
        s_nr.inc(env.sfence(sc))
    NM2 = s_nr.n

    @env.block.vector
    def _(v):
        v.wait_ge(s_nr.h, NM2)
        env.vfence(v)
        v.tensor_scalar(negm16[0:1, :], mrow[0:1, :], -1.0, None, ALU.mult)
        s_nr.inc(env.vfence(v))
    NM3 = s_nr.n

    @env.block.sync
    def _(s):
        s.wait_ge(s_nr.h, NM3)
        s_pt.inc(s.dma_start(out=nmt16.ap().rearrange("p (t r) -> p t r", r=16),
                             in_=negm16[:, :], transpose=True), 16)
    NPT0 = s_pt.n

    @env.block.scalar
    def _(sc):
        sc.wait_ge(s_pt.h, NPT0)
        ins = sc.copy(nmtf[:, :], nmt16.ap().rearrange("p (t r) -> p t r", r=16)[:, :, 0])
        s_px.inc(ins)
    PX0 = s_px.n

    SCN = []
    PXN = []
    DNN = []
    PTN = []
    for qt in range(NQT):
        nc_q = nco(qt)

        @env.block.gpsimd
        def _(g, qt=qt, nc_q=nc_q):
            g.wait_ge(s_pt.h, s_pt.n)
            if qt >= 1:
                g.wait_ge(s_sc.h, SCN[qt - 1])
            for kc in range(nc_q):
                srcap = AP(brevD.tensor, offset=1920 - qt * 128 + kc * 512,
                           ap=[[1, 128], [1, 512]])
                s_mk.inc(g.dma_start(
                    out=mskb[:, kc * 512:(kc + 1) * 512], in_=srcap), 16)
        MKN = s_mk.n

        for kc in range(nc_q):
            @env.block.tensor
            def _(t, qt=qt, kc=kc, MKN=MKN):
                t.wait_ge(s_mk.h, MKN)
                if kc == 0:
                    t.wait_ge(s_px.h, PX0)
                    if qt > 0:
                        t.wait_ge(s_pt.h, PTN[qt - 1])
                if s_px.n >= 2:
                    t.wait_ge(s_px.h, s_px.n - 1)
                t.matmul(psS[kc % 2][:, :], st_sb[:, 0 * 2048 + qt * 128:0 * 2048 + (qt + 1) * 128],
                         act_sb[:, 0 * LK + kc * 512:0 * LK + (kc + 1) * 512],
                         start=True, stop=False)
                t.matmul(psS[kc % 2][:, :], st_sb[:, 1 * 2048 + qt * 128:1 * 2048 + (qt + 1) * 128],
                         act_sb[:, 1 * LK + kc * 512:1 * LK + (kc + 1) * 512],
                         start=False, stop=False)
                ins = t.matmul(psS[kc % 2][:, :], jm_sb[:, :],
                               mskb[:, kc * 512:(kc + 1) * 512],
                               start=False, stop=True)
                s_sc.inc(ins)

            @env.block.scalar
            def _(sc, qt=qt, kc=kc):
                sc.wait_ge(s_sc.h, s_sc.n)
                if kc == 0 and qt > 0:
                    sc.wait_ge(s_dn.h, DNN[qt - 1])
                ins = sc.activation(p_sb[:, kc * 512:(kc + 1) * 512], psS[kc % 2][:, :],
                                    AF.Exp, bias=nmtf[:, qt:qt + 1],
                                    accum_out=den_sb[:, kc:kc + 1])
                s_px.inc(ins)
        SCN.append(s_sc.n)
        PXN.append(s_px.n)

        @env.block.vector
        def _(v, qt=qt, nc_q=nc_q):
            v.wait_ge(s_px.h, PXN[qt])
            env.vfence(v)
            v.reduce_sum(dcol[:, 0:1], den_sb[:, 0:nc_q], axis=AX.X)
            env.vspace(v)
            v.reciprocal(dcol[:, 0:1], dcol[:, 0:1])
            env.vfence(v)
            for kc in range(nc_q):
                v.tensor_scalar(p_sb[:, kc * 512:(kc + 1) * 512],
                                p_sb[:, kc * 512:(kc + 1) * 512],
                                dcol[:, 0:1], None, ALU.mult)
            s_dn.inc(env.vfence(v))
        DNN.append(s_dn.n)

        @env.block.sync
        def _(s, qt=qt, nc_q=nc_q):
            s.wait_ge(s_dn.h, DNN[qt])
            s_pt.inc(s.dma_start(
                out=pt_sb.ap()[:, 0:nc_q * 4 * 128].rearrange("p (t q) -> p t q", q=128),
                in_=p_sb[:, 0:nc_q * 512], transpose=True), 16)
            s_ps.inc(s.dma_start(
                out=ptd[qt].rearrange("t p q -> p t q")[:, 0:nc_q * 4, :],
                in_=pt_sb.ap()[:, 0:nc_q * 4 * 128].rearrange("p (t q) -> p t q", q=128)), 16)
        PTN.append(s_pt.n)
    NPS = s_ps.n

    OMN = {}
    for chh in range(2):
        @env.block.sync
        def _(s, chh=chh):
            if chh > 0:
                s.wait_ge(s_om.h, OMN[(0, NQT - 1, 1)])
            else:
                s.wait_ge(s_ps.h, NPS)
            for kt in range(32):
                s_vl.inc(s.dma_start(out=vt_sb[:, kt * 1024:(kt + 1) * 1024],
                                     in_=xvv[kt][:, chh * 1024:(chh + 1) * 1024]), 16)
        VLN = s_vl.n

        for qt in range(NQT):
            nc_q = nco(qt)

            @env.block.sync
            def _(s, chh=chh, qt=qt, nc_q=nc_q):
                if qt >= 2:
                    s.wait_ge(s_om.h, OMN[(chh, qt - 2, 1)])
                s_pl.inc(s.dma_start(
                    out=ptb.ap()[:, (qt % 2) * 4096:(qt % 2) * 4096 + nc_q * 4 * 128]
                        .rearrange("p (t q) -> p t q", q=128),
                    in_=ptd[qt].rearrange("t p q -> p t q")[:, 0:nc_q * 4, :]), 16)

            for c5 in range(2):
                @env.block.tensor
                def _(t, chh=chh, qt=qt, c5=c5, nc_q=nc_q, VLN=VLN):
                    t.wait_ge(s_vl.h, VLN)
                    t.wait_ge(s_pl.h, s_pl.n)
                    if s_oe.n >= 2:
                        t.wait_ge(s_oe.h, s_oe.n - 1)
                    for kt in range(nc_q * 4):
                        ins = t.matmul(psO[c5][:, :],
                                       ptb[:, (qt % 2) * 4096 + kt * 128:(qt % 2) * 4096 + (kt + 1) * 128],
                                       vt_sb[:, kt * 1024 + c5 * 512:kt * 1024 + (c5 + 1) * 512],
                                       start=(kt == 0), stop=(kt == nc_q * 4 - 1))
                    s_om.inc(ins)
                OMN[(chh, qt, c5)] = s_om.n

                @env.block.scalar
                def _(sc, c5=c5):
                    sc.wait_ge(s_om.h, s_om.n)
                    if s_os.n >= 32:
                        sc.wait_ge(s_os.h, s_os.n - 16)
                    ins = sc.copy(ot_sb[:, c5 * 512:(c5 + 1) * 512], psO[c5][:, :])
                    s_oe.inc(ins)

                @env.block.sync
                def _(s, chh=chh, qt=qt, c5=c5):
                    s.wait_ge(s_oe.h, s_oe.n)
                    s_os.inc(s.dma_start(
                        out=otd[qt * 128:(qt + 1) * 128,
                                chh * 1024 + c5 * 512:chh * 1024 + (c5 + 1) * 512],
                        in_=ot_sb[:, c5 * 512:(c5 + 1) * 512]), 16)
            OMN[(chh, qt, 1)] = s_om.n

    @env.block.sync
    def _(s):
        s.wait_ge(s_os.h, s_os.n)

    return env.close()


def host_prep_D(xn2_full):
    maps = []
    u = np.arange(6144)
    for c in range(NCORE):
        b, qh = c // 2, c % 2
        dd = (2047 - u) + qh * 2048
        brev = np.where(dd >= 0, 0.0, -1e30).astype(np.float32)[None, :]
        xa = np.ascontiguousarray(xn2_full[b].T)
        xq = np.ascontiguousarray(xa[:, qh * 2048:(qh + 1) * 2048])
        xv = np.ascontiguousarray(xn2_full[b])
        maps.append(dict(xq=xq, xa=xa, xv=xv, brevD=brev,
                         jmm=np.eye(128, dtype=np.float32)[::-1].copy()))
    return maps



def build_launch_E1(T_=TOWN):
    env = Env("E1")
    nc = env.nc
    CW = 512
    NCH = T_ // CW

    oat = nc.dram_tensor("oat", [D, T_], BF16, kind="ExternalInput").ap()
    x0c = nc.dram_tensor("x0c", [D, T_], F32, kind="ExternalInput").ap()
    woT = nc.dram_tensor("woT", [D, D], F32, kind="ExternalInput").ap()
    bob = nc.dram_tensor("bob", [128, KT], F32, kind="ExternalInput").ap()
    x1o = nc.dram_tensor("x1o", [D, T_], F32, kind="ExternalOutput").ap()
    xno = nc.dram_tensor("xno", [D, T_], BF16, kind="ExternalOutput").ap()
    oat_t = dram_tiles(oat)
    x0_t = dram_tiles(x0c)
    wo_t = dram_tiles(woT)
    x1_t = dram_tiles(x1o)
    xn_t = dram_tiles(xno)

    wo_sb = env.sbuf("wo_sb", [128, KT * D], BF16)
    oat_sb = env.sbuf("oat_sb", [128, KT * CW], BF16)
    x0p = env.sbuf("x0p", [128, 4 * CW], F32)
    x1p = env.sbuf("x1p", [128, 4 * CW], F32)
    xb = env.sbuf("xb", [128, KT * CW], BF16)
    sqb = env.sbuf("sqb", [128, KT * CW], BF16)
    row = env.sbuf("row", [1, 6 * CW], F32)
    rB = env.sbuf("rB", [128, CW], BF16)
    mB = env.sbuf("mB", [128, CW], BF16)
    tmpb = env.sbuf("tmpb", [128, CW], BF16)
    xnb = env.sbuf("xnb", [128, 2 * CW], BF16)
    bos = env.sbuf("bos", [128, KT], F32)
    psm = [env.psum("e1_ps0", [128, CW]), env.psum("e1_ps1", [128, CW])]
    psA = env.psum("e1_psA", [1, CW])
    psB = env.psum("e1_psB", [1, CW])
    psR = env.psum("e1_psR", [128, CW])
    psM = env.psum("e1_psM", [128, CW])
    ones_b = env.sbuf("ones_b", [128, 1], BF16)
    ones_f = env.sbuf("ones_f", [1, 128], F32)
    env.set_junk(env.sbuf("junk", [1, 1024], F32))

    s_w = env.sem("g_w")
    s_ws = env.sem("g_ws")
    s_oa = env.sem("g_oa")
    s_x0 = env.sem("g_x0")
    s_mm = env.sem("g_mm")
    s_ev = env.sem("g_ev")
    s_sx = env.sem("g_sx")
    s_sq = env.sem("g_sq")
    s_st = env.sem("g_st")
    s_rw = env.sem("g_rw")
    s_e2 = env.sem("g_e2")
    s_nm = env.sem("g_nm")
    s_sn = env.sem("g_sn")
    ini = env.sem("ini")

    def evn(t):
        return t + 1 + t // KT

    env.open_block()
    emit_consts(env, ones_b, ones_f)

    @env.block.gpsimd
    def _(g):
        for et in range(KT):
            s_w.inc(g.dma_start(out=wo_sb[:, et * D:(et + 1) * D], in_=wo_t[et][:, :]), 16)
    NW = s_w.n

    @env.block.sync
    def _(s):
        s_ws.inc(s.dma_start(out=bos[:, :], in_=bob[:, :]), 16)
    NWS = s_ws.n

    for ci in range(NCH):
        c0, c1 = ci * CW, (ci + 1) * CW

        @env.block.gpsimd
        def _(g, ci=ci, c0=c0, c1=c1):
            if ci > 0:
                g.wait_ge(s_mm.h, 16 * ci)
            for et in range(KT):
                s_oa.inc(g.dma_start(out=oat_sb[:, et * CW:(et + 1) * CW],
                                     in_=oat_t[et][:, c0:c1]), 16)
            for dt in range(KT):
                gidx = ci * KT + dt
                if gidx >= 4:
                    g.wait_ge(s_ev.h, evn(gidx - 4))
                s_x0.inc(g.dma_start(out=x0p[:, (gidx % 4) * CW:(gidx % 4 + 1) * CW],
                                     in_=x0_t[dt][:, c0:c1]), 16)

        for dt in range(KT):
            gidx = ci * KT + dt

            @env.block.tensor
            def _(t, ci=ci, dt=dt, gidx=gidx):
                if gidx == 0:
                    t.wait_ge(s_w.h, NW)
                t.wait_ge(s_oa.h, 16 * KT * (ci + 1))
                if gidx >= 2:
                    t.wait_ge(s_ev.h, evn(gidx - 2))
                for et in range(KT):
                    ins = t.matmul(psm[gidx % 2][:, :],
                                   wo_sb[:, et * D + dt * 128:et * D + (dt + 1) * 128],
                                   oat_sb[:, et * CW:(et + 1) * CW],
                                   start=(et == 0), stop=(et == KT - 1))
                s_mm.inc(ins)

            @env.block.vector
            def _(v, ci=ci, dt=dt, gidx=gidx):
                v.wait_ge(s_mm.h, gidx + 1)
                v.wait_ge(s_x0.h, 16 * (gidx + 1))
                if gidx == 0:
                    v.wait_ge(s_ws.h, NWS)
                if gidx >= 4:
                    v.wait_ge(s_sx.h, 16 * (gidx - 3))
                if dt == 0 and ci > 0:
                    v.wait_ge(s_sq.h, ci)
                env.vfence(v)
                sl = x1p[:, (gidx % 4) * CW:(gidx % 4 + 1) * CW]
                v.tensor_add(sl, psm[gidx % 2][:, :], x0p[:, (gidx % 4) * CW:(gidx % 4 + 1) * CW])
                ins = v.tensor_scalar(sl, sl, bos[:, dt:dt + 1], None, ALU.add)
                s_ev.inc(ins)

            @env.block.sync
            def _(s, ci=ci, dt=dt, gidx=gidx, c0=c0, c1=c1):
                s.wait_ge(s_ev.h, min(evn(gidx + 1), 17 * (ci + 1)))
                s_sx.inc(s.dma_start(out=x1_t[dt][:, c0:c1],
                                     in_=x1p[:, (gidx % 4) * CW:(gidx % 4 + 1) * CW]), 16)

        @env.block.vector
        def _(v):
            s_ev.inc(env.vfence(v))

        @env.block.scalar
        def _(sc, ci=ci):
            for dt in range(KT):
                gidx = ci * KT + dt
                sc.wait_ge(s_ev.h, min(evn(gidx + 1), 17 * (ci + 1)))
                sc.copy(xb[:, dt * CW:(dt + 1) * CW],
                        x1p[:, (gidx % 4) * CW:(gidx % 4 + 1) * CW])
                sc.activation(sqb[:, dt * CW:(dt + 1) * CW], xb[:, dt * CW:(dt + 1) * CW],
                              AF.Square)
            s_sq.inc(env.sfence(sc))

        @env.block.tensor
        def _(t, ci=ci):
            t.wait_ge(ini.h, 2)
            t.wait_ge(s_sq.h, ci + 1)
            if ci >= 1:
                t.wait_ge(s_rw.h, 3 * (ci - 1) + 1)
            for k in range(KT):
                t.matmul(psA[0:1, :], ones_b[:, :], xb[:, k * CW:(k + 1) * CW],
                         start=(k == 0), stop=(k == KT - 1))
            for k in range(KT):
                ins = t.matmul(psB[0:1, :], ones_b[:, :], sqb[:, k * CW:(k + 1) * CW],
                               start=(k == 0), stop=(k == KT - 1))
            s_st.inc(ins)

        @env.block.vector
        def _(v, ci=ci):
            v.wait_ge(s_st.h, 2 * ci + 1)
            env.vfence(v)
            v.tensor_scalar(row[0:1, 0:CW], psA[0:1, :], 1.0 / D, None, ALU.mult)
            v.tensor_scalar(row[0:1, CW:2 * CW], psB[0:1, :], 1.0 / D, None, ALU.mult)
            v.tensor_mul(row[0:1, 4 * CW:5 * CW], row[0:1, 0:CW], row[0:1, 0:CW])
            v.tensor_sub(row[0:1, 2 * CW:3 * CW], row[0:1, CW:2 * CW], row[0:1, 4 * CW:5 * CW])
            v.tensor_scalar(row[0:1, 2 * CW:3 * CW], row[0:1, 2 * CW:3 * CW], EPS, None, ALU.add)
            ins = v.reciprocal(row[0:1, 2 * CW:3 * CW], row[0:1, 2 * CW:3 * CW])
            s_rw.inc(ins)
            v.wait_ge(s_rw.h, 3 * ci + 2)
            v.tensor_mul(row[0:1, 4 * CW:5 * CW], row[0:1, 0:CW], row[0:1, 3 * CW:4 * CW])
            s_rw.inc(env.vfence(v))

        @env.block.scalar
        def _(sc, ci=ci):
            sc.wait_ge(s_rw.h, 3 * ci + 1)
            env.sfence(sc)
            sc.activation(row[0:1, 3 * CW:4 * CW], row[0:1, 2 * CW:3 * CW], AF.Sqrt)
            s_rw.inc(env.sfence(sc))

        @env.block.tensor
        def _(t, ci=ci):
            t.wait_ge(s_rw.h, 3 * ci + 3)
            if ci >= 1:
                t.wait_ge(s_e2.h, 2 * ci)
            t.matmul(psR[:, :], ones_f[:, :], row[0:1, 3 * CW:4 * CW], start=True, stop=True)
            ins = t.matmul(psM[:, :], ones_f[:, :], row[0:1, 4 * CW:5 * CW], start=True, stop=True)
            s_st.inc(ins)

        @env.block.vector
        def _(v, ci=ci):
            v.wait_ge(s_st.h, 2 * ci + 2)
            env.vfence(v)
            s_e2.inc(v.tensor_copy(rB[:, :], psR[:, :]))
            s_e2.inc(v.tensor_copy(mB[:, :], psM[:, :]))
            for k in range(KT):
                gk = ci * KT + k
                if gk >= 2:
                    v.wait_ge(s_sn.h, 16 * (gk - 1))
                v.tensor_mul(tmpb[:, :], xb[:, k * CW:(k + 1) * CW], rB[:, :])
                ins = v.tensor_sub(xnb[:, (gk % 2) * CW:(gk % 2 + 1) * CW],
                                   tmpb[:, :], mB[:, :])
                s_nm.inc(ins)
            s_nm.inc(env.vfence(v))

        @env.block.sync
        def _(s, ci=ci, c0=c0, c1=c1):
            for k in range(KT):
                gk = ci * KT + k
                s.wait_ge(s_nm.h, min(gk + 2 + gk // KT, 17 * (ci + 1)))
                s_sn.inc(s.dma_start(out=xn_t[k][:, c0:c1],
                                     in_=xnb[:, (gk % 2) * CW:(gk % 2 + 1) * CW]), 16)

    @env.block.sync
    def _(s):
        s.wait_ge(s_sx.h, s_sx.n)
        s.wait_ge(s_sn.h, s_sn.n)

    return env.close()


def host_prep_E1(inputs, oat1_full):
    Wo = inputs["L1_attn_Wo"]
    bo = inputs["L1_attn_bo"]
    woT = np.ascontiguousarray(Wo.T).astype(np.float32)
    bob = np.ascontiguousarray(bo.reshape(KT, 128).T).astype(np.float32)
    maps = []
    for c in range(NCORE):
        cols = np.concatenate([np.arange(b_ * S + SOWN * c, b_ * S + SOWN * (c + 1))
                               for b_ in range(B)])
        oatc = np.ascontiguousarray(oat1_full[:, cols])
        maps.append(dict(oat=oatc, woT=woT, bob=bob))
    return maps



def build_launch_E2a(T_=TOWN):
    env = Env("E2a")
    nc = env.nc
    CW = 512
    NCH = T_ // CW
    IN8 = 8192

    xn3 = nc.dram_tensor("xn3", [D, T_], BF16, kind="ExternalInput").ap()
    wupT = nc.dram_tensor("wupT", [D, IN8], F32, kind="ExternalInput").ap()
    wgtT = nc.dram_tensor("wgtT", [D, IN8], F32, kind="ExternalInput").ap()
    supm = nc.dram_tensor("supm", [128, KT], F32, kind="ExternalInput").ap()
    wsub = nc.dram_tensor("wsub", [128, 64], F32, kind="ExternalInput").ap()
    sudb = nc.dram_tensor("sudb", [128, 64], F32, kind="ExternalInput").ap()
    wsgb = nc.dram_tensor("wsgb", [128, 64], F32, kind="ExternalInput").ap()
    hmat = nc.dram_tensor("hmat", [128, 128], F32, kind="ExternalInput").ap()
    h2d = nc.dram_tensor("h2d", [IN8, T_], BF16, kind="ExternalOutput").ap()
    xn_t = dram_tiles(xn3)
    wup_t = dram_tiles(wupT)
    wgt_t = dram_tiles(wgtT)
    h2_t = h2d.rearrange("(o p) t -> o p t", p=128)

    AR1 = env.sbuf("AR1", [128, 32768], BF16)
    ARf = AR1.ap().bitcast(F32)
    yb = env.sbuf("yb", [128, KT * T_], BF16)
    xch = env.sbuf("xch", [128, KT * CW], BF16)
    wub = env.sbuf("wub", [128, 2 * KT * 128], BF16)
    wgf = env.sbuf("wgf", [128, 2 * KT * 128], F32)
    wgb = env.sbuf("wgb", [128, 2 * KT * 128], BF16)
    sgp = env.sbuf("sgp", [128, 2 * CW], BF16)
    ftmp = env.sbuf("ftmp", [128, CW], F32)
    h2blk = env.sbuf("h2blk", [128, 2 * T_], BF16)
    sup = env.sbuf("sup", [128, KT], F32)
    wsu_s = env.sbuf("wsu_s", [128, 64], F32)
    sud_s = env.sbuf("sud_s", [128, 64], F32)
    wsg_s = env.sbuf("wsg_s", [128, 64], F32)
    hm_sb = env.sbuf("hm_sb", [128, 128], BF16)
    psH = [env.psum("ea_h0", [128, CW]), env.psum("ea_h1", [128, CW])]
    psU = [env.psum("ea_u0", [128, CW]), env.psum("ea_u1", [128, CW])]
    psG = [env.psum("ea_g0", [128, CW]), env.psum("ea_g1", [128, CW])]
    env.set_junk(env.sbuf("junk", [1, 1024], F32))

    s_w = env.sem("h_w")
    s_iv = env.sem("h_iv")
    s_xc = env.sem("h_xc")
    s_su = env.sem("h_su")
    s_hm = env.sem("h_hm")
    s_he = env.sem("h_he")
    s_yc = env.sem("h_yc")
    s_wu = env.sem("h_wu")
    s_tq = env.sem("h_tq")
    s_gc = env.sem("h_gc")
    s_um = env.sem("h_um")
    s_sg = env.sem("h_sg")
    s_hv = env.sem("h_hv")
    s_hs = env.sem("h_hs")

    env.open_block()

    @env.block.gpsimd
    def _(g):
        s_w.inc(g.dma_start(out=hm_sb[:, :], in_=hmat[:, :]), 16)
        s_w.inc(g.dma_start(out=sup[:, :], in_=supm[:, :]), 16)
        s_w.inc(g.dma_start(out=wsu_s[:, :], in_=wsub[:, :]), 16)
        s_w.inc(g.dma_start(out=sud_s[:, :], in_=sudb[:, :]), 16)
        s_w.inc(g.dma_start(out=wsg_s[:, :], in_=wsgb[:, :]), 16)
    NW = s_w.n

    @env.block.vector
    def _(v):
        v.wait_ge(s_w.h, NW)
        env.vfence(v)
        v.tensor_mul(wsu_s[:, :], wsu_s[:, :], sud_s[:, :])
        env.vspace(v)
        v.tensor_scalar(wsu_s[:, :], wsu_s[:, :], 1.0 / np.sqrt(2048.0), None, ALU.mult)
        v.tensor_scalar(wsg_s[:, :], wsg_s[:, :], 1.0 / np.sqrt(2048.0), None, ALU.mult)
        s_iv.inc(env.vfence(v))

    for c in range(NCH):
        @env.block.sync
        def _(s, c=c):
            if c > 0:
                s.wait_ge(s_hm.h, 16 * c)
            for k in range(KT):
                s_xc.inc(s.dma_start(out=xch[:, k * CW:(k + 1) * CW],
                                     in_=xn_t[k][:, c * CW:(c + 1) * CW]), 16)

        @env.block.vector
        def _(v, c=c):
            v.wait_ge(s_xc.h, 16 * 16 * (c + 1))
            if c == 0:
                v.wait_ge(s_iv.h, 1)
            env.vfence(v)
            for k in range(KT):
                v.tensor_scalar(xch[:, k * CW:(k + 1) * CW], xch[:, k * CW:(k + 1) * CW],
                                sup[:, k:k + 1], None, ALU.mult)
            s_su.inc(env.vfence(v))

        @env.block.tensor
        def _(t, c=c):
            t.wait_ge(s_su.h, c + 1)
            for k in range(KT):
                if 16 * c + k >= 2:
                    t.wait_ge(s_he.h, 16 * c + k - 1)
                ins = t.matmul(psH[k % 2][:, :], hm_sb[:, :], xch[:, k * CW:(k + 1) * CW],
                               start=True, stop=True)
                s_hm.inc(ins)

        @env.block.scalar
        def _(sc, c=c):
            for k in range(KT):
                sc.wait_ge(s_hm.h, 16 * c + k + 1)
                ins = sc.copy(ARf[:, k * CW:(k + 1) * CW], psH[k % 2][:, :])
                s_he.inc(ins)

        @env.block.vector
        def _(v, c=c):
            v.wait_ge(s_he.h, 16 * (c + 1))
            env.vfence(v)
            A = [ARf[:, k * CW:(k + 1) * CW] for k in range(KT)]
            Bv = [ARf[:, (16 + k) * CW:(17 + k) * CW] for k in range(KT)]
            for li, s in enumerate((1, 2, 4, 8)):
                src, dst = (A, Bv) if li % 2 == 0 else (Bv, A)
                if li == 3:
                    dst = [yb.ap()[:, k * T_ + c * CW:k * T_ + (c + 1) * CW]
                           for k in range(KT)]
                for t in range(KT):
                    if t & s == 0:
                        v.tensor_add(dst[t], src[t], src[t + s])
                    else:
                        v.tensor_sub(dst[t], src[t - s], src[t])
            s_yc.inc(env.vfence(v))

    for ot in range(64):
        @env.block.gpsimd
        def _(g, ot=ot):
            if ot >= 2:
                g.wait_ge(s_um.h, 4 * (ot - 1))
            for et in range(KT):
                s_wu.inc(g.dma_start(
                    out=wub[:, (ot % 2) * 2048 + et * 128:(ot % 2) * 2048 + (et + 1) * 128],
                    in_=wup_t[et][:, ot * 128:(ot + 1) * 128]), 16)

        @env.block.scalar
        def _(sc, ot=ot):
            def issue_loads(o2):
                for et in range(KT):
                    s_tq.inc(sc.dma_start(
                        out=wgf[:, (o2 % 2) * 2048 + et * 128:(o2 % 2) * 2048 + (et + 1) * 128],
                        in_=wgt_t[et][:, o2 * 128:(o2 + 1) * 128]), 16)
            if ot == 0:
                issue_loads(0)
                issue_loads(1)
            sc.wait_ge(s_tq.h, 16 * 16 * (ot + 1))
            if ot >= 2:
                sc.wait_ge(s_um.h, 4 * (ot - 1))
            for et in range(KT):
                sl = slice((ot % 2) * 2048 + et * 128, (ot % 2) * 2048 + (et + 1) * 128)
                sc.copy(wgb[:, sl], wgf[:, sl])
            s_gc.inc(env.sfence(sc))
            if ot + 2 < 64:
                issue_loads(ot + 2)

        for c in range(NCH):
            uc = ot * 4 + c

            @env.block.tensor
            def _(t, ot=ot, c=c, uc=uc):
                if uc == 0:
                    t.wait_ge(s_yc.h, NCH)
                t.wait_ge(s_wu.h, 16 * 16 * (ot + 1))
                t.wait_ge(s_gc.h, ot + 1)
                if uc >= 2:
                    t.wait_ge(s_hv.h, uc - 1)
                for et in range(KT):
                    t.matmul(psU[uc % 2][:, :],
                             wub[:, (ot % 2) * 2048 + et * 128:(ot % 2) * 2048 + (et + 1) * 128],
                             yb[:, et * T_ + c * CW:et * T_ + (c + 1) * CW],
                             start=(et == 0), stop=(et == KT - 1))
                for et in range(KT):
                    ins = t.matmul(psG[uc % 2][:, :],
                                   wgb[:, (ot % 2) * 2048 + et * 128:(ot % 2) * 2048 + (et + 1) * 128],
                                   yb[:, et * T_ + c * CW:et * T_ + (c + 1) * CW],
                                   start=(et == 0), stop=(et == KT - 1))
                s_um.inc(ins)

            @env.block.scalar
            def _(sc, ot=ot, uc=uc):
                sc.wait_ge(s_um.h, uc + 1)
                if uc >= 2:
                    sc.wait_ge(s_hv.h, uc - 1)
                ins = sc.activation(sgp[:, (uc % 2) * CW:(uc % 2 + 1) * CW],
                                    psG[uc % 2][:, :], AF.Silu,
                                    scale=wsg_s[:, ot:ot + 1])
                s_sg.inc(ins)

            @env.block.vector
            def _(v, ot=ot, c=c, uc=uc):
                v.wait_ge(s_sg.h, uc + 1)
                if c == 0 and ot >= 2:
                    v.wait_ge(s_hs.h, 16 * (ot - 1))
                env.vfence(v)
                v.tensor_scalar(ftmp[:, :], psU[uc % 2][:, :], wsu_s[:, ot:ot + 1],
                                None, ALU.mult)
                ins = v.tensor_mul(h2blk[:, (ot % 2) * T_ + c * CW:
                                         (ot % 2) * T_ + (c + 1) * CW],
                                   ftmp[:, :], sgp[:, (uc % 2) * CW:(uc % 2 + 1) * CW])
                if c == NCH - 1:
                    ins = env.vfence(v)
                s_hv.inc(ins)

        @env.block.sync
        def _(s, ot=ot):
            s.wait_ge(s_hv.h, 4 * (ot + 1))
            s_hs.inc(s.dma_start(out=h2_t[ot],
                                 in_=h2blk[:, (ot % 2) * T_:(ot % 2 + 1) * T_]), 16)

    @env.block.sync
    def _(s):
        s.wait_ge(s_hs.h, s_hs.n)

    return env.close()



def build_launch_E2b(T_=TOWN):
    env = Env("E2b")
    nc = env.nc
    CW = 1024
    NCH = T_ // CW
    IN8 = 8192
    NOT = 64

    h2d = nc.dram_tensor("h2d", [IN8, T_], BF16, kind="ExternalInput").ap()
    x3c = nc.dram_tensor("x3c", [D, T_], F32, kind="ExternalInput").ap()
    wdnT = nc.dram_tensor("wdnT", [IN8, D], F32, kind="ExternalInput").ap()
    wsdb = nc.dram_tensor("wsdb", [128, KT], F32, kind="ExternalInput").ap()
    hmat = nc.dram_tensor("hmat", [128, 128], F32, kind="ExternalInput").ap()
    x4o = nc.dram_tensor("x4o", [D, T_], F32, kind="ExternalOutput").ap()
    h2_t = h2d.rearrange("(o p) t -> o p t", p=128)
    wdn_t = wdnT.rearrange("(o p) t -> o p t", p=128)
    x3_t = dram_tiles(x3c)
    x4_t = dram_tiles(x4o)

    Z = env.sbuf("Z", [128, NOT * CW], BF16)
    tb = env.sbuf("tb", [128, CW], BF16)
    wdb = env.sbuf("wdb", [128, 2 * NOT * 128], BF16)
    x3p = env.sbuf("x3p", [128, 4 * 512], F32)
    x4p = env.sbuf("x4p", [128, 4 * 512], F32)
    ftmp = env.sbuf("ftmp", [128, 512], F32)
    wsd_s = env.sbuf("wsd_s", [128, KT], F32)
    hm_sb = env.sbuf("hm_sb", [128, 128], BF16)
    psH = [env.psum("eb_h0", [128, 512]), env.psum("eb_h1", [128, 512])]
    psD = [env.psum("eb_d0", [128, 512]), env.psum("eb_d1", [128, 512])]
    env.set_junk(env.sbuf("junk", [1, 1024], F32))

    s_w = env.sem("i_w")
    s_iv = env.sem("i_iv")
    s_zl = env.sem("i_zl")
    s_bf = env.sem("i_bf")
    s_hm = env.sem("i_hm")
    s_he = env.sem("i_he")
    s_wd = env.sem("i_wd")
    s_dm = env.sem("i_dm")
    s_de = env.sem("i_de")
    s_xl = env.sem("i_xl")
    s_xs = env.sem("i_xs")

    env.open_block()

    @env.block.gpsimd
    def _(g):
        s_w.inc(g.dma_start(out=hm_sb[:, :], in_=hmat[:, :]), 16)
        s_w.inc(g.dma_start(out=wsd_s[:, :], in_=wsdb[:, :]), 16)
    NW = s_w.n

    @env.block.vector
    def _(v):
        v.wait_ge(s_w.h, NW)
        env.vfence(v)
        v.tensor_scalar(wsd_s[:, :], wsd_s[:, :], 1.0 / np.sqrt(8192.0), None, ALU.mult)
        s_iv.inc(env.vfence(v))

    for cc in range(NCH):
        DM0 = s_dm.n

        @env.block.sync
        def _(s, cc=cc, DM0=DM0):
            if cc > 0:
                s.wait_ge(s_dm.h, DM0)
            for t in range(NOT):
                s_zl.inc(s.dma_start(out=Z[:, t * CW:(t + 1) * CW],
                                     in_=h2_t[t][:, cc * CW:(cc + 1) * CW]), 16)

        @env.block.vector
        def _(v, cc=cc):
            v.wait_ge(s_zl.h, 16 * NOT * (cc + 1))
            env.vfence(v)
            for s in (1, 2, 4, 8, 16, 32):
                for t in range(NOT):
                    if t & s == 0:
                        u = t + s
                        v.tensor_sub(tb[:, :], Z[:, t * CW:(t + 1) * CW],
                                     Z[:, u * CW:(u + 1) * CW])
                        v.tensor_add(Z[:, t * CW:(t + 1) * CW],
                                     Z[:, t * CW:(t + 1) * CW], Z[:, u * CW:(u + 1) * CW])
                        v.tensor_copy(Z[:, u * CW:(u + 1) * CW], tb[:, :])
            s_bf.inc(env.vfence(v))

        @env.block.tensor
        def _(t, cc=cc):
            t.wait_ge(s_bf.h, cc + 1)
            if cc == 0:
                t.wait_ge(s_iv.h, 1)
            for i in range(2 * NOT):
                if 128 * cc + i >= 2:
                    t.wait_ge(s_he.h, 129 * cc + i - 1)
                tt_ = i // 2
                th = i % 2
                ins = t.matmul(psH[i % 2][:, :], hm_sb[:, :],
                               Z[:, tt_ * CW + th * 512:tt_ * CW + (th + 1) * 512],
                               start=True, stop=True)
                s_hm.inc(ins)

        @env.block.scalar
        def _(sc, cc=cc):
            for i in range(2 * NOT):
                sc.wait_ge(s_hm.h, 128 * cc + i + 1)
                tt_ = i // 2
                th = i % 2
                ins = sc.copy(Z[:, tt_ * CW + th * 512:tt_ * CW + (th + 1) * 512],
                              psH[i % 2][:, :])
                s_he.inc(ins)
            s_he.inc(env.sfence(sc))

        for dt in range(KT):
            gd = cc * KT + dt

            @env.block.gpsimd
            def _(g, cc=cc, dt=dt, gd=gd):
                if gd >= 2:
                    g.wait_ge(s_dm.h, 2 * (gd - 1))
                for it in range(NOT):
                    s_wd.inc(g.dma_start(
                        out=wdb[:, (gd % 2) * 8192 + it * 128:(gd % 2) * 8192 + (it + 1) * 128],
                        in_=wdn_t[it][:, dt * 128:(dt + 1) * 128]), 16)

            @env.block.scalar
            def _(sc, cc=cc, dt=dt, gd=gd):
                if gd >= 2:
                    sc.wait_ge(s_de.h, 3 * (gd - 1))
                for th in range(2):
                    s_xl.inc(sc.dma_start(
                        out=x3p[:, ((2 * gd + th) % 4) * 512:((2 * gd + th) % 4 + 1) * 512],
                        in_=x3_t[dt][:, cc * CW + th * 512:cc * CW + (th + 1) * 512]), 16)

            @env.block.tensor
            def _(t, cc=cc, dt=dt, gd=gd):
                t.wait_ge(s_he.h, 129 * (cc + 1))
                t.wait_ge(s_wd.h, 16 * NOT * (gd + 1))
                for th in range(2):
                    if gd >= 1:
                        t.wait_ge(s_de.h, 3 * (gd - 1) + th + 1)
                    for it in range(NOT):
                        ins = t.matmul(psD[th][:, :],
                                       wdb[:, (gd % 2) * 8192 + it * 128:(gd % 2) * 8192 + (it + 1) * 128],
                                       Z[:, it * CW + th * 512:it * CW + (th + 1) * 512],
                                       start=(it == 0), stop=(it == NOT - 1))
                    s_dm.inc(ins)

            @env.block.vector
            def _(v, cc=cc, dt=dt, gd=gd):
                v.wait_ge(s_dm.h, 2 * gd + 1)
                v.wait_ge(s_xl.h, 32 * (gd + 1))
                if gd >= 2:
                    v.wait_ge(s_xs.h, 32 * (gd - 1))
                env.vfence(v)
                for th in range(2):
                    if th == 1:
                        v.wait_ge(s_dm.h, 2 * gd + 2)
                    v.tensor_scalar(ftmp[:, :], psD[th][:, :], wsd_s[:, dt:dt + 1],
                                    None, ALU.mult)
                    ins = v.tensor_add(x4p[:, ((2 * gd + th) % 4) * 512:((2 * gd + th) % 4 + 1) * 512],
                                       ftmp[:, :], x3p[:, ((2 * gd + th) % 4) * 512:((2 * gd + th) % 4 + 1) * 512])
                    s_de.inc(ins)
                s_de.inc(env.vfence(v))

            @env.block.sync
            def _(s, cc=cc, dt=dt, gd=gd):
                s.wait_ge(s_de.h, 3 * (gd + 1))
                for th in range(2):
                    s_xs.inc(s.dma_start(
                        out=x4_t[dt][:, cc * CW + th * 512:cc * CW + (th + 1) * 512],
                        in_=x4p[:, ((2 * gd + th) % 4) * 512:((2 * gd + th) % 4 + 1) * 512]), 16)

    @env.block.sync
    def _(s):
        s.wait_ge(s_xs.h, s_xs.n)

    return env.close()


def host_prep_E2(inputs):
    W_up, W_gate, W_down = inputs["L1_ffn_W_up"], inputs["L1_ffn_W_gate"], inputs["L1_ffn_W_down"]
    su_ug, su_dp = inputs["L1_ffn_SU_ug"], inputs["L1_ffn_SU_dp"]
    ws_ug, ws_dp = inputs["L1_ffn_Wscale_ug"], inputs["L1_ffn_Wscale_dp"]
    wupT = np.ascontiguousarray(W_up.T).astype(np.float32)
    wgtT = np.ascontiguousarray(W_gate.T).astype(np.float32)
    wdnT = np.ascontiguousarray(W_down.T).astype(np.float32)
    supm = np.ascontiguousarray(su_ug.reshape(KT, 128).T).astype(np.float32)
    wsub = np.ascontiguousarray(ws_ug[:8192].reshape(64, 128).T).astype(np.float32)
    sudb = np.ascontiguousarray(su_dp.reshape(64, 128).T).astype(np.float32)
    wsgb = np.ascontiguousarray(ws_ug[8192:].reshape(64, 128).T).astype(np.float32)
    wsdb = np.ascontiguousarray(ws_dp.reshape(KT, 128).T).astype(np.float32)
    i = np.arange(128)
    hmat = np.where((i[:, None] & i[None, :] & 0).astype(bool), 0, 0)
    pc = np.zeros((128, 128), dtype=np.int64)
    for a in range(128):
        for b_ in range(128):
            pc[a, b_] = bin(a & b_).count("1")
    hmat = np.where(pc % 2 == 0, 1.0, -1.0).astype(np.float32)
    mA = dict(wupT=wupT, wgtT=wgtT, supm=supm, wsub=wsub, sudb=sudb, wsgb=wsgb,
              hmat=hmat)
    mB = dict(wdnT=wdnT, wsdb=wsdb, hmat=hmat)
    return [dict(mA) for _ in range(NCORE)], [dict(mB) for _ in range(NCORE)]



def _shard_ch_major(x):
    out = []
    for c in range(NCORE):
        xs = x[:, SOWN * c:SOWN * (c + 1), :]
        out.append(np.ascontiguousarray(xs.transpose(2, 0, 1).reshape(D, TOWN)))
    return out


def kernel(**inputs):
    hs = np.asarray(inputs["L0_attn_hidden_states"], dtype=np.float32)

    ncA = build_launch_A()
    shards = _shard_ch_major(hs)
    rA = run_bass_kernel_spmd(ncA, [dict(x=s) for s in shards], core_ids=list(range(NCORE)))
    xn0_full = assemble_xn0([np.asarray(rA.results[c]["xn"]) for c in range(NCORE)])

    ncB = build_launch_B()
    mapsB = host_prep_B(inputs)
    for m in mapsB:
        m["xn0"] = xn0_full
    rB = run_bass_kernel_spmd(ncB, mapsB, core_ids=list(range(NCORE)))
    oat_full = np.zeros((D, B * S), dtype=BFD)
    for c in range(NCORE):
        oat_full[c * 256:(c + 1) * 256] = unpack_oslc(rB.results[c]["oslc"]).astype(BFD)

    ncC1 = build_launch_C1()
    mapsC1 = host_prep_C1(inputs, oat_full)
    rC1 = run_bass_kernel_spmd(ncC1, mapsC1, core_ids=list(range(NCORE)))
    x1s = [np.asarray(rC1.results[c]["x1o"]) for c in range(NCORE)]
    xcs = [np.asarray(rC1.results[c]["xco"]) for c in range(NCORE)]

    ncC2 = build_launch_C2()
    mapsC2 = host_prep_C2(inputs)
    for c in range(NCORE):
        mapsC2[c]["xco"] = xcs[c]
    rC2 = run_bass_kernel_spmd(ncC2, mapsC2, core_ids=list(range(NCORE)))

    ncC3 = build_launch_C3()
    mapsC3 = host_prep_C3(inputs)
    for c in range(NCORE):
        mapsC3[c]["xco"] = xcs[c]
        mapsC3[c]["onc"] = onrm_to_ch_major(np.asarray(rC2.results[c]["onrm"]))
        mapsC3[c]["x1o"] = x1s[c]
    rC3 = run_bass_kernel_spmd(ncC3, mapsC3, core_ids=list(range(NCORE)))
    x2s = [np.asarray(rC3.results[c]["x2o"]) for c in range(NCORE)]
    xn2s = [np.asarray(rC3.results[c]["xn2o"]) for c in range(NCORE)]

    xn2_full = np.zeros((B, S, D), dtype=BFD)
    for c in range(NCORE):
        blk = np.asarray(xn2s[c]).reshape(D, B, SOWN)
        for b_ in range(B):
            xn2_full[b_, SOWN * c:SOWN * (c + 1), :] = blk[:, b_, :].T
    ncD = build_launch_D()
    mapsD = host_prep_D(xn2_full)
    Ws = np.ascontiguousarray(inputs["L1_attn_Ws"]).astype(np.float32)
    Wa = np.ascontiguousarray(inputs["L1_attn_Wa"]).astype(np.float32)
    for m in mapsD:
        m["wsw"] = Ws
        m["waw"] = Wa
    rD = run_bass_kernel_spmd(ncD, mapsD, core_ids=list(range(NCORE)))

    oat1_full = np.zeros((D, B * S), dtype=BFD)
    for c in range(NCORE):
        b_, qh = c // 2, c % 2
        ot = np.asarray(rD.results[c]["otd"])
        oat1_full[:, b_ * S + qh * 2048:b_ * S + (qh + 1) * 2048] = ot.T
    ncE1 = build_launch_E1()
    mapsE1 = host_prep_E1(inputs, oat1_full)
    for c in range(NCORE):
        mapsE1[c]["x0c"] = x2s[c]
    rE1 = run_bass_kernel_spmd(ncE1, mapsE1, core_ids=list(range(NCORE)))
    x3s = [np.asarray(rE1.results[c]["x1o"]) for c in range(NCORE)]
    xn3s = [np.asarray(rE1.results[c]["xno"]) for c in range(NCORE)]

    mA, mB = host_prep_E2(inputs)
    ncE2a = build_launch_E2a()
    for c in range(NCORE):
        mA[c]["xn3"] = xn3s[c]
    rE2a = run_bass_kernel_spmd(ncE2a, mA, core_ids=list(range(NCORE)))
    ncE2b = build_launch_E2b()
    for c in range(NCORE):
        mB[c]["h2d"] = np.asarray(rE2a.results[c]["h2d"])
        mB[c]["x3c"] = x3s[c]
    rE2b = run_bass_kernel_spmd(ncE2b, mB, core_ids=list(range(NCORE)))

    out = np.zeros((B, S, D), dtype=np.float32)
    for c in range(NCORE):
        x4 = np.asarray(rE2b.results[c]["x4o"]).reshape(D, B, SOWN)
        for b_ in range(B):
            out[b_, SOWN * c:SOWN * (c + 1), :] = x4[:, b_, :].T
    return out
